# version 02
# baseline (speedup 1.0000x reference)
import numpy as np
import concourse.bass as bass
import concourse.mybir as mybir
from concourse.bass_utils import run_bass_kernel_spmd

F32 = mybir.dt.float32
F16 = mybir.dt.float16
BF16 = mybir.dt.bfloat16
AF = mybir.ActivationFunctionType
ALU = mybir.AluOpType
AXL = mybir.AxisListType

NC = 8
B = 2
D = 1024
H_FFN = 4096
NHEAD = 16
DHEAD = 64
CHUNK = 256
LORA_R = 16
N_SINKS = 4
WINDOW = 512
CTX = N_SINKS + WINDOW
QBLK = 640
NQT = QBLK // 128
LN_EPS = 1e-5
TTT_LR = 0.02
SCALING = 2.0
NEG = -1e30
TWOPI = float(2.0 / np.pi)
C23 = 8388608.0

P = 128
DCH = D // P
HCH = H_FFN // P
CG = 4

SEQ = SEQ_PER = TOK = NPT = NPB = NCHB = None


def set_dims(seq):
    global SEQ, SEQ_PER, TOK, NPT, NPB, NCHB
    SEQ = seq
    SEQ_PER = seq // NC
    TOK = 2 * SEQ_PER
    NPT = TOK // P
    NPB = NPT // 2
    NCHB = SEQ_PER // CHUNK


set_dims(8192)


class Ctx:
    ENGS = ("vector", "scalar", "tensor", "gpsimd", "sync")

    def __init__(self, nc, block):
        self.nc = nc
        self.block = block
        self.sems = {}
        self.cnt = {}
        self.waited = {e: {} for e in self.ENGS}
        self.lastw = {}
        self.readers = {}
        for e in self.ENGS:
            self._mk("S_" + e)
        self.NDL = 8
        for i in range(self.NDL):
            self._mk(f"S_dma{i}")
        self._mk("S_dmag")
        self._mk("S_coll")
        self._lane = 0

    def _mk(self, name):
        self.sems[name] = self.nc.alloc_semaphore(name)
        self.cnt[name] = 0

    def _need(self, eng, reads, writes):
        need = {}

        def add(s, c):
            if need.get(s, 0) < c:
                need[s] = c
        for r in reads:
            lw = self.lastw.get(r)
            if lw:
                add(*lw)
        for r in writes:
            lw = self.lastw.get(r)
            if lw:
                add(*lw)
            for s, c in self.readers.get(r, {}).items():
                add(s, c)
        return [(s, c) for s, c in need.items() if self.waited[eng].get(s, 0) < c]

    def _emit(self, eng, builds, reads, writes, sem, inc):
        waits = self._need(eng, reads, writes)
        self.cnt[sem] += inc
        tgt = self.cnt[sem]
        semh, sems = self.sems[sem], self.sems

        def body(e):
            for s, c in waits:
                e.wait_ge(sems[s], c)
            last = None
            for b in builds:
                last = b(e)
            last.then_inc(semh, inc)
        getattr(self.block, eng)(body)
        w = self.waited[eng]
        for s, c in waits:
            if w.get(s, 0) < c:
                w[s] = c

        for r in writes:
            self.lastw[r] = (sem, tgt)
            self.readers.pop(r, None)
        for r in reads:
            d = self.readers.setdefault(r, {})
            if d.get(sem, 0) < tgt:
                d[sem] = tgt

    def op(self, eng, build, reads=(), writes=(), dma=False, coll=False):
        if coll:
            sem, inc = "S_coll", 1
        elif dma:
            if eng == "gpsimd":
                sem, inc = "S_dmag", 16
            else:
                sem = f"S_dma{self._lane}"
                self._lane = (self._lane + 1) % self.NDL
                inc = 16
        else:
            sem, inc = "S_" + eng, 1
        if inc == 16 or coll:
            prev = self.cnt[sem]
            if prev > 0 and self.waited[eng].get(sem, 0) < prev:
                semh = self.sems[sem]

                def b2(e, _b=build, _s=semh, _p=prev):
                    e.wait_ge(_s, _p)
                    return _b(e)
                build = b2
                self.waited[eng][sem] = prev
        self._emit(eng, [build], reads, writes, sem, inc)

    def group(self, eng, builds, reads=(), writes=()):
        self._emit(eng, builds, reads, writes, "S_" + eng, 1)

    def barrier(self):
        snap = [(s, c) for s, c in self.cnt.items() if c > 0]
        sems = self.sems
        for eng in self.ENGS:
            todo = [(s, c) for s, c in snap if self.waited[eng].get(s, 0) < c]
            if not todo:
                continue

            def body(e, todo=tuple(todo)):
                for s, c in todo:
                    e.wait_ge(sems[s], c)
            getattr(self.block, eng)(body)
            for s, c in todo:
                self.waited[eng][s] = c


class NS:
    pass


def ln_tile(cx, bf, x_ap, out_ap, xkey, okey, gamma=None, beta=None, gkey=None,
            nrow=P, F=D):
    inv_n = 1.0 / F
    st = bf.STAT
    cx.op("vector", lambda v: v.reduce_sum(st[:nrow, 0:1], x_ap, axis=AXL.X),
          reads=[xkey], writes=["STAT"])
    cx.op("vector", lambda v: v.tensor_scalar(out=st[:nrow, 2:3], in0=st[:nrow, 0:1],
                                              scalar1=inv_n, scalar2=None, op0=ALU.mult),
          reads=["STAT"], writes=["STAT"])
    cx.op("vector", lambda v: v.scalar_tensor_tensor(
        out=bf.LSCR[:nrow, :F], in0=x_ap, scalar=st[:nrow, 2:3], in1=x_ap,
        op0=ALU.subtract, op1=ALU.mult, accum_out=st[:nrow, 1:2]),
        reads=[xkey, "STAT"], writes=["STAT", "LSCR"])
    cx.op("scalar", lambda s: s.activation(out=st[:nrow, 3:4], in_=st[:nrow, 1:2],
                                           func=AF.Sqrt, scale=inv_n,
                                           bias=bf.EPSC[:nrow, :]),
          reads=["STAT", "SMALL"], writes=["STAT"])
    cx.op("vector", lambda v: v.reciprocal(out=st[:nrow, 3:4], in_=st[:nrow, 3:4]),
          reads=["STAT"], writes=["STAT"])
    cx.op("vector", lambda v: v.tensor_scalar(out=out_ap, in0=x_ap,
                                              scalar1=st[:nrow, 2:3],
                                              scalar2=st[:nrow, 3:4],
                                              op0=ALU.subtract, op1=ALU.mult),
          reads=[xkey, "STAT"], writes=[okey])
    if gamma is not None:
        cx.op("vector", lambda v: v.tensor_mul(out=out_ap, in0=out_ap, in1=gamma),
              reads=[okey, gkey], writes=[okey])
        cx.op("vector", lambda v: v.tensor_add(out=out_ap, in0=out_ap, in1=beta),
              reads=[okey, gkey], writes=[okey])


def tr128(cx, bf, src_ap, dst_ap, skey, dkey, nout=None):
    npart = src_ap.shape[0]
    ncols = src_ap.shape[-1]
    cx.op("tensor", lambda t: t.transpose(bf.PT[:ncols, :npart], src_ap,
                                          bf.IDENT[:npart, :npart]),
          reads=[skey, "IDENT"], writes=["pt"])
    cx.op("vector", lambda v: v.tensor_copy(out=dst_ap, in_=bf.PT[:ncols, :npart]),
          reads=["pt"], writes=[dkey])


def bcast_scalar(cx, bf, src11, dst_col, skey, dkey):
    cx.op("tensor", lambda t: t.matmul(bf.PT[:, 0:1], bf.ONESR[0:1, :], src11,
                                       start=True, stop=True),
          reads=[skey, "ONESR"], writes=["pt"])
    cx.op("vector", lambda v: v.tensor_copy(out=dst_col, in_=bf.PT[:, 0:1]),
          reads=["pt"], writes=[dkey])


def rep_row_tile(cx, bf, src_col, dst_tile, skey, dkey):
    cx.op("tensor", lambda t: t.transpose(bf.PT[0:1, 0:P], src_col,
                                          bf.IDENT[:, :]),
          reads=[skey, "IDENT"], writes=["pt"])
    cx.op("vector", lambda v: v.tensor_copy(out=bf.LSCR[0:1, 0:P], in_=bf.PT[0:1, :P]),
          reads=["pt"], writes=["LSCR"])
    cx.op("tensor", lambda t: t.matmul(bf.PT[:, 0:P], bf.ONESR[0:1, :],
                                       bf.LSCR[0:1, 0:P], start=True, stop=True),
          reads=["LSCR", "ONESR"], writes=["pt"])
    cx.op("vector", lambda v: v.tensor_copy(out=dst_tile, in_=bf.PT[:, 0:P]),
          reads=["pt"], writes=[dkey])


def build_program():
    nc = bass.Bass("TRN2", target_bir_lowering=False, debug=False)
    ap = {}

    def din(name, shape, dt=F32):
        ap[name] = nc.dram_tensor(name, shape, dt, kind="ExternalInput").ap()

    def dout(name, shape, dt=F32):
        ap[name] = nc.dram_tensor(name, shape, dt, kind="ExternalOutput").ap()

    def dscr(name, shape, dt=F32):
        ap[name] = nc.dram_tensor(name, shape, dt).ap()

    din("x0", [B, SEQ_PER, D])
    dout("y", [B, SEQ_PER, D])
    din("aa_blk", [B, QBLK, D])
    din("aa_sink", [B, N_SINKS, D])
    din("aa_Wq", [D, D])
    din("aa_Wk", [D, D])
    din("aa_Wv", [D, D])
    din("aa_Wo", [D, D])
    din("aa_maskw", [QBLK, WINDOW])
    din("aa_masks", [QBLK, N_SINKS])
    din("aa_addmask", [QBLK, 1])
    din("f0_w1t", [D, H_FFN])
    din("f0_w1", [H_FFN, D])
    din("f0_b1", [P, HCH])
    din("f0_w2t", [H_FFN, D])
    din("f0_w2", [D, H_FFN])
    din("f0_b2", [P, D])
    din("t_Wbt", [D, D])
    din("t_Wpt", [D, D])
    din("t_Wp", [D, D])
    din("t_iA", [D, LORA_R])
    din("t_iB_t", [D, LORA_R])
    din("t_gamma", [P, D])
    din("t_beta", [P, D])
    din("onehot_next", [NC, 1])
    din("prefw_rep", [P, NC])
    din("q_w1", [H_FFN, D])
    din("q_b1", [P, HCH])
    din("q_w2", [D, H_FFN])
    din("q_b2", [P, DCH])
    din("c_Wqt", [D, D])
    din("c_Wkt", [D, D])
    din("c_Wvt", [D, D])
    din("c_Wot", [D, D])
    din("c_bo", [P, D])
    din("ehead", [DCH, P, NHEAD])
    din("eheadt", [DCH, NHEAD, P])
    din("ident", [P, P])
    din("ones_col", [P, 1])
    din("ones_row", [1, P])
    din("consts", [P, 2])

    dscr("s_w1s", [D, H_FFN])
    dscr("s_w2s", [H_FFN, D])
    dscr("s_q1ct", [DCH, P, H_FFN], BF16)
    dscr("s_q2ct", [HCH, P, D], BF16)
    dscr("s_xt", [B, DCH, P, SEQ_PER])
    dscr("s_xnt", [B, DCH, P, SEQ_PER + 1])
    dscr("s_qt", [DCH, P, SEQ_PER])
    dscr("s_vtok", [NQT, P, D])
    dscr("s_xc", [SEQ_PER, D], BF16)
    dscr("s_h", [B * NPB, HCH, P, P])
    dscr("s_ln", [B * NPB, P, D])
    dscr("s_da", [B, NCHB, P, P])
    dscr("s_db", [B, NCHB, P, P])
    dscr("g_first_in", [B, D])
    dscr("g_first_out", [NC, B, D])
    dscr("g_sums_in", [P, 2 * 2 * P])
    dscr("g_sums_out", [NC, P, 2 * 2 * P])
    dscr("g_max_in0", [1, 1])
    dscr("g_max_out0", [NC, 1])
    dscr("g_max_in1", [1, 1])
    dscr("g_max_out1", [NC, 1])
    dscr("g_kt", [D, TOK])
    dscr("g_kt_all", [NC, D, TOK])
    dscr("g_v", [TOK, D])
    dscr("g_v_all", [NC, TOK, D])

    bf = NS()
    import contextlib
    with contextlib.ExitStack() as _stack:
        def _sb(name, shape, dt=F32):
            return _stack.enter_context(nc.sbuf_tensor(name, shape, dt))

        def _psum(name, shape):
            return _stack.enter_context(nc.psum_tensor(name, shape, F32))
        _X = _sb("X", [P, NPT * D])
        _BA = _sb("BA", [P, 8192])
        _BB = _sb("BB", [P, 8192])
        _CB = _sb("CB", [P, 4224])
        _CL = _sb("CL", [P, 5248])
        _LNT = _sb("LNT", [P, D])
        _LSCR = _sb("LSCR", [P, D])
        _SCR = _sb("SCR", [P, 1056])
        _SCR2 = _sb("SCR2", [P, 1056])
        _SCR3 = _sb("SCR3", [P, 1056])
        _WTA = _sb("WTA", [P, 1024])
        _WTB = _sb("WTB", [P, 1024])
        _SMALL = _sb("SMALL", [P, 2048])
        _STAT = _sb("STAT", [P, 16])
        _IDENT = _sb("IDENT", [P, P])
        _ONESR = _sb("ONESR", [1, P])
        _F16T = _sb("F16T", [P, 1024], F16)
        _PS0 = _psum("PS0", [P, 512])
        _PS1 = _psum("PS1", [P, 512])
        _PS2 = _psum("PS2", [P, 512])
        _PS3 = _psum("PS3", [P, 512])
        _PSW = _psum("PSW", [P, 1024])
        _PT = _psum("PT", [P, 512])
        block = _stack.enter_context(nc.Block())
        bf.X, bf.BA, bf.BB, bf.CB, bf.CL = _X, _BA, _BB, _CB, _CL
        bf.LNT, bf.LSCR, bf.SCR, bf.SCR2, bf.SCR3 = _LNT, _LSCR, _SCR, _SCR2, _SCR3
        bf.WTA, bf.WTB, bf.SMALL, bf.STAT = _WTA, _WTB, _SMALL, _STAT
        bf.IDENT, bf.ONESR, bf.F16T = _IDENT, _ONESR, _F16T
        bf.PS0, bf.PS1, bf.PS2, bf.PS3, bf.PSW, bf.PT = _PS0, _PS1, _PS2, _PS3, _PSW, _PT
        bf.A1 = _SMALL[:, 0:32]
        bf.A2 = _SMALL[:, 32:40]
        bf.SW1 = _SMALL[:, 40:72]
        bf.SW1R = _SMALL[:, 72:104]
        bf.SW2 = _SMALL[:, 104:112]
        bf.SW2R = _SMALL[:, 112:120]
        bf.IBT = _SMALL[:, 128:256]
        bf.IA = _SMALL[:, 256:384]
        bf.G = _SMALL[:, 384:512]
        bf.PIN = _SMALL[:, 512:640]
        bf.PERR = _SMALL[:, 640:768]
        bf.BQ1 = _SMALL[:, 768:800]
        bf.SA1 = _SMALL[:, 800:832]
        bf.SA1R = _SMALL[:, 832:864]
        bf.PREFW = _SMALL[:, 1024:1024 + NC]
        bf.ONESC = _SMALL[:, 1040:1041]
        bf.EPSC = _SMALL[:, 1044:1045]
        bf.HALFC = _SMALL[:, 1045:1046]
        cx = Ctx(nc, block)
        bf.xtile = lambda t: bf.X[:, t * D:(t + 1) * D]

        cx.op("sync", lambda s: s.dma_start(
            out=bf.X[:, :], in_=ap["x0"].rearrange("b (t p) d -> p b t d", p=P)),
            dma=True, writes=["X"])
        cx.op("sync", lambda s: s.dma_start(out=bf.IDENT[:, :], in_=ap["ident"]),
              dma=True, writes=["IDENT"])
        cx.op("sync", lambda s: s.dma_start(out=bf.ONESR[:, :], in_=ap["ones_row"]),
              dma=True, writes=["ONESR"])
        cx.op("sync", lambda s: s.dma_start(out=bf.PREFW, in_=ap["prefw_rep"]),
              dma=True, writes=["SMALL"])
        cx.op("sync", lambda s: s.dma_start(out=bf.ONESC, in_=ap["ones_col"]),
              dma=True, writes=["SMALL"])
        cx.op("sync", lambda s: s.dma_start(out=_SMALL[:, 1044:1046],
                                            in_=ap["consts"]),
              dma=True, writes=["SMALL"])
        cx.op("sync", lambda s: s.dma_start(
            out=bf.IBT, in_=ap["t_iB_t"].rearrange("(c p) r -> p c r", p=P)),
            dma=True, writes=["SMALL"])
        cx.op("sync", lambda s: s.dma_start(
            out=bf.IA, in_=ap["t_iA"].rearrange("(c p) r -> p c r", p=P)),
            dma=True, writes=["SMALL"])

        import os as _os
        phases = _os.environ.get("KMOD_PHASES", "all")
        def on(name):
            return phases == "all" or name in phases.split(",")
        emit_setup(nc, cx, ap, bf)
        cx.barrier()
        if on("attn_a"):
            emit_attn_a(nc, cx, ap, bf)
            cx.barrier()
        if on("ffn0a"):
            emit_ffn0(nc, cx, ap, bf)
            cx.barrier()
        if on("attn_b0"):
            emit_attn_b0(nc, cx, ap, bf)
            cx.barrier()
        if on("ffn0b"):
            emit_ffn0(nc, cx, ap, bf)
            cx.barrier()
        if on("ttt"):
            emit_ttt(nc, cx, ap, bf)
            cx.barrier()
        if on("qffn1"):
            emit_qffn(nc, cx, ap, bf)
            cx.barrier()
        if on("attn_b1"):
            emit_attn_b1(nc, cx, ap, bf)
            cx.barrier()
        if on("qffn2"):
            emit_qffn(nc, cx, ap, bf)
            cx.barrier()

        cx.op("sync", lambda s: s.dma_start(
            out=ap["y"].rearrange("b (t p) d -> p b t d", p=P), in_=bf.X[:, :]),
            dma=True, reads=["X"])
        sems = cx.sems
        finals = [(f"S_dma{i}", cx.cnt[f"S_dma{i}"]) for i in range(cx.NDL)]

        def _fin(e):
            for s, c in finals:
                if c > 0:
                    e.wait_ge(sems[s], c)
        block.sync(_fin)
    return nc


def emit_setup(nc, cx, ap, bf):
    for dc in range(DCH):
        for hc in range(DCH):
            cx.op("sync", lambda s, hc=hc, dc=dc: s.dma_start(
                out=bf.WTA[:, 0:P],
                in_=ap["t_Wp"][hc * P:(hc + 1) * P, dc * P:(dc + 1) * P]),
                dma=True, writes=["WTA"])
            cx.op("tensor", lambda t, hc=hc: t.matmul(
                bf.PS0[:, 0:LORA_R], bf.WTA[:, 0:P],
                bf.IA[:, hc * LORA_R:(hc + 1) * LORA_R],
                start=(hc == 0), stop=(hc == DCH - 1)),
                reads=["WTA", "SMALL"], writes=["ps0"])
        cx.op("vector", lambda v, dc=dc: v.tensor_copy(
            out=bf.G[:, dc * LORA_R:(dc + 1) * LORA_R], in_=bf.PS0[:, 0:LORA_R]),
            reads=["ps0"], writes=["SMALL"])
    for i in range(DCH):
        cx.op("sync", lambda s, i=i: s.dma_start(
            out=bf.BA[:, 0:H_FFN // 2],
            in_=ap["f0_w1t"][i * P:(i + 1) * P, 0:H_FFN // 2]),
            dma=True, writes=["BA"])
        cx.op("scalar", lambda s: s.activation(out=bf.BA[:, 0:H_FFN // 2],
                                               in_=bf.BA[:, 0:H_FFN // 2],
                                               func=AF.Sign),
              reads=["BA"], writes=["BA"])
        cx.op("sync", lambda s, i=i: s.dma_start(
            out=ap["s_w1s"][i * P:(i + 1) * P, 0:H_FFN // 2],
            in_=bf.BA[:, 0:H_FFN // 2]),
            dma=True, reads=["BA"], writes=["s_w1s"])
        cx.op("sync", lambda s, i=i: s.dma_start(
            out=bf.BA[:, 0:H_FFN // 2],
            in_=ap["f0_w1t"][i * P:(i + 1) * P, H_FFN // 2:]),
            dma=True, writes=["BA"])
        cx.op("scalar", lambda s: s.activation(out=bf.BA[:, 0:H_FFN // 2],
                                               in_=bf.BA[:, 0:H_FFN // 2],
                                               func=AF.Sign),
              reads=["BA"], writes=["BA"])
        cx.op("sync", lambda s, i=i: s.dma_start(
            out=ap["s_w1s"][i * P:(i + 1) * P, H_FFN // 2:],
            in_=bf.BA[:, 0:H_FFN // 2]),
            dma=True, reads=["BA"], writes=["s_w1s"])
    for i in range(HCH):
        cx.op("sync", lambda s, i=i: s.dma_start(
            out=bf.BA[:, 0:D], in_=ap["f0_w2t"][i * P:(i + 1) * P, :]),
            dma=True, writes=["BA"])
        cx.op("scalar", lambda s: s.activation(out=bf.BA[:, 0:D], in_=bf.BA[:, 0:D],
                                               func=AF.Sign),
              reads=["BA"], writes=["BA"])
        cx.op("sync", lambda s, i=i: s.dma_start(
            out=ap["s_w2s"][i * P:(i + 1) * P, :], in_=bf.BA[:, 0:D]),
            dma=True, reads=["BA"], writes=["s_w2s"])
    for i in range(HCH):
        cx.op("sync", lambda s, i=i: s.dma_start(
            out=bf.BA[:, 0:D], in_=ap["f0_w1"][i * P:(i + 1) * P, :]),
            dma=True, writes=["BA"])
        cx.op("vector", lambda v, i=i: v.reduce_sum(
            bf.A1[:, i:i + 1], bf.BA[:, 0:D], axis=AXL.X,
            apply_absolute_value=True),
            reads=["BA"], writes=["SMALL"])
    cx.op("vector", lambda v: v.tensor_scalar(out=bf.A1, in0=bf.A1,
                                              scalar1=1.0 / D, scalar2=None,
                                              op0=ALU.mult),
          reads=["SMALL"], writes=["SMALL"])
    for i in range(DCH):
        cx.op("sync", lambda s, i=i: s.dma_start(
            out=bf.BA[:, 0:H_FFN], in_=ap["f0_w2"][i * P:(i + 1) * P, :]),
            dma=True, writes=["BA"])
        cx.op("vector", lambda v, i=i: v.reduce_sum(
            bf.A2[:, i:i + 1], bf.BA[:, 0:H_FFN], axis=AXL.X,
            apply_absolute_value=True),
            reads=["BA"], writes=["SMALL"])
    cx.op("vector", lambda v: v.tensor_scalar(out=bf.A2, in0=bf.A2,
                                              scalar1=1.0 / H_FFN, scalar2=None,
                                              op0=ALU.mult),
          reads=["SMALL"], writes=["SMALL"])
    for i in range(HCH):
        cx.op("sync", lambda s, i=i: s.dma_start(
            out=bf.BA[:, 0:D], in_=ap["q_w1"][i * P:(i + 1) * P, :]),
            dma=True, writes=["BA"])
        cx.op("vector", lambda v: v.reduce_max(bf.STAT[:, 8:9], bf.BA[:, 0:D],
                                               axis=AXL.X, apply_absolute_value=True),
              reads=["BA"], writes=["STAT"])
        cx.op("vector", lambda v, i=i: v.tensor_scalar(
            out=bf.SW1[:, i:i + 1], in0=bf.STAT[:, 8:9], scalar1=1e-8,
            scalar2=1.0 / 127.0, op0=ALU.max, op1=ALU.mult),
            reads=["STAT"], writes=["SMALL"])
        cx.op("vector", lambda v, i=i: v.reciprocal(out=bf.SW1R[:, i:i + 1],
                                                    in_=bf.SW1[:, i:i + 1]),
              reads=["SMALL"], writes=["SMALL"])
        cx.op("vector", lambda v, i=i: v.tensor_scalar(
            out=bf.F16T[:, 0:D], in0=bf.BA[:, 0:D], scalar1=bf.SW1R[:, i:i + 1],
            scalar2=1536.0, op0=ALU.mult, op1=ALU.add),
            reads=["BA", "SMALL"], writes=["F16T"])
        cx.op("vector", lambda v: v.tensor_scalar(
            out=bf.BB[:, 0:D], in0=bf.F16T[:, 0:D], scalar1=-1536.0,
            scalar2=-128.0, op0=ALU.add, op1=ALU.max),
            reads=["F16T"], writes=["BB"])
        cx.op("vector", lambda v: v.tensor_scalar(
            out=bf.BB[:, 0:D], in0=bf.BB[:, 0:D], scalar1=127.0, scalar2=None,
            op0=ALU.min), reads=["BB"], writes=["BB"])
        for dc in range(DCH):
            tr128(cx, bf, bf.BB[:, dc * P:(dc + 1) * P], bf.LNT[:, 0:P], "BB", "LNT")
            cx.op("vector", lambda v: v.tensor_copy(
                out=bf.F16T[:, 512:512 + P].bitcast(BF16),
                in_=bf.LNT[:, 0:P]),
                reads=["LNT"], writes=["F16T"])
            cx.op("sync", lambda s, i=i, dc=dc: s.dma_start(
                out=ap["s_q1ct"][dc, :, i * P:(i + 1) * P],
                in_=bf.F16T[:, 512:512 + P].bitcast(BF16)),
                dma=True, reads=["F16T"], writes=["s_q1ct"])
    for i in range(DCH):
        cx.op("sync", lambda s, i=i: s.dma_start(
            out=bf.BA[:, 0:H_FFN], in_=ap["q_w2"][i * P:(i + 1) * P, :]),
            dma=True, writes=["BA"])
        cx.op("vector", lambda v: v.reduce_max(bf.STAT[:, 8:9], bf.BA[:, 0:H_FFN],
                                               axis=AXL.X, apply_absolute_value=True),
              reads=["BA"], writes=["STAT"])
        cx.op("vector", lambda v, i=i: v.tensor_scalar(
            out=bf.SW2[:, i:i + 1], in0=bf.STAT[:, 8:9], scalar1=1e-8,
            scalar2=1.0 / 127.0, op0=ALU.max, op1=ALU.mult),
            reads=["STAT"], writes=["SMALL"])
        cx.op("vector", lambda v, i=i: v.reciprocal(out=bf.SW2R[:, i:i + 1],
                                                    in_=bf.SW2[:, i:i + 1]),
              reads=["SMALL"], writes=["SMALL"])
        for q in range(4):
            sl = slice(q * D, (q + 1) * D)
            cx.op("vector", lambda v, i=i, sl=sl: v.tensor_scalar(
                out=bf.F16T[:, 0:D], in0=bf.BA[:, sl], scalar1=bf.SW2R[:, i:i + 1],
                scalar2=1536.0, op0=ALU.mult, op1=ALU.add),
                reads=["BA", "SMALL"], writes=["F16T"])
            cx.op("vector", lambda v, sl=sl: v.tensor_scalar(
                out=bf.BB[:, sl], in0=bf.F16T[:, 0:D], scalar1=-1536.0,
                scalar2=-128.0, op0=ALU.add, op1=ALU.max),
                reads=["F16T"], writes=["BB"])
        cx.op("vector", lambda v: v.tensor_scalar(
            out=bf.BB[:, 0:H_FFN], in0=bf.BB[:, 0:H_FFN], scalar1=127.0,
            scalar2=None, op0=ALU.min), reads=["BB"], writes=["BB"])
        for hc in range(HCH):
            tr128(cx, bf, bf.BB[:, hc * P:(hc + 1) * P], bf.LNT[:, 0:P], "BB", "LNT")
            cx.op("vector", lambda v: v.tensor_copy(
                out=bf.F16T[:, 512:512 + P].bitcast(BF16),
                in_=bf.LNT[:, 0:P]),
                reads=["LNT"], writes=["F16T"])
            cx.op("sync", lambda s, i=i, hc=hc: s.dma_start(
                out=ap["s_q2ct"][hc, :, i * P:(i + 1) * P],
                in_=bf.F16T[:, 512:512 + P].bitcast(BF16)),
                dma=True, reads=["F16T"], writes=["s_q2ct"])


def emit_attn_a(nc, cx, ap, bf):
    wqr = ap["aa_Wq"].rearrange("(dc p) o -> p dc o", p=P)
    wkr = ap["aa_Wk"].rearrange("(dc p) o -> p dc o", p=P)
    wvr = ap["aa_Wv"].rearrange("(dc p) o -> p dc o", p=P)
    for b in range(B):
        for j in range(NQT):
            cx.op("sync", lambda s, b=b, j=j: s.dma_start(
                out=bf.LNT[:, :], in_=ap["aa_blk"][b, j * P:(j + 1) * P, :]),
                dma=True, writes=["LNT"])
            ln_tile(cx, bf, bf.LNT[:, :], bf.SCR[:, 0:D], "LNT", "SCR")
            for dc in range(DCH):
                tr128(cx, bf, bf.SCR[:, dc * P:(dc + 1) * P],
                      bf.BA[:, dc * QBLK + j * P: dc * QBLK + (j + 1) * P],
                      "SCR", "BA")
        cx.op("sync", lambda s, b=b: s.dma_start(
            out=bf.LNT[0:N_SINKS, :], in_=ap["aa_sink"][b, :, :]),
            dma=True, writes=["LNT"])
        ln_tile(cx, bf, bf.LNT[0:N_SINKS, :], bf.SCR[0:N_SINKS, 0:D], "LNT", "SCR",
                nrow=N_SINKS)
        for dc in range(DCH):
            tr128(cx, bf, bf.SCR[0:N_SINKS, dc * P:(dc + 1) * P],
                  bf.SCR3[:, dc * N_SINKS:(dc + 1) * N_SINKS], "SCR", "SCR3",
                  nout=N_SINKS)

        def blkT(dc, c0, c1):
            return bf.BA[:, dc * QBLK + c0: dc * QBLK + c1]

        for oc in range(DCH):
            cx.op("sync", lambda s, oc=oc: s.dma_start(
                out=bf.WTA[:, :], in_=wqr[:, :, oc * P:(oc + 1) * P]),
                dma=True, writes=["WTA"])
            for c0, c1 in ((0, 512), (512, QBLK)):
                n = c1 - c0
                builds = [(lambda t, dc=dc, c0=c0, c1=c1, n=n: t.matmul(
                    bf.PS0[:, :n], bf.WTA[:, dc * P:(dc + 1) * P],
                    blkT(dc, c0, c1), start=(dc == 0), stop=(dc == DCH - 1)))
                    for dc in range(DCH)]
                cx.group("tensor", builds, reads=["WTA", "BA"], writes=["ps0"])
                cx.op("vector", lambda v, oc=oc, c0=c0, c1=c1, n=n: v.tensor_copy(
                    out=bf.BB[:, oc * QBLK + c0: oc * QBLK + c1], in_=bf.PS0[:, :n]),
                    reads=["ps0"], writes=["BB"])
        for oc in range(DCH):
            cx.op("sync", lambda s, oc=oc: s.dma_start(
                out=bf.WTA[:, :], in_=wkr[:, :, oc * P:(oc + 1) * P]),
                dma=True, writes=["WTA"])
            builds = [(lambda t, dc=dc, oc=oc: t.matmul(
                bf.PS0[:, 0:N_SINKS], bf.WTA[:, dc * P:(dc + 1) * P],
                bf.SCR3[:, dc * N_SINKS:(dc + 1) * N_SINKS],
                start=(dc == 0), stop=(dc == DCH - 1))) for dc in range(DCH)]
            cx.group("tensor", builds, reads=["WTA", "SCR3"], writes=["ps0"])
            cx.op("vector", lambda v, oc=oc: v.tensor_copy(
                out=bf.CB[:, oc * CTX:oc * CTX + N_SINKS], in_=bf.PS0[:, 0:N_SINKS]),
                reads=["ps0"], writes=["CB"])
            for half in range(2):
                c0 = P + half * 256
                builds = [(lambda t, dc=dc, c0=c0: t.matmul(
                    bf.PS0[:, 0:256], bf.WTA[:, dc * P:(dc + 1) * P],
                    blkT(dc, c0, c0 + 256), start=(dc == 0), stop=(dc == DCH - 1)))
                    for dc in range(DCH)]
                cx.group("tensor", builds, reads=["WTA", "BA"], writes=["ps0"])
                cx.op("vector", lambda v, oc=oc, half=half: v.tensor_copy(
                    out=bf.CB[:, oc * CTX + N_SINKS + half * 256:
                              oc * CTX + N_SINKS + (half + 1) * 256],
                    in_=bf.PS0[:, 0:256]),
                    reads=["ps0"], writes=["CB"])
        for ci in range(NQT):
            nrow = N_SINKS if ci == 0 else P
            for dh in range(2):
                for dc in range(DCH):
                    if ci == 0:
                        lh = bf.SCR3[:, dc * N_SINKS:(dc + 1) * N_SINKS]
                        lk = "SCR3"
                    else:
                        lh = blkT(dc, ci * P, (ci + 1) * P)
                        lk = "BA"
                    cx.op("sync", lambda s, dc=dc, dh=dh: s.dma_start(
                        out=bf.WTB[:, 0:512],
                        in_=ap["aa_Wv"][dc * P:(dc + 1) * P,
                                        dh * 512:(dh + 1) * 512]),
                        dma=True, writes=["WTB"])
                    cx.op("tensor", lambda t, dc=dc, lh=lh, nrow=nrow: t.matmul(
                        bf.PS2[:nrow, :], lh, bf.WTB[:, 0:512],
                        start=(dc == 0), stop=(dc == DCH - 1)),
                        reads=["WTB", lk], writes=["ps2"])
                cx.op("vector", lambda v, nrow=nrow, dh=dh: v.tensor_copy(
                    out=bf.LSCR[:nrow, dh * 512:(dh + 1) * 512],
                    in_=bf.PS2[:nrow, :]),
                    reads=["ps2"], writes=["LSCR"])
            cx.op("sync", lambda s, ci=ci, nrow=nrow: s.dma_start(
                out=ap["s_vtok"][ci, 0:nrow, :], in_=bf.LSCR[0:nrow, :]),
                dma=True, reads=["LSCR"], writes=["s_vtok"])
        cx.barrier()
        for j in range(NQT):
            cx.op("sync", lambda s, j=j: s.dma_start(
                out=bf.CL[:, 0:WINDOW], in_=ap["aa_maskw"][j * P:(j + 1) * P, :]),
                dma=True, writes=["CL"])
            cx.op("sync", lambda s, j=j: s.dma_start(
                out=bf.CL[:, WINDOW:CTX], in_=ap["aa_masks"][j * P:(j + 1) * P, :]),
                dma=True, writes=["CL"])
            cx.op("sync", lambda s, j=j: s.dma_start(
                out=bf.STAT[:, 12:13], in_=ap["aa_addmask"][j * P:(j + 1) * P, :]),
                dma=True, writes=["STAT"])
            for h in range(NHEAD):
                oc, pb = h // 2, (h % 2) * DHEAD
                qT = bf.BB[pb:pb + DHEAD, oc * QBLK + j * P: oc * QBLK + (j + 1) * P]
                cx.group("tensor", [
                    lambda t: t.matmul(bf.PSW[:, 512:512 + N_SINKS], qT,
                                       bf.CB[pb:pb + DHEAD,
                                             oc * CTX:oc * CTX + N_SINKS],
                                       start=True, stop=True),
                    lambda t: t.matmul(bf.PSW[:, 0:256], qT,
                                       bf.CB[pb:pb + DHEAD,
                                             oc * CTX + N_SINKS:
                                             oc * CTX + N_SINKS + 256],
                                       start=True, stop=True),
                    lambda t: t.matmul(bf.PSW[:, 256:512], qT,
                                       bf.CB[pb:pb + DHEAD,
                                             oc * CTX + N_SINKS + 256:oc * CTX + CTX],
                                       start=True, stop=True),
                ], reads=["BB", "CB"], writes=["psw"])
                cx.op("vector", lambda v: v.scalar_tensor_tensor(
                    out=bf.SCR[:, 0:N_SINKS], in0=bf.PSW[:, 512:512 + N_SINKS],
                    scalar=0.125, op0=ALU.mult, op1=ALU.add,
                    in1=bf.CL[:, WINDOW:CTX]),
                    reads=["psw", "CL"], writes=["SCR"])
                cx.op("vector", lambda v: v.scalar_tensor_tensor(
                    out=bf.SCR[:, N_SINKS:CTX], in0=bf.PSW[:, 0:WINDOW],
                    scalar=0.125, op0=ALU.mult, op1=ALU.add, in1=bf.CL[:, 0:WINDOW]),
                    reads=["psw", "CL"], writes=["SCR"])
                cx.op("vector", lambda v: v.reduce_max(
                    bf.STAT[:, 8:9], bf.SCR[:, 0:CTX], axis=AXL.X, negate=True),
                    reads=["SCR"], writes=["STAT"])
                cx.op("scalar", lambda s: s.activation(
                    out=bf.SCR[:, 0:CTX], in_=bf.SCR[:, 0:CTX], func=AF.Exp,
                    bias=bf.STAT[:, 8:9], accum_out=bf.STAT[:, 9:10]),
                    reads=["SCR", "STAT"], writes=["SCR", "STAT"])
                cx.op("vector", lambda v: v.reciprocal(out=bf.STAT[:, 9:10],
                                                       in_=bf.STAT[:, 9:10]),
                      reads=["STAT"], writes=["STAT"])
                cx.op("vector", lambda v: v.tensor_scalar(
                    out=bf.SCR[:, 0:CTX], in0=bf.SCR[:, 0:CTX],
                    scalar1=bf.STAT[:, 9:10], scalar2=None, op0=ALU.mult),
                    reads=["SCR", "STAT"], writes=["SCR"])
                for ci in range(NQT):
                    if ci == 0:
                        pc, nrow = slice(0, N_SINKS), N_SINKS
                    else:
                        pc, nrow = slice(N_SINKS + (ci - 1) * P, N_SINKS + ci * P), P
                    cx.op("tensor", lambda t, pc=pc, nrow=nrow: t.transpose(
                        bf.PT[:nrow, 0:P], bf.SCR[:, pc], bf.IDENT[:, :]),
                        reads=["SCR", "IDENT"], writes=["pt"])
                    cx.op("vector", lambda v, nrow=nrow: v.tensor_copy(
                        out=bf.SCR2[0:nrow, 0:P], in_=bf.PT[0:nrow, 0:P]),
                        reads=["pt"], writes=["SCR2"])
                    cx.op("sync", lambda s, ci=ci, h=h, nrow=nrow: s.dma_start(
                        out=bf.SCR2[0:nrow, 128:128 + DHEAD],
                        in_=ap["s_vtok"][ci, 0:nrow, h * DHEAD:(h + 1) * DHEAD]),
                        dma=True, reads=["s_vtok"], writes=["SCR2"])
                    cx.op("tensor", lambda t, nrow=nrow, ci=ci: t.matmul(
                        bf.PS3[0:DHEAD, 0:P], bf.SCR2[0:nrow, 128:128 + DHEAD],
                        bf.SCR2[0:nrow, 0:P], start=(ci == 0), stop=(ci == NQT - 1)),
                        reads=["SCR2"], writes=["ps3"])
                cx.op("vector", lambda v: v.tensor_copy(
                    out=bf.SCR2[0:DHEAD, 256:256 + P], in_=bf.PS3[0:DHEAD, 0:P]),
                    reads=["ps3"], writes=["SCR2"])
                for dh in range(2):
                    cx.op("sync", lambda s, h=h, dh=dh: s.dma_start(
                        out=bf.WTA[0:DHEAD, 0:512],
                        in_=ap["aa_Wo"][h * DHEAD:(h + 1) * DHEAD,
                                        dh * 512:(dh + 1) * 512]),
                        dma=True, writes=["WTA"])
                    ps = bf.PS0 if dh == 0 else bf.PS1
                    cx.op("tensor", lambda t, h=h, ps=ps: t.matmul(
                        ps[:, :], bf.SCR2[0:DHEAD, 256:256 + P],
                        bf.WTA[0:DHEAD, 0:512],
                        start=(h == 0), stop=(h == NHEAD - 1)),
                        reads=["SCR2", "WTA"], writes=["ps0" if dh == 0 else "ps1"])
            qstart_t = (SEQ - QBLK) // P
            tj = b * NPB + ((qstart_t + j) % NPB)
            for dh in range(2):
                ps = bf.PS0 if dh == 0 else bf.PS1
                cx.op("vector", lambda v, tj=tj, dh=dh, ps=ps: v.scalar_tensor_tensor(
                    out=bf.xtile(tj)[:, dh * 512:(dh + 1) * 512], in0=ps[:, :],
                    scalar=bf.STAT[:, 12:13], op0=ALU.mult, op1=ALU.add,
                    in1=bf.xtile(tj)[:, dh * 512:(dh + 1) * 512]),
                    reads=["ps0" if dh == 0 else "ps1", "STAT", "X"], writes=["X"])
        cx.barrier()


def emit_ffn0(nc, cx, ap, bf):
    cx.op("sync", lambda s: s.dma_start(out=bf.SCR3[:, 0:HCH], in_=ap["f0_b1"]),
          dma=True, writes=["SCR3"])
    cx.op("sync", lambda s: s.dma_start(out=bf.SCR3[:, 32:32 + D], in_=ap["f0_b2"]),
          dma=True, writes=["SCR3"])
    for dc in range(DCH):
        rep_row_tile(cx, bf, bf.A2[:, dc:dc + 1],
                     bf.CL[:, dc * P:(dc + 1) * P], "SMALL", "CL")
    w1r = ap["s_w1s"].rearrange("(dc p) h -> p dc h", p=P)
    for b in range(B):
        for j in range(NPB):
            tj = b * NPB + j
            ln_tile(cx, bf, bf.xtile(tj), bf.LNT[:, :], "X", "LNT")
            for dc in range(DCH):
                tr128(cx, bf, bf.LNT[:, dc * P:(dc + 1) * P],
                      bf.BA[:, dc * SEQ_PER + j * P: dc * SEQ_PER + (j + 1) * P],
                      "LNT", "BA")
        TT = 256
        for tt in range(SEQ_PER // TT):
            c0 = tt * TT
            for hc in range(HCH):
                cx.op("sync", lambda s, hc=hc: s.dma_start(
                    out=bf.WTA[:, :], in_=w1r[:, :, hc * P:(hc + 1) * P]),
                    dma=True, writes=["WTA"])
                cx.op("sync", lambda s, hc=hc: s.dma_start(
                    out=bf.WTB[:, :], in_=ap["s_w2s"][hc * P:(hc + 1) * P, :]),
                    dma=True, writes=["WTB"])
                builds = [(lambda t, dc=dc, c0=c0: t.matmul(
                    bf.PSW[:, 0:TT], bf.WTA[:, dc * P:(dc + 1) * P],
                    bf.BA[:, dc * SEQ_PER + c0: dc * SEQ_PER + c0 + TT],
                    start=(dc == 0), stop=(dc == DCH - 1))) for dc in range(DCH)]
                cx.group("tensor", builds, reads=["WTA", "BA"], writes=["psw"])
                cx.op("scalar", lambda s, hc=hc: s.activation(
                    out=bf.SCR[:, 0:TT], in_=bf.PSW[:, 0:TT], func=AF.Gelu,
                    scale=bf.A1[:, hc:hc + 1], bias=bf.SCR3[:, hc:hc + 1]),
                    reads=["psw", "SMALL", "SCR3"], writes=["SCR"])
                for half in range(2):
                    for dh in range(2):
                        pk = f"ps{half * 2 + dh}"
                        ps = (bf.PS0, bf.PS1, bf.PS2, bf.PS3)[half * 2 + dh]
                        cx.op("tensor", lambda t, half=half, dh=dh, hc=hc, ps=ps:
                              t.matmul(ps[:, :], bf.SCR[:, half * P:(half + 1) * P],
                                       bf.WTB[:, dh * 512:(dh + 1) * 512],
                                       start=(hc == 0), stop=(hc == HCH - 1)),
                              reads=["SCR", "WTB"], writes=[pk])
            for half in range(2):
                tj = b * NPB + (c0 + half * P) // P
                for dh in range(2):
                    pk = f"ps{half * 2 + dh}"
                    ps = (bf.PS0, bf.PS1, bf.PS2, bf.PS3)[half * 2 + dh]
                    sl = slice(dh * 512, (dh + 1) * 512)
                    cx.op("vector", lambda v, ps=ps, sl=sl, pk=pk: v.tensor_mul(
                        out=bf.SCR2[:, 0:512], in0=ps[:, :], in1=bf.CL[:, sl]),
                        reads=[pk, "CL"], writes=["SCR2"])
                    cx.op("vector", lambda v, sl=sl: v.tensor_add(
                        out=bf.SCR2[:, 0:512], in0=bf.SCR2[:, 0:512],
                        in1=bf.SCR3[:, 32 + sl.start:32 + sl.stop]),
                        reads=["SCR2", "SCR3"], writes=["SCR2"])
                    cx.op("vector", lambda v, tj=tj, sl=sl: v.tensor_add(
                        out=bf.xtile(tj)[:, sl], in0=bf.xtile(tj)[:, sl],
                        in1=bf.SCR2[:, 0:512]),
                        reads=["SCR2", "X"], writes=["X"])
        cx.barrier()


def emit_attn_b0(nc, cx, ap, bf):
    inv_sq = float(1.0 / np.sqrt(D))
    st = bf.STAT
    for j in range(NPB):
        t0, t1 = j, NPB + j
        ln_tile(cx, bf, bf.xtile(t0), bf.SCR[:, 0:D], "X", "SCR")
        ln_tile(cx, bf, bf.xtile(t1), bf.SCR2[:, 0:D], "X", "SCR2")
        for (c, i0, i1, k0, k1) in ((8, bf.SCR, bf.SCR, "SCR", "SCR"),
                                    (9, bf.SCR, bf.SCR2, "SCR", "SCR2"),
                                    (10, bf.SCR2, bf.SCR2, "SCR2", "SCR2")):
            cx.op("vector", lambda v, c=c, i0=i0, i1=i1: v.scalar_tensor_tensor(
                out=bf.LSCR[:, :], in0=i0[:, 0:D], scalar=0.0, op0=ALU.add,
                op1=ALU.mult, in1=i1[:, 0:D], accum_out=st[:, c:c + 1]),
                reads=[k0, k1], writes=["STAT", "LSCR"])
        cx.op("vector", lambda v: v.tensor_scalar(
            out=st[:, 8:11], in0=st[:, 8:11], scalar1=inv_sq, scalar2=None,
            op0=ALU.mult), reads=["STAT"], writes=["STAT"])
        cx.op("vector", lambda v: v.tensor_max(out=st[:, 11:12], in0=st[:, 8:9],
                                               in1=st[:, 9:10]),
              reads=["STAT"], writes=["STAT"])
        cx.op("vector", lambda v: v.tensor_scalar(out=st[:, 11:12],
                                                  in0=st[:, 11:12], scalar1=-1.0,
                                                  scalar2=None, op0=ALU.mult),
              reads=["STAT"], writes=["STAT"])
        cx.op("vector", lambda v: v.tensor_max(out=st[:, 12:13], in0=st[:, 9:10],
                                               in1=st[:, 10:11]),
              reads=["STAT"], writes=["STAT"])
        cx.op("vector", lambda v: v.tensor_scalar(out=st[:, 12:13],
                                                  in0=st[:, 12:13], scalar1=-1.0,
                                                  scalar2=None, op0=ALU.mult),
              reads=["STAT"], writes=["STAT"])
        for (c, sc, mc) in ((0, 8, 11), (1, 9, 11), (2, 9, 12), (3, 10, 12)):
            cx.op("scalar", lambda s, c=c, sc=sc, mc=mc: s.activation(
                out=st[:, c:c + 1], in_=st[:, sc:sc + 1], func=AF.Exp,
                bias=st[:, mc:mc + 1]),
                reads=["STAT"], writes=["STAT"])
        cx.op("vector", lambda v: v.tensor_add(out=st[:, 4:5], in0=st[:, 0:1],
                                               in1=st[:, 1:2]),
              reads=["STAT"], writes=["STAT"])
        cx.op("vector", lambda v: v.reciprocal(out=st[:, 4:5], in_=st[:, 4:5]),
              reads=["STAT"], writes=["STAT"])
        cx.op("vector", lambda v: v.tensor_add(out=st[:, 5:6], in0=st[:, 2:3],
                                               in1=st[:, 3:4]),
              reads=["STAT"], writes=["STAT"])
        cx.op("vector", lambda v: v.reciprocal(out=st[:, 5:6], in_=st[:, 5:6]),
              reads=["STAT"], writes=["STAT"])
        for (c, r) in ((0, 4), (1, 4), (2, 5), (3, 5)):
            cx.op("vector", lambda v, c=c, r=r: v.tensor_scalar(
                out=st[:, c:c + 1], in0=st[:, c:c + 1], scalar1=st[:, r:r + 1],
                scalar2=None, op0=ALU.mult),
                reads=["STAT"], writes=["STAT"])
        cx.op("vector", lambda v, t0=t0: v.scalar_tensor_tensor(
            out=bf.LSCR[:, :], in0=bf.SCR2[:, 0:D], scalar=st[:, 1:2],
            op0=ALU.mult, op1=ALU.add, in1=bf.xtile(t0)),
            reads=["SCR2", "STAT", "X"], writes=["LSCR"])
        cx.op("vector", lambda v, t0=t0: v.scalar_tensor_tensor(
            out=bf.xtile(t0), in0=bf.SCR[:, 0:D], scalar=st[:, 0:1],
            op0=ALU.mult, op1=ALU.add, in1=bf.LSCR[:, :]),
            reads=["SCR", "STAT", "LSCR"], writes=["X"])
        cx.op("vector", lambda v, t1=t1: v.scalar_tensor_tensor(
            out=bf.LSCR[:, :], in0=bf.SCR[:, 0:D], scalar=st[:, 2:3],
            op0=ALU.mult, op1=ALU.add, in1=bf.xtile(t1)),
            reads=["SCR", "STAT", "X"], writes=["LSCR"])
        cx.op("vector", lambda v, t1=t1: v.scalar_tensor_tensor(
            out=bf.xtile(t1), in0=bf.SCR2[:, 0:D], scalar=st[:, 3:4],
            op0=ALU.mult, op1=ALU.add, in1=bf.LSCR[:, :]),
            reads=["SCR2", "STAT", "LSCR"], writes=["X"])


def emit_ttt(nc, cx, ap, bf):
    SEG = SEQ_PER + 1
    for b in range(B):
        cx.op("sync", lambda s, b=b: s.dma_start(
            out=ap["g_first_in"][b:b + 1, :],
            in_=bf.X[0:1, b * NPB * D: b * NPB * D + D]),
            dma=True, reads=["X"], writes=["g_first_in"])
    cx.op("gpsimd", lambda g: g.collective_compute(
        "AllGather", ALU.bypass, replica_groups=[list(range(NC))],
        ins=[ap["g_first_in"].opt()], outs=[ap["g_first_out"].opt()]),
        coll=True, reads=["g_first_in"], writes=["g_first_out"])
    cx.op("sync", lambda s: s.dma_start(
        out=bf.CL[0:NC, 0:B * D],
        in_=ap["g_first_out"].rearrange("n b d -> n (b d)")),
        dma=True, reads=["g_first_out"], writes=["CL"])
    cx.op("sync", lambda s: s.dma_start(out=bf.SCR2[0:NC, 0:1],
                                        in_=ap["onehot_next"]),
          dma=True, writes=["SCR2"])
    for q in range(B * D // 512):
        cx.op("tensor", lambda t, q=q: t.matmul(
            bf.PSW[0:1, 0:512], bf.SCR2[0:NC, 0:1],
            bf.CL[0:NC, q * 512:(q + 1) * 512], start=True, stop=True),
            reads=["CL", "SCR2"], writes=["psw"])
        cx.op("vector", lambda v, q=q: v.tensor_copy(
            out=bf.CL[0:1, 2048 + q * 512:2048 + (q + 1) * 512],
            in_=bf.PSW[0:1, 0:512]),
            reads=["psw"], writes=["CL"])
    cx.op("sync", lambda s: s.dma_start(out=bf.CB[:, 0:D], in_=ap["t_gamma"]),
          dma=True, writes=["CB"])
    cx.op("sync", lambda s: s.dma_start(out=bf.CB[:, D:2 * D], in_=ap["t_beta"]),
          dma=True, writes=["CB"])
    wbtr = ap["t_Wbt"].rearrange("(dc p) o -> p dc o", p=P)
    wptr = ap["t_Wpt"].rearrange("(dc p) o -> p dc o", p=P)
    for b in range(B):
        sxtr = ap["s_xt"][b].rearrange("c p t -> p c t")
        sxntr = ap["s_xnt"][b].rearrange("c p t -> p c t")
        for j in range(NPB):
            tj = b * NPB + j
            ln_tile(cx, bf, bf.xtile(tj), bf.BB[:, j * D:(j + 1) * D], "X", "BB")
            for dc in range(DCH):
                tr128(cx, bf, bf.BB[:, j * D + dc * P: j * D + (dc + 1) * P],
                      bf.LNT[:, dc * P:(dc + 1) * P], "BB", "LNT")
            cx.op("sync", lambda s, j=j: s.dma_start(
                out=sxtr[:, :, j * P:(j + 1) * P],
                in_=bf.LNT.rearrange("p (c q) -> p c q", q=P)),
                dma=True, reads=["LNT"], writes=["s_xt"])
            ln_tile(cx, bf, bf.BB[:, j * D:(j + 1) * D], bf.LSCR[:, :], "BB",
                    "LSCR", gamma=bf.CB[:, 0:D], beta=bf.CB[:, D:2 * D], gkey="CB")
            for dc in range(DCH):
                tr128(cx, bf, bf.LSCR[:, dc * P:(dc + 1) * P],
                      bf.LNT[:, dc * P:(dc + 1) * P], "LSCR", "LNT")
            cx.op("sync", lambda s, j=j: s.dma_start(
                out=sxntr[:, :, j * P:(j + 1) * P],
                in_=bf.LNT.rearrange("p (c q) -> p c q", q=P)),
                dma=True, reads=["LNT"], writes=["s_xnt"])
        brow = bf.CL[0:1, 2048 + b * D:2048 + (b + 1) * D]
        ln_tile(cx, bf, brow, bf.SCR3[0:1, 0:D], "CL", "SCR3", nrow=1)
        ln_tile(cx, bf, bf.SCR3[0:1, 0:D], bf.SCR2[0:1, 0:D], "SCR3", "SCR2",
                gamma=bf.CB[0:1, 0:D], beta=bf.CB[0:1, D:2 * D], gkey="CB", nrow=1)
        for dc in range(DCH):
            tr128(cx, bf, bf.SCR2[0:1, dc * P:(dc + 1) * P],
                  bf.LNT[:, dc:dc + 1], "SCR2", "LNT", nout=P)
        cx.op("sync", lambda s: s.dma_start(
            out=sxntr[:, :, SEQ_PER:SEQ_PER + 1], in_=bf.LNT[:, 0:DCH],
            allow_slow_non_contiguous=True),
            dma=True, reads=["LNT"], writes=["s_xnt"])
        cx.barrier()
        for j in range(NPB):
            tj = b * NPB + j
            cx.op("sync", lambda s, j=j: s.dma_start(
                out=bf.LNT.rearrange("p (c q) -> p c q", q=P),
                in_=sxtr[:, :, j * P:(j + 1) * P]),
                dma=True, reads=["s_xt"], writes=["LNT"])
            for dh in range(2):
                for dc in range(DCH):
                    cx.op("sync", lambda s, dc=dc, dh=dh: s.dma_start(
                        out=bf.WTA[:, 0:512],
                        in_=ap["t_Wbt"][dc * P:(dc + 1) * P,
                                        dh * 512:(dh + 1) * 512]),
                        dma=True, writes=["WTA"])
                    cx.op("tensor", lambda t, dc=dc: t.matmul(
                        bf.PSW[:, 0:512], bf.LNT[:, dc * P:(dc + 1) * P],
                        bf.WTA[:, 0:512],
                        start=(dc == 0), stop=(dc == DCH - 1)),
                        reads=["LNT", "WTA"], writes=["psw"])
                cx.op("vector", lambda v, tj=tj, dh=dh: v.tensor_add(
                    out=bf.xtile(tj)[:, dh * 512:(dh + 1) * 512],
                    in0=bf.xtile(tj)[:, dh * 512:(dh + 1) * 512],
                    in1=bf.PSW[:, 0:512]),
                    reads=["psw", "X"], writes=["X"])
            builds = [(lambda t, dc=dc: t.matmul(
                bf.PS0[:, 0:LORA_R], bf.LNT[:, dc * P:(dc + 1) * P],
                bf.IBT[:, dc * LORA_R:(dc + 1) * LORA_R],
                start=(dc == 0), stop=(dc == DCH - 1))) for dc in range(DCH)]
            cx.group("tensor", builds, reads=["LNT", "SMALL"], writes=["ps0"])
            cx.op("vector", lambda v, j=j: v.tensor_copy(
                out=bf.PIN[:, j * LORA_R:(j + 1) * LORA_R],
                in_=bf.PS0[:, 0:LORA_R]),
                reads=["ps0"], writes=["SMALL"])
            cx.op("sync", lambda s, j=j: s.dma_start(
                out=bf.LSCR.rearrange("p (c q) -> p c q", q=P),
                in_=sxntr[:, :, j * P + 1:(j + 1) * P + 1]),
                dma=True, reads=["s_xnt"], writes=["LSCR"])
            builds = [(lambda t, dc=dc: t.matmul(
                bf.PS1[:, 0:LORA_R], bf.LSCR[:, dc * P:(dc + 1) * P],
                bf.G[:, dc * LORA_R:(dc + 1) * LORA_R],
                start=(dc == 0), stop=(dc == DCH - 1))) for dc in range(DCH)]
            cx.group("tensor", builds, reads=["LSCR", "SMALL"], writes=["ps1"])
            cx.op("vector", lambda v, j=j: v.tensor_copy(
                out=bf.PERR[:, j * LORA_R:(j + 1) * LORA_R],
                in_=bf.PS1[:, 0:LORA_R]),
                reads=["ps1"], writes=["SMALL"])
            for dh in range(2):
                for dc in range(DCH):
                    cx.op("sync", lambda s, dc=dc, dh=dh: s.dma_start(
                        out=bf.WTB[:, 0:512],
                        in_=ap["t_Wpt"][dc * P:(dc + 1) * P,
                                        dh * 512:(dh + 1) * 512]),
                        dma=True, writes=["WTB"])
                    cx.op("tensor", lambda t, dc=dc: t.matmul(
                        bf.PSW[:, 512:1024], bf.LSCR[:, dc * P:(dc + 1) * P],
                        bf.WTB[:, 0:512],
                        start=(dc == 0), stop=(dc == DCH - 1)),
                        reads=["LSCR", "WTB"], writes=["psw"])
                cx.op("vector", lambda v, j=j, dh=dh: v.tensor_copy(
                    out=bf.BA[:, j * D + dh * 512: j * D + dh * 512 + 512],
                    in_=bf.PSW[:, 512:1024]),
                    reads=["psw"], writes=["BA"])
        for c in range(NCHB):
            jj = (2 * c, 2 * c + 1)
            for hc in range(DCH):
                builds = []
                for j in jj:
                    builds.append(lambda t, j=j, hc=hc: t.matmul(
                        bf.PS2[:, 0:LORA_R],
                        bf.BA[:, j * D + hc * P: j * D + (hc + 1) * P],
                        bf.PIN[:, j * LORA_R:(j + 1) * LORA_R],
                        start=(j == jj[0]), stop=(j == jj[1])))
                cx.group("tensor", builds, reads=["BA", "SMALL"], writes=["ps2"])
                cx.op("vector", lambda v, c=c, hc=hc: v.tensor_copy(
                    out=bf.SCR[:, c * P + hc * LORA_R: c * P + (hc + 1) * LORA_R],
                    in_=bf.PS2[:, 0:LORA_R]), reads=["ps2"], writes=["SCR"])
                builds = []
                for j in jj:
                    builds.append(lambda t, j=j, hc=hc: t.matmul(
                        bf.PS3[:, 0:LORA_R],
                        bf.BB[:, j * D + hc * P: j * D + (hc + 1) * P],
                        bf.PERR[:, j * LORA_R:(j + 1) * LORA_R],
                        start=(j == jj[0]), stop=(j == jj[1])))
                cx.group("tensor", builds, reads=["BB", "SMALL"], writes=["ps3"])
                cx.op("vector", lambda v, c=c, hc=hc: v.tensor_copy(
                    out=bf.SCR2[:, c * P + hc * LORA_R: c * P + (hc + 1) * LORA_R],
                    in_=bf.PS3[:, 0:LORA_R]), reads=["ps3"], writes=["SCR2"])
            cx.op("sync", lambda s, b=b, c=c: s.dma_start(
                out=ap["s_da"][b, c, :, :], in_=bf.SCR[:, c * P:(c + 1) * P]),
                dma=True, reads=["SCR"], writes=["s_da"])
            cx.op("sync", lambda s, b=b, c=c: s.dma_start(
                out=ap["s_db"][b, c, :, :], in_=bf.SCR2[:, c * P:(c + 1) * P]),
                dma=True, reads=["SCR2"], writes=["s_db"])
        for (buf, bk, off) in ((bf.SCR, "SCR", 0), (bf.SCR2, "SCR2", P)):
            cx.op("vector", lambda v, buf=buf: v.tensor_copy(
                out=bf.LSCR[:, 0:P], in_=buf[:, 0:P]),
                reads=[bk], writes=["LSCR"])
            for c in range(1, NCHB):
                cx.op("vector", lambda v, buf=buf, c=c: v.tensor_add(
                    out=bf.LSCR[:, 0:P], in0=bf.LSCR[:, 0:P],
                    in1=buf[:, c * P:(c + 1) * P]),
                    reads=[bk, "LSCR"], writes=["LSCR"])
            cx.op("sync", lambda s, b=b, off=off: s.dma_start(
                out=ap["g_sums_in"][:, b * 2 * P + off: b * 2 * P + off + P],
                in_=bf.LSCR[:, 0:P]),
                dma=True, reads=["LSCR"], writes=["g_sums_in"])
        cx.barrier()
    cx.op("gpsimd", lambda g: g.collective_compute(
        "AllGather", ALU.bypass, replica_groups=[list(range(NC))],
        ins=[ap["g_sums_in"].opt()], outs=[ap["g_sums_out"].opt()]),
        coll=True, reads=["g_sums_in"], writes=["g_sums_out"])
    for b in range(B):
        sxtr = ap["s_xt"][b].rearrange("c p t -> p c t")
        accA = bf.CB[:, 2 * D + 2 * P:2 * D + 3 * P]
        accB = bf.CB[:, 2 * D + 3 * P:2 * D + 4 * P]
        for (dst, off) in ((accA, 0), (accB, P)):
            for r in range(NC):
                cx.op("sync", lambda s, r=r, b=b, off=off: s.dma_start(
                    out=bf.CL[:, r * P:(r + 1) * P],
                    in_=ap["g_sums_out"][r, :, b * 2 * P + off: b * 2 * P + off + P]),
                    dma=True, reads=["g_sums_out"], writes=["CL"])
            cx.op("vector", lambda v: v.tensor_scalar(
                out=bf.LSCR[:, 0:P], in0=bf.CL[:, 0:P], scalar1=bf.PREFW[:, 0:1],
                scalar2=None, op0=ALU.mult),
                reads=["CL", "SMALL"], writes=["LSCR"])
            for r in range(1, NC):
                cx.op("vector", lambda v, r=r: v.scalar_tensor_tensor(
                    out=bf.LSCR[:, 0:P], in0=bf.CL[:, r * P:(r + 1) * P],
                    scalar=bf.PREFW[:, r:r + 1], op0=ALU.mult, op1=ALU.add,
                    in1=bf.LSCR[:, 0:P]),
                    reads=["CL", "SMALL", "LSCR"], writes=["LSCR"])
            cx.op("vector", lambda v, dst=dst: v.tensor_copy(out=dst,
                                                             in_=bf.LSCR[:, 0:P]),
                  reads=["LSCR"], writes=["CB"])
        for c in range(NCHB):
            cx.op("vector", lambda v: v.scalar_tensor_tensor(
                out=bf.LSCR[:, 0:P], in0=accA, scalar=0.0, op0=ALU.add,
                op1=ALU.mult, in1=accA, accum_out=bf.STAT[:, 8:9]),
                reads=["CB"], writes=["STAT", "LSCR"])
            cx.op("vector", lambda v: v.scalar_tensor_tensor(
                out=bf.LSCR[:, 0:P], in0=accB, scalar=0.0, op0=ALU.add,
                op1=ALU.mult, in1=accB, accum_out=bf.STAT[:, 9:10]),
                reads=["CB"], writes=["STAT", "LSCR"])
            cx.op("vector", lambda v: v.tensor_add(out=bf.STAT[:, 8:9],
                                                   in0=bf.STAT[:, 8:9],
                                                   in1=bf.STAT[:, 9:10]),
                  reads=["STAT"], writes=["STAT"])
            cx.op("tensor", lambda t: t.matmul(bf.PT[0:1, 0:1], bf.STAT[:, 8:9],
                                               bf.ONESC, start=True, stop=True),
                  reads=["STAT", "SMALL"], writes=["pt"])
            cx.op("vector", lambda v: v.tensor_copy(out=bf.STAT[0:1, 10:11],
                                                    in_=bf.PT[0:1, 0:1]),
                  reads=["pt"], writes=["STAT"])
            cx.op("scalar", lambda s: s.activation(
                out=bf.STAT[0:1, 10:11], in_=bf.STAT[0:1, 10:11], func=AF.Sqrt,
                scale=TTT_LR * TTT_LR), reads=["STAT"], writes=["STAT"])
            cx.op("vector", lambda v: v.tensor_scalar(
                out=bf.STAT[0:1, 10:11], in0=bf.STAT[0:1, 10:11], scalar1=1e-6,
                scalar2=None, op0=ALU.add), reads=["STAT"], writes=["STAT"])
            cx.op("vector", lambda v: v.reciprocal(out=bf.STAT[0:1, 10:11],
                                                   in_=bf.STAT[0:1, 10:11]),
                  reads=["STAT"], writes=["STAT"])
            cx.op("vector", lambda v: v.tensor_scalar(
                out=bf.STAT[0:1, 10:11], in0=bf.STAT[0:1, 10:11], scalar1=1.0,
                scalar2=-TTT_LR, op0=ALU.min, op1=ALU.mult),
                reads=["STAT"], writes=["STAT"])
            bcast_scalar(cx, bf, bf.STAT[0:1, 10:11], bf.STAT[:, 11:12],
                         "STAT", "STAT")
            cx.op("vector", lambda v: v.scalar_tensor_tensor(
                out=bf.SCR[:, 0:P], in0=accA, scalar=bf.STAT[:, 11:12],
                op0=ALU.mult, op1=ALU.add, in1=bf.IA),
                reads=["CB", "STAT", "SMALL"], writes=["SCR"])
            cx.op("vector", lambda v: v.scalar_tensor_tensor(
                out=bf.SCR[:, P:2 * P], in0=accB, scalar=bf.STAT[:, 11:12],
                op0=ALU.mult, op1=ALU.add, in1=bf.IBT),
                reads=["CB", "STAT", "SMALL"], writes=["SCR"])
            for hc in range(DCH):
                tr128(cx, bf, bf.SCR[:, hc * LORA_R:(hc + 1) * LORA_R],
                      bf.SCR2[0:LORA_R, hc * P:(hc + 1) * P], "SCR", "SCR2",
                      nout=LORA_R)
            for j in (2 * c, 2 * c + 1):
                tj = b * NPB + j
                cx.op("sync", lambda s, j=j: s.dma_start(
                    out=bf.LNT.rearrange("p (c q) -> p c q", q=P),
                    in_=sxtr[:, :, j * P:(j + 1) * P]),
                    dma=True, reads=["s_xt"], writes=["LNT"])
                builds = [(lambda t, dc=dc: t.matmul(
                    bf.PS0[:, 0:LORA_R], bf.LNT[:, dc * P:(dc + 1) * P],
                    bf.SCR[:, P + dc * LORA_R:P + (dc + 1) * LORA_R],
                    start=(dc == 0), stop=(dc == DCH - 1))) for dc in range(DCH)]
                cx.group("tensor", builds, reads=["LNT", "SCR"], writes=["ps0"])
                cx.op("vector", lambda v: v.tensor_copy(out=bf.LSCR[:, 0:LORA_R],
                                                        in_=bf.PS0[:, 0:LORA_R]),
                      reads=["ps0"], writes=["LSCR"])
                tr128(cx, bf, bf.LSCR[:, 0:LORA_R], bf.LSCR[0:LORA_R, 256:256 + P],
                      "LSCR", "LSCR", nout=LORA_R)
                for dh in range(2):
                    cx.op("tensor", lambda t, dh=dh: t.matmul(
                        bf.PSW[:, dh * 512:(dh + 1) * 512],
                        bf.LSCR[0:LORA_R, 256:256 + P],
                        bf.SCR2[0:LORA_R, dh * 512:(dh + 1) * 512],
                        start=True, stop=True),
                        reads=["LSCR", "SCR2"], writes=["psw"])
                    cx.op("vector", lambda v, tj=tj, dh=dh: v.scalar_tensor_tensor(
                        out=bf.xtile(tj)[:, dh * 512:(dh + 1) * 512],
                        in0=bf.PSW[:, dh * 512:(dh + 1) * 512], scalar=SCALING,
                        op0=ALU.mult, op1=ALU.add,
                        in1=bf.xtile(tj)[:, dh * 512:(dh + 1) * 512]),
                        reads=["psw", "X"], writes=["X"])
            if c < NCHB - 1:
                for (dst, nm) in ((accA, "s_da"), (accB, "s_db")):
                    cx.op("sync", lambda s, b=b, c=c, nm=nm: s.dma_start(
                        out=bf.LSCR[:, 512:512 + P], in_=ap[nm][b, c, :, :]),
                        dma=True, reads=[nm], writes=["LSCR"])
                    cx.op("vector", lambda v, dst=dst: v.tensor_add(
                        out=dst, in0=dst, in1=bf.LSCR[:, 512:512 + P]),
                        reads=["LSCR", "CB"], writes=["CB"])
        cx.barrier()


def emit_qffn(nc, cx, ap, bf):
    XC16 = bf.BA[:, :].bitcast(BF16)
    W16 = bf.BB[:, :].bitcast(BF16)
    CT2 = bf.CB[:, :].bitcast(BF16)
    cx.op("sync", lambda s: s.dma_start(out=bf.SCR3[:, 0:HCH], in_=ap["q_b1"]),
          dma=True, writes=["SCR3"])
    cx.op("sync", lambda s: s.dma_start(out=bf.SCR3[:, 32:32 + DCH],
                                        in_=ap["q_b2"]),
          dma=True, writes=["SCR3"])
    for t in range(NPT):
        ln_tile(cx, bf, bf.xtile(t), bf.LSCR[:, :], "X", "LSCR")
        cx.op("vector", lambda v, t=t: v.reduce_max(
            bf.SCR[:, t:t + 1], bf.LSCR[:, :], axis=AXL.X),
            reads=["LSCR"], writes=["SCR"])
        cx.op("sync", lambda s, t=t: s.dma_start(
            out=ap["s_ln"][t, :, :], in_=bf.LSCR[:, :]),
            dma=True, reads=["LSCR"], writes=["s_ln"])
    cx.op("vector", lambda v: v.reduce_max(bf.STAT[:, 8:9], bf.SCR[:, 0:NPT],
                                           axis=AXL.X),
          reads=["SCR"], writes=["STAT"])
    glob_max_allgather(nc, cx, ap, bf, slot=0)
    cx.op("vector", lambda v: v.tensor_scalar(
        out=bf.SA1, in0=bf.SW1, scalar1=bf.STAT[:, 14:15], scalar2=None,
        op0=ALU.mult), reads=["SMALL", "STAT"], writes=["SMALL"])
    cx.op("vector", lambda v: v.tensor_copy(out=bf.SA1R, in_=bf.SA1),
          reads=["SMALL"], writes=["SMALL"])
    cx.op("vector", lambda v: v.reciprocal(out=bf.SA1R, in_=bf.SA1R),
          reads=["SMALL"], writes=["SMALL"])
    cx.op("vector", lambda v: v.tensor_mul(out=bf.BQ1, in0=bf.SCR3[:, 0:HCH],
                                           in1=bf.SA1R),
          reads=["SCR3", "SMALL"], writes=["SMALL"])
    cx.op("vector", lambda v: v.tensor_scalar(out=bf.BQ1, in0=bf.BQ1,
                                              scalar1=C23, scalar2=-C23,
                                              op0=ALU.add, op1=ALU.add),
          reads=["SMALL"], writes=["SMALL"])
    cx.op("vector", lambda v: v.tensor_mul(out=bf.BQ1, in0=bf.BQ1, in1=bf.SA1),
          reads=["SMALL"], writes=["SMALL"])
    cx.op("vector", lambda v: v.memset(bf.SCR2[:, 0:1], 0.0), writes=["SCR2"])
    for b in range(B):
        for j in range(NPB):
            t = b * NPB + j
            cx.op("sync", lambda s, t=t: s.dma_start(
                out=bf.LSCR[:, :], in_=ap["s_ln"][t, :, :]),
                dma=True, reads=["s_ln"], writes=["LSCR"])
            cx.op("vector", lambda v: v.tensor_scalar(
                out=bf.SCR[:, 0:D], in0=bf.LSCR[:, :], scalar1=bf.STAT[:, 11:12],
                scalar2=-1.0, op0=ALU.mult, op1=ALU.max),
                reads=["LSCR", "STAT"], writes=["SCR"])
            cx.op("vector", lambda v: v.tensor_scalar(
                out=bf.F16T[:, 0:D], in0=bf.SCR[:, 0:D], scalar1=1536.0,
                scalar2=None, op0=ALU.add),
                reads=["SCR"], writes=["F16T"])
            cx.op("vector", lambda v: v.tensor_scalar(
                out=bf.SCR[:, 0:512].bitcast(BF16)[:, 0:D], in0=bf.F16T[:, 0:D],
                scalar1=-1536.0, scalar2=0.0, op0=ALU.add, op1=ALU.max),
                reads=["F16T"], writes=["SCR"])
            cx.op("sync", lambda s, j=j: s.dma_start(
                out=ap["s_xc"][j * P:(j + 1) * P, :],
                in_=bf.SCR[:, 0:512].bitcast(BF16)[:, 0:D]),
                dma=True, reads=["SCR"], writes=["s_xc"])
        cx.barrier()
        for dc in range(DCH):
            cx.op("sync", lambda s, dc=dc: s.dma_start(
                out=XC16[:, dc * SEQ_PER:(dc + 1) * SEQ_PER],
                in_=ap["s_xc"][:, dc * P:(dc + 1) * P], transpose=True),
                dma=True, reads=["s_xc"], writes=["BA"])
        for hc in range(HCH):
            cx.op("sync", lambda s, hc=hc: s.dma_start(
                out=W16[:, 0:DCH * P],
                in_=ap["s_q1ct"][:, :, hc * P:(hc + 1) * P].rearrange(
                    "c p h -> p c h")),
                dma=True, reads=["s_q1ct"], writes=["BB"])
            for j in range(NPB):
                builds = [(lambda t, dc=dc, j=j: t.matmul(
                    bf.PSW[:, 0:P], W16[:, dc * P:(dc + 1) * P],
                    XC16[:, dc * SEQ_PER + j * P: dc * SEQ_PER + (j + 1) * P],
                    start=(dc == 0), stop=(dc == DCH - 1))) for dc in range(DCH)]
                cx.group("tensor", builds, reads=["BB", "BA"], writes=["psw"])
                cx.op("scalar", lambda s, hc=hc: s.activation(
                    out=bf.LSCR[:, 0:P], in_=bf.PSW[:, 0:P], func=AF.Relu,
                    scale=bf.SA1[:, hc:hc + 1], bias=bf.BQ1[:, hc:hc + 1]),
                    reads=["psw", "SMALL"], writes=["LSCR"])
                cx.op("vector", lambda v: v.reduce_max(
                    bf.STAT[:, 8:9], bf.LSCR[:, 0:P], axis=AXL.X),
                    reads=["LSCR"], writes=["STAT"])
                cx.op("vector", lambda v: v.tensor_max(
                    out=bf.SCR2[:, 0:1], in0=bf.SCR2[:, 0:1], in1=bf.STAT[:, 8:9]),
                    reads=["STAT", "SCR2"], writes=["SCR2"])
                cx.op("sync", lambda s, b=b, j=j, hc=hc: s.dma_start(
                    out=ap["s_h"][b * NPB + j, hc, :, :], in_=bf.LSCR[:, 0:P]),
                    dma=True, reads=["LSCR"], writes=["s_h"])
        cx.barrier()
    cx.op("vector", lambda v: v.tensor_copy(out=bf.STAT[:, 8:9],
                                            in_=bf.SCR2[:, 0:1]),
          reads=["SCR2"], writes=["STAT"])
    glob_max_allgather(nc, cx, ap, bf, slot=1)
    cx.op("vector", lambda v: v.tensor_scalar(
        out=bf.SCR[:, 0:DCH], in0=bf.SW2, scalar1=bf.STAT[:, 14:15],
        scalar2=None, op0=ALU.mult), reads=["SMALL", "STAT"], writes=["SCR"])
    cx.op("vector", lambda v: v.tensor_copy(out=bf.SCR[:, 8:8 + DCH],
                                            in_=bf.SCR[:, 0:DCH]),
          reads=["SCR"], writes=["SCR"])
    cx.op("vector", lambda v: v.reciprocal(out=bf.SCR[:, 8:8 + DCH],
                                           in_=bf.SCR[:, 8:8 + DCH]),
          reads=["SCR"], writes=["SCR"])
    cx.op("vector", lambda v: v.tensor_mul(out=bf.SCR[:, 16:16 + DCH],
                                           in0=bf.SCR3[:, 32:32 + DCH],
                                           in1=bf.SCR[:, 8:8 + DCH]),
          reads=["SCR3", "SCR"], writes=["SCR"])
    cx.op("vector", lambda v: v.tensor_scalar(out=bf.SCR[:, 16:16 + DCH],
                                              in0=bf.SCR[:, 16:16 + DCH],
                                              scalar1=C23, scalar2=-C23,
                                              op0=ALU.add, op1=ALU.add),
          reads=["SCR"], writes=["SCR"])
    cx.op("vector", lambda v: v.tensor_mul(out=bf.SCR[:, 16:16 + DCH],
                                           in0=bf.SCR[:, 16:16 + DCH],
                                           in1=bf.SCR[:, 0:DCH]),
          reads=["SCR"], writes=["SCR"])
    for dc in range(DCH):
        rep_row_tile(cx, bf, bf.SCR[:, dc:dc + 1], bf.CL[:, dc * P:(dc + 1) * P],
                     "SCR", "CL")
        rep_row_tile(cx, bf, bf.SCR[:, 16 + dc:17 + dc],
                     bf.CL[:, D + dc * P:D + (dc + 1) * P], "SCR", "CL")
    for dh in range(2):
        cx.op("sync", lambda s, dh=dh: s.dma_start(
            out=W16[:, 0:HCH * 512],
            in_=ap["s_q2ct"][:, :, dh * 512:(dh + 1) * 512].rearrange(
                "c p d -> p c d")),
            dma=True, reads=["s_q2ct"], writes=["BB"])
        for t in range(NPT):
            for hc in range(HCH):
                cx.op("sync", lambda s, t=t, hc=hc: s.dma_start(
                    out=bf.LSCR[:, 0:P], in_=ap["s_h"][t, hc, :, :]),
                    dma=True, reads=["s_h"], writes=["LSCR"])
                cx.op("vector", lambda v, hc=hc: v.tensor_scalar(
                    out=bf.F16T[:, 0:P], in0=bf.LSCR[:, 0:P],
                    scalar1=bf.STAT[:, 11:12], scalar2=1536.0,
                    op0=ALU.mult, op1=ALU.add),
                    reads=["LSCR", "STAT"], writes=["F16T"])
                cx.op("vector", lambda v, hc=hc: v.tensor_scalar(
                    out=CT2[:, hc * P:(hc + 1) * P], in0=bf.F16T[:, 0:P],
                    scalar1=-1536.0, scalar2=None, op0=ALU.add),
                    reads=["F16T"], writes=["CB"])
            builds = [(lambda t2, hc=hc: t2.matmul(
                bf.PSW[:, 0:512], CT2[:, hc * P:(hc + 1) * P],
                W16[:, hc * 512:(hc + 1) * 512],
                start=(hc == 0), stop=(hc == HCH - 1))) for hc in range(HCH)]
            cx.group("tensor", builds, reads=["CB", "BB"], writes=["psw"])
            sl = slice(dh * 512, (dh + 1) * 512)
            cx.op("vector", lambda v, sl=sl: v.tensor_mul(
                out=bf.SCR2[:, 0:512], in0=bf.PSW[:, 0:512],
                in1=bf.CL[:, sl]),
                reads=["psw", "CL"], writes=["SCR2"])
            cx.op("vector", lambda v, sl=sl: v.tensor_add(
                out=bf.SCR2[:, 0:512], in0=bf.SCR2[:, 0:512],
                in1=bf.CL[:, D + sl.start:D + sl.stop]),
                reads=["SCR2", "CL"], writes=["SCR2"])
            cx.op("vector", lambda v, t=t, sl=sl: v.tensor_add(
                out=bf.xtile(t)[:, sl], in0=bf.xtile(t)[:, sl],
                in1=bf.SCR2[:, 0:512]),
                reads=["SCR2", "X"], writes=["X"])
        cx.barrier()


def glob_max_allgather(nc, cx, ap, bf, slot):
    cx.op("tensor", lambda t: t.transpose(bf.PT[0:1, 0:P], bf.STAT[:, 8:9],
                                          bf.IDENT[:, :]),
          reads=["STAT", "IDENT"], writes=["pt"])
    cx.op("vector", lambda v: v.tensor_copy(out=bf.LSCR[0:1, 0:P],
                                            in_=bf.PT[0:1, 0:P]),
          reads=["pt"], writes=["LSCR"])
    cx.op("vector", lambda v: v.reduce_max(bf.STAT[0:1, 9:10],
                                           bf.LSCR[0:1, 0:P], axis=AXL.X),
          reads=["LSCR"], writes=["STAT"])
    gi, go = f"g_max_in{slot}", f"g_max_out{slot}"
    cx.op("sync", lambda s: s.dma_start(
        out=ap[gi][0:1, 0:1], in_=bf.STAT[0:1, 9:10]),
        dma=True, reads=["STAT"], writes=[gi])
    cx.op("gpsimd", lambda g: g.collective_compute(
        "AllGather", ALU.bypass, replica_groups=[list(range(NC))],
        ins=[ap[gi].opt()], outs=[ap[go].opt()]),
        coll=True, reads=[gi], writes=[go])
    cx.op("sync", lambda s: s.dma_start(
        out=bf.LSCR[0:1, 0:NC], in_=ap[go].rearrange("n o -> (n o)")),
        dma=True, reads=[go], writes=["LSCR"])
    cx.op("vector", lambda v: v.reduce_max(bf.STAT[0:1, 9:10],
                                           bf.LSCR[0:1, 0:NC], axis=AXL.X),
          reads=["LSCR"], writes=["STAT"])
    cx.op("vector", lambda v: v.tensor_scalar(
        out=bf.STAT[0:1, 9:10], in0=bf.STAT[0:1, 9:10], scalar1=1e-8,
        scalar2=1.0 / 255.0, op0=ALU.max, op1=ALU.mult),
        reads=["STAT"], writes=["STAT"])
    cx.op("vector", lambda v: v.reciprocal(out=bf.STAT[0:1, 10:11],
                                           in_=bf.STAT[0:1, 9:10]),
          reads=["STAT"], writes=["STAT"])
    bcast_scalar(cx, bf, bf.STAT[0:1, 10:11], bf.STAT[:, 11:12], "STAT", "STAT")
    bcast_scalar(cx, bf, bf.STAT[0:1, 9:10], bf.STAT[:, 14:15], "STAT", "STAT")


def emit_attn_b1(nc, cx, ap, bf):
    for b in range(B):
        for j in range(NPB):
            tj = b * NPB + j
            ln_tile(cx, bf, bf.xtile(tj), bf.LNT[:, :], "X", "LNT")
            for dc in range(DCH):
                tr128(cx, bf, bf.LNT[:, dc * P:(dc + 1) * P],
                      bf.BA[:, dc * SEQ_PER + j * P: dc * SEQ_PER + (j + 1) * P],
                      "LNT", "BA")
        proj_fm_norm(nc, cx, ap, bf, b, "c_Wkt", "g_kt")
        for j in range(NPB):
            for dh in range(2):
                for dc in range(DCH):
                    cx.op("sync", lambda s, dc=dc, dh=dh: s.dma_start(
                        out=bf.WTA[:, 0:512],
                        in_=ap["c_Wvt"][dc * P:(dc + 1) * P,
                                        dh * 512:(dh + 1) * 512]),
                        dma=True, writes=["WTA"])
                    cx.op("tensor", lambda t, dc=dc, j=j: t.matmul(
                        bf.PSW[:, 0:512],
                        bf.BA[:, dc * SEQ_PER + j * P: dc * SEQ_PER + (j + 1) * P],
                        bf.WTA[:, 0:512],
                        start=(dc == 0), stop=(dc == DCH - 1)),
                        reads=["BA", "WTA"], writes=["psw"])
                cx.op("vector", lambda v, dh=dh: v.tensor_copy(
                    out=bf.LSCR[:, dh * 512:(dh + 1) * 512], in_=bf.PSW[:, 0:512]),
                    reads=["psw"], writes=["LSCR"])
            cx.op("sync", lambda s, b=b, j=j: s.dma_start(
                out=ap["g_v"][b * SEQ_PER + j * P: b * SEQ_PER + (j + 1) * P, :],
                in_=bf.LSCR[:, :]),
                dma=True, reads=["LSCR"], writes=["g_v"])
        proj_fm_norm(nc, cx, ap, bf, b, "c_Wqt", None)
        for dc in range(DCH):
            cx.op("sync", lambda s, dc=dc, b=b: s.dma_start(
                out=ap["s_qt"][dc, :, :] if b == 1 else
                ap["s_xnt"][0, dc, :, 0:SEQ_PER],
                in_=bf.BB[:, dc * SEQ_PER:(dc + 1) * SEQ_PER]),
                dma=True, reads=["BB"],
                writes=["s_qt" if b == 1 else "s_xnt"])
        cx.barrier()
    cx.op("gpsimd", lambda g: g.collective_compute(
        "AllGather", ALU.bypass, replica_groups=[list(range(NC))],
        ins=[ap["g_kt"].opt()], outs=[ap["g_kt_all"].opt()]),
        coll=True, reads=["g_kt"], writes=["g_kt_all"])
    cx.op("gpsimd", lambda g: g.collective_compute(
        "AllGather", ALU.bypass, replica_groups=[list(range(NC))],
        ins=[ap["g_v"].opt()], outs=[ap["g_v_all"].opt()]),
        coll=True, reads=["g_v"], writes=["g_v_all"])
    NKC = SEQ // P
    LCH = SEQ_PER // P
    for b in range(B):
        src = "s_xnt" if b == 0 else "s_qt"
        for dc in range(DCH):
            cx.op("sync", lambda s, dc=dc, src=src: s.dma_start(
                out=bf.BA[:, dc * SEQ_PER:(dc + 1) * SEQ_PER],
                in_=(ap["s_xnt"][0, dc, :, 0:SEQ_PER] if src == "s_xnt" else
                     ap["s_qt"][dc, :, :])),
                dma=True, reads=[src], writes=["BA"])
        for h in range(NHEAD):
            oc, pb = h // 2, (h % 2) * DHEAD
            for kc in range(NKC):
                r, lc = kc // LCH, kc % LCH
                g = kc % CG
                cx.op("sync", lambda s, r=r, lc=lc, b=b, oc=oc, pb=pb: s.dma_start(
                    out=bf.SCR3[pb:pb + DHEAD, 0:P],
                    in_=ap["g_kt_all"][r, oc * P + pb: oc * P + pb + DHEAD,
                                       b * SEQ_PER + lc * P:
                                       b * SEQ_PER + (lc + 1) * P]),
                    dma=True, reads=["g_kt_all"], writes=["SCR3"])
                v0 = 256 + g * 80
                cx.op("sync", lambda s, r=r, lc=lc, b=b, h=h, v0=v0: s.dma_start(
                    out=bf.SCR3[:, v0:v0 + DHEAD],
                    in_=ap["g_v_all"][r, b * SEQ_PER + lc * P:
                                      b * SEQ_PER + (lc + 1) * P,
                                      h * DHEAD:(h + 1) * DHEAD]),
                    dma=True, reads=["g_v_all"], writes=["SCR3"])
                if kc < CG:
                    cx.op("vector", lambda v, v0=v0: v.tensor_copy(
                        out=bf.SCR3[:, v0 + DHEAD:v0 + DHEAD + 1], in_=bf.ONESC),
                        reads=["SMALL"], writes=["SCR3"])
                QW = min(512, SEQ_PER)
                for qh in range(SEQ_PER // QW):
                    cx.op("tensor", lambda t, qh=qh, oc=oc, pb=pb, QW=QW: t.matmul(
                        bf.PSW[:, qh * QW:(qh + 1) * QW],
                        bf.SCR3[pb:pb + DHEAD, 0:P],
                        bf.BA[pb:pb + DHEAD,
                              oc * SEQ_PER + qh * QW:
                              oc * SEQ_PER + (qh + 1) * QW],
                        start=True, stop=True),
                        reads=["SCR3", "BA"], writes=["psw"])
                cx.op("vector", lambda v, g=g: v.tensor_scalar(
                    out=bf.CB[:, g * 1024:g * 1024 + SEQ_PER],
                    in0=bf.PSW[:, 0:SEQ_PER], scalar1=0.999, scalar2=-0.999,
                    op0=ALU.min, op1=ALU.max),
                    reads=["psw"], writes=["CB"])
                cx.op("vector", lambda v, g=g: v.tensor_mul(
                    out=bf.CL[:, g * 1024:g * 1024 + SEQ_PER],
                    in0=bf.CB[:, g * 1024:g * 1024 + SEQ_PER],
                    in1=bf.CB[:, g * 1024:g * 1024 + SEQ_PER]),
                    reads=["CB"], writes=["CL"])
                if g == CG - 1 or kc == NKC - 1:
                    gs = range(g + 1)
                    base = kc - g
                    for gg in gs:
                        cx.op("scalar", lambda s, gg=gg: s.activation(
                            out=bf.CL[:, gg * 1024:gg * 1024 + SEQ_PER],
                            in_=bf.CL[:, gg * 1024:gg * 1024 + SEQ_PER],
                            func=AF.Ln, scale=-1.0, bias=1.0),
                            reads=["CL"], writes=["CL"])
                    for gg in gs:
                        cx.op("scalar", lambda s, gg=gg: s.activation(
                            out=bf.CL[:, gg * 1024:gg * 1024 + SEQ_PER],
                            in_=bf.CL[:, gg * 1024:gg * 1024 + SEQ_PER],
                            func=AF.Sigmoid, scale=-0.5),
                            reads=["CL"], writes=["CL"])
                        cx.op("vector", lambda v, gg=gg: v.tensor_mul(
                            out=bf.CB[:, gg * 1024:gg * 1024 + SEQ_PER],
                            in0=bf.CB[:, gg * 1024:gg * 1024 + SEQ_PER],
                            in1=bf.CL[:, gg * 1024:gg * 1024 + SEQ_PER]),
                            reads=["CB", "CL"], writes=["CB"])
                        cx.op("scalar", lambda s, gg=gg: s.activation(
                            out=bf.CB[:, gg * 1024:gg * 1024 + SEQ_PER],
                            in_=bf.CB[:, gg * 1024:gg * 1024 + SEQ_PER],
                            func=AF.Arctan),
                            reads=["CB"], writes=["CB"])
                    for gg in gs:
                        cx.op("scalar", lambda s, gg=gg: s.activation(
                            out=bf.CB[:, gg * 1024:gg * 1024 + SEQ_PER],
                            in_=bf.CB[:, gg * 1024:gg * 1024 + SEQ_PER],
                            func=AF.Ln, scale=TWOPI, bias=bf.HALFC),
                            reads=["CB", "SMALL"], writes=["CB"])
                        cx.op("scalar", lambda s, gg=gg: s.activation(
                            out=bf.CB[:, gg * 1024:gg * 1024 + SEQ_PER],
                            in_=bf.CB[:, gg * 1024:gg * 1024 + SEQ_PER],
                            func=AF.Exp, scale=16.0),
                            reads=["CB"], writes=["CB"])
                    for gg in gs:
                        kk = base + gg
                        v0 = 256 + gg * 80
                        QW = min(512, SEQ_PER)
                        for qh in range(SEQ_PER // QW):
                            ps = bf.PS2 if qh == 0 else bf.PS3
                            pk = "ps2" if qh == 0 else "ps3"
                            cx.op("tensor", lambda t, gg=gg, kk=kk, qh=qh, ps=ps,
                                  v0=v0, QW=QW: t.matmul(
                                      ps[0:DHEAD + 1, 0:QW],
                                      bf.SCR3[:, v0:v0 + DHEAD + 1],
                                      bf.CB[:, gg * 1024 + qh * QW:
                                            gg * 1024 + (qh + 1) * QW],
                                      start=(kk == 0), stop=(kk == NKC - 1)),
                                  reads=["CB", "SCR3"], writes=[pk])
            finish_head(nc, cx, ap, bf, b, h)
        wo_residual(nc, cx, ap, bf, b)
        cx.barrier()


def proj_fm_norm(nc, cx, ap, bf, b, wname, store):
    wr = ap[wname].rearrange("(dc p) o -> p dc o", p=P)
    QW = min(512, SEQ_PER)
    NQH = SEQ_PER // QW
    for oc in range(DCH):
        cx.op("sync", lambda s, oc=oc: s.dma_start(
            out=bf.WTA[:, :], in_=wr[:, :, oc * P:(oc + 1) * P]),
            dma=True, writes=["WTA"])
        cx.op("sync", lambda s, oc=oc: s.dma_start(
            out=bf.WTB[:, 0:NHEAD], in_=ap["ehead"][oc, :, :]),
            dma=True, writes=["WTB"])
        for qh in range(NQH):
            builds = [(lambda t, dc=dc, oc=oc, qh=qh: t.matmul(
                bf.PSW[:, 0:QW], bf.WTA[:, dc * P:(dc + 1) * P],
                bf.BA[:, dc * SEQ_PER + qh * QW: dc * SEQ_PER + qh * QW + QW],
                start=(dc == 0), stop=(dc == DCH - 1))) for dc in range(DCH)]
            cx.group("tensor", builds, reads=["WTA", "BA"], writes=["psw"])
            cx.op("vector", lambda v, oc=oc, qh=qh: v.tensor_copy(
                out=bf.BB[:, oc * SEQ_PER + qh * QW: oc * SEQ_PER + qh * QW + QW],
                in_=bf.PSW[:, 0:QW]),
                reads=["psw"], writes=["BB"])
            cx.op("vector", lambda v, oc=oc, qh=qh: v.tensor_mul(
                out=bf.LSCR[:, 0:QW],
                in0=bf.BB[:, oc * SEQ_PER + qh * QW: oc * SEQ_PER + qh * QW + QW],
                in1=bf.BB[:, oc * SEQ_PER + qh * QW: oc * SEQ_PER + qh * QW + QW]),
                reads=["BB"], writes=["LSCR"])
            ps = bf.PS0 if qh == 0 else bf.PS1
            pk = "ps0" if qh == 0 else "ps1"
            cx.op("tensor", lambda t, oc=oc, ps=ps: t.matmul(
                ps[0:NHEAD, 0:QW], bf.WTB[:, 0:NHEAD], bf.LSCR[:, 0:QW],
                start=(oc == 0), stop=(oc == DCH - 1)),
                reads=["WTB", "LSCR"], writes=[pk])
    for qh in range(NQH):
        ps = bf.PS0 if qh == 0 else bf.PS1
        pk = "ps0" if qh == 0 else "ps1"
        cx.op("scalar", lambda s, ps=ps, qh=qh: s.activation(
            out=bf.SCR2[0:NHEAD, qh * QW:(qh + 1) * QW], in_=ps[0:NHEAD, 0:QW],
            func=AF.Sqrt), reads=[pk], writes=["SCR2"])
    cx.op("vector", lambda v: v.tensor_scalar(
        out=bf.SCR2[0:NHEAD, 0:SEQ_PER], in0=bf.SCR2[0:NHEAD, 0:SEQ_PER],
        scalar1=1e-6, scalar2=None, op0=ALU.max),
        reads=["SCR2"], writes=["SCR2"])
    cx.op("vector", lambda v: v.reciprocal(out=bf.SCR2[0:NHEAD, 0:SEQ_PER],
                                           in_=bf.SCR2[0:NHEAD, 0:SEQ_PER]),
          reads=["SCR2"], writes=["SCR2"])
    for oc in range(DCH):
        cx.op("sync", lambda s, oc=oc: s.dma_start(
            out=bf.WTB[0:NHEAD, 0:P], in_=ap["eheadt"][oc, :, :]),
            dma=True, writes=["WTB"])
        for qh in range(NQH):
            cx.op("tensor", lambda t, oc=oc, qh=qh: t.matmul(
                bf.PSW[:, 0:QW], bf.WTB[0:NHEAD, 0:P],
                bf.SCR2[0:NHEAD, qh * QW:(qh + 1) * QW], start=True, stop=True),
                reads=["WTB", "SCR2"], writes=["psw"])
            cx.op("vector", lambda v, oc=oc, qh=qh: v.tensor_mul(
                out=bf.BB[:, oc * SEQ_PER + qh * QW: oc * SEQ_PER + qh * QW + QW],
                in0=bf.BB[:, oc * SEQ_PER + qh * QW: oc * SEQ_PER + qh * QW + QW],
                in1=bf.PSW[:, 0:QW]),
                reads=["BB", "psw"], writes=["BB"])
    if store is not None:
        for dc in range(DCH):
            cx.op("sync", lambda s, dc=dc, b=b: s.dma_start(
                out=ap[store][dc * P:(dc + 1) * P, b * SEQ_PER:(b + 1) * SEQ_PER],
                in_=bf.BB[:, dc * SEQ_PER:(dc + 1) * SEQ_PER]),
                dma=True, reads=["BB"], writes=[store])


def finish_head(nc, cx, ap, bf, b, h):
    oc, pb = h // 2, (h % 2) * DHEAD
    QW = min(512, SEQ_PER)
    NQH = SEQ_PER // QW
    for qh in range(NQH):
        ps = bf.PS2 if qh == 0 else bf.PS3
        pk = "ps2" if qh == 0 else "ps3"
        cx.op("vector", lambda v, ps=ps, qh=qh, pk=pk: v.tensor_scalar(
            out=bf.LSCR[0:1, qh * QW:(qh + 1) * QW],
            in0=ps[DHEAD:DHEAD + 1, 0:QW], scalar1=1e-6, scalar2=None,
            op0=ALU.add), reads=[pk], writes=["LSCR"])
    cx.op("vector", lambda v: v.reciprocal(out=bf.LSCR[0:1, 0:SEQ_PER],
                                           in_=bf.LSCR[0:1, 0:SEQ_PER]),
          reads=["LSCR"], writes=["LSCR"])
    for qh in range(NQH):
        ps = bf.PS2 if qh == 0 else bf.PS3
        pk = "ps2" if qh == 0 else "ps3"
        cx.op("tensor", lambda t, qh=qh: t.matmul(
            bf.PT[0:DHEAD, 0:QW], bf.ONESR[0:1, 0:DHEAD],
            bf.LSCR[0:1, qh * QW:(qh + 1) * QW], start=True, stop=True),
            reads=["LSCR", "ONESR"], writes=["pt"])
        cx.op("vector", lambda v, qh=qh: v.tensor_copy(
            out=bf.SCR2[0:DHEAD, 0:QW], in_=bf.PT[0:DHEAD, 0:QW]),
            reads=["pt"], writes=["SCR2"])
        cx.op("vector", lambda v, ps=ps, qh=qh, oc=oc, pb=pb, pk=pk: v.tensor_mul(
            out=bf.BB[pb:pb + DHEAD,
                      oc * SEQ_PER + qh * QW: oc * SEQ_PER + qh * QW + QW],
            in0=ps[0:DHEAD, 0:QW], in1=bf.SCR2[0:DHEAD, 0:QW]),
            reads=[pk, "SCR2"], writes=["BB"])


def wo_residual(nc, cx, ap, bf, b):
    cx.op("sync", lambda s: s.dma_start(out=bf.CL[:, 0:D], in_=ap["c_bo"]),
          dma=True, writes=["CL"])
    for j in range(NPB):
        tj = b * NPB + j
        for dh in range(2):
            for oc in range(DCH):
                cx.op("sync", lambda s, oc=oc, dh=dh: s.dma_start(
                    out=bf.WTA[:, 0:512],
                    in_=ap["c_Wot"][oc * P:(oc + 1) * P, dh * 512:(dh + 1) * 512]),
                    dma=True, writes=["WTA"])
                cx.op("tensor", lambda t, oc=oc, j=j: t.matmul(
                    bf.PSW[:, 0:512],
                    bf.BB[:, oc * SEQ_PER + j * P: oc * SEQ_PER + (j + 1) * P],
                    bf.WTA[:, 0:512], start=(oc == 0), stop=(oc == DCH - 1)),
                    reads=["BB", "WTA"], writes=["psw"])
            cx.op("vector", lambda v, dh=dh: v.tensor_add(
                out=bf.SCR2[:, 0:512], in0=bf.PSW[:, 0:512],
                in1=bf.CL[:, dh * 512:(dh + 1) * 512]),
                reads=["psw", "CL"], writes=["SCR2"])
            cx.op("vector", lambda v, tj=tj, dh=dh: v.tensor_add(
                out=bf.xtile(tj)[:, dh * 512:(dh + 1) * 512],
                in0=bf.xtile(tj)[:, dh * 512:(dh + 1) * 512],
                in1=bf.SCR2[:, 0:512]),
                reads=["SCR2", "X"], writes=["X"])


def host_inputs(full, seq=None):
    if seq is None:
        seq = full["L0_attn_a_x"].shape[1]
    set_dims(seq)
    x = np.ascontiguousarray(full["L0_attn_a_x"], dtype=np.float32)
    qstart = seq - QBLK
    rows = np.arange(QBLK) + qstart
    thr = rows - (seq - CTX)
    masks = np.where(np.arange(N_SINKS)[None, :] <= thr[:, None], 0.0, NEG)
    maskw = np.where((np.arange(WINDOW)[None, :] + N_SINKS) <= thr[:, None],
                     0.0, NEG)
    addmask = (thr >= 0).astype(np.float32)[:, None]
    eh = np.zeros((DCH, P, NHEAD), np.float32)
    eht = np.zeros((DCH, NHEAD, P), np.float32)
    for oc in range(DCH):
        for pp in range(P):
            hh = (oc * P + pp) // DHEAD
            eh[oc, pp, hh] = 1.0
            eht[oc, hh, pp] = 1.0
    base = {
        "aa_blk": x[:, qstart:, :],
        "aa_sink": x[:, 0:N_SINKS, :],
        "aa_Wq": full["L0_attn_a_Wq"], "aa_Wk": full["L0_attn_a_Wk"],
        "aa_Wv": full["L0_attn_a_Wv"], "aa_Wo": full["L0_attn_a_Wo"],
        "aa_maskw": maskw, "aa_masks": masks,
        "f0_w1t": full["L0_ffn_w1"].T, "f0_w1": full["L0_ffn_w1"],
        "f0_b1": full["L0_ffn_b1"].reshape(HCH, P).T,
        "f0_w2t": full["L0_ffn_w2"].T, "f0_w2": full["L0_ffn_w2"],
        "f0_b2": np.tile(full["L0_ffn_b2"][None, :], (P, 1)),
        "t_Wbt": full["L1_attn_a_Wb"].T,
        "t_Wpt": full["L1_attn_a_Wp"].T, "t_Wp": full["L1_attn_a_Wp"],
        "t_iA": full["L1_attn_a_init_A"],
        "t_iB_t": full["L1_attn_a_init_B"].T,
        "t_gamma": np.tile(full["L1_attn_a_ln_gamma"][None, :], (P, 1)),
        "t_beta": np.tile(full["L1_attn_a_ln_beta"][None, :], (P, 1)),
        "q_w1": full["L1_ffn_w1"],
        "q_b1": full["L1_ffn_b1"].reshape(HCH, P).T,
        "q_w2": full["L1_ffn_w2"],
        "q_b2": full["L1_ffn_b2"].reshape(DCH, P).T,
        "c_Wqt": full["L1_attn_b_Wq"].T, "c_Wkt": full["L1_attn_b_Wk"].T,
        "c_Wvt": full["L1_attn_b_Wv"].T, "c_Wot": full["L1_attn_b_Wo"].T,
        "c_bo": np.tile(full["L1_attn_b_bo"][None, :], (P, 1)),
        "ehead": eh, "eheadt": eht,
        "ident": np.eye(P, dtype=np.float32),
        "ones_col": np.ones((P, 1), np.float32),
        "ones_row": np.ones((1, P), np.float32),
        "consts": np.tile(np.array([[LN_EPS, 0.5]], np.float32), (P, 1)),
    }
    base = {k: np.ascontiguousarray(v, dtype=np.float32) for k, v in base.items()}
    maps = []
    for i in range(NC):
        m = dict(base)
        m["x0"] = np.ascontiguousarray(x[:, i * SEQ_PER:(i + 1) * SEQ_PER, :])
        own = ((rows >= i * SEQ_PER) & (rows < (i + 1) * SEQ_PER))
        m["aa_addmask"] = np.ascontiguousarray(
            addmask * own[:, None].astype(np.float32), dtype=np.float32)
        oh = np.zeros((NC, 1), np.float32)
        if i + 1 < NC:
            oh[i + 1, 0] = 1.0
        m["onehot_next"] = oh
        pw = np.zeros((P, NC), np.float32)
        pw[:, :i] = 1.0
        m["prefw_rep"] = pw
        maps.append(m)
    return maps


def kernel(**inputs):
    maps = host_inputs(inputs, seq=inputs["L0_attn_a_x"].shape[1])
    nc = build_program()
    res = run_bass_kernel_spmd(nc, maps, core_ids=list(range(NC)))
    y = np.concatenate([r["y"] for r in res.results], axis=1)
    return np.ascontiguousarray(y, dtype=np.float32)


# version 03
# speedup vs baseline: 1.1266x; 1.1266x over previous
import numpy as np
import concourse.bass as bass
import concourse.mybir as mybir
from concourse.bass_utils import run_bass_kernel_spmd

F32 = mybir.dt.float32
F16 = mybir.dt.float16
BF16 = mybir.dt.bfloat16
AF = mybir.ActivationFunctionType
ALU = mybir.AluOpType
AXL = mybir.AxisListType

NC = 8
B = 2
D = 1024
H_FFN = 4096
NHEAD = 16
DHEAD = 64
CHUNK = 256
LORA_R = 16
N_SINKS = 4
WINDOW = 512
CTX = N_SINKS + WINDOW
QBLK = 640
NQT = QBLK // 128
LN_EPS = 1e-5
TTT_LR = 0.02
SCALING = 2.0
NEG = -1e30
TWOPI = float(2.0 / np.pi)
C23 = 8388608.0

P = 128
DCH = D // P
HCH = H_FFN // P
CG = 4

SEQ = SEQ_PER = TOK = NPT = NPB = NCHB = None


def set_dims(seq):
    global SEQ, SEQ_PER, TOK, NPT, NPB, NCHB
    SEQ = seq
    SEQ_PER = seq // NC
    TOK = 2 * SEQ_PER
    NPT = TOK // P
    NPB = NPT // 2
    NCHB = SEQ_PER // CHUNK


set_dims(8192)


class Ctx:
    ENGS = ("vector", "scalar", "tensor", "gpsimd", "sync")

    def __init__(self, nc, block):
        self.nc = nc
        self.block = block
        self.sems = {}
        self.cnt = {}
        self.waited = {e: {} for e in self.ENGS}
        self.lastw = {}
        self.readers = {}
        for e in self.ENGS:
            self._mk("S_" + e)
        self.NDL = 8
        for i in range(self.NDL):
            self._mk(f"S_dma{i}")
        self._mk("S_dmag")
        self._mk("S_coll")
        self._lane = 0

    def _mk(self, name):
        self.sems[name] = self.nc.alloc_semaphore(name)
        self.cnt[name] = 0

    def _need(self, eng, reads, writes):
        need = {}

        def add(s, c):
            if need.get(s, 0) < c:
                need[s] = c
        for r in reads:
            lw = self.lastw.get(r)
            if lw:
                add(*lw)
        for r in writes:
            lw = self.lastw.get(r)
            if lw:
                add(*lw)
            for s, c in self.readers.get(r, {}).items():
                add(s, c)
        return [(s, c) for s, c in need.items() if self.waited[eng].get(s, 0) < c]

    def _emit(self, eng, builds, reads, writes, sem, inc):
        waits = self._need(eng, reads, writes)
        self.cnt[sem] += inc
        tgt = self.cnt[sem]
        semh, sems = self.sems[sem], self.sems

        def body(e):
            for s, c in waits:
                e.wait_ge(sems[s], c)
            last = None
            for b in builds:
                last = b(e)
            last.then_inc(semh, inc)
        getattr(self.block, eng)(body)
        w = self.waited[eng]
        for s, c in waits:
            if w.get(s, 0) < c:
                w[s] = c

        for r in writes:
            self.lastw[r] = (sem, tgt)
            self.readers.pop(r, None)
        for r in reads:
            d = self.readers.setdefault(r, {})
            if d.get(sem, 0) < tgt:
                d[sem] = tgt

    def op(self, eng, build, reads=(), writes=(), dma=False, coll=False):
        if coll:
            sem, inc = "S_coll", 1
        elif dma:
            if eng == "gpsimd":
                sem, inc = "S_dmag", 16
            else:
                sem = f"S_dma{self._lane}"
                self._lane = (self._lane + 1) % self.NDL
                inc = 16
        else:
            sem, inc = "S_" + eng, 1
        if inc == 16 or coll:
            prev = self.cnt[sem]
            if prev > 0 and self.waited[eng].get(sem, 0) < prev:
                semh = self.sems[sem]

                def b2(e, _b=build, _s=semh, _p=prev):
                    e.wait_ge(_s, _p)
                    return _b(e)
                build = b2
                self.waited[eng][sem] = prev
        self._emit(eng, [build], reads, writes, sem, inc)

    def group(self, eng, builds, reads=(), writes=()):
        self._emit(eng, builds, reads, writes, "S_" + eng, 1)

    def barrier(self):
        snap = [(s, c) for s, c in self.cnt.items() if c > 0]
        sems = self.sems
        for eng in self.ENGS:
            todo = [(s, c) for s, c in snap if self.waited[eng].get(s, 0) < c]
            if not todo:
                continue

            def body(e, todo=tuple(todo)):
                for s, c in todo:
                    e.wait_ge(sems[s], c)
            getattr(self.block, eng)(body)
            for s, c in todo:
                self.waited[eng][s] = c


class NS:
    pass


def ln_tile(cx, bf, x_ap, out_ap, xkey, okey, gamma=None, beta=None, gkey=None,
            nrow=P, F=D):
    inv_n = 1.0 / F
    st = bf.STAT
    cx.op("vector", lambda v: v.reduce_sum(st[:nrow, 0:1], x_ap, axis=AXL.X),
          reads=[xkey], writes=["STAT"])
    cx.op("vector", lambda v: v.tensor_scalar(out=st[:nrow, 2:3], in0=st[:nrow, 0:1],
                                              scalar1=inv_n, scalar2=None, op0=ALU.mult),
          reads=["STAT"], writes=["STAT"])
    cx.op("vector", lambda v: v.scalar_tensor_tensor(
        out=bf.LSCR[:nrow, :F], in0=x_ap, scalar=st[:nrow, 2:3], in1=x_ap,
        op0=ALU.subtract, op1=ALU.mult, accum_out=st[:nrow, 1:2]),
        reads=[xkey, "STAT"], writes=["STAT", "LSCR"])
    cx.op("scalar", lambda s: s.activation(out=st[:nrow, 3:4], in_=st[:nrow, 1:2],
                                           func=AF.Sqrt, scale=inv_n,
                                           bias=bf.EPSC[:nrow, :]),
          reads=["STAT", "SMALL"], writes=["STAT"])
    cx.op("vector", lambda v: v.reciprocal(out=st[:nrow, 3:4], in_=st[:nrow, 3:4]),
          reads=["STAT"], writes=["STAT"])
    cx.op("vector", lambda v: v.tensor_scalar(out=out_ap, in0=x_ap,
                                              scalar1=st[:nrow, 2:3],
                                              scalar2=st[:nrow, 3:4],
                                              op0=ALU.subtract, op1=ALU.mult),
          reads=[xkey, "STAT"], writes=[okey])
    if gamma is not None:
        cx.op("vector", lambda v: v.tensor_mul(out=out_ap, in0=out_ap, in1=gamma),
              reads=[okey, gkey], writes=[okey])
        cx.op("vector", lambda v: v.tensor_add(out=out_ap, in0=out_ap, in1=beta),
              reads=[okey, gkey], writes=[okey])


def tr128(cx, bf, src_ap, dst_ap, skey, dkey, nout=None):
    npart = src_ap.shape[0]
    ncols = src_ap.shape[-1]
    cx.op("tensor", lambda t: t.transpose(bf.PT[:ncols, :npart], src_ap,
                                          bf.IDENT[:npart, :npart]),
          reads=[skey, "IDENT"], writes=["pt"])
    cx.op("vector", lambda v: v.tensor_copy(out=dst_ap, in_=bf.PT[:ncols, :npart]),
          reads=["pt"], writes=[dkey])


def bcast_scalar(cx, bf, src11, dst_col, skey, dkey):
    cx.op("tensor", lambda t: t.matmul(bf.PT[:, 0:1], bf.ONESR[0:1, :], src11,
                                       start=True, stop=True),
          reads=[skey, "ONESR"], writes=["pt"])
    cx.op("vector", lambda v: v.tensor_copy(out=dst_col, in_=bf.PT[:, 0:1]),
          reads=["pt"], writes=[dkey])


def rep_row_tile(cx, bf, src_col, dst_tile, skey, dkey):
    cx.op("tensor", lambda t: t.transpose(bf.PT[0:1, 0:P], src_col,
                                          bf.IDENT[:, :]),
          reads=[skey, "IDENT"], writes=["pt"])
    cx.op("vector", lambda v: v.tensor_copy(out=bf.LSCR[0:1, 0:P], in_=bf.PT[0:1, :P]),
          reads=["pt"], writes=["LSCR"])
    cx.op("tensor", lambda t: t.matmul(bf.PT[:, 0:P], bf.ONESR[0:1, :],
                                       bf.LSCR[0:1, 0:P], start=True, stop=True),
          reads=["LSCR", "ONESR"], writes=["pt"])
    cx.op("vector", lambda v: v.tensor_copy(out=dst_tile, in_=bf.PT[:, 0:P]),
          reads=["pt"], writes=[dkey])


def build_program():
    nc = bass.Bass("TRN2", target_bir_lowering=False, debug=False)
    ap = {}

    def din(name, shape, dt=F32):
        ap[name] = nc.dram_tensor(name, shape, dt, kind="ExternalInput").ap()

    def dout(name, shape, dt=F32):
        ap[name] = nc.dram_tensor(name, shape, dt, kind="ExternalOutput").ap()

    def dscr(name, shape, dt=F32):
        ap[name] = nc.dram_tensor(name, shape, dt).ap()

    din("x0", [B, SEQ_PER, D])
    dout("y", [B, SEQ_PER, D])
    din("aa_blk", [B, QBLK, D])
    din("aa_sink", [B, N_SINKS, D])
    din("aa_Wq", [D, D])
    din("aa_Wk", [D, D])
    din("aa_Wv", [D, D])
    din("aa_Wo", [D, D])
    din("aa_maskw", [QBLK, WINDOW])
    din("aa_masks", [QBLK, N_SINKS])
    din("aa_addmask", [QBLK, 1])
    din("f0_w1t", [D, H_FFN])
    din("f0_w1", [H_FFN, D])
    din("f0_b1", [P, HCH])
    din("f0_w2t", [H_FFN, D])
    din("f0_w2", [D, H_FFN])
    din("f0_b2", [P, D])
    din("t_Wbt", [D, D])
    din("t_Wpt", [D, D])
    din("t_Wp", [D, D])
    din("t_iA", [D, LORA_R])
    din("t_iB_t", [D, LORA_R])
    din("t_gamma", [P, D])
    din("t_beta", [P, D])
    din("onehot_next", [NC, 1])
    din("prefw_rep", [P, NC])
    din("q_w1", [H_FFN, D])
    din("q_b1", [P, HCH])
    din("q_w2", [D, H_FFN])
    din("q_b2", [P, DCH])
    din("c_Wqt", [D, D])
    din("c_Wkt", [D, D])
    din("c_Wvt", [D, D])
    din("c_Wot", [D, D])
    din("c_bo", [P, D])
    din("ehead", [DCH, P, NHEAD])
    din("eheadt", [DCH, NHEAD, P])
    din("ident", [P, P])
    din("ones_col", [P, 1])
    din("ones_row", [1, P])
    din("consts", [P, 2])

    dscr("s_w1s", [D, H_FFN])
    dscr("s_w2s", [H_FFN, D])
    dscr("s_q1ct", [DCH, P, H_FFN], BF16)
    dscr("s_q2ct", [HCH, P, D], BF16)
    dscr("s_xt", [B, DCH, P, SEQ_PER])
    dscr("s_xnt", [B, DCH, P, SEQ_PER + 1])
    dscr("s_qt", [DCH, P, SEQ_PER])
    dscr("s_vtok", [NQT, P, D])
    dscr("s_xc", [SEQ_PER, D], BF16)
    dscr("s_h", [B * NPB, HCH, P, P])
    dscr("s_ln", [B * NPB, P, D])
    dscr("s_da", [B, NCHB, P, P])
    dscr("s_db", [B, NCHB, P, P])
    dscr("g_first_in", [B, D])
    dscr("g_first_out", [NC, B, D])
    dscr("g_sums_in", [P, 2 * 2 * P])
    dscr("g_sums_out", [NC, P, 2 * 2 * P])
    dscr("g_max_in0", [1, 1])
    dscr("g_max_out0", [NC, 1])
    dscr("g_max_in1", [1, 1])
    dscr("g_max_out1", [NC, 1])
    dscr("g_kt", [D, TOK])
    dscr("g_kt_all", [NC, D, TOK])
    dscr("g_v", [TOK, D])
    dscr("g_v_all", [NC, TOK, D])

    bf = NS()
    import contextlib
    with contextlib.ExitStack() as _stack:
        def _sb(name, shape, dt=F32):
            return _stack.enter_context(nc.sbuf_tensor(name, shape, dt))

        def _psum(name, shape):
            return _stack.enter_context(nc.psum_tensor(name, shape, F32))
        _X = _sb("X", [P, NPT * D])
        _BA = _sb("BA", [P, 8192])
        _BB = _sb("BB", [P, 8192])
        _CB = _sb("CB", [P, 4224])
        _CL = _sb("CL", [P, 4224])
        _LNT = _sb("LNT", [P, D])
        _LSCR = _sb("LSCR", [P, D])
        _SCR = _sb("SCR", [P, 1056])
        _SCR2 = _sb("SCR2", [P, 1056])
        _SCR3 = _sb("SCR3", [P, 1056])
        _WTA = _sb("WTA", [P, 1024])
        _WTB = _sb("WTB", [P, 1024])
        _SMALL = _sb("SMALL", [P, 1152])
        _STAT = _sb("STAT", [P, 16])
        _IDENT = _sb("IDENT", [P, P])
        _ONESR = _sb("ONESR", [1, P])
        _KRK = _sb("KRK", [64, 2048])
        _VRK = _sb("VRK", [P, 1040])
        _PS0 = _psum("PS0", [P, 512])
        _PS1 = _psum("PS1", [P, 512])
        _PS2 = _psum("PS2", [P, 512])
        _PS3 = _psum("PS3", [P, 512])
        _PSW = _psum("PSW", [P, 1024])
        _PT = _psum("PT", [P, 512])
        block = _stack.enter_context(nc.Block())
        bf.X, bf.BA, bf.BB, bf.CB, bf.CL = _X, _BA, _BB, _CB, _CL
        bf.LNT, bf.LSCR, bf.SCR, bf.SCR2, bf.SCR3 = _LNT, _LSCR, _SCR, _SCR2, _SCR3
        bf.WTA, bf.WTB, bf.SMALL, bf.STAT = _WTA, _WTB, _SMALL, _STAT
        bf.IDENT, bf.ONESR = _IDENT, _ONESR
        bf.KRK, bf.VRK = _KRK, _VRK
        bf.F16T = _SCR2[:, 0:512].bitcast(F16)
        bf.PS0, bf.PS1, bf.PS2, bf.PS3, bf.PSW, bf.PT = _PS0, _PS1, _PS2, _PS3, _PSW, _PT
        bf.A1 = _SMALL[:, 0:32]
        bf.A2 = _SMALL[:, 32:40]
        bf.SW1 = _SMALL[:, 40:72]
        bf.SW1R = _SMALL[:, 72:104]
        bf.SW2 = _SMALL[:, 104:112]
        bf.SW2R = _SMALL[:, 112:120]
        bf.IBT = _SMALL[:, 128:256]
        bf.IA = _SMALL[:, 256:384]
        bf.G = _SMALL[:, 384:512]
        bf.PIN = _SMALL[:, 512:640]
        bf.PERR = _SMALL[:, 640:768]
        bf.BQ1 = _SMALL[:, 768:800]
        bf.SA1 = _SMALL[:, 800:832]
        bf.SA1R = _SMALL[:, 832:864]
        bf.PREFW = _SMALL[:, 1024:1024 + NC]
        bf.ONESC = _SMALL[:, 1040:1041]
        bf.EPSC = _SMALL[:, 1044:1045]
        bf.HALFC = _SMALL[:, 1045:1046]
        cx = Ctx(nc, block)
        bf.xtile = lambda t: bf.X[:, t * D:(t + 1) * D]

        cx.op("sync", lambda s: s.dma_start(
            out=bf.X[:, :], in_=ap["x0"].rearrange("b (t p) d -> p b t d", p=P)),
            dma=True, writes=["X"])
        cx.op("sync", lambda s: s.dma_start(out=bf.IDENT[:, :], in_=ap["ident"]),
              dma=True, writes=["IDENT"])
        cx.op("sync", lambda s: s.dma_start(out=bf.ONESR[:, :], in_=ap["ones_row"]),
              dma=True, writes=["ONESR"])
        cx.op("sync", lambda s: s.dma_start(out=bf.PREFW, in_=ap["prefw_rep"]),
              dma=True, writes=["SMALL"])
        cx.op("sync", lambda s: s.dma_start(out=bf.ONESC, in_=ap["ones_col"]),
              dma=True, writes=["SMALL"])
        cx.op("sync", lambda s: s.dma_start(out=_SMALL[:, 1044:1046],
                                            in_=ap["consts"]),
              dma=True, writes=["SMALL"])
        cx.op("sync", lambda s: s.dma_start(
            out=bf.IBT, in_=ap["t_iB_t"].rearrange("(c p) r -> p c r", p=P)),
            dma=True, writes=["SMALL"])
        cx.op("sync", lambda s: s.dma_start(
            out=bf.IA, in_=ap["t_iA"].rearrange("(c p) r -> p c r", p=P)),
            dma=True, writes=["SMALL"])

        import os as _os
        phases = _os.environ.get("KMOD_PHASES", "all")
        def on(name):
            return phases == "all" or name in phases.split(",")
        emit_setup(nc, cx, ap, bf)
        cx.barrier()
        if on("attn_a"):
            emit_attn_a(nc, cx, ap, bf)
            cx.barrier()
        if on("ffn0a"):
            emit_ffn0(nc, cx, ap, bf)
            cx.barrier()
        if on("attn_b0"):
            emit_attn_b0(nc, cx, ap, bf)
            cx.barrier()
        if on("ffn0b"):
            emit_ffn0(nc, cx, ap, bf)
            cx.barrier()
        if on("ttt"):
            emit_ttt(nc, cx, ap, bf)
            cx.barrier()
        if on("qffn1"):
            emit_qffn(nc, cx, ap, bf)
            cx.barrier()
        if on("attn_b1"):
            emit_attn_b1(nc, cx, ap, bf)
            cx.barrier()
        if on("qffn2"):
            emit_qffn(nc, cx, ap, bf)
            cx.barrier()

        cx.op("sync", lambda s: s.dma_start(
            out=ap["y"].rearrange("b (t p) d -> p b t d", p=P), in_=bf.X[:, :]),
            dma=True, reads=["X"])
        sems = cx.sems
        finals = [(f"S_dma{i}", cx.cnt[f"S_dma{i}"]) for i in range(cx.NDL)]

        def _fin(e):
            for s, c in finals:
                if c > 0:
                    e.wait_ge(sems[s], c)
        block.sync(_fin)
    return nc


def emit_setup(nc, cx, ap, bf):
    for dc in range(DCH):
        for hc in range(DCH):
            cx.op("sync", lambda s, hc=hc, dc=dc: s.dma_start(
                out=bf.WTA[:, 0:P],
                in_=ap["t_Wp"][hc * P:(hc + 1) * P, dc * P:(dc + 1) * P]),
                dma=True, writes=["WTA"])
            cx.op("tensor", lambda t, hc=hc: t.matmul(
                bf.PS0[:, 0:LORA_R], bf.WTA[:, 0:P],
                bf.IA[:, hc * LORA_R:(hc + 1) * LORA_R],
                start=(hc == 0), stop=(hc == DCH - 1)),
                reads=["WTA", "SMALL"], writes=["ps0"])
        cx.op("vector", lambda v, dc=dc: v.tensor_copy(
            out=bf.G[:, dc * LORA_R:(dc + 1) * LORA_R], in_=bf.PS0[:, 0:LORA_R]),
            reads=["ps0"], writes=["SMALL"])
    for i in range(DCH):
        cx.op("sync", lambda s, i=i: s.dma_start(
            out=bf.BA[:, 0:H_FFN // 2],
            in_=ap["f0_w1t"][i * P:(i + 1) * P, 0:H_FFN // 2]),
            dma=True, writes=["BA"])
        cx.op("scalar", lambda s: s.activation(out=bf.BA[:, 0:H_FFN // 2],
                                               in_=bf.BA[:, 0:H_FFN // 2],
                                               func=AF.Sign),
              reads=["BA"], writes=["BA"])
        cx.op("sync", lambda s, i=i: s.dma_start(
            out=ap["s_w1s"][i * P:(i + 1) * P, 0:H_FFN // 2],
            in_=bf.BA[:, 0:H_FFN // 2]),
            dma=True, reads=["BA"], writes=["s_w1s"])
        cx.op("sync", lambda s, i=i: s.dma_start(
            out=bf.BA[:, 0:H_FFN // 2],
            in_=ap["f0_w1t"][i * P:(i + 1) * P, H_FFN // 2:]),
            dma=True, writes=["BA"])
        cx.op("scalar", lambda s: s.activation(out=bf.BA[:, 0:H_FFN // 2],
                                               in_=bf.BA[:, 0:H_FFN // 2],
                                               func=AF.Sign),
              reads=["BA"], writes=["BA"])
        cx.op("sync", lambda s, i=i: s.dma_start(
            out=ap["s_w1s"][i * P:(i + 1) * P, H_FFN // 2:],
            in_=bf.BA[:, 0:H_FFN // 2]),
            dma=True, reads=["BA"], writes=["s_w1s"])
    for i in range(HCH):
        cx.op("sync", lambda s, i=i: s.dma_start(
            out=bf.BA[:, 0:D], in_=ap["f0_w2t"][i * P:(i + 1) * P, :]),
            dma=True, writes=["BA"])
        cx.op("scalar", lambda s: s.activation(out=bf.BA[:, 0:D], in_=bf.BA[:, 0:D],
                                               func=AF.Sign),
              reads=["BA"], writes=["BA"])
        cx.op("sync", lambda s, i=i: s.dma_start(
            out=ap["s_w2s"][i * P:(i + 1) * P, :], in_=bf.BA[:, 0:D]),
            dma=True, reads=["BA"], writes=["s_w2s"])
    for i in range(HCH):
        cx.op("sync", lambda s, i=i: s.dma_start(
            out=bf.BA[:, 0:D], in_=ap["f0_w1"][i * P:(i + 1) * P, :]),
            dma=True, writes=["BA"])
        cx.op("vector", lambda v, i=i: v.reduce_sum(
            bf.A1[:, i:i + 1], bf.BA[:, 0:D], axis=AXL.X,
            apply_absolute_value=True),
            reads=["BA"], writes=["SMALL"])
    cx.op("vector", lambda v: v.tensor_scalar(out=bf.A1, in0=bf.A1,
                                              scalar1=1.0 / D, scalar2=None,
                                              op0=ALU.mult),
          reads=["SMALL"], writes=["SMALL"])
    for i in range(DCH):
        cx.op("sync", lambda s, i=i: s.dma_start(
            out=bf.BA[:, 0:H_FFN], in_=ap["f0_w2"][i * P:(i + 1) * P, :]),
            dma=True, writes=["BA"])
        cx.op("vector", lambda v, i=i: v.reduce_sum(
            bf.A2[:, i:i + 1], bf.BA[:, 0:H_FFN], axis=AXL.X,
            apply_absolute_value=True),
            reads=["BA"], writes=["SMALL"])
    cx.op("vector", lambda v: v.tensor_scalar(out=bf.A2, in0=bf.A2,
                                              scalar1=1.0 / H_FFN, scalar2=None,
                                              op0=ALU.mult),
          reads=["SMALL"], writes=["SMALL"])
    for i in range(HCH):
        cx.op("sync", lambda s, i=i: s.dma_start(
            out=bf.BA[:, 0:D], in_=ap["q_w1"][i * P:(i + 1) * P, :]),
            dma=True, writes=["BA"])
        cx.op("vector", lambda v: v.reduce_max(bf.STAT[:, 8:9], bf.BA[:, 0:D],
                                               axis=AXL.X, apply_absolute_value=True),
              reads=["BA"], writes=["STAT"])
        cx.op("vector", lambda v, i=i: v.tensor_scalar(
            out=bf.SW1[:, i:i + 1], in0=bf.STAT[:, 8:9], scalar1=1e-8,
            scalar2=1.0 / 127.0, op0=ALU.max, op1=ALU.mult),
            reads=["STAT"], writes=["SMALL"])
        cx.op("vector", lambda v, i=i: v.reciprocal(out=bf.SW1R[:, i:i + 1],
                                                    in_=bf.SW1[:, i:i + 1]),
              reads=["SMALL"], writes=["SMALL"])
        cx.op("vector", lambda v, i=i: v.tensor_scalar(
            out=bf.F16T[:, 0:D], in0=bf.BA[:, 0:D], scalar1=bf.SW1R[:, i:i + 1],
            scalar2=1536.0, op0=ALU.mult, op1=ALU.add),
            reads=["BA", "SMALL"], writes=["SCR2"])
        cx.op("vector", lambda v: v.tensor_scalar(
            out=bf.BB[:, 0:D], in0=bf.F16T[:, 0:D], scalar1=-1536.0,
            scalar2=-128.0, op0=ALU.add, op1=ALU.max),
            reads=["SCR2"], writes=["BB"])
        cx.op("vector", lambda v: v.tensor_scalar(
            out=bf.BB[:, 0:D], in0=bf.BB[:, 0:D], scalar1=127.0, scalar2=None,
            op0=ALU.min), reads=["BB"], writes=["BB"])
        for dc in range(DCH):
            tr128(cx, bf, bf.BB[:, dc * P:(dc + 1) * P], bf.LNT[:, 0:P], "BB", "LNT")
            cx.op("vector", lambda v: v.tensor_copy(
                out=bf.SCR2[:, 512:576].bitcast(BF16),
                in_=bf.LNT[:, 0:P]),
                reads=["LNT"], writes=["SCR2"])
            cx.op("sync", lambda s, i=i, dc=dc: s.dma_start(
                out=ap["s_q1ct"][dc, :, i * P:(i + 1) * P],
                in_=bf.SCR2[:, 512:576].bitcast(BF16)),
                dma=True, reads=["SCR2"], writes=["s_q1ct"])
    for i in range(DCH):
        cx.op("sync", lambda s, i=i: s.dma_start(
            out=bf.BA[:, 0:H_FFN], in_=ap["q_w2"][i * P:(i + 1) * P, :]),
            dma=True, writes=["BA"])
        cx.op("vector", lambda v: v.reduce_max(bf.STAT[:, 8:9], bf.BA[:, 0:H_FFN],
                                               axis=AXL.X, apply_absolute_value=True),
              reads=["BA"], writes=["STAT"])
        cx.op("vector", lambda v, i=i: v.tensor_scalar(
            out=bf.SW2[:, i:i + 1], in0=bf.STAT[:, 8:9], scalar1=1e-8,
            scalar2=1.0 / 127.0, op0=ALU.max, op1=ALU.mult),
            reads=["STAT"], writes=["SMALL"])
        cx.op("vector", lambda v, i=i: v.reciprocal(out=bf.SW2R[:, i:i + 1],
                                                    in_=bf.SW2[:, i:i + 1]),
              reads=["SMALL"], writes=["SMALL"])
        for q in range(4):
            sl = slice(q * D, (q + 1) * D)
            cx.op("vector", lambda v, i=i, sl=sl: v.tensor_scalar(
                out=bf.F16T[:, 0:D], in0=bf.BA[:, sl], scalar1=bf.SW2R[:, i:i + 1],
                scalar2=1536.0, op0=ALU.mult, op1=ALU.add),
                reads=["BA", "SMALL"], writes=["SCR2"])
            cx.op("vector", lambda v, sl=sl: v.tensor_scalar(
                out=bf.BB[:, sl], in0=bf.F16T[:, 0:D], scalar1=-1536.0,
                scalar2=-128.0, op0=ALU.add, op1=ALU.max),
                reads=["SCR2"], writes=["BB"])
        cx.op("vector", lambda v: v.tensor_scalar(
            out=bf.BB[:, 0:H_FFN], in0=bf.BB[:, 0:H_FFN], scalar1=127.0,
            scalar2=None, op0=ALU.min), reads=["BB"], writes=["BB"])
        for hc in range(HCH):
            tr128(cx, bf, bf.BB[:, hc * P:(hc + 1) * P], bf.LNT[:, 0:P], "BB", "LNT")
            cx.op("vector", lambda v: v.tensor_copy(
                out=bf.SCR2[:, 512:576].bitcast(BF16),
                in_=bf.LNT[:, 0:P]),
                reads=["LNT"], writes=["SCR2"])
            cx.op("sync", lambda s, i=i, hc=hc: s.dma_start(
                out=ap["s_q2ct"][hc, :, i * P:(i + 1) * P],
                in_=bf.SCR2[:, 512:576].bitcast(BF16)),
                dma=True, reads=["SCR2"], writes=["s_q2ct"])


def emit_attn_a(nc, cx, ap, bf):
    wqr = ap["aa_Wq"].rearrange("(dc p) o -> p dc o", p=P)
    wkr = ap["aa_Wk"].rearrange("(dc p) o -> p dc o", p=P)
    wvr = ap["aa_Wv"].rearrange("(dc p) o -> p dc o", p=P)
    for b in range(B):
        for j in range(NQT):
            cx.op("sync", lambda s, b=b, j=j: s.dma_start(
                out=bf.LNT[:, :], in_=ap["aa_blk"][b, j * P:(j + 1) * P, :]),
                dma=True, writes=["LNT"])
            ln_tile(cx, bf, bf.LNT[:, :], bf.SCR[:, 0:D], "LNT", "SCR")
            for dc in range(DCH):
                tr128(cx, bf, bf.SCR[:, dc * P:(dc + 1) * P],
                      bf.BA[:, dc * QBLK + j * P: dc * QBLK + (j + 1) * P],
                      "SCR", "BA")
        cx.op("sync", lambda s, b=b: s.dma_start(
            out=bf.LNT[0:N_SINKS, :], in_=ap["aa_sink"][b, :, :]),
            dma=True, writes=["LNT"])
        ln_tile(cx, bf, bf.LNT[0:N_SINKS, :], bf.SCR[0:N_SINKS, 0:D], "LNT", "SCR",
                nrow=N_SINKS)
        for dc in range(DCH):
            tr128(cx, bf, bf.SCR[0:N_SINKS, dc * P:(dc + 1) * P],
                  bf.SCR3[:, dc * N_SINKS:(dc + 1) * N_SINKS], "SCR", "SCR3",
                  nout=N_SINKS)

        def blkT(dc, c0, c1):
            return bf.BA[:, dc * QBLK + c0: dc * QBLK + c1]

        for oc in range(DCH):
            cx.op("sync", lambda s, oc=oc: s.dma_start(
                out=bf.WTA[:, :], in_=wqr[:, :, oc * P:(oc + 1) * P]),
                dma=True, writes=["WTA"])
            for c0, c1 in ((0, 512), (512, QBLK)):
                n = c1 - c0
                builds = [(lambda t, dc=dc, c0=c0, c1=c1, n=n: t.matmul(
                    bf.PS0[:, :n], bf.WTA[:, dc * P:(dc + 1) * P],
                    blkT(dc, c0, c1), start=(dc == 0), stop=(dc == DCH - 1)))
                    for dc in range(DCH)]
                cx.group("tensor", builds, reads=["WTA", "BA"], writes=["ps0"])
                cx.op("vector", lambda v, oc=oc, c0=c0, c1=c1, n=n: v.tensor_copy(
                    out=bf.BB[:, oc * QBLK + c0: oc * QBLK + c1], in_=bf.PS0[:, :n]),
                    reads=["ps0"], writes=["BB"])
        for oc in range(DCH):
            cx.op("sync", lambda s, oc=oc: s.dma_start(
                out=bf.WTA[:, :], in_=wkr[:, :, oc * P:(oc + 1) * P]),
                dma=True, writes=["WTA"])
            builds = [(lambda t, dc=dc, oc=oc: t.matmul(
                bf.PS0[:, 0:N_SINKS], bf.WTA[:, dc * P:(dc + 1) * P],
                bf.SCR3[:, dc * N_SINKS:(dc + 1) * N_SINKS],
                start=(dc == 0), stop=(dc == DCH - 1))) for dc in range(DCH)]
            cx.group("tensor", builds, reads=["WTA", "SCR3"], writes=["ps0"])
            cx.op("vector", lambda v, oc=oc: v.tensor_copy(
                out=bf.CB[:, oc * CTX:oc * CTX + N_SINKS], in_=bf.PS0[:, 0:N_SINKS]),
                reads=["ps0"], writes=["CB"])
            for half in range(2):
                c0 = P + half * 256
                builds = [(lambda t, dc=dc, c0=c0: t.matmul(
                    bf.PS0[:, 0:256], bf.WTA[:, dc * P:(dc + 1) * P],
                    blkT(dc, c0, c0 + 256), start=(dc == 0), stop=(dc == DCH - 1)))
                    for dc in range(DCH)]
                cx.group("tensor", builds, reads=["WTA", "BA"], writes=["ps0"])
                cx.op("vector", lambda v, oc=oc, half=half: v.tensor_copy(
                    out=bf.CB[:, oc * CTX + N_SINKS + half * 256:
                              oc * CTX + N_SINKS + (half + 1) * 256],
                    in_=bf.PS0[:, 0:256]),
                    reads=["ps0"], writes=["CB"])
        for ci in range(NQT):
            nrow = N_SINKS if ci == 0 else P
            for dh in range(2):
                for dc in range(DCH):
                    if ci == 0:
                        lh = bf.SCR3[:, dc * N_SINKS:(dc + 1) * N_SINKS]
                        lk = "SCR3"
                    else:
                        lh = blkT(dc, ci * P, (ci + 1) * P)
                        lk = "BA"
                    cx.op("sync", lambda s, dc=dc, dh=dh: s.dma_start(
                        out=bf.WTB[:, 0:512],
                        in_=ap["aa_Wv"][dc * P:(dc + 1) * P,
                                        dh * 512:(dh + 1) * 512]),
                        dma=True, writes=["WTB"])
                    cx.op("tensor", lambda t, dc=dc, lh=lh, nrow=nrow: t.matmul(
                        bf.PS2[:nrow, :], lh, bf.WTB[:, 0:512],
                        start=(dc == 0), stop=(dc == DCH - 1)),
                        reads=["WTB", lk], writes=["ps2"])
                cx.op("vector", lambda v, nrow=nrow, dh=dh: v.tensor_copy(
                    out=bf.LSCR[:nrow, dh * 512:(dh + 1) * 512],
                    in_=bf.PS2[:nrow, :]),
                    reads=["ps2"], writes=["LSCR"])
            cx.op("sync", lambda s, ci=ci, nrow=nrow: s.dma_start(
                out=ap["s_vtok"][ci, 0:nrow, :], in_=bf.LSCR[0:nrow, :]),
                dma=True, reads=["LSCR"], writes=["s_vtok"])
        cx.barrier()
        for j in range(NQT):
            cx.op("sync", lambda s, j=j: s.dma_start(
                out=bf.CL[:, 0:WINDOW], in_=ap["aa_maskw"][j * P:(j + 1) * P, :]),
                dma=True, writes=["CL"])
            cx.op("sync", lambda s, j=j: s.dma_start(
                out=bf.CL[:, WINDOW:CTX], in_=ap["aa_masks"][j * P:(j + 1) * P, :]),
                dma=True, writes=["CL"])
            cx.op("sync", lambda s, j=j: s.dma_start(
                out=bf.STAT[:, 12:13], in_=ap["aa_addmask"][j * P:(j + 1) * P, :]),
                dma=True, writes=["STAT"])
            for h in range(NHEAD):
                oc, pb = h // 2, (h % 2) * DHEAD
                qT = bf.BB[pb:pb + DHEAD, oc * QBLK + j * P: oc * QBLK + (j + 1) * P]
                cx.group("tensor", [
                    lambda t: t.matmul(bf.PSW[:, 512:512 + N_SINKS], qT,
                                       bf.CB[pb:pb + DHEAD,
                                             oc * CTX:oc * CTX + N_SINKS],
                                       start=True, stop=True),
                    lambda t: t.matmul(bf.PSW[:, 0:256], qT,
                                       bf.CB[pb:pb + DHEAD,
                                             oc * CTX + N_SINKS:
                                             oc * CTX + N_SINKS + 256],
                                       start=True, stop=True),
                    lambda t: t.matmul(bf.PSW[:, 256:512], qT,
                                       bf.CB[pb:pb + DHEAD,
                                             oc * CTX + N_SINKS + 256:oc * CTX + CTX],
                                       start=True, stop=True),
                ], reads=["BB", "CB"], writes=["psw"])
                cx.op("vector", lambda v: v.scalar_tensor_tensor(
                    out=bf.SCR[:, 0:N_SINKS], in0=bf.PSW[:, 512:512 + N_SINKS],
                    scalar=0.125, op0=ALU.mult, op1=ALU.add,
                    in1=bf.CL[:, WINDOW:CTX]),
                    reads=["psw", "CL"], writes=["SCR"])
                cx.op("vector", lambda v: v.scalar_tensor_tensor(
                    out=bf.SCR[:, N_SINKS:CTX], in0=bf.PSW[:, 0:WINDOW],
                    scalar=0.125, op0=ALU.mult, op1=ALU.add, in1=bf.CL[:, 0:WINDOW]),
                    reads=["psw", "CL"], writes=["SCR"])
                cx.op("vector", lambda v: v.reduce_max(
                    bf.STAT[:, 8:9], bf.SCR[:, 0:CTX], axis=AXL.X, negate=True),
                    reads=["SCR"], writes=["STAT"])
                cx.op("scalar", lambda s: s.activation(
                    out=bf.SCR[:, 0:CTX], in_=bf.SCR[:, 0:CTX], func=AF.Exp,
                    bias=bf.STAT[:, 8:9], accum_out=bf.STAT[:, 9:10]),
                    reads=["SCR", "STAT"], writes=["SCR", "STAT"])
                cx.op("vector", lambda v: v.reciprocal(out=bf.STAT[:, 9:10],
                                                       in_=bf.STAT[:, 9:10]),
                      reads=["STAT"], writes=["STAT"])
                cx.op("vector", lambda v: v.tensor_scalar(
                    out=bf.SCR[:, 0:CTX], in0=bf.SCR[:, 0:CTX],
                    scalar1=bf.STAT[:, 9:10], scalar2=None, op0=ALU.mult),
                    reads=["SCR", "STAT"], writes=["SCR"])
                for ci in range(NQT):
                    if ci == 0:
                        pc, nrow = slice(0, N_SINKS), N_SINKS
                    else:
                        pc, nrow = slice(N_SINKS + (ci - 1) * P, N_SINKS + ci * P), P
                    cx.op("tensor", lambda t, pc=pc, nrow=nrow: t.transpose(
                        bf.PT[:nrow, 0:P], bf.SCR[:, pc], bf.IDENT[:, :]),
                        reads=["SCR", "IDENT"], writes=["pt"])
                    cx.op("vector", lambda v, nrow=nrow: v.tensor_copy(
                        out=bf.SCR2[0:nrow, 0:P], in_=bf.PT[0:nrow, 0:P]),
                        reads=["pt"], writes=["SCR2"])
                    cx.op("sync", lambda s, ci=ci, h=h, nrow=nrow: s.dma_start(
                        out=bf.SCR2[0:nrow, 128:128 + DHEAD],
                        in_=ap["s_vtok"][ci, 0:nrow, h * DHEAD:(h + 1) * DHEAD]),
                        dma=True, reads=["s_vtok"], writes=["SCR2"])
                    cx.op("tensor", lambda t, nrow=nrow, ci=ci: t.matmul(
                        bf.PS3[0:DHEAD, 0:P], bf.SCR2[0:nrow, 128:128 + DHEAD],
                        bf.SCR2[0:nrow, 0:P], start=(ci == 0), stop=(ci == NQT - 1)),
                        reads=["SCR2"], writes=["ps3"])
                cx.op("vector", lambda v: v.tensor_copy(
                    out=bf.SCR2[0:DHEAD, 256:256 + P], in_=bf.PS3[0:DHEAD, 0:P]),
                    reads=["ps3"], writes=["SCR2"])
                for dh in range(2):
                    cx.op("sync", lambda s, h=h, dh=dh: s.dma_start(
                        out=bf.WTA[0:DHEAD, 0:512],
                        in_=ap["aa_Wo"][h * DHEAD:(h + 1) * DHEAD,
                                        dh * 512:(dh + 1) * 512]),
                        dma=True, writes=["WTA"])
                    ps = bf.PS0 if dh == 0 else bf.PS1
                    cx.op("tensor", lambda t, h=h, ps=ps: t.matmul(
                        ps[:, :], bf.SCR2[0:DHEAD, 256:256 + P],
                        bf.WTA[0:DHEAD, 0:512],
                        start=(h == 0), stop=(h == NHEAD - 1)),
                        reads=["SCR2", "WTA"], writes=["ps0" if dh == 0 else "ps1"])
            qstart_t = (SEQ - QBLK) // P
            tj = b * NPB + ((qstart_t + j) % NPB)
            for dh in range(2):
                ps = bf.PS0 if dh == 0 else bf.PS1
                cx.op("vector", lambda v, tj=tj, dh=dh, ps=ps: v.scalar_tensor_tensor(
                    out=bf.xtile(tj)[:, dh * 512:(dh + 1) * 512], in0=ps[:, :],
                    scalar=bf.STAT[:, 12:13], op0=ALU.mult, op1=ALU.add,
                    in1=bf.xtile(tj)[:, dh * 512:(dh + 1) * 512]),
                    reads=["ps0" if dh == 0 else "ps1", "STAT", "X"], writes=["X"])
        cx.barrier()


def emit_ffn0(nc, cx, ap, bf):
    cx.op("sync", lambda s: s.dma_start(out=bf.SCR3[:, 0:HCH], in_=ap["f0_b1"]),
          dma=True, writes=["SCR3"])
    cx.op("sync", lambda s: s.dma_start(out=bf.SCR3[:, 32:32 + D], in_=ap["f0_b2"]),
          dma=True, writes=["SCR3"])
    for dc in range(DCH):
        rep_row_tile(cx, bf, bf.A2[:, dc:dc + 1],
                     bf.CL[:, dc * P:(dc + 1) * P], "SMALL", "CL")
    w1r = ap["s_w1s"].rearrange("(dc p) h -> p dc h", p=P)
    for b in range(B):
        for j in range(NPB):
            tj = b * NPB + j
            ln_tile(cx, bf, bf.xtile(tj), bf.LNT[:, :], "X", "LNT")
            for dc in range(DCH):
                tr128(cx, bf, bf.LNT[:, dc * P:(dc + 1) * P],
                      bf.BA[:, dc * SEQ_PER + j * P: dc * SEQ_PER + (j + 1) * P],
                      "LNT", "BA")
        TT = 256
        for tt in range(SEQ_PER // TT):
            c0 = tt * TT
            for hc in range(HCH):
                cx.op("sync", lambda s, hc=hc: s.dma_start(
                    out=bf.WTA[:, :], in_=w1r[:, :, hc * P:(hc + 1) * P]),
                    dma=True, writes=["WTA"])
                cx.op("sync", lambda s, hc=hc: s.dma_start(
                    out=bf.WTB[:, :], in_=ap["s_w2s"][hc * P:(hc + 1) * P, :]),
                    dma=True, writes=["WTB"])
                builds = [(lambda t, dc=dc, c0=c0: t.matmul(
                    bf.PSW[:, 0:TT], bf.WTA[:, dc * P:(dc + 1) * P],
                    bf.BA[:, dc * SEQ_PER + c0: dc * SEQ_PER + c0 + TT],
                    start=(dc == 0), stop=(dc == DCH - 1))) for dc in range(DCH)]
                cx.group("tensor", builds, reads=["WTA", "BA"], writes=["psw"])
                cx.op("scalar", lambda s, hc=hc: s.activation(
                    out=bf.SCR[:, 0:TT], in_=bf.PSW[:, 0:TT], func=AF.Gelu,
                    scale=bf.A1[:, hc:hc + 1], bias=bf.SCR3[:, hc:hc + 1]),
                    reads=["psw", "SMALL", "SCR3"], writes=["SCR"])
                for half in range(2):
                    for dh in range(2):
                        pk = f"ps{half * 2 + dh}"
                        ps = (bf.PS0, bf.PS1, bf.PS2, bf.PS3)[half * 2 + dh]
                        cx.op("tensor", lambda t, half=half, dh=dh, hc=hc, ps=ps:
                              t.matmul(ps[:, :], bf.SCR[:, half * P:(half + 1) * P],
                                       bf.WTB[:, dh * 512:(dh + 1) * 512],
                                       start=(hc == 0), stop=(hc == HCH - 1)),
                              reads=["SCR", "WTB"], writes=[pk])
            for half in range(2):
                tj = b * NPB + (c0 + half * P) // P
                for dh in range(2):
                    pk = f"ps{half * 2 + dh}"
                    ps = (bf.PS0, bf.PS1, bf.PS2, bf.PS3)[half * 2 + dh]
                    sl = slice(dh * 512, (dh + 1) * 512)
                    cx.op("vector", lambda v, ps=ps, sl=sl, pk=pk: v.tensor_mul(
                        out=bf.SCR2[:, 0:512], in0=ps[:, :], in1=bf.CL[:, sl]),
                        reads=[pk, "CL"], writes=["SCR2"])
                    cx.op("vector", lambda v, sl=sl: v.tensor_add(
                        out=bf.SCR2[:, 0:512], in0=bf.SCR2[:, 0:512],
                        in1=bf.SCR3[:, 32 + sl.start:32 + sl.stop]),
                        reads=["SCR2", "SCR3"], writes=["SCR2"])
                    cx.op("vector", lambda v, tj=tj, sl=sl: v.tensor_add(
                        out=bf.xtile(tj)[:, sl], in0=bf.xtile(tj)[:, sl],
                        in1=bf.SCR2[:, 0:512]),
                        reads=["SCR2", "X"], writes=["X"])
        cx.barrier()


def emit_attn_b0(nc, cx, ap, bf):
    inv_sq = float(1.0 / np.sqrt(D))
    st = bf.STAT
    for j in range(NPB):
        t0, t1 = j, NPB + j
        ln_tile(cx, bf, bf.xtile(t0), bf.SCR[:, 0:D], "X", "SCR")
        ln_tile(cx, bf, bf.xtile(t1), bf.SCR2[:, 0:D], "X", "SCR2")
        for (c, i0, i1, k0, k1) in ((8, bf.SCR, bf.SCR, "SCR", "SCR"),
                                    (9, bf.SCR, bf.SCR2, "SCR", "SCR2"),
                                    (10, bf.SCR2, bf.SCR2, "SCR2", "SCR2")):
            cx.op("vector", lambda v, c=c, i0=i0, i1=i1: v.scalar_tensor_tensor(
                out=bf.LSCR[:, :], in0=i0[:, 0:D], scalar=0.0, op0=ALU.add,
                op1=ALU.mult, in1=i1[:, 0:D], accum_out=st[:, c:c + 1]),
                reads=[k0, k1], writes=["STAT", "LSCR"])
        cx.op("vector", lambda v: v.tensor_scalar(
            out=st[:, 8:11], in0=st[:, 8:11], scalar1=inv_sq, scalar2=None,
            op0=ALU.mult), reads=["STAT"], writes=["STAT"])
        cx.op("vector", lambda v: v.tensor_max(out=st[:, 11:12], in0=st[:, 8:9],
                                               in1=st[:, 9:10]),
              reads=["STAT"], writes=["STAT"])
        cx.op("vector", lambda v: v.tensor_scalar(out=st[:, 11:12],
                                                  in0=st[:, 11:12], scalar1=-1.0,
                                                  scalar2=None, op0=ALU.mult),
              reads=["STAT"], writes=["STAT"])
        cx.op("vector", lambda v: v.tensor_max(out=st[:, 12:13], in0=st[:, 9:10],
                                               in1=st[:, 10:11]),
              reads=["STAT"], writes=["STAT"])
        cx.op("vector", lambda v: v.tensor_scalar(out=st[:, 12:13],
                                                  in0=st[:, 12:13], scalar1=-1.0,
                                                  scalar2=None, op0=ALU.mult),
              reads=["STAT"], writes=["STAT"])
        for (c, sc, mc) in ((0, 8, 11), (1, 9, 11), (2, 9, 12), (3, 10, 12)):
            cx.op("scalar", lambda s, c=c, sc=sc, mc=mc: s.activation(
                out=st[:, c:c + 1], in_=st[:, sc:sc + 1], func=AF.Exp,
                bias=st[:, mc:mc + 1]),
                reads=["STAT"], writes=["STAT"])
        cx.op("vector", lambda v: v.tensor_add(out=st[:, 4:5], in0=st[:, 0:1],
                                               in1=st[:, 1:2]),
              reads=["STAT"], writes=["STAT"])
        cx.op("vector", lambda v: v.reciprocal(out=st[:, 4:5], in_=st[:, 4:5]),
              reads=["STAT"], writes=["STAT"])
        cx.op("vector", lambda v: v.tensor_add(out=st[:, 5:6], in0=st[:, 2:3],
                                               in1=st[:, 3:4]),
              reads=["STAT"], writes=["STAT"])
        cx.op("vector", lambda v: v.reciprocal(out=st[:, 5:6], in_=st[:, 5:6]),
              reads=["STAT"], writes=["STAT"])
        for (c, r) in ((0, 4), (1, 4), (2, 5), (3, 5)):
            cx.op("vector", lambda v, c=c, r=r: v.tensor_scalar(
                out=st[:, c:c + 1], in0=st[:, c:c + 1], scalar1=st[:, r:r + 1],
                scalar2=None, op0=ALU.mult),
                reads=["STAT"], writes=["STAT"])
        cx.op("vector", lambda v, t0=t0: v.scalar_tensor_tensor(
            out=bf.LSCR[:, :], in0=bf.SCR2[:, 0:D], scalar=st[:, 1:2],
            op0=ALU.mult, op1=ALU.add, in1=bf.xtile(t0)),
            reads=["SCR2", "STAT", "X"], writes=["LSCR"])
        cx.op("vector", lambda v, t0=t0: v.scalar_tensor_tensor(
            out=bf.xtile(t0), in0=bf.SCR[:, 0:D], scalar=st[:, 0:1],
            op0=ALU.mult, op1=ALU.add, in1=bf.LSCR[:, :]),
            reads=["SCR", "STAT", "LSCR"], writes=["X"])
        cx.op("vector", lambda v, t1=t1: v.scalar_tensor_tensor(
            out=bf.LSCR[:, :], in0=bf.SCR[:, 0:D], scalar=st[:, 2:3],
            op0=ALU.mult, op1=ALU.add, in1=bf.xtile(t1)),
            reads=["SCR", "STAT", "X"], writes=["LSCR"])
        cx.op("vector", lambda v, t1=t1: v.scalar_tensor_tensor(
            out=bf.xtile(t1), in0=bf.SCR2[:, 0:D], scalar=st[:, 3:4],
            op0=ALU.mult, op1=ALU.add, in1=bf.LSCR[:, :]),
            reads=["SCR2", "STAT", "LSCR"], writes=["X"])


def emit_ttt(nc, cx, ap, bf):
    SEG = SEQ_PER + 1
    for b in range(B):
        cx.op("sync", lambda s, b=b: s.dma_start(
            out=ap["g_first_in"][b:b + 1, :],
            in_=bf.X[0:1, b * NPB * D: b * NPB * D + D]),
            dma=True, reads=["X"], writes=["g_first_in"])
    cx.op("gpsimd", lambda g: g.collective_compute(
        "AllGather", ALU.bypass, replica_groups=[list(range(NC))],
        ins=[ap["g_first_in"].opt()], outs=[ap["g_first_out"].opt()]),
        coll=True, reads=["g_first_in"], writes=["g_first_out"])
    cx.op("sync", lambda s: s.dma_start(
        out=bf.CL[0:NC, 0:B * D],
        in_=ap["g_first_out"].rearrange("n b d -> n (b d)")),
        dma=True, reads=["g_first_out"], writes=["CL"])
    cx.op("sync", lambda s: s.dma_start(out=bf.SCR2[0:NC, 0:1],
                                        in_=ap["onehot_next"]),
          dma=True, writes=["SCR2"])
    for q in range(B * D // 512):
        cx.op("tensor", lambda t, q=q: t.matmul(
            bf.PSW[0:1, 0:512], bf.SCR2[0:NC, 0:1],
            bf.CL[0:NC, q * 512:(q + 1) * 512], start=True, stop=True),
            reads=["CL", "SCR2"], writes=["psw"])
        cx.op("vector", lambda v, q=q: v.tensor_copy(
            out=bf.CL[0:1, 2048 + q * 512:2048 + (q + 1) * 512],
            in_=bf.PSW[0:1, 0:512]),
            reads=["psw"], writes=["CL"])
    cx.op("sync", lambda s: s.dma_start(out=bf.CB[:, 0:D], in_=ap["t_gamma"]),
          dma=True, writes=["CB"])
    cx.op("sync", lambda s: s.dma_start(out=bf.CB[:, D:2 * D], in_=ap["t_beta"]),
          dma=True, writes=["CB"])
    wbtr = ap["t_Wbt"].rearrange("(dc p) o -> p dc o", p=P)
    wptr = ap["t_Wpt"].rearrange("(dc p) o -> p dc o", p=P)
    for b in range(B):
        sxtr = ap["s_xt"][b].rearrange("c p t -> p c t")
        sxntr = ap["s_xnt"][b].rearrange("c p t -> p c t")
        for j in range(NPB):
            tj = b * NPB + j
            ln_tile(cx, bf, bf.xtile(tj), bf.BB[:, j * D:(j + 1) * D], "X", "BB")
            for dc in range(DCH):
                tr128(cx, bf, bf.BB[:, j * D + dc * P: j * D + (dc + 1) * P],
                      bf.LNT[:, dc * P:(dc + 1) * P], "BB", "LNT")
            cx.op("sync", lambda s, j=j: s.dma_start(
                out=sxtr[:, :, j * P:(j + 1) * P],
                in_=bf.LNT.rearrange("p (c q) -> p c q", q=P)),
                dma=True, reads=["LNT"], writes=["s_xt"])
            ln_tile(cx, bf, bf.BB[:, j * D:(j + 1) * D], bf.LSCR[:, :], "BB",
                    "LSCR", gamma=bf.CB[:, 0:D], beta=bf.CB[:, D:2 * D], gkey="CB")
            for dc in range(DCH):
                tr128(cx, bf, bf.LSCR[:, dc * P:(dc + 1) * P],
                      bf.LNT[:, dc * P:(dc + 1) * P], "LSCR", "LNT")
            cx.op("sync", lambda s, j=j: s.dma_start(
                out=sxntr[:, :, j * P:(j + 1) * P],
                in_=bf.LNT.rearrange("p (c q) -> p c q", q=P)),
                dma=True, reads=["LNT"], writes=["s_xnt"])
        brow = bf.CL[0:1, 2048 + b * D:2048 + (b + 1) * D]
        ln_tile(cx, bf, brow, bf.SCR3[0:1, 0:D], "CL", "SCR3", nrow=1)
        ln_tile(cx, bf, bf.SCR3[0:1, 0:D], bf.SCR2[0:1, 0:D], "SCR3", "SCR2",
                gamma=bf.CB[0:1, 0:D], beta=bf.CB[0:1, D:2 * D], gkey="CB", nrow=1)
        for dc in range(DCH):
            tr128(cx, bf, bf.SCR2[0:1, dc * P:(dc + 1) * P],
                  bf.LNT[:, dc:dc + 1], "SCR2", "LNT", nout=P)
        cx.op("sync", lambda s: s.dma_start(
            out=sxntr[:, :, SEQ_PER:SEQ_PER + 1], in_=bf.LNT[:, 0:DCH],
            allow_slow_non_contiguous=True),
            dma=True, reads=["LNT"], writes=["s_xnt"])
        cx.barrier()
        for j in range(NPB):
            tj = b * NPB + j
            cx.op("sync", lambda s, j=j: s.dma_start(
                out=bf.LNT.rearrange("p (c q) -> p c q", q=P),
                in_=sxtr[:, :, j * P:(j + 1) * P]),
                dma=True, reads=["s_xt"], writes=["LNT"])
            for dh in range(2):
                for dc in range(DCH):
                    cx.op("sync", lambda s, dc=dc, dh=dh: s.dma_start(
                        out=bf.WTA[:, 0:512],
                        in_=ap["t_Wbt"][dc * P:(dc + 1) * P,
                                        dh * 512:(dh + 1) * 512]),
                        dma=True, writes=["WTA"])
                    cx.op("tensor", lambda t, dc=dc: t.matmul(
                        bf.PSW[:, 0:512], bf.LNT[:, dc * P:(dc + 1) * P],
                        bf.WTA[:, 0:512],
                        start=(dc == 0), stop=(dc == DCH - 1)),
                        reads=["LNT", "WTA"], writes=["psw"])
                cx.op("vector", lambda v, tj=tj, dh=dh: v.tensor_add(
                    out=bf.xtile(tj)[:, dh * 512:(dh + 1) * 512],
                    in0=bf.xtile(tj)[:, dh * 512:(dh + 1) * 512],
                    in1=bf.PSW[:, 0:512]),
                    reads=["psw", "X"], writes=["X"])
            builds = [(lambda t, dc=dc: t.matmul(
                bf.PS0[:, 0:LORA_R], bf.LNT[:, dc * P:(dc + 1) * P],
                bf.IBT[:, dc * LORA_R:(dc + 1) * LORA_R],
                start=(dc == 0), stop=(dc == DCH - 1))) for dc in range(DCH)]
            cx.group("tensor", builds, reads=["LNT", "SMALL"], writes=["ps0"])
            cx.op("vector", lambda v, j=j: v.tensor_copy(
                out=bf.PIN[:, j * LORA_R:(j + 1) * LORA_R],
                in_=bf.PS0[:, 0:LORA_R]),
                reads=["ps0"], writes=["SMALL"])
            cx.op("sync", lambda s, j=j: s.dma_start(
                out=bf.LSCR.rearrange("p (c q) -> p c q", q=P),
                in_=sxntr[:, :, j * P + 1:(j + 1) * P + 1]),
                dma=True, reads=["s_xnt"], writes=["LSCR"])
            builds = [(lambda t, dc=dc: t.matmul(
                bf.PS1[:, 0:LORA_R], bf.LSCR[:, dc * P:(dc + 1) * P],
                bf.G[:, dc * LORA_R:(dc + 1) * LORA_R],
                start=(dc == 0), stop=(dc == DCH - 1))) for dc in range(DCH)]
            cx.group("tensor", builds, reads=["LSCR", "SMALL"], writes=["ps1"])
            cx.op("vector", lambda v, j=j: v.tensor_copy(
                out=bf.PERR[:, j * LORA_R:(j + 1) * LORA_R],
                in_=bf.PS1[:, 0:LORA_R]),
                reads=["ps1"], writes=["SMALL"])
            for dh in range(2):
                for dc in range(DCH):
                    cx.op("sync", lambda s, dc=dc, dh=dh: s.dma_start(
                        out=bf.WTB[:, 0:512],
                        in_=ap["t_Wpt"][dc * P:(dc + 1) * P,
                                        dh * 512:(dh + 1) * 512]),
                        dma=True, writes=["WTB"])
                    cx.op("tensor", lambda t, dc=dc: t.matmul(
                        bf.PSW[:, 512:1024], bf.LSCR[:, dc * P:(dc + 1) * P],
                        bf.WTB[:, 0:512],
                        start=(dc == 0), stop=(dc == DCH - 1)),
                        reads=["LSCR", "WTB"], writes=["psw"])
                cx.op("vector", lambda v, j=j, dh=dh: v.tensor_copy(
                    out=bf.BA[:, j * D + dh * 512: j * D + dh * 512 + 512],
                    in_=bf.PSW[:, 512:1024]),
                    reads=["psw"], writes=["BA"])
        for c in range(NCHB):
            jj = (2 * c, 2 * c + 1)
            for hc in range(DCH):
                builds = []
                for j in jj:
                    builds.append(lambda t, j=j, hc=hc: t.matmul(
                        bf.PS2[:, 0:LORA_R],
                        bf.BA[:, j * D + hc * P: j * D + (hc + 1) * P],
                        bf.PIN[:, j * LORA_R:(j + 1) * LORA_R],
                        start=(j == jj[0]), stop=(j == jj[1])))
                cx.group("tensor", builds, reads=["BA", "SMALL"], writes=["ps2"])
                cx.op("vector", lambda v, c=c, hc=hc: v.tensor_copy(
                    out=bf.SCR[:, c * P + hc * LORA_R: c * P + (hc + 1) * LORA_R],
                    in_=bf.PS2[:, 0:LORA_R]), reads=["ps2"], writes=["SCR"])
                builds = []
                for j in jj:
                    builds.append(lambda t, j=j, hc=hc: t.matmul(
                        bf.PS3[:, 0:LORA_R],
                        bf.BB[:, j * D + hc * P: j * D + (hc + 1) * P],
                        bf.PERR[:, j * LORA_R:(j + 1) * LORA_R],
                        start=(j == jj[0]), stop=(j == jj[1])))
                cx.group("tensor", builds, reads=["BB", "SMALL"], writes=["ps3"])
                cx.op("vector", lambda v, c=c, hc=hc: v.tensor_copy(
                    out=bf.SCR2[:, c * P + hc * LORA_R: c * P + (hc + 1) * LORA_R],
                    in_=bf.PS3[:, 0:LORA_R]), reads=["ps3"], writes=["SCR2"])
            cx.op("sync", lambda s, b=b, c=c: s.dma_start(
                out=ap["s_da"][b, c, :, :], in_=bf.SCR[:, c * P:(c + 1) * P]),
                dma=True, reads=["SCR"], writes=["s_da"])
            cx.op("sync", lambda s, b=b, c=c: s.dma_start(
                out=ap["s_db"][b, c, :, :], in_=bf.SCR2[:, c * P:(c + 1) * P]),
                dma=True, reads=["SCR2"], writes=["s_db"])
        for (buf, bk, off) in ((bf.SCR, "SCR", 0), (bf.SCR2, "SCR2", P)):
            cx.op("vector", lambda v, buf=buf: v.tensor_copy(
                out=bf.LSCR[:, 0:P], in_=buf[:, 0:P]),
                reads=[bk], writes=["LSCR"])
            for c in range(1, NCHB):
                cx.op("vector", lambda v, buf=buf, c=c: v.tensor_add(
                    out=bf.LSCR[:, 0:P], in0=bf.LSCR[:, 0:P],
                    in1=buf[:, c * P:(c + 1) * P]),
                    reads=[bk, "LSCR"], writes=["LSCR"])
            cx.op("sync", lambda s, b=b, off=off: s.dma_start(
                out=ap["g_sums_in"][:, b * 2 * P + off: b * 2 * P + off + P],
                in_=bf.LSCR[:, 0:P]),
                dma=True, reads=["LSCR"], writes=["g_sums_in"])
        cx.barrier()
    cx.op("gpsimd", lambda g: g.collective_compute(
        "AllGather", ALU.bypass, replica_groups=[list(range(NC))],
        ins=[ap["g_sums_in"].opt()], outs=[ap["g_sums_out"].opt()]),
        coll=True, reads=["g_sums_in"], writes=["g_sums_out"])
    for b in range(B):
        sxtr = ap["s_xt"][b].rearrange("c p t -> p c t")
        accA = bf.CB[:, 2 * D + 2 * P:2 * D + 3 * P]
        accB = bf.CB[:, 2 * D + 3 * P:2 * D + 4 * P]
        for (dst, off) in ((accA, 0), (accB, P)):
            for r in range(NC):
                cx.op("sync", lambda s, r=r, b=b, off=off: s.dma_start(
                    out=bf.CL[:, r * P:(r + 1) * P],
                    in_=ap["g_sums_out"][r, :, b * 2 * P + off: b * 2 * P + off + P]),
                    dma=True, reads=["g_sums_out"], writes=["CL"])
            cx.op("vector", lambda v: v.tensor_scalar(
                out=bf.LSCR[:, 0:P], in0=bf.CL[:, 0:P], scalar1=bf.PREFW[:, 0:1],
                scalar2=None, op0=ALU.mult),
                reads=["CL", "SMALL"], writes=["LSCR"])
            for r in range(1, NC):
                cx.op("vector", lambda v, r=r: v.scalar_tensor_tensor(
                    out=bf.LSCR[:, 0:P], in0=bf.CL[:, r * P:(r + 1) * P],
                    scalar=bf.PREFW[:, r:r + 1], op0=ALU.mult, op1=ALU.add,
                    in1=bf.LSCR[:, 0:P]),
                    reads=["CL", "SMALL", "LSCR"], writes=["LSCR"])
            cx.op("vector", lambda v, dst=dst: v.tensor_copy(out=dst,
                                                             in_=bf.LSCR[:, 0:P]),
                  reads=["LSCR"], writes=["CB"])
        for c in range(NCHB):
            cx.op("vector", lambda v: v.scalar_tensor_tensor(
                out=bf.LSCR[:, 0:P], in0=accA, scalar=0.0, op0=ALU.add,
                op1=ALU.mult, in1=accA, accum_out=bf.STAT[:, 8:9]),
                reads=["CB"], writes=["STAT", "LSCR"])
            cx.op("vector", lambda v: v.scalar_tensor_tensor(
                out=bf.LSCR[:, 0:P], in0=accB, scalar=0.0, op0=ALU.add,
                op1=ALU.mult, in1=accB, accum_out=bf.STAT[:, 9:10]),
                reads=["CB"], writes=["STAT", "LSCR"])
            cx.op("vector", lambda v: v.tensor_add(out=bf.STAT[:, 8:9],
                                                   in0=bf.STAT[:, 8:9],
                                                   in1=bf.STAT[:, 9:10]),
                  reads=["STAT"], writes=["STAT"])
            cx.op("tensor", lambda t: t.matmul(bf.PT[0:1, 0:1], bf.STAT[:, 8:9],
                                               bf.ONESC, start=True, stop=True),
                  reads=["STAT", "SMALL"], writes=["pt"])
            cx.op("vector", lambda v: v.tensor_copy(out=bf.STAT[0:1, 10:11],
                                                    in_=bf.PT[0:1, 0:1]),
                  reads=["pt"], writes=["STAT"])
            cx.op("scalar", lambda s: s.activation(
                out=bf.STAT[0:1, 10:11], in_=bf.STAT[0:1, 10:11], func=AF.Sqrt,
                scale=TTT_LR * TTT_LR), reads=["STAT"], writes=["STAT"])
            cx.op("vector", lambda v: v.tensor_scalar(
                out=bf.STAT[0:1, 10:11], in0=bf.STAT[0:1, 10:11], scalar1=1e-6,
                scalar2=None, op0=ALU.add), reads=["STAT"], writes=["STAT"])
            cx.op("vector", lambda v: v.reciprocal(out=bf.STAT[0:1, 10:11],
                                                   in_=bf.STAT[0:1, 10:11]),
                  reads=["STAT"], writes=["STAT"])
            cx.op("vector", lambda v: v.tensor_scalar(
                out=bf.STAT[0:1, 10:11], in0=bf.STAT[0:1, 10:11], scalar1=1.0,
                scalar2=-TTT_LR, op0=ALU.min, op1=ALU.mult),
                reads=["STAT"], writes=["STAT"])
            bcast_scalar(cx, bf, bf.STAT[0:1, 10:11], bf.STAT[:, 11:12],
                         "STAT", "STAT")
            cx.op("vector", lambda v: v.scalar_tensor_tensor(
                out=bf.SCR[:, 0:P], in0=accA, scalar=bf.STAT[:, 11:12],
                op0=ALU.mult, op1=ALU.add, in1=bf.IA),
                reads=["CB", "STAT", "SMALL"], writes=["SCR"])
            cx.op("vector", lambda v: v.scalar_tensor_tensor(
                out=bf.SCR[:, P:2 * P], in0=accB, scalar=bf.STAT[:, 11:12],
                op0=ALU.mult, op1=ALU.add, in1=bf.IBT),
                reads=["CB", "STAT", "SMALL"], writes=["SCR"])
            for hc in range(DCH):
                tr128(cx, bf, bf.SCR[:, hc * LORA_R:(hc + 1) * LORA_R],
                      bf.SCR2[0:LORA_R, hc * P:(hc + 1) * P], "SCR", "SCR2",
                      nout=LORA_R)
            for j in (2 * c, 2 * c + 1):
                tj = b * NPB + j
                cx.op("sync", lambda s, j=j: s.dma_start(
                    out=bf.LNT.rearrange("p (c q) -> p c q", q=P),
                    in_=sxtr[:, :, j * P:(j + 1) * P]),
                    dma=True, reads=["s_xt"], writes=["LNT"])
                builds = [(lambda t, dc=dc: t.matmul(
                    bf.PS0[:, 0:LORA_R], bf.LNT[:, dc * P:(dc + 1) * P],
                    bf.SCR[:, P + dc * LORA_R:P + (dc + 1) * LORA_R],
                    start=(dc == 0), stop=(dc == DCH - 1))) for dc in range(DCH)]
                cx.group("tensor", builds, reads=["LNT", "SCR"], writes=["ps0"])
                cx.op("vector", lambda v: v.tensor_copy(out=bf.LSCR[:, 0:LORA_R],
                                                        in_=bf.PS0[:, 0:LORA_R]),
                      reads=["ps0"], writes=["LSCR"])
                tr128(cx, bf, bf.LSCR[:, 0:LORA_R], bf.LSCR[0:LORA_R, 256:256 + P],
                      "LSCR", "LSCR", nout=LORA_R)
                for dh in range(2):
                    cx.op("tensor", lambda t, dh=dh: t.matmul(
                        bf.PSW[:, dh * 512:(dh + 1) * 512],
                        bf.LSCR[0:LORA_R, 256:256 + P],
                        bf.SCR2[0:LORA_R, dh * 512:(dh + 1) * 512],
                        start=True, stop=True),
                        reads=["LSCR", "SCR2"], writes=["psw"])
                    cx.op("vector", lambda v, tj=tj, dh=dh: v.scalar_tensor_tensor(
                        out=bf.xtile(tj)[:, dh * 512:(dh + 1) * 512],
                        in0=bf.PSW[:, dh * 512:(dh + 1) * 512], scalar=SCALING,
                        op0=ALU.mult, op1=ALU.add,
                        in1=bf.xtile(tj)[:, dh * 512:(dh + 1) * 512]),
                        reads=["psw", "X"], writes=["X"])
            if c < NCHB - 1:
                for (dst, nm) in ((accA, "s_da"), (accB, "s_db")):
                    cx.op("sync", lambda s, b=b, c=c, nm=nm: s.dma_start(
                        out=bf.LSCR[:, 512:512 + P], in_=ap[nm][b, c, :, :]),
                        dma=True, reads=[nm], writes=["LSCR"])
                    cx.op("vector", lambda v, dst=dst: v.tensor_add(
                        out=dst, in0=dst, in1=bf.LSCR[:, 512:512 + P]),
                        reads=["LSCR", "CB"], writes=["CB"])
        cx.barrier()


def emit_qffn(nc, cx, ap, bf):
    XC16 = bf.BA[:, :].bitcast(BF16)
    W16 = bf.BB[:, :].bitcast(BF16)
    CT2 = bf.CB[:, :].bitcast(BF16)
    cx.op("sync", lambda s: s.dma_start(out=bf.SCR3[:, 0:HCH], in_=ap["q_b1"]),
          dma=True, writes=["SCR3"])
    cx.op("sync", lambda s: s.dma_start(out=bf.SCR3[:, 32:32 + DCH],
                                        in_=ap["q_b2"]),
          dma=True, writes=["SCR3"])
    for t in range(NPT):
        ln_tile(cx, bf, bf.xtile(t), bf.LSCR[:, :], "X", "LSCR")
        cx.op("vector", lambda v, t=t: v.reduce_max(
            bf.SCR[:, t:t + 1], bf.LSCR[:, :], axis=AXL.X),
            reads=["LSCR"], writes=["SCR"])
        cx.op("sync", lambda s, t=t: s.dma_start(
            out=ap["s_ln"][t, :, :], in_=bf.LSCR[:, :]),
            dma=True, reads=["LSCR"], writes=["s_ln"])
    cx.op("vector", lambda v: v.reduce_max(bf.STAT[:, 8:9], bf.SCR[:, 0:NPT],
                                           axis=AXL.X),
          reads=["SCR"], writes=["STAT"])
    glob_max_allgather(nc, cx, ap, bf, slot=0)
    cx.op("vector", lambda v: v.tensor_scalar(
        out=bf.SA1, in0=bf.SW1, scalar1=bf.STAT[:, 14:15], scalar2=None,
        op0=ALU.mult), reads=["SMALL", "STAT"], writes=["SMALL"])
    cx.op("vector", lambda v: v.tensor_copy(out=bf.SA1R, in_=bf.SA1),
          reads=["SMALL"], writes=["SMALL"])
    cx.op("vector", lambda v: v.reciprocal(out=bf.SA1R, in_=bf.SA1R),
          reads=["SMALL"], writes=["SMALL"])
    cx.op("vector", lambda v: v.tensor_mul(out=bf.BQ1, in0=bf.SCR3[:, 0:HCH],
                                           in1=bf.SA1R),
          reads=["SCR3", "SMALL"], writes=["SMALL"])
    cx.op("vector", lambda v: v.tensor_scalar(out=bf.BQ1, in0=bf.BQ1,
                                              scalar1=C23, scalar2=-C23,
                                              op0=ALU.add, op1=ALU.add),
          reads=["SMALL"], writes=["SMALL"])
    cx.op("vector", lambda v: v.tensor_mul(out=bf.BQ1, in0=bf.BQ1, in1=bf.SA1),
          reads=["SMALL"], writes=["SMALL"])
    cx.op("vector", lambda v: v.memset(bf.STAT[:, 0:1], 0.0), writes=["STAT"])
    for b in range(B):
        for j in range(NPB):
            t = b * NPB + j
            cx.op("sync", lambda s, t=t: s.dma_start(
                out=bf.LSCR[:, :], in_=ap["s_ln"][t, :, :]),
                dma=True, reads=["s_ln"], writes=["LSCR"])
            cx.op("vector", lambda v: v.tensor_scalar(
                out=bf.SCR[:, 0:D], in0=bf.LSCR[:, :], scalar1=bf.STAT[:, 11:12],
                scalar2=-1.0, op0=ALU.mult, op1=ALU.max),
                reads=["LSCR", "STAT"], writes=["SCR"])
            cx.op("vector", lambda v: v.tensor_scalar(
                out=bf.F16T[:, 0:D], in0=bf.SCR[:, 0:D], scalar1=1536.0,
                scalar2=None, op0=ALU.add),
                reads=["SCR"], writes=["SCR2"])
            cx.op("vector", lambda v: v.tensor_scalar(
                out=bf.SCR[:, 0:512].bitcast(BF16)[:, 0:D], in0=bf.F16T[:, 0:D],
                scalar1=-1536.0, scalar2=0.0, op0=ALU.add, op1=ALU.max),
                reads=["SCR2"], writes=["SCR"])
            cx.op("sync", lambda s, j=j: s.dma_start(
                out=ap["s_xc"][j * P:(j + 1) * P, :],
                in_=bf.SCR[:, 0:512].bitcast(BF16)[:, 0:D]),
                dma=True, reads=["SCR"], writes=["s_xc"])
        cx.barrier()
        for dc in range(DCH):
            cx.op("sync", lambda s, dc=dc: s.dma_start(
                out=XC16[:, dc * SEQ_PER:(dc + 1) * SEQ_PER],
                in_=ap["s_xc"][:, dc * P:(dc + 1) * P], transpose=True),
                dma=True, reads=["s_xc"], writes=["BA"])
        for hc in range(HCH):
            cx.op("sync", lambda s, hc=hc: s.dma_start(
                out=W16[:, 0:DCH * P],
                in_=ap["s_q1ct"][:, :, hc * P:(hc + 1) * P].rearrange(
                    "c p h -> p c h")),
                dma=True, reads=["s_q1ct"], writes=["BB"])
            for j in range(NPB):
                builds = [(lambda t, dc=dc, j=j: t.matmul(
                    bf.PSW[:, 0:P], W16[:, dc * P:(dc + 1) * P],
                    XC16[:, dc * SEQ_PER + j * P: dc * SEQ_PER + (j + 1) * P],
                    start=(dc == 0), stop=(dc == DCH - 1))) for dc in range(DCH)]
                cx.group("tensor", builds, reads=["BB", "BA"], writes=["psw"])
                cx.op("scalar", lambda s, hc=hc: s.activation(
                    out=bf.LSCR[:, 0:P], in_=bf.PSW[:, 0:P], func=AF.Relu,
                    scale=bf.SA1[:, hc:hc + 1], bias=bf.BQ1[:, hc:hc + 1]),
                    reads=["psw", "SMALL"], writes=["LSCR"])
                cx.op("vector", lambda v: v.reduce_max(
                    bf.STAT[:, 8:9], bf.LSCR[:, 0:P], axis=AXL.X),
                    reads=["LSCR"], writes=["STAT"])
                cx.op("vector", lambda v: v.tensor_max(
                    out=bf.STAT[:, 0:1], in0=bf.STAT[:, 0:1], in1=bf.STAT[:, 8:9]),
                    reads=["STAT"], writes=["STAT"])
                cx.op("sync", lambda s, b=b, j=j, hc=hc: s.dma_start(
                    out=ap["s_h"][b * NPB + j, hc, :, :], in_=bf.LSCR[:, 0:P]),
                    dma=True, reads=["LSCR"], writes=["s_h"])
        cx.barrier()
    cx.op("vector", lambda v: v.tensor_copy(out=bf.STAT[:, 8:9],
                                            in_=bf.STAT[:, 0:1]),
          reads=["STAT"], writes=["STAT"])
    glob_max_allgather(nc, cx, ap, bf, slot=1)
    cx.op("vector", lambda v: v.tensor_scalar(
        out=bf.SCR[:, 0:DCH], in0=bf.SW2, scalar1=bf.STAT[:, 14:15],
        scalar2=None, op0=ALU.mult), reads=["SMALL", "STAT"], writes=["SCR"])
    cx.op("vector", lambda v: v.tensor_copy(out=bf.SCR[:, 8:8 + DCH],
                                            in_=bf.SCR[:, 0:DCH]),
          reads=["SCR"], writes=["SCR"])
    cx.op("vector", lambda v: v.reciprocal(out=bf.SCR[:, 8:8 + DCH],
                                           in_=bf.SCR[:, 8:8 + DCH]),
          reads=["SCR"], writes=["SCR"])
    cx.op("vector", lambda v: v.tensor_mul(out=bf.SCR[:, 16:16 + DCH],
                                           in0=bf.SCR3[:, 32:32 + DCH],
                                           in1=bf.SCR[:, 8:8 + DCH]),
          reads=["SCR3", "SCR"], writes=["SCR"])
    cx.op("vector", lambda v: v.tensor_scalar(out=bf.SCR[:, 16:16 + DCH],
                                              in0=bf.SCR[:, 16:16 + DCH],
                                              scalar1=C23, scalar2=-C23,
                                              op0=ALU.add, op1=ALU.add),
          reads=["SCR"], writes=["SCR"])
    cx.op("vector", lambda v: v.tensor_mul(out=bf.SCR[:, 16:16 + DCH],
                                           in0=bf.SCR[:, 16:16 + DCH],
                                           in1=bf.SCR[:, 0:DCH]),
          reads=["SCR"], writes=["SCR"])
    for dc in range(DCH):
        rep_row_tile(cx, bf, bf.SCR[:, dc:dc + 1], bf.CL[:, dc * P:(dc + 1) * P],
                     "SCR", "CL")
        rep_row_tile(cx, bf, bf.SCR[:, 16 + dc:17 + dc],
                     bf.CL[:, D + dc * P:D + (dc + 1) * P], "SCR", "CL")
    for dh in range(2):
        cx.op("sync", lambda s, dh=dh: s.dma_start(
            out=W16[:, 0:HCH * 512],
            in_=ap["s_q2ct"][:, :, dh * 512:(dh + 1) * 512].rearrange(
                "c p d -> p c d")),
            dma=True, reads=["s_q2ct"], writes=["BB"])
        for t in range(NPT):
            for hc in range(HCH):
                cx.op("sync", lambda s, t=t, hc=hc: s.dma_start(
                    out=bf.LSCR[:, 0:P], in_=ap["s_h"][t, hc, :, :]),
                    dma=True, reads=["s_h"], writes=["LSCR"])
                cx.op("vector", lambda v, hc=hc: v.tensor_scalar(
                    out=bf.F16T[:, 0:P], in0=bf.LSCR[:, 0:P],
                    scalar1=bf.STAT[:, 11:12], scalar2=1536.0,
                    op0=ALU.mult, op1=ALU.add),
                    reads=["LSCR", "STAT"], writes=["SCR2"])
                cx.op("vector", lambda v, hc=hc: v.tensor_scalar(
                    out=CT2[:, hc * P:(hc + 1) * P], in0=bf.F16T[:, 0:P],
                    scalar1=-1536.0, scalar2=None, op0=ALU.add),
                    reads=["SCR2"], writes=["CB"])
            builds = [(lambda t2, hc=hc: t2.matmul(
                bf.PSW[:, 0:512], CT2[:, hc * P:(hc + 1) * P],
                W16[:, hc * 512:(hc + 1) * 512],
                start=(hc == 0), stop=(hc == HCH - 1))) for hc in range(HCH)]
            cx.group("tensor", builds, reads=["CB", "BB"], writes=["psw"])
            sl = slice(dh * 512, (dh + 1) * 512)
            cx.op("vector", lambda v, sl=sl: v.tensor_mul(
                out=bf.SCR2[:, 0:512], in0=bf.PSW[:, 0:512],
                in1=bf.CL[:, sl]),
                reads=["psw", "CL"], writes=["SCR2"])
            cx.op("vector", lambda v, sl=sl: v.tensor_add(
                out=bf.SCR2[:, 0:512], in0=bf.SCR2[:, 0:512],
                in1=bf.CL[:, D + sl.start:D + sl.stop]),
                reads=["SCR2", "CL"], writes=["SCR2"])
            cx.op("vector", lambda v, t=t, sl=sl: v.tensor_add(
                out=bf.xtile(t)[:, sl], in0=bf.xtile(t)[:, sl],
                in1=bf.SCR2[:, 0:512]),
                reads=["SCR2", "X"], writes=["X"])
        cx.barrier()


def glob_max_allgather(nc, cx, ap, bf, slot):
    cx.op("tensor", lambda t: t.transpose(bf.PT[0:1, 0:P], bf.STAT[:, 8:9],
                                          bf.IDENT[:, :]),
          reads=["STAT", "IDENT"], writes=["pt"])
    cx.op("vector", lambda v: v.tensor_copy(out=bf.LSCR[0:1, 0:P],
                                            in_=bf.PT[0:1, 0:P]),
          reads=["pt"], writes=["LSCR"])
    cx.op("vector", lambda v: v.reduce_max(bf.STAT[0:1, 9:10],
                                           bf.LSCR[0:1, 0:P], axis=AXL.X),
          reads=["LSCR"], writes=["STAT"])
    gi, go = f"g_max_in{slot}", f"g_max_out{slot}"
    cx.op("sync", lambda s: s.dma_start(
        out=ap[gi][0:1, 0:1], in_=bf.STAT[0:1, 9:10]),
        dma=True, reads=["STAT"], writes=[gi])
    cx.op("gpsimd", lambda g: g.collective_compute(
        "AllGather", ALU.bypass, replica_groups=[list(range(NC))],
        ins=[ap[gi].opt()], outs=[ap[go].opt()]),
        coll=True, reads=[gi], writes=[go])
    cx.op("sync", lambda s: s.dma_start(
        out=bf.LSCR[0:1, 0:NC], in_=ap[go].rearrange("n o -> (n o)")),
        dma=True, reads=[go], writes=["LSCR"])
    cx.op("vector", lambda v: v.reduce_max(bf.STAT[0:1, 9:10],
                                           bf.LSCR[0:1, 0:NC], axis=AXL.X),
          reads=["LSCR"], writes=["STAT"])
    cx.op("vector", lambda v: v.tensor_scalar(
        out=bf.STAT[0:1, 9:10], in0=bf.STAT[0:1, 9:10], scalar1=1e-8,
        scalar2=1.0 / 255.0, op0=ALU.max, op1=ALU.mult),
        reads=["STAT"], writes=["STAT"])
    cx.op("vector", lambda v: v.reciprocal(out=bf.STAT[0:1, 10:11],
                                           in_=bf.STAT[0:1, 9:10]),
          reads=["STAT"], writes=["STAT"])
    bcast_scalar(cx, bf, bf.STAT[0:1, 10:11], bf.STAT[:, 11:12], "STAT", "STAT")
    bcast_scalar(cx, bf, bf.STAT[0:1, 9:10], bf.STAT[:, 14:15], "STAT", "STAT")


def emit_attn_b1(nc, cx, ap, bf):
    for b in range(B):
        for j in range(NPB):
            tj = b * NPB + j
            ln_tile(cx, bf, bf.xtile(tj), bf.LNT[:, :], "X", "LNT")
            for dc in range(DCH):
                tr128(cx, bf, bf.LNT[:, dc * P:(dc + 1) * P],
                      bf.BA[:, dc * SEQ_PER + j * P: dc * SEQ_PER + (j + 1) * P],
                      "LNT", "BA")
        proj_fm_norm(nc, cx, ap, bf, b, "c_Wkt", "g_kt")
        for j in range(NPB):
            for dh in range(2):
                for dc in range(DCH):
                    cx.op("sync", lambda s, dc=dc, dh=dh: s.dma_start(
                        out=bf.WTA[:, 0:512],
                        in_=ap["c_Wvt"][dc * P:(dc + 1) * P,
                                        dh * 512:(dh + 1) * 512]),
                        dma=True, writes=["WTA"])
                    cx.op("tensor", lambda t, dc=dc, j=j: t.matmul(
                        bf.PSW[:, 0:512],
                        bf.BA[:, dc * SEQ_PER + j * P: dc * SEQ_PER + (j + 1) * P],
                        bf.WTA[:, 0:512],
                        start=(dc == 0), stop=(dc == DCH - 1)),
                        reads=["BA", "WTA"], writes=["psw"])
                cx.op("vector", lambda v, dh=dh: v.tensor_copy(
                    out=bf.LSCR[:, dh * 512:(dh + 1) * 512], in_=bf.PSW[:, 0:512]),
                    reads=["psw"], writes=["LSCR"])
            cx.op("sync", lambda s, b=b, j=j: s.dma_start(
                out=ap["g_v"][b * SEQ_PER + j * P: b * SEQ_PER + (j + 1) * P, :],
                in_=bf.LSCR[:, :]),
                dma=True, reads=["LSCR"], writes=["g_v"])
        proj_fm_norm(nc, cx, ap, bf, b, "c_Wqt", None)
        for dc in range(DCH):
            cx.op("sync", lambda s, dc=dc, b=b: s.dma_start(
                out=ap["s_qt"][dc, :, :] if b == 1 else
                ap["s_xnt"][0, dc, :, 0:SEQ_PER],
                in_=bf.BB[:, dc * SEQ_PER:(dc + 1) * SEQ_PER]),
                dma=True, reads=["BB"],
                writes=["s_qt" if b == 1 else "s_xnt"])
        cx.barrier()
    cx.op("gpsimd", lambda g: g.collective_compute(
        "AllGather", ALU.bypass, replica_groups=[list(range(NC))],
        ins=[ap["g_kt"].opt()], outs=[ap["g_kt_all"].opt()]),
        coll=True, reads=["g_kt"], writes=["g_kt_all"])
    cx.op("gpsimd", lambda g: g.collective_compute(
        "AllGather", ALU.bypass, replica_groups=[list(range(NC))],
        ins=[ap["g_v"].opt()], outs=[ap["g_v_all"].opt()]),
        coll=True, reads=["g_v"], writes=["g_v_all"])
    NKC = SEQ // P
    LCH = SEQ_PER // P
    for b in range(B):
        src = "s_xnt" if b == 0 else "s_qt"
        for dc in range(DCH):
            cx.op("sync", lambda s, dc=dc, src=src: s.dma_start(
                out=bf.BA[:, dc * SEQ_PER:(dc + 1) * SEQ_PER],
                in_=(ap["s_xnt"][0, dc, :, 0:SEQ_PER] if src == "s_xnt" else
                     ap["s_qt"][dc, :, :])),
                dma=True, reads=[src], writes=["BA"])
        for buf in range(2):
            vk_ = "VRKa" if buf == 0 else "VRKb"
            for lc in range(LCH):
                cx.op("vector", lambda v, buf=buf, lc=lc: v.tensor_copy(
                    out=bf.VRK[:, buf * 520 + lc * 65 + DHEAD:
                               buf * 520 + lc * 65 + DHEAD + 1],
                    in_=bf.ONESC),
                    reads=["SMALL"], writes=[vk_])
        for h in range(NHEAD):
            oc, pb = h // 2, (h % 2) * DHEAD

            def load_rank(r, b=b, h=h, oc=oc, pb=pb):
                buf = r % 2
                kk_ = "KRKa" if buf == 0 else "KRKb"
                vk_ = "VRKa" if buf == 0 else "VRKb"
                cx.op("sync", lambda s, r=r, buf=buf: s.dma_start(
                    out=bf.KRK[0:DHEAD, buf * SEQ_PER:(buf + 1) * SEQ_PER],
                    in_=ap["g_kt_all"][r, oc * P + pb: oc * P + pb + DHEAD,
                                       b * SEQ_PER:(b + 1) * SEQ_PER]),
                    dma=True, reads=["g_kt_all"], writes=[kk_])
                vdst = bf.VRK[:, buf * 520: buf * 520 + LCH * 65].rearrange(
                    "p (c d) -> p c d", d=65)[:, :, 0:DHEAD]
                cx.op("sync", lambda s, r=r, vdst=vdst: s.dma_start(
                    out=vdst,
                    in_=ap["g_v_all"][r, b * SEQ_PER:(b + 1) * SEQ_PER,
                                      h * DHEAD:(h + 1) * DHEAD].rearrange(
                        "(c p) d -> p c d", p=P)),
                    dma=True, reads=["g_v_all"], writes=[vk_])
            if pb:
                cx.op("sync", lambda s, oc=oc: s.dma_start(
                    out=bf.SCR3[0:DHEAD, 0:SEQ_PER],
                    in_=bf.BA[DHEAD:P, oc * SEQ_PER:(oc + 1) * SEQ_PER]),
                    dma=True, reads=["BA"], writes=["SCR3"])
            load_rank(0)
            load_rank(1)
            next_load = 2
            for kc in range(NKC):
                r, lc = kc // LCH, kc % LCH
                buf = r % 2
                g = kc % CG
                kk_ = "KRKa" if buf == 0 else "KRKb"
                QW = min(512, SEQ_PER)
                for qh in range(SEQ_PER // QW):
                    if pb:
                        rhs = bf.SCR3[0:DHEAD, qh * QW:(qh + 1) * QW]
                        rk = "SCR3"
                    else:
                        rhs = bf.BA[0:DHEAD, oc * SEQ_PER + qh * QW:
                                    oc * SEQ_PER + (qh + 1) * QW]
                        rk = "BA"
                    cx.op("tensor", lambda t, qh=qh, QW=QW, buf=buf, lc=lc,
                          rhs=rhs: t.matmul(
                        bf.PSW[:, qh * QW:(qh + 1) * QW],
                        bf.KRK[0:DHEAD, buf * SEQ_PER + lc * P:
                               buf * SEQ_PER + (lc + 1) * P],
                        rhs, start=True, stop=True),
                        reads=[kk_, rk], writes=["psw"])
                cx.op("vector", lambda v, g=g: v.tensor_scalar(
                    out=bf.CB[:, g * 1024:g * 1024 + SEQ_PER],
                    in0=bf.PSW[:, 0:SEQ_PER], scalar1=0.999, scalar2=-0.999,
                    op0=ALU.min, op1=ALU.max),
                    reads=["psw"], writes=["CB"])
                cx.op("vector", lambda v, g=g: v.tensor_mul(
                    out=bf.CL[:, g * 1024:g * 1024 + SEQ_PER],
                    in0=bf.CB[:, g * 1024:g * 1024 + SEQ_PER],
                    in1=bf.CB[:, g * 1024:g * 1024 + SEQ_PER]),
                    reads=["CB"], writes=["CL"])
                if g == CG - 1 or kc == NKC - 1:
                    gs = range(g + 1)
                    base = kc - g
                    for gg in gs:
                        cx.op("scalar", lambda s, gg=gg: s.activation(
                            out=bf.CL[:, gg * 1024:gg * 1024 + SEQ_PER],
                            in_=bf.CL[:, gg * 1024:gg * 1024 + SEQ_PER],
                            func=AF.Ln, scale=-1.0, bias=1.0),
                            reads=["CL"], writes=["CL"])
                    for gg in gs:
                        cx.op("scalar", lambda s, gg=gg: s.activation(
                            out=bf.CL[:, gg * 1024:gg * 1024 + SEQ_PER],
                            in_=bf.CL[:, gg * 1024:gg * 1024 + SEQ_PER],
                            func=AF.Sigmoid, scale=-0.5),
                            reads=["CL"], writes=["CL"])
                        cx.op("vector", lambda v, gg=gg: v.tensor_mul(
                            out=bf.CB[:, gg * 1024:gg * 1024 + SEQ_PER],
                            in0=bf.CB[:, gg * 1024:gg * 1024 + SEQ_PER],
                            in1=bf.CL[:, gg * 1024:gg * 1024 + SEQ_PER]),
                            reads=["CB", "CL"], writes=["CB"])
                        cx.op("scalar", lambda s, gg=gg: s.activation(
                            out=bf.CB[:, gg * 1024:gg * 1024 + SEQ_PER],
                            in_=bf.CB[:, gg * 1024:gg * 1024 + SEQ_PER],
                            func=AF.Arctan),
                            reads=["CB"], writes=["CB"])
                    for gg in gs:
                        cx.op("scalar", lambda s, gg=gg: s.activation(
                            out=bf.CB[:, gg * 1024:gg * 1024 + SEQ_PER],
                            in_=bf.CB[:, gg * 1024:gg * 1024 + SEQ_PER],
                            func=AF.Ln, scale=TWOPI, bias=bf.HALFC),
                            reads=["CB", "SMALL"], writes=["CB"])
                        cx.op("scalar", lambda s, gg=gg: s.activation(
                            out=bf.CB[:, gg * 1024:gg * 1024 + SEQ_PER],
                            in_=bf.CB[:, gg * 1024:gg * 1024 + SEQ_PER],
                            func=AF.Exp, scale=16.0),
                            reads=["CB"], writes=["CB"])
                    for gg in gs:
                        kk = base + gg
                        rr, ll = kk // LCH, kk % LCH
                        vbuf = rr % 2
                        vk_ = "VRKa" if vbuf == 0 else "VRKb"
                        v0 = vbuf * 520 + ll * 65
                        QW = min(512, SEQ_PER)
                        for qh in range(SEQ_PER // QW):
                            ps = bf.PS2 if qh == 0 else bf.PS3
                            pk = "ps2" if qh == 0 else "ps3"
                            cx.op("tensor", lambda t, gg=gg, kk=kk, qh=qh, ps=ps,
                                  v0=v0, QW=QW: t.matmul(
                                      ps[0:DHEAD + 1, 0:QW],
                                      bf.VRK[:, v0:v0 + DHEAD + 1],
                                      bf.CB[:, gg * 1024 + qh * QW:
                                            gg * 1024 + (qh + 1) * QW],
                                      start=(kk == 0), stop=(kk == NKC - 1)),
                                  reads=["CB", vk_], writes=[pk])
                    while (next_load < NC and
                           (next_load - 2) * LCH + LCH - 1 <= kc):
                        load_rank(next_load)
                        next_load += 1
            finish_head(nc, cx, ap, bf, b, h)
        wo_residual(nc, cx, ap, bf, b)
        cx.barrier()


def proj_fm_norm(nc, cx, ap, bf, b, wname, store):
    wr = ap[wname].rearrange("(dc p) o -> p dc o", p=P)
    QW = min(512, SEQ_PER)
    NQH = SEQ_PER // QW
    for oc in range(DCH):
        cx.op("sync", lambda s, oc=oc: s.dma_start(
            out=bf.WTA[:, :], in_=wr[:, :, oc * P:(oc + 1) * P]),
            dma=True, writes=["WTA"])
        cx.op("sync", lambda s, oc=oc: s.dma_start(
            out=bf.WTB[:, 0:NHEAD], in_=ap["ehead"][oc, :, :]),
            dma=True, writes=["WTB"])
        for qh in range(NQH):
            builds = [(lambda t, dc=dc, oc=oc, qh=qh: t.matmul(
                bf.PSW[:, 0:QW], bf.WTA[:, dc * P:(dc + 1) * P],
                bf.BA[:, dc * SEQ_PER + qh * QW: dc * SEQ_PER + qh * QW + QW],
                start=(dc == 0), stop=(dc == DCH - 1))) for dc in range(DCH)]
            cx.group("tensor", builds, reads=["WTA", "BA"], writes=["psw"])
            cx.op("vector", lambda v, oc=oc, qh=qh: v.tensor_copy(
                out=bf.BB[:, oc * SEQ_PER + qh * QW: oc * SEQ_PER + qh * QW + QW],
                in_=bf.PSW[:, 0:QW]),
                reads=["psw"], writes=["BB"])
            cx.op("vector", lambda v, oc=oc, qh=qh: v.tensor_mul(
                out=bf.LSCR[:, 0:QW],
                in0=bf.BB[:, oc * SEQ_PER + qh * QW: oc * SEQ_PER + qh * QW + QW],
                in1=bf.BB[:, oc * SEQ_PER + qh * QW: oc * SEQ_PER + qh * QW + QW]),
                reads=["BB"], writes=["LSCR"])
            ps = bf.PS0 if qh == 0 else bf.PS1
            pk = "ps0" if qh == 0 else "ps1"
            cx.op("tensor", lambda t, oc=oc, ps=ps: t.matmul(
                ps[0:NHEAD, 0:QW], bf.WTB[:, 0:NHEAD], bf.LSCR[:, 0:QW],
                start=(oc == 0), stop=(oc == DCH - 1)),
                reads=["WTB", "LSCR"], writes=[pk])
    for qh in range(NQH):
        ps = bf.PS0 if qh == 0 else bf.PS1
        pk = "ps0" if qh == 0 else "ps1"
        cx.op("scalar", lambda s, ps=ps, qh=qh: s.activation(
            out=bf.SCR2[0:NHEAD, qh * QW:(qh + 1) * QW], in_=ps[0:NHEAD, 0:QW],
            func=AF.Sqrt), reads=[pk], writes=["SCR2"])
    cx.op("vector", lambda v: v.tensor_scalar(
        out=bf.SCR2[0:NHEAD, 0:SEQ_PER], in0=bf.SCR2[0:NHEAD, 0:SEQ_PER],
        scalar1=1e-6, scalar2=None, op0=ALU.max),
        reads=["SCR2"], writes=["SCR2"])
    cx.op("vector", lambda v: v.reciprocal(out=bf.SCR2[0:NHEAD, 0:SEQ_PER],
                                           in_=bf.SCR2[0:NHEAD, 0:SEQ_PER]),
          reads=["SCR2"], writes=["SCR2"])
    for oc in range(DCH):
        cx.op("sync", lambda s, oc=oc: s.dma_start(
            out=bf.WTB[0:NHEAD, 0:P], in_=ap["eheadt"][oc, :, :]),
            dma=True, writes=["WTB"])
        for qh in range(NQH):
            cx.op("tensor", lambda t, oc=oc, qh=qh: t.matmul(
                bf.PSW[:, 0:QW], bf.WTB[0:NHEAD, 0:P],
                bf.SCR2[0:NHEAD, qh * QW:(qh + 1) * QW], start=True, stop=True),
                reads=["WTB", "SCR2"], writes=["psw"])
            cx.op("vector", lambda v, oc=oc, qh=qh: v.tensor_mul(
                out=bf.BB[:, oc * SEQ_PER + qh * QW: oc * SEQ_PER + qh * QW + QW],
                in0=bf.BB[:, oc * SEQ_PER + qh * QW: oc * SEQ_PER + qh * QW + QW],
                in1=bf.PSW[:, 0:QW]),
                reads=["BB", "psw"], writes=["BB"])
    if store is not None:
        for dc in range(DCH):
            cx.op("sync", lambda s, dc=dc, b=b: s.dma_start(
                out=ap[store][dc * P:(dc + 1) * P, b * SEQ_PER:(b + 1) * SEQ_PER],
                in_=bf.BB[:, dc * SEQ_PER:(dc + 1) * SEQ_PER]),
                dma=True, reads=["BB"], writes=[store])


def finish_head(nc, cx, ap, bf, b, h):
    oc, pb = h // 2, (h % 2) * DHEAD
    QW = min(512, SEQ_PER)
    NQH = SEQ_PER // QW
    for qh in range(NQH):
        ps = bf.PS2 if qh == 0 else bf.PS3
        pk = "ps2" if qh == 0 else "ps3"
        cx.op("vector", lambda v, ps=ps, qh=qh, pk=pk: v.tensor_scalar(
            out=bf.LSCR[0:1, qh * QW:(qh + 1) * QW],
            in0=ps[DHEAD:DHEAD + 1, 0:QW], scalar1=1e-6, scalar2=None,
            op0=ALU.add), reads=[pk], writes=["LSCR"])
    cx.op("vector", lambda v: v.reciprocal(out=bf.LSCR[0:1, 0:SEQ_PER],
                                           in_=bf.LSCR[0:1, 0:SEQ_PER]),
          reads=["LSCR"], writes=["LSCR"])
    for qh in range(NQH):
        ps = bf.PS2 if qh == 0 else bf.PS3
        pk = "ps2" if qh == 0 else "ps3"
        cx.op("tensor", lambda t, qh=qh: t.matmul(
            bf.PT[0:DHEAD, 0:QW], bf.ONESR[0:1, 0:DHEAD],
            bf.LSCR[0:1, qh * QW:(qh + 1) * QW], start=True, stop=True),
            reads=["LSCR", "ONESR"], writes=["pt"])
        cx.op("vector", lambda v, qh=qh: v.tensor_copy(
            out=bf.SCR2[0:DHEAD, 0:QW], in_=bf.PT[0:DHEAD, 0:QW]),
            reads=["pt"], writes=["SCR2"])
        cx.op("vector", lambda v, ps=ps, qh=qh, oc=oc, pb=pb, pk=pk: v.tensor_mul(
            out=bf.BB[pb:pb + DHEAD,
                      oc * SEQ_PER + qh * QW: oc * SEQ_PER + qh * QW + QW],
            in0=ps[0:DHEAD, 0:QW], in1=bf.SCR2[0:DHEAD, 0:QW]),
            reads=[pk, "SCR2"], writes=["BB"])


def wo_residual(nc, cx, ap, bf, b):
    cx.op("sync", lambda s: s.dma_start(out=bf.CL[:, 0:D], in_=ap["c_bo"]),
          dma=True, writes=["CL"])
    cx.op("sync", lambda s: s.dma_start(
        out=bf.BA[:, :],
        in_=ap["c_Wot"].rearrange("(oc p) d -> p oc d", p=P)),
        dma=True, writes=["BA"])
    for j in range(NPB):
        tj = b * NPB + j
        for dh in range(2):
            for oc in range(DCH):
                cx.op("tensor", lambda t, oc=oc, j=j, dh=dh: t.matmul(
                    bf.PSW[:, 0:512],
                    bf.BB[:, oc * SEQ_PER + j * P: oc * SEQ_PER + (j + 1) * P],
                    bf.BA[:, oc * D + dh * 512:oc * D + dh * 512 + 512],
                    start=(oc == 0), stop=(oc == DCH - 1)),
                    reads=["BB", "BA"], writes=["psw"])
            cx.op("vector", lambda v, dh=dh: v.tensor_add(
                out=bf.SCR2[:, 0:512], in0=bf.PSW[:, 0:512],
                in1=bf.CL[:, dh * 512:(dh + 1) * 512]),
                reads=["psw", "CL"], writes=["SCR2"])
            cx.op("vector", lambda v, tj=tj, dh=dh: v.tensor_add(
                out=bf.xtile(tj)[:, dh * 512:(dh + 1) * 512],
                in0=bf.xtile(tj)[:, dh * 512:(dh + 1) * 512],
                in1=bf.SCR2[:, 0:512]),
                reads=["SCR2", "X"], writes=["X"])


def host_inputs(full, seq=None):
    if seq is None:
        seq = full["L0_attn_a_x"].shape[1]
    set_dims(seq)
    x = np.ascontiguousarray(full["L0_attn_a_x"], dtype=np.float32)
    qstart = seq - QBLK
    rows = np.arange(QBLK) + qstart
    thr = rows - (seq - CTX)
    masks = np.where(np.arange(N_SINKS)[None, :] <= thr[:, None], 0.0, NEG)
    maskw = np.where((np.arange(WINDOW)[None, :] + N_SINKS) <= thr[:, None],
                     0.0, NEG)
    addmask = (thr >= 0).astype(np.float32)[:, None]
    eh = np.zeros((DCH, P, NHEAD), np.float32)
    eht = np.zeros((DCH, NHEAD, P), np.float32)
    for oc in range(DCH):
        for pp in range(P):
            hh = (oc * P + pp) // DHEAD
            eh[oc, pp, hh] = 1.0
            eht[oc, hh, pp] = 1.0
    base = {
        "aa_blk": x[:, qstart:, :],
        "aa_sink": x[:, 0:N_SINKS, :],
        "aa_Wq": full["L0_attn_a_Wq"], "aa_Wk": full["L0_attn_a_Wk"],
        "aa_Wv": full["L0_attn_a_Wv"], "aa_Wo": full["L0_attn_a_Wo"],
        "aa_maskw": maskw, "aa_masks": masks,
        "f0_w1t": full["L0_ffn_w1"].T, "f0_w1": full["L0_ffn_w1"],
        "f0_b1": full["L0_ffn_b1"].reshape(HCH, P).T,
        "f0_w2t": full["L0_ffn_w2"].T, "f0_w2": full["L0_ffn_w2"],
        "f0_b2": np.tile(full["L0_ffn_b2"][None, :], (P, 1)),
        "t_Wbt": full["L1_attn_a_Wb"].T,
        "t_Wpt": full["L1_attn_a_Wp"].T, "t_Wp": full["L1_attn_a_Wp"],
        "t_iA": full["L1_attn_a_init_A"],
        "t_iB_t": full["L1_attn_a_init_B"].T,
        "t_gamma": np.tile(full["L1_attn_a_ln_gamma"][None, :], (P, 1)),
        "t_beta": np.tile(full["L1_attn_a_ln_beta"][None, :], (P, 1)),
        "q_w1": full["L1_ffn_w1"],
        "q_b1": full["L1_ffn_b1"].reshape(HCH, P).T,
        "q_w2": full["L1_ffn_w2"],
        "q_b2": full["L1_ffn_b2"].reshape(DCH, P).T,
        "c_Wqt": full["L1_attn_b_Wq"].T, "c_Wkt": full["L1_attn_b_Wk"].T,
        "c_Wvt": full["L1_attn_b_Wv"].T, "c_Wot": full["L1_attn_b_Wo"].T,
        "c_bo": np.tile(full["L1_attn_b_bo"][None, :], (P, 1)),
        "ehead": eh, "eheadt": eht,
        "ident": np.eye(P, dtype=np.float32),
        "ones_col": np.ones((P, 1), np.float32),
        "ones_row": np.ones((1, P), np.float32),
        "consts": np.tile(np.array([[LN_EPS, 0.5]], np.float32), (P, 1)),
    }
    base = {k: np.ascontiguousarray(v, dtype=np.float32) for k, v in base.items()}
    maps = []
    for i in range(NC):
        m = dict(base)
        m["x0"] = np.ascontiguousarray(x[:, i * SEQ_PER:(i + 1) * SEQ_PER, :])
        own = ((rows >= i * SEQ_PER) & (rows < (i + 1) * SEQ_PER))
        m["aa_addmask"] = np.ascontiguousarray(
            addmask * own[:, None].astype(np.float32), dtype=np.float32)
        oh = np.zeros((NC, 1), np.float32)
        if i + 1 < NC:
            oh[i + 1, 0] = 1.0
        m["onehot_next"] = oh
        pw = np.zeros((P, NC), np.float32)
        pw[:, :i] = 1.0
        m["prefw_rep"] = pw
        maps.append(m)
    return maps


def kernel(**inputs):
    maps = host_inputs(inputs, seq=inputs["L0_attn_a_x"].shape[1])
    nc = build_program()
    res = run_bass_kernel_spmd(nc, maps, core_ids=list(range(NC)))
    y = np.concatenate([r["y"] for r in res.results], axis=1)
    return np.ascontiguousarray(y, dtype=np.float32)


# version 04
# speedup vs baseline: 1.1556x; 1.0257x over previous
import numpy as np
import concourse.bass as bass
import concourse.mybir as mybir
from concourse.bass_utils import run_bass_kernel_spmd

F32 = mybir.dt.float32
F16 = mybir.dt.float16
BF16 = mybir.dt.bfloat16
AF = mybir.ActivationFunctionType
ALU = mybir.AluOpType
AXL = mybir.AxisListType

NC = 8
B = 2
D = 1024
H_FFN = 4096
NHEAD = 16
DHEAD = 64
CHUNK = 256
LORA_R = 16
N_SINKS = 4
WINDOW = 512
CTX = N_SINKS + WINDOW
QBLK = 640
NQT = QBLK // 128
LN_EPS = 1e-5
TTT_LR = 0.02
SCALING = 2.0
NEG = -1e30
TWOPI = float(2.0 / np.pi)
C23 = 8388608.0

P = 128
DCH = D // P
HCH = H_FFN // P
CG = 4

SEQ = SEQ_PER = TOK = NPT = NPB = NCHB = None


def set_dims(seq):
    global SEQ, SEQ_PER, TOK, NPT, NPB, NCHB
    SEQ = seq
    SEQ_PER = seq // NC
    TOK = 2 * SEQ_PER
    NPT = TOK // P
    NPB = NPT // 2
    NCHB = SEQ_PER // CHUNK


set_dims(8192)


class Ctx:
    ENGS = ("vector", "scalar", "tensor", "gpsimd", "sync")

    def __init__(self, nc, block):
        self.nc = nc
        self.block = block
        self.sems = {}
        self.cnt = {}
        self.waited = {e: {} for e in self.ENGS}
        self.lastw = {}
        self.readers = {}
        for e in self.ENGS:
            self._mk("S_" + e)
        self.NDL = 8
        for i in range(self.NDL):
            self._mk(f"S_dma{i}")
        self._mk("S_dmag")
        self._mk("S_coll")
        self._lane = 0

    def _mk(self, name):
        self.sems[name] = self.nc.alloc_semaphore(name)
        self.cnt[name] = 0

    def _need(self, eng, reads, writes):
        need = {}

        def add(s, c):
            if need.get(s, 0) < c:
                need[s] = c
        for r in reads:
            lw = self.lastw.get(r)
            if lw:
                add(*lw)
        for r in writes:
            lw = self.lastw.get(r)
            if lw:
                add(*lw)
            for s, c in self.readers.get(r, {}).items():
                add(s, c)
        return [(s, c) for s, c in need.items() if self.waited[eng].get(s, 0) < c]

    def _emit(self, eng, builds, reads, writes, sem, inc):
        waits = self._need(eng, reads, writes)
        self.cnt[sem] += inc
        tgt = self.cnt[sem]
        semh, sems = self.sems[sem], self.sems

        def body(e):
            for s, c in waits:
                e.wait_ge(sems[s], c)
            last = None
            for b in builds:
                last = b(e)
            last.then_inc(semh, inc)
        getattr(self.block, eng)(body)
        w = self.waited[eng]
        for s, c in waits:
            if w.get(s, 0) < c:
                w[s] = c

        for r in writes:
            self.lastw[r] = (sem, tgt)
            self.readers.pop(r, None)
        for r in reads:
            d = self.readers.setdefault(r, {})
            if d.get(sem, 0) < tgt:
                d[sem] = tgt

    def op(self, eng, build, reads=(), writes=(), dma=False, coll=False):
        if coll:
            sem, inc = "S_coll", 1
        elif dma:
            if eng == "gpsimd":
                sem, inc = "S_dmag", 16
            else:
                sem = f"S_dma{self._lane}"
                self._lane = (self._lane + 1) % self.NDL
                inc = 16
        else:
            sem, inc = "S_" + eng, 1
        if inc == 16 or coll:
            prev = self.cnt[sem]
            if prev > 0 and self.waited[eng].get(sem, 0) < prev:
                semh = self.sems[sem]

                def b2(e, _b=build, _s=semh, _p=prev):
                    e.wait_ge(_s, _p)
                    return _b(e)
                build = b2
                self.waited[eng][sem] = prev
        self._emit(eng, [build], reads, writes, sem, inc)

    def group(self, eng, builds, reads=(), writes=()):
        self._emit(eng, builds, reads, writes, "S_" + eng, 1)

    def barrier(self):
        snap = [(s, c) for s, c in self.cnt.items() if c > 0]
        sems = self.sems
        for eng in self.ENGS:
            todo = [(s, c) for s, c in snap if self.waited[eng].get(s, 0) < c]
            if not todo:
                continue

            def body(e, todo=tuple(todo)):
                for s, c in todo:
                    e.wait_ge(sems[s], c)
            getattr(self.block, eng)(body)
            for s, c in todo:
                self.waited[eng][s] = c


class NS:
    pass


def ln_tile(cx, bf, x_ap, out_ap, xkey, okey, gamma=None, beta=None, gkey=None,
            nrow=P, F=D):
    inv_n = 1.0 / F
    st = bf.STAT
    cx.op("vector", lambda v: v.reduce_sum(st[:nrow, 0:1], x_ap, axis=AXL.X),
          reads=[xkey], writes=["STAT"])
    cx.op("vector", lambda v: v.tensor_scalar(out=st[:nrow, 2:3], in0=st[:nrow, 0:1],
                                              scalar1=inv_n, scalar2=None, op0=ALU.mult),
          reads=["STAT"], writes=["STAT"])
    cx.op("vector", lambda v: v.scalar_tensor_tensor(
        out=bf.LSCR[:nrow, :F], in0=x_ap, scalar=st[:nrow, 2:3], in1=x_ap,
        op0=ALU.subtract, op1=ALU.mult, accum_out=st[:nrow, 1:2]),
        reads=[xkey, "STAT"], writes=["STAT", "LSCR"])
    cx.op("scalar", lambda s: s.activation(out=st[:nrow, 3:4], in_=st[:nrow, 1:2],
                                           func=AF.Sqrt, scale=inv_n,
                                           bias=bf.EPSC[:nrow, :]),
          reads=["STAT", "SMALL"], writes=["STAT"])
    cx.op("vector", lambda v: v.reciprocal(out=st[:nrow, 3:4], in_=st[:nrow, 3:4]),
          reads=["STAT"], writes=["STAT"])
    cx.op("vector", lambda v: v.tensor_scalar(out=out_ap, in0=x_ap,
                                              scalar1=st[:nrow, 2:3],
                                              scalar2=st[:nrow, 3:4],
                                              op0=ALU.subtract, op1=ALU.mult),
          reads=[xkey, "STAT"], writes=[okey])
    if gamma is not None:
        cx.op("vector", lambda v: v.tensor_mul(out=out_ap, in0=out_ap, in1=gamma),
              reads=[okey, gkey], writes=[okey])
        cx.op("vector", lambda v: v.tensor_add(out=out_ap, in0=out_ap, in1=beta),
              reads=[okey, gkey], writes=[okey])


def tr128(cx, bf, src_ap, dst_ap, skey, dkey, nout=None):
    npart = src_ap.shape[0]
    ncols = src_ap.shape[-1]
    cx.op("tensor", lambda t: t.transpose(bf.PT[:ncols, :npart], src_ap,
                                          bf.IDENT[:npart, :npart]),
          reads=[skey, "IDENT"], writes=["pt"])
    cx.op("vector", lambda v: v.tensor_copy(out=dst_ap, in_=bf.PT[:ncols, :npart]),
          reads=["pt"], writes=[dkey])


def bcast_scalar(cx, bf, src11, dst_col, skey, dkey):
    cx.op("tensor", lambda t: t.matmul(bf.PT[:, 0:1], bf.ONESR[0:1, :], src11,
                                       start=True, stop=True),
          reads=[skey, "ONESR"], writes=["pt"])
    cx.op("vector", lambda v: v.tensor_copy(out=dst_col, in_=bf.PT[:, 0:1]),
          reads=["pt"], writes=[dkey])


def rep_row_tile(cx, bf, src_col, dst_tile, skey, dkey):
    cx.op("tensor", lambda t: t.transpose(bf.PT[0:1, 0:P], src_col,
                                          bf.IDENT[:, :]),
          reads=[skey, "IDENT"], writes=["pt"])
    cx.op("vector", lambda v: v.tensor_copy(out=bf.LSCR[0:1, 0:P], in_=bf.PT[0:1, :P]),
          reads=["pt"], writes=["LSCR"])
    cx.op("tensor", lambda t: t.matmul(bf.PT[:, 0:P], bf.ONESR[0:1, :],
                                       bf.LSCR[0:1, 0:P], start=True, stop=True),
          reads=["LSCR", "ONESR"], writes=["pt"])
    cx.op("vector", lambda v: v.tensor_copy(out=dst_tile, in_=bf.PT[:, 0:P]),
          reads=["pt"], writes=[dkey])


def build_program():
    nc = bass.Bass("TRN2", target_bir_lowering=False, debug=False)
    ap = {}

    def din(name, shape, dt=F32):
        ap[name] = nc.dram_tensor(name, shape, dt, kind="ExternalInput").ap()

    def dout(name, shape, dt=F32):
        ap[name] = nc.dram_tensor(name, shape, dt, kind="ExternalOutput").ap()

    def dscr(name, shape, dt=F32):
        ap[name] = nc.dram_tensor(name, shape, dt).ap()

    din("x0", [B, SEQ_PER, D])
    dout("y", [B, SEQ_PER, D])
    din("aa_blk", [B, QBLK, D])
    din("aa_sink", [B, N_SINKS, D])
    din("aa_Wq", [D, D])
    din("aa_Wk", [D, D])
    din("aa_Wv", [D, D])
    din("aa_Wo", [D, D])
    din("aa_maskw", [QBLK, WINDOW])
    din("aa_masks", [QBLK, N_SINKS])
    din("aa_addmask", [QBLK, 1])
    din("f0_w1t", [D, H_FFN])
    din("f0_w1", [H_FFN, D])
    din("f0_b1", [P, HCH])
    din("f0_w2t", [H_FFN, D])
    din("f0_w2", [D, H_FFN])
    din("f0_b2", [P, D])
    din("t_Wbt", [D, D])
    din("t_Wpt", [D, D])
    din("t_Wp", [D, D])
    din("t_iA", [D, LORA_R])
    din("t_iB_t", [D, LORA_R])
    din("t_gamma", [P, D])
    din("t_beta", [P, D])
    din("onehot_next", [NC, 1])
    din("prefw_rep", [P, NC])
    din("q_w1", [H_FFN, D])
    din("q_b1", [P, HCH])
    din("q_w2", [D, H_FFN])
    din("q_b2", [P, DCH])
    din("c_Wqt", [DCH, P, D])
    din("c_Wkt", [DCH, P, D])
    din("c_Wvt", [D, D])
    din("c_Wot", [DCH, P, D])
    din("c_bo", [P, D])
    din("ehead", [DCH, P, NHEAD])
    din("eheadt", [DCH, NHEAD, P])
    din("ident", [P, P])
    din("ones_col", [P, 1])
    din("ones_row", [1, P])
    din("consts", [P, 2])

    dscr("s_w1sl", [HCH, P, D])
    dscr("s_w2s", [H_FFN, D])
    dscr("s_q1ctl", [HCH, P, D], BF16)
    dscr("s_q2ct", [HCH, P, D], BF16)
    dscr("s_xt", [B, DCH, P, SEQ_PER])
    dscr("s_xnt", [B, DCH, P, SEQ_PER + 1])
    dscr("s_qt", [DCH, P, SEQ_PER])
    dscr("s_vtok", [NQT, P, D])
    dscr("s_xc", [SEQ_PER, D], BF16)
    dscr("s_hl", [HCH, P, TOK])
    dscr("s_ln", [B * NPB, P, D])
    dscr("s_da", [B, NCHB, P, P])
    dscr("s_db", [B, NCHB, P, P])
    dscr("g_first_in", [B, D])
    dscr("g_first_out", [NC, B, D])
    dscr("g_sums_in", [P, 2 * 2 * P])
    dscr("g_sums_out", [NC, P, 2 * 2 * P])
    dscr("g_max_in0", [1, 1])
    dscr("g_max_out0", [NC, 1])
    dscr("g_max_in1", [1, 1])
    dscr("g_max_out1", [NC, 1])
    dscr("g_kt", [D, TOK])
    dscr("g_kt_all", [NC, D, TOK])
    dscr("g_v", [TOK, D])
    dscr("g_v_all", [NC, TOK, D])

    bf = NS()
    import contextlib
    with contextlib.ExitStack() as _stack:
        def _sb(name, shape, dt=F32):
            return _stack.enter_context(nc.sbuf_tensor(name, shape, dt))

        def _psum(name, shape):
            return _stack.enter_context(nc.psum_tensor(name, shape, F32))
        _X = _sb("X", [P, NPT * D])
        _BA = _sb("BA", [P, 8192])
        _BB = _sb("BB", [P, 8192])
        _CB = _sb("CB", [P, 4224])
        _CL = _sb("CL", [P, 4224])
        _LNT = _sb("LNT", [P, D])
        _LSCR = _sb("LSCR", [P, D])
        _SCR = _sb("SCR", [P, 1056])
        _SCR2 = _sb("SCR2", [P, 1056])
        _SCR3 = _sb("SCR3", [P, 1056])
        _WTA = _sb("WTA", [P, 1024])
        _WTB = _sb("WTB", [P, 1024])
        _SMALL = _sb("SMALL", [P, 1152])
        _STAT = _sb("STAT", [P, 16])
        _IDENT = _sb("IDENT", [P, P])
        _ONESR = _sb("ONESR", [1, P])
        _KRK = _sb("KRK", [64, 2048])
        _VRK = _sb("VRK", [P, 1040])
        _PS0 = _psum("PS0", [P, 512])
        _PS1 = _psum("PS1", [P, 512])
        _PS2 = _psum("PS2", [P, 512])
        _PS3 = _psum("PS3", [P, 512])
        _PSW = _psum("PSW", [P, 1024])
        _PT = _psum("PT", [P, 512])
        block = _stack.enter_context(nc.Block())
        bf.X, bf.BA, bf.BB, bf.CB, bf.CL = _X, _BA, _BB, _CB, _CL
        bf.LNT, bf.LSCR, bf.SCR, bf.SCR2, bf.SCR3 = _LNT, _LSCR, _SCR, _SCR2, _SCR3
        bf.WTA, bf.WTB, bf.SMALL, bf.STAT = _WTA, _WTB, _SMALL, _STAT
        bf.IDENT, bf.ONESR = _IDENT, _ONESR
        bf.KRK, bf.VRK = _KRK, _VRK
        bf.F16T = _SCR2[:, 0:512].bitcast(F16)
        bf.PS0, bf.PS1, bf.PS2, bf.PS3, bf.PSW, bf.PT = _PS0, _PS1, _PS2, _PS3, _PSW, _PT
        bf.A1 = _SMALL[:, 0:32]
        bf.A2 = _SMALL[:, 32:40]
        bf.SW1 = _SMALL[:, 40:72]
        bf.SW1R = _SMALL[:, 72:104]
        bf.SW2 = _SMALL[:, 104:112]
        bf.SW2R = _SMALL[:, 112:120]
        bf.IBT = _SMALL[:, 128:256]
        bf.IA = _SMALL[:, 256:384]
        bf.G = _SMALL[:, 384:512]
        bf.PIN = _SMALL[:, 512:640]
        bf.PERR = _SMALL[:, 640:768]
        bf.BQ1 = _SMALL[:, 768:800]
        bf.SA1 = _SMALL[:, 800:832]
        bf.SA1R = _SMALL[:, 832:864]
        bf.PREFW = _SMALL[:, 1024:1024 + NC]
        bf.ONESC = _SMALL[:, 1040:1041]
        bf.EPSC = _SMALL[:, 1044:1045]
        bf.HALFC = _SMALL[:, 1045:1046]
        cx = Ctx(nc, block)
        bf.xtile = lambda t: bf.X[:, t * D:(t + 1) * D]

        cx.op("sync", lambda s: s.dma_start(
            out=bf.X[:, :], in_=ap["x0"].rearrange("b (t p) d -> p b t d", p=P)),
            dma=True, writes=["X"])
        cx.op("sync", lambda s: s.dma_start(out=bf.IDENT[:, :], in_=ap["ident"]),
              dma=True, writes=["IDENT"])
        cx.op("sync", lambda s: s.dma_start(out=bf.ONESR[:, :], in_=ap["ones_row"]),
              dma=True, writes=["ONESR"])
        cx.op("sync", lambda s: s.dma_start(out=bf.PREFW, in_=ap["prefw_rep"]),
              dma=True, writes=["SMALL"])
        cx.op("sync", lambda s: s.dma_start(out=bf.ONESC, in_=ap["ones_col"]),
              dma=True, writes=["SMALL"])
        cx.op("sync", lambda s: s.dma_start(out=_SMALL[:, 1044:1046],
                                            in_=ap["consts"]),
              dma=True, writes=["SMALL"])
        cx.op("sync", lambda s: s.dma_start(
            out=bf.IBT, in_=ap["t_iB_t"].rearrange("(c p) r -> p c r", p=P)),
            dma=True, writes=["SMALL"])
        cx.op("sync", lambda s: s.dma_start(
            out=bf.IA, in_=ap["t_iA"].rearrange("(c p) r -> p c r", p=P)),
            dma=True, writes=["SMALL"])

        import os as _os
        phases = _os.environ.get("KMOD_PHASES", "all")
        def on(name):
            return phases == "all" or name in phases.split(",")
        emit_setup(nc, cx, ap, bf)
        cx.barrier()
        if on("attn_a"):
            emit_attn_a(nc, cx, ap, bf)
            cx.barrier()
        if on("ffn0a"):
            emit_ffn0(nc, cx, ap, bf)
            cx.barrier()
        if on("attn_b0"):
            emit_attn_b0(nc, cx, ap, bf)
            cx.barrier()
        if on("ffn0b"):
            emit_ffn0(nc, cx, ap, bf)
            cx.barrier()
        if on("ttt"):
            emit_ttt(nc, cx, ap, bf)
            cx.barrier()
        if on("qffn1"):
            emit_qffn(nc, cx, ap, bf)
            cx.barrier()
        if on("attn_b1"):
            emit_attn_b1(nc, cx, ap, bf)
            cx.barrier()
        if on("qffn2"):
            emit_qffn(nc, cx, ap, bf)
            cx.barrier()

        cx.op("sync", lambda s: s.dma_start(
            out=ap["y"].rearrange("b (t p) d -> p b t d", p=P), in_=bf.X[:, :]),
            dma=True, reads=["X"])
        sems = cx.sems
        finals = [(f"S_dma{i}", cx.cnt[f"S_dma{i}"]) for i in range(cx.NDL)]

        def _fin(e):
            for s, c in finals:
                if c > 0:
                    e.wait_ge(sems[s], c)
        block.sync(_fin)
    return nc


def emit_setup(nc, cx, ap, bf):
    for dc in range(DCH):
        for hc in range(DCH):
            cx.op("sync", lambda s, hc=hc, dc=dc: s.dma_start(
                out=bf.WTA[:, 0:P],
                in_=ap["t_Wp"][hc * P:(hc + 1) * P, dc * P:(dc + 1) * P]),
                dma=True, writes=["WTA"])
            cx.op("tensor", lambda t, hc=hc: t.matmul(
                bf.PS0[:, 0:LORA_R], bf.WTA[:, 0:P],
                bf.IA[:, hc * LORA_R:(hc + 1) * LORA_R],
                start=(hc == 0), stop=(hc == DCH - 1)),
                reads=["WTA", "SMALL"], writes=["ps0"])
        cx.op("vector", lambda v, dc=dc: v.tensor_copy(
            out=bf.G[:, dc * LORA_R:(dc + 1) * LORA_R], in_=bf.PS0[:, 0:LORA_R]),
            reads=["ps0"], writes=["SMALL"])
    for i in range(DCH):
        cx.op("sync", lambda s, i=i: s.dma_start(
            out=bf.BA[:, 0:H_FFN // 2],
            in_=ap["f0_w1t"][i * P:(i + 1) * P, 0:H_FFN // 2]),
            dma=True, writes=["BA"])
        cx.op("scalar", lambda s: s.activation(out=bf.BA[:, 0:H_FFN // 2],
                                               in_=bf.BA[:, 0:H_FFN // 2],
                                               func=AF.Sign),
              reads=["BA"], writes=["BA"])
        for hc in range(HCH // 2):
            cx.op("sync", lambda s, i=i, hc=hc: s.dma_start(
                out=ap["s_w1sl"][hc, :, i * P:(i + 1) * P],
                in_=bf.BA[:, hc * P:(hc + 1) * P]),
                dma=True, reads=["BA"], writes=["s_w1sl"])
        cx.op("sync", lambda s, i=i: s.dma_start(
            out=bf.BA[:, 0:H_FFN // 2],
            in_=ap["f0_w1t"][i * P:(i + 1) * P, H_FFN // 2:]),
            dma=True, writes=["BA"])
        cx.op("scalar", lambda s: s.activation(out=bf.BA[:, 0:H_FFN // 2],
                                               in_=bf.BA[:, 0:H_FFN // 2],
                                               func=AF.Sign),
              reads=["BA"], writes=["BA"])
        for hc in range(HCH // 2):
            cx.op("sync", lambda s, i=i, hc=hc: s.dma_start(
                out=ap["s_w1sl"][HCH // 2 + hc, :, i * P:(i + 1) * P],
                in_=bf.BA[:, hc * P:(hc + 1) * P]),
                dma=True, reads=["BA"], writes=["s_w1sl"])
    for i in range(HCH):
        cx.op("sync", lambda s, i=i: s.dma_start(
            out=bf.BA[:, 0:D], in_=ap["f0_w2t"][i * P:(i + 1) * P, :]),
            dma=True, writes=["BA"])
        cx.op("scalar", lambda s: s.activation(out=bf.BA[:, 0:D], in_=bf.BA[:, 0:D],
                                               func=AF.Sign),
              reads=["BA"], writes=["BA"])
        cx.op("sync", lambda s, i=i: s.dma_start(
            out=ap["s_w2s"][i * P:(i + 1) * P, :], in_=bf.BA[:, 0:D]),
            dma=True, reads=["BA"], writes=["s_w2s"])
    for i in range(HCH):
        cx.op("sync", lambda s, i=i: s.dma_start(
            out=bf.BA[:, 0:D], in_=ap["f0_w1"][i * P:(i + 1) * P, :]),
            dma=True, writes=["BA"])
        cx.op("vector", lambda v, i=i: v.reduce_sum(
            bf.A1[:, i:i + 1], bf.BA[:, 0:D], axis=AXL.X,
            apply_absolute_value=True),
            reads=["BA"], writes=["SMALL"])
    cx.op("vector", lambda v: v.tensor_scalar(out=bf.A1, in0=bf.A1,
                                              scalar1=1.0 / D, scalar2=None,
                                              op0=ALU.mult),
          reads=["SMALL"], writes=["SMALL"])
    for i in range(DCH):
        cx.op("sync", lambda s, i=i: s.dma_start(
            out=bf.BA[:, 0:H_FFN], in_=ap["f0_w2"][i * P:(i + 1) * P, :]),
            dma=True, writes=["BA"])
        cx.op("vector", lambda v, i=i: v.reduce_sum(
            bf.A2[:, i:i + 1], bf.BA[:, 0:H_FFN], axis=AXL.X,
            apply_absolute_value=True),
            reads=["BA"], writes=["SMALL"])
    cx.op("vector", lambda v: v.tensor_scalar(out=bf.A2, in0=bf.A2,
                                              scalar1=1.0 / H_FFN, scalar2=None,
                                              op0=ALU.mult),
          reads=["SMALL"], writes=["SMALL"])
    for i in range(HCH):
        cx.op("sync", lambda s, i=i: s.dma_start(
            out=bf.BA[:, 0:D], in_=ap["q_w1"][i * P:(i + 1) * P, :]),
            dma=True, writes=["BA"])
        cx.op("vector", lambda v: v.reduce_max(bf.STAT[:, 8:9], bf.BA[:, 0:D],
                                               axis=AXL.X, apply_absolute_value=True),
              reads=["BA"], writes=["STAT"])
        cx.op("vector", lambda v, i=i: v.tensor_scalar(
            out=bf.SW1[:, i:i + 1], in0=bf.STAT[:, 8:9], scalar1=1e-8,
            scalar2=1.0 / 127.0, op0=ALU.max, op1=ALU.mult),
            reads=["STAT"], writes=["SMALL"])
        cx.op("vector", lambda v, i=i: v.reciprocal(out=bf.SW1R[:, i:i + 1],
                                                    in_=bf.SW1[:, i:i + 1]),
              reads=["SMALL"], writes=["SMALL"])
        cx.op("vector", lambda v, i=i: v.tensor_scalar(
            out=bf.F16T[:, 0:D], in0=bf.BA[:, 0:D], scalar1=bf.SW1R[:, i:i + 1],
            scalar2=1536.0, op0=ALU.mult, op1=ALU.add),
            reads=["BA", "SMALL"], writes=["SCR2"])
        cx.op("vector", lambda v: v.tensor_scalar(
            out=bf.BB[:, 0:D], in0=bf.F16T[:, 0:D], scalar1=-1536.0,
            scalar2=-128.0, op0=ALU.add, op1=ALU.max),
            reads=["SCR2"], writes=["BB"])
        cx.op("vector", lambda v: v.tensor_scalar(
            out=bf.BB[:, 0:D], in0=bf.BB[:, 0:D], scalar1=127.0, scalar2=None,
            op0=ALU.min), reads=["BB"], writes=["BB"])
        for dc in range(DCH):
            tr128(cx, bf, bf.BB[:, dc * P:(dc + 1) * P], bf.LNT[:, 0:P], "BB", "LNT")
            cx.op("vector", lambda v: v.tensor_copy(
                out=bf.SCR2[:, 512:576].bitcast(BF16),
                in_=bf.LNT[:, 0:P]),
                reads=["LNT"], writes=["SCR2"])
            cx.op("sync", lambda s, i=i, dc=dc: s.dma_start(
                out=ap["s_q1ctl"][i, :, dc * P:(dc + 1) * P],
                in_=bf.SCR2[:, 512:576].bitcast(BF16)),
                dma=True, reads=["SCR2"], writes=["s_q1ctl"])
    for i in range(DCH):
        cx.op("sync", lambda s, i=i: s.dma_start(
            out=bf.BA[:, 0:H_FFN], in_=ap["q_w2"][i * P:(i + 1) * P, :]),
            dma=True, writes=["BA"])
        cx.op("vector", lambda v: v.reduce_max(bf.STAT[:, 8:9], bf.BA[:, 0:H_FFN],
                                               axis=AXL.X, apply_absolute_value=True),
              reads=["BA"], writes=["STAT"])
        cx.op("vector", lambda v, i=i: v.tensor_scalar(
            out=bf.SW2[:, i:i + 1], in0=bf.STAT[:, 8:9], scalar1=1e-8,
            scalar2=1.0 / 127.0, op0=ALU.max, op1=ALU.mult),
            reads=["STAT"], writes=["SMALL"])
        cx.op("vector", lambda v, i=i: v.reciprocal(out=bf.SW2R[:, i:i + 1],
                                                    in_=bf.SW2[:, i:i + 1]),
              reads=["SMALL"], writes=["SMALL"])
        for q in range(4):
            sl = slice(q * D, (q + 1) * D)
            cx.op("vector", lambda v, i=i, sl=sl: v.tensor_scalar(
                out=bf.F16T[:, 0:D], in0=bf.BA[:, sl], scalar1=bf.SW2R[:, i:i + 1],
                scalar2=1536.0, op0=ALU.mult, op1=ALU.add),
                reads=["BA", "SMALL"], writes=["SCR2"])
            cx.op("vector", lambda v, sl=sl: v.tensor_scalar(
                out=bf.BB[:, sl], in0=bf.F16T[:, 0:D], scalar1=-1536.0,
                scalar2=-128.0, op0=ALU.add, op1=ALU.max),
                reads=["SCR2"], writes=["BB"])
        cx.op("vector", lambda v: v.tensor_scalar(
            out=bf.BB[:, 0:H_FFN], in0=bf.BB[:, 0:H_FFN], scalar1=127.0,
            scalar2=None, op0=ALU.min), reads=["BB"], writes=["BB"])
        for hc in range(HCH):
            tr128(cx, bf, bf.BB[:, hc * P:(hc + 1) * P], bf.LNT[:, 0:P], "BB", "LNT")
            cx.op("vector", lambda v: v.tensor_copy(
                out=bf.SCR2[:, 512:576].bitcast(BF16),
                in_=bf.LNT[:, 0:P]),
                reads=["LNT"], writes=["SCR2"])
            cx.op("sync", lambda s, i=i, hc=hc: s.dma_start(
                out=ap["s_q2ct"][hc, :, i * P:(i + 1) * P],
                in_=bf.SCR2[:, 512:576].bitcast(BF16)),
                dma=True, reads=["SCR2"], writes=["s_q2ct"])


def emit_attn_a(nc, cx, ap, bf):
    wqr = ap["aa_Wq"].rearrange("(dc p) o -> p dc o", p=P)
    wkr = ap["aa_Wk"].rearrange("(dc p) o -> p dc o", p=P)
    wvr = ap["aa_Wv"].rearrange("(dc p) o -> p dc o", p=P)
    for b in range(B):
        for j in range(NQT):
            cx.op("sync", lambda s, b=b, j=j: s.dma_start(
                out=bf.LNT[:, :], in_=ap["aa_blk"][b, j * P:(j + 1) * P, :]),
                dma=True, writes=["LNT"])
            ln_tile(cx, bf, bf.LNT[:, :], bf.SCR[:, 0:D], "LNT", "SCR")
            for dc in range(DCH):
                tr128(cx, bf, bf.SCR[:, dc * P:(dc + 1) * P],
                      bf.BA[:, dc * QBLK + j * P: dc * QBLK + (j + 1) * P],
                      "SCR", "BA")
        cx.op("sync", lambda s, b=b: s.dma_start(
            out=bf.LNT[0:N_SINKS, :], in_=ap["aa_sink"][b, :, :]),
            dma=True, writes=["LNT"])
        ln_tile(cx, bf, bf.LNT[0:N_SINKS, :], bf.SCR[0:N_SINKS, 0:D], "LNT", "SCR",
                nrow=N_SINKS)
        for dc in range(DCH):
            tr128(cx, bf, bf.SCR[0:N_SINKS, dc * P:(dc + 1) * P],
                  bf.SCR3[:, dc * N_SINKS:(dc + 1) * N_SINKS], "SCR", "SCR3",
                  nout=N_SINKS)

        def blkT(dc, c0, c1):
            return bf.BA[:, dc * QBLK + c0: dc * QBLK + c1]

        for oc in range(DCH):
            cx.op("sync", lambda s, oc=oc: s.dma_start(
                out=bf.WTA[:, :], in_=wqr[:, :, oc * P:(oc + 1) * P]),
                dma=True, writes=["WTA"])
            for c0, c1 in ((0, 512), (512, QBLK)):
                n = c1 - c0
                builds = [(lambda t, dc=dc, c0=c0, c1=c1, n=n: t.matmul(
                    bf.PS0[:, :n], bf.WTA[:, dc * P:(dc + 1) * P],
                    blkT(dc, c0, c1), start=(dc == 0), stop=(dc == DCH - 1)))
                    for dc in range(DCH)]
                cx.group("tensor", builds, reads=["WTA", "BA"], writes=["ps0"])
                cx.op("vector", lambda v, oc=oc, c0=c0, c1=c1, n=n: v.tensor_copy(
                    out=bf.BB[:, oc * QBLK + c0: oc * QBLK + c1], in_=bf.PS0[:, :n]),
                    reads=["ps0"], writes=["BB"])
        for oc in range(DCH):
            cx.op("sync", lambda s, oc=oc: s.dma_start(
                out=bf.WTA[:, :], in_=wkr[:, :, oc * P:(oc + 1) * P]),
                dma=True, writes=["WTA"])
            builds = [(lambda t, dc=dc, oc=oc: t.matmul(
                bf.PS0[:, 0:N_SINKS], bf.WTA[:, dc * P:(dc + 1) * P],
                bf.SCR3[:, dc * N_SINKS:(dc + 1) * N_SINKS],
                start=(dc == 0), stop=(dc == DCH - 1))) for dc in range(DCH)]
            cx.group("tensor", builds, reads=["WTA", "SCR3"], writes=["ps0"])
            cx.op("vector", lambda v, oc=oc: v.tensor_copy(
                out=bf.CB[:, oc * CTX:oc * CTX + N_SINKS], in_=bf.PS0[:, 0:N_SINKS]),
                reads=["ps0"], writes=["CB"])
            for half in range(2):
                c0 = P + half * 256
                builds = [(lambda t, dc=dc, c0=c0: t.matmul(
                    bf.PS0[:, 0:256], bf.WTA[:, dc * P:(dc + 1) * P],
                    blkT(dc, c0, c0 + 256), start=(dc == 0), stop=(dc == DCH - 1)))
                    for dc in range(DCH)]
                cx.group("tensor", builds, reads=["WTA", "BA"], writes=["ps0"])
                cx.op("vector", lambda v, oc=oc, half=half: v.tensor_copy(
                    out=bf.CB[:, oc * CTX + N_SINKS + half * 256:
                              oc * CTX + N_SINKS + (half + 1) * 256],
                    in_=bf.PS0[:, 0:256]),
                    reads=["ps0"], writes=["CB"])
        for ci in range(NQT):
            nrow = N_SINKS if ci == 0 else P
            for dh in range(2):
                for dc in range(DCH):
                    if ci == 0:
                        lh = bf.SCR3[:, dc * N_SINKS:(dc + 1) * N_SINKS]
                        lk = "SCR3"
                    else:
                        lh = blkT(dc, ci * P, (ci + 1) * P)
                        lk = "BA"
                    cx.op("sync", lambda s, dc=dc, dh=dh: s.dma_start(
                        out=bf.WTB[:, 0:512],
                        in_=ap["aa_Wv"][dc * P:(dc + 1) * P,
                                        dh * 512:(dh + 1) * 512]),
                        dma=True, writes=["WTB"])
                    cx.op("tensor", lambda t, dc=dc, lh=lh, nrow=nrow: t.matmul(
                        bf.PS2[:nrow, :], lh, bf.WTB[:, 0:512],
                        start=(dc == 0), stop=(dc == DCH - 1)),
                        reads=["WTB", lk], writes=["ps2"])
                cx.op("vector", lambda v, nrow=nrow, dh=dh: v.tensor_copy(
                    out=bf.LSCR[:nrow, dh * 512:(dh + 1) * 512],
                    in_=bf.PS2[:nrow, :]),
                    reads=["ps2"], writes=["LSCR"])
            cx.op("sync", lambda s, ci=ci, nrow=nrow: s.dma_start(
                out=ap["s_vtok"][ci, 0:nrow, :], in_=bf.LSCR[0:nrow, :]),
                dma=True, reads=["LSCR"], writes=["s_vtok"])
        cx.barrier()
        for j in range(NQT):
            cx.op("sync", lambda s, j=j: s.dma_start(
                out=bf.CL[:, 0:WINDOW], in_=ap["aa_maskw"][j * P:(j + 1) * P, :]),
                dma=True, writes=["CL"])
            cx.op("sync", lambda s, j=j: s.dma_start(
                out=bf.CL[:, WINDOW:CTX], in_=ap["aa_masks"][j * P:(j + 1) * P, :]),
                dma=True, writes=["CL"])
            cx.op("sync", lambda s, j=j: s.dma_start(
                out=bf.STAT[:, 12:13], in_=ap["aa_addmask"][j * P:(j + 1) * P, :]),
                dma=True, writes=["STAT"])
            for h in range(NHEAD):
                oc, pb = h // 2, (h % 2) * DHEAD
                qT = bf.BB[pb:pb + DHEAD, oc * QBLK + j * P: oc * QBLK + (j + 1) * P]
                cx.group("tensor", [
                    lambda t: t.matmul(bf.PSW[:, 512:512 + N_SINKS], qT,
                                       bf.CB[pb:pb + DHEAD,
                                             oc * CTX:oc * CTX + N_SINKS],
                                       start=True, stop=True),
                    lambda t: t.matmul(bf.PSW[:, 0:256], qT,
                                       bf.CB[pb:pb + DHEAD,
                                             oc * CTX + N_SINKS:
                                             oc * CTX + N_SINKS + 256],
                                       start=True, stop=True),
                    lambda t: t.matmul(bf.PSW[:, 256:512], qT,
                                       bf.CB[pb:pb + DHEAD,
                                             oc * CTX + N_SINKS + 256:oc * CTX + CTX],
                                       start=True, stop=True),
                ], reads=["BB", "CB"], writes=["psw"])
                cx.op("vector", lambda v: v.scalar_tensor_tensor(
                    out=bf.SCR[:, 0:N_SINKS], in0=bf.PSW[:, 512:512 + N_SINKS],
                    scalar=0.125, op0=ALU.mult, op1=ALU.add,
                    in1=bf.CL[:, WINDOW:CTX]),
                    reads=["psw", "CL"], writes=["SCR"])
                cx.op("vector", lambda v: v.scalar_tensor_tensor(
                    out=bf.SCR[:, N_SINKS:CTX], in0=bf.PSW[:, 0:WINDOW],
                    scalar=0.125, op0=ALU.mult, op1=ALU.add, in1=bf.CL[:, 0:WINDOW]),
                    reads=["psw", "CL"], writes=["SCR"])
                cx.op("vector", lambda v: v.reduce_max(
                    bf.STAT[:, 8:9], bf.SCR[:, 0:CTX], axis=AXL.X, negate=True),
                    reads=["SCR"], writes=["STAT"])
                cx.op("scalar", lambda s: s.activation(
                    out=bf.SCR[:, 0:CTX], in_=bf.SCR[:, 0:CTX], func=AF.Exp,
                    bias=bf.STAT[:, 8:9], accum_out=bf.STAT[:, 9:10]),
                    reads=["SCR", "STAT"], writes=["SCR", "STAT"])
                cx.op("vector", lambda v: v.reciprocal(out=bf.STAT[:, 9:10],
                                                       in_=bf.STAT[:, 9:10]),
                      reads=["STAT"], writes=["STAT"])
                cx.op("vector", lambda v: v.tensor_scalar(
                    out=bf.SCR[:, 0:CTX], in0=bf.SCR[:, 0:CTX],
                    scalar1=bf.STAT[:, 9:10], scalar2=None, op0=ALU.mult),
                    reads=["SCR", "STAT"], writes=["SCR"])
                for ci in range(NQT):
                    if ci == 0:
                        pc, nrow = slice(0, N_SINKS), N_SINKS
                    else:
                        pc, nrow = slice(N_SINKS + (ci - 1) * P, N_SINKS + ci * P), P
                    cx.op("tensor", lambda t, pc=pc, nrow=nrow: t.transpose(
                        bf.PT[:nrow, 0:P], bf.SCR[:, pc], bf.IDENT[:, :]),
                        reads=["SCR", "IDENT"], writes=["pt"])
                    cx.op("vector", lambda v, nrow=nrow: v.tensor_copy(
                        out=bf.SCR2[0:nrow, 0:P], in_=bf.PT[0:nrow, 0:P]),
                        reads=["pt"], writes=["SCR2"])
                    cx.op("sync", lambda s, ci=ci, h=h, nrow=nrow: s.dma_start(
                        out=bf.SCR2[0:nrow, 128:128 + DHEAD],
                        in_=ap["s_vtok"][ci, 0:nrow, h * DHEAD:(h + 1) * DHEAD]),
                        dma=True, reads=["s_vtok"], writes=["SCR2"])
                    cx.op("tensor", lambda t, nrow=nrow, ci=ci: t.matmul(
                        bf.PS3[0:DHEAD, 0:P], bf.SCR2[0:nrow, 128:128 + DHEAD],
                        bf.SCR2[0:nrow, 0:P], start=(ci == 0), stop=(ci == NQT - 1)),
                        reads=["SCR2"], writes=["ps3"])
                cx.op("vector", lambda v: v.tensor_copy(
                    out=bf.SCR2[0:DHEAD, 256:256 + P], in_=bf.PS3[0:DHEAD, 0:P]),
                    reads=["ps3"], writes=["SCR2"])
                for dh in range(2):
                    cx.op("sync", lambda s, h=h, dh=dh: s.dma_start(
                        out=bf.WTA[0:DHEAD, 0:512],
                        in_=ap["aa_Wo"][h * DHEAD:(h + 1) * DHEAD,
                                        dh * 512:(dh + 1) * 512]),
                        dma=True, writes=["WTA"])
                    ps = bf.PS0 if dh == 0 else bf.PS1
                    cx.op("tensor", lambda t, h=h, ps=ps: t.matmul(
                        ps[:, :], bf.SCR2[0:DHEAD, 256:256 + P],
                        bf.WTA[0:DHEAD, 0:512],
                        start=(h == 0), stop=(h == NHEAD - 1)),
                        reads=["SCR2", "WTA"], writes=["ps0" if dh == 0 else "ps1"])
            qstart_t = (SEQ - QBLK) // P
            tj = b * NPB + ((qstart_t + j) % NPB)
            for dh in range(2):
                ps = bf.PS0 if dh == 0 else bf.PS1
                cx.op("vector", lambda v, tj=tj, dh=dh, ps=ps: v.scalar_tensor_tensor(
                    out=bf.xtile(tj)[:, dh * 512:(dh + 1) * 512], in0=ps[:, :],
                    scalar=bf.STAT[:, 12:13], op0=ALU.mult, op1=ALU.add,
                    in1=bf.xtile(tj)[:, dh * 512:(dh + 1) * 512]),
                    reads=["ps0" if dh == 0 else "ps1", "STAT", "X"], writes=["X"])
        cx.barrier()


def emit_ffn0(nc, cx, ap, bf):
    cx.op("sync", lambda s: s.dma_start(out=bf.SCR3[:, 0:HCH], in_=ap["f0_b1"]),
          dma=True, writes=["SCR3"])
    cx.op("sync", lambda s: s.dma_start(out=bf.SCR3[:, 32:32 + D], in_=ap["f0_b2"]),
          dma=True, writes=["SCR3"])
    for dc in range(DCH):
        rep_row_tile(cx, bf, bf.A2[:, dc:dc + 1],
                     bf.CL[:, dc * P:(dc + 1) * P], "SMALL", "CL")
    def load_wpair(idx):
        hc, buf = idx % HCH, idx % 2
        cx.op("sync", lambda s, hc=hc, buf=buf: s.dma_start(
            out=bf.BB[:, buf * 1024:(buf + 1) * 1024], in_=ap["s_w1sl"][hc]),
            dma=True, reads=["s_w1sl"], writes=[f"BBw{buf}"])
        cx.op("sync", lambda s, hc=hc, buf=buf: s.dma_start(
            out=bf.BB[:, 2048 + buf * 1024:2048 + (buf + 1) * 1024],
            in_=ap["s_w2s"][hc * P:(hc + 1) * P, :]),
            dma=True, reads=["s_w2s"], writes=[f"BBx{buf}"])
    for b in range(B):
        for j in range(NPB):
            tj = b * NPB + j
            ln_tile(cx, bf, bf.xtile(tj), bf.LNT[:, :], "X", "LNT")
            for dc in range(DCH):
                tr128(cx, bf, bf.LNT[:, dc * P:(dc + 1) * P],
                      bf.BA[:, dc * SEQ_PER + j * P: dc * SEQ_PER + (j + 1) * P],
                      "LNT", "BA")
        TT = 256
        load_wpair(0)
        load_wpair(1)
        NIT = (SEQ_PER // TT) * HCH
        for tt in range(SEQ_PER // TT):
            c0 = tt * TT
            for hc in range(HCH):
                idx = tt * HCH + hc
                buf = idx % 2
                builds = [(lambda t, dc=dc, c0=c0, buf=buf: t.matmul(
                    bf.PSW[:, buf * 512:buf * 512 + TT],
                    bf.BB[:, buf * 1024 + dc * P:buf * 1024 + (dc + 1) * P],
                    bf.BA[:, dc * SEQ_PER + c0: dc * SEQ_PER + c0 + TT],
                    start=(dc == 0), stop=(dc == DCH - 1))) for dc in range(DCH)]
                cx.group("tensor", builds, reads=[f"BBw{buf}", "BA"],
                         writes=[f"psw{buf}"])
                cx.op("scalar", lambda s, hc=hc, buf=buf: s.activation(
                    out=bf.SCR[:, buf * 256:buf * 256 + TT],
                    in_=bf.PSW[:, buf * 512:buf * 512 + TT], func=AF.Gelu,
                    scale=bf.A1[:, hc:hc + 1], bias=bf.SCR3[:, hc:hc + 1]),
                    reads=[f"psw{buf}", "SMALL", "SCR3"], writes=[f"SCRg{buf}"])
                for half in range(2):
                    for dh in range(2):
                        pk = f"ps{half * 2 + dh}"
                        ps = (bf.PS0, bf.PS1, bf.PS2, bf.PS3)[half * 2 + dh]
                        cx.op("tensor", lambda t, half=half, dh=dh, hc=hc, ps=ps,
                              buf=buf: t.matmul(
                                  ps[:, :],
                                  bf.SCR[:, buf * 256 + half * P:
                                         buf * 256 + (half + 1) * P],
                                  bf.BB[:, 2048 + buf * 1024 + dh * 512:
                                        2048 + buf * 1024 + (dh + 1) * 512],
                                  start=(hc == 0), stop=(hc == HCH - 1)),
                              reads=[f"SCRg{buf}", f"BBx{buf}"], writes=[pk])
                if idx + 2 < NIT:
                    load_wpair(idx + 2)
            for half in range(2):
                tj = b * NPB + (c0 + half * P) // P
                for dh in range(2):
                    pk = f"ps{half * 2 + dh}"
                    ps = (bf.PS0, bf.PS1, bf.PS2, bf.PS3)[half * 2 + dh]
                    sl = slice(dh * 512, (dh + 1) * 512)
                    cx.op("vector", lambda v, ps=ps, sl=sl, pk=pk: v.tensor_mul(
                        out=bf.SCR2[:, 0:512], in0=ps[:, :], in1=bf.CL[:, sl]),
                        reads=[pk, "CL"], writes=["SCR2"])
                    cx.op("vector", lambda v, sl=sl: v.tensor_add(
                        out=bf.SCR2[:, 0:512], in0=bf.SCR2[:, 0:512],
                        in1=bf.SCR3[:, 32 + sl.start:32 + sl.stop]),
                        reads=["SCR2", "SCR3"], writes=["SCR2"])
                    cx.op("vector", lambda v, tj=tj, sl=sl: v.tensor_add(
                        out=bf.xtile(tj)[:, sl], in0=bf.xtile(tj)[:, sl],
                        in1=bf.SCR2[:, 0:512]),
                        reads=["SCR2", "X"], writes=["X"])
        cx.barrier()


def emit_attn_b0(nc, cx, ap, bf):
    inv_sq = float(1.0 / np.sqrt(D))
    st = bf.STAT
    for j in range(NPB):
        t0, t1 = j, NPB + j
        ln_tile(cx, bf, bf.xtile(t0), bf.SCR[:, 0:D], "X", "SCR")
        ln_tile(cx, bf, bf.xtile(t1), bf.SCR2[:, 0:D], "X", "SCR2")
        for (c, i0, i1, k0, k1) in ((8, bf.SCR, bf.SCR, "SCR", "SCR"),
                                    (9, bf.SCR, bf.SCR2, "SCR", "SCR2"),
                                    (10, bf.SCR2, bf.SCR2, "SCR2", "SCR2")):
            cx.op("vector", lambda v, c=c, i0=i0, i1=i1: v.scalar_tensor_tensor(
                out=bf.LSCR[:, :], in0=i0[:, 0:D], scalar=0.0, op0=ALU.add,
                op1=ALU.mult, in1=i1[:, 0:D], accum_out=st[:, c:c + 1]),
                reads=[k0, k1], writes=["STAT", "LSCR"])
        cx.op("vector", lambda v: v.tensor_scalar(
            out=st[:, 8:11], in0=st[:, 8:11], scalar1=inv_sq, scalar2=None,
            op0=ALU.mult), reads=["STAT"], writes=["STAT"])
        cx.op("vector", lambda v: v.tensor_max(out=st[:, 11:12], in0=st[:, 8:9],
                                               in1=st[:, 9:10]),
              reads=["STAT"], writes=["STAT"])
        cx.op("vector", lambda v: v.tensor_scalar(out=st[:, 11:12],
                                                  in0=st[:, 11:12], scalar1=-1.0,
                                                  scalar2=None, op0=ALU.mult),
              reads=["STAT"], writes=["STAT"])
        cx.op("vector", lambda v: v.tensor_max(out=st[:, 12:13], in0=st[:, 9:10],
                                               in1=st[:, 10:11]),
              reads=["STAT"], writes=["STAT"])
        cx.op("vector", lambda v: v.tensor_scalar(out=st[:, 12:13],
                                                  in0=st[:, 12:13], scalar1=-1.0,
                                                  scalar2=None, op0=ALU.mult),
              reads=["STAT"], writes=["STAT"])
        for (c, sc, mc) in ((0, 8, 11), (1, 9, 11), (2, 9, 12), (3, 10, 12)):
            cx.op("scalar", lambda s, c=c, sc=sc, mc=mc: s.activation(
                out=st[:, c:c + 1], in_=st[:, sc:sc + 1], func=AF.Exp,
                bias=st[:, mc:mc + 1]),
                reads=["STAT"], writes=["STAT"])
        cx.op("vector", lambda v: v.tensor_add(out=st[:, 4:5], in0=st[:, 0:1],
                                               in1=st[:, 1:2]),
              reads=["STAT"], writes=["STAT"])
        cx.op("vector", lambda v: v.reciprocal(out=st[:, 4:5], in_=st[:, 4:5]),
              reads=["STAT"], writes=["STAT"])
        cx.op("vector", lambda v: v.tensor_add(out=st[:, 5:6], in0=st[:, 2:3],
                                               in1=st[:, 3:4]),
              reads=["STAT"], writes=["STAT"])
        cx.op("vector", lambda v: v.reciprocal(out=st[:, 5:6], in_=st[:, 5:6]),
              reads=["STAT"], writes=["STAT"])
        for (c, r) in ((0, 4), (1, 4), (2, 5), (3, 5)):
            cx.op("vector", lambda v, c=c, r=r: v.tensor_scalar(
                out=st[:, c:c + 1], in0=st[:, c:c + 1], scalar1=st[:, r:r + 1],
                scalar2=None, op0=ALU.mult),
                reads=["STAT"], writes=["STAT"])
        cx.op("vector", lambda v, t0=t0: v.scalar_tensor_tensor(
            out=bf.LSCR[:, :], in0=bf.SCR2[:, 0:D], scalar=st[:, 1:2],
            op0=ALU.mult, op1=ALU.add, in1=bf.xtile(t0)),
            reads=["SCR2", "STAT", "X"], writes=["LSCR"])
        cx.op("vector", lambda v, t0=t0: v.scalar_tensor_tensor(
            out=bf.xtile(t0), in0=bf.SCR[:, 0:D], scalar=st[:, 0:1],
            op0=ALU.mult, op1=ALU.add, in1=bf.LSCR[:, :]),
            reads=["SCR", "STAT", "LSCR"], writes=["X"])
        cx.op("vector", lambda v, t1=t1: v.scalar_tensor_tensor(
            out=bf.LSCR[:, :], in0=bf.SCR[:, 0:D], scalar=st[:, 2:3],
            op0=ALU.mult, op1=ALU.add, in1=bf.xtile(t1)),
            reads=["SCR", "STAT", "X"], writes=["LSCR"])
        cx.op("vector", lambda v, t1=t1: v.scalar_tensor_tensor(
            out=bf.xtile(t1), in0=bf.SCR2[:, 0:D], scalar=st[:, 3:4],
            op0=ALU.mult, op1=ALU.add, in1=bf.LSCR[:, :]),
            reads=["SCR2", "STAT", "LSCR"], writes=["X"])


def emit_ttt(nc, cx, ap, bf):
    SEG = SEQ_PER + 1
    for b in range(B):
        cx.op("sync", lambda s, b=b: s.dma_start(
            out=ap["g_first_in"][b:b + 1, :],
            in_=bf.X[0:1, b * NPB * D: b * NPB * D + D]),
            dma=True, reads=["X"], writes=["g_first_in"])
    cx.op("gpsimd", lambda g: g.collective_compute(
        "AllGather", ALU.bypass, replica_groups=[list(range(NC))],
        ins=[ap["g_first_in"].opt()], outs=[ap["g_first_out"].opt()]),
        coll=True, reads=["g_first_in"], writes=["g_first_out"])
    cx.op("sync", lambda s: s.dma_start(
        out=bf.CL[0:NC, 0:B * D],
        in_=ap["g_first_out"].rearrange("n b d -> n (b d)")),
        dma=True, reads=["g_first_out"], writes=["CL"])
    cx.op("sync", lambda s: s.dma_start(out=bf.SCR2[0:NC, 0:1],
                                        in_=ap["onehot_next"]),
          dma=True, writes=["SCR2"])
    for q in range(B * D // 512):
        cx.op("tensor", lambda t, q=q: t.matmul(
            bf.PSW[0:1, 0:512], bf.SCR2[0:NC, 0:1],
            bf.CL[0:NC, q * 512:(q + 1) * 512], start=True, stop=True),
            reads=["CL", "SCR2"], writes=["psw"])
        cx.op("vector", lambda v, q=q: v.tensor_copy(
            out=bf.CL[0:1, 2048 + q * 512:2048 + (q + 1) * 512],
            in_=bf.PSW[0:1, 0:512]),
            reads=["psw"], writes=["CL"])
    cx.op("sync", lambda s: s.dma_start(out=bf.CB[:, 0:D], in_=ap["t_gamma"]),
          dma=True, writes=["CB"])
    cx.op("sync", lambda s: s.dma_start(out=bf.CB[:, D:2 * D], in_=ap["t_beta"]),
          dma=True, writes=["CB"])
    wbtr = ap["t_Wbt"].rearrange("(dc p) o -> p dc o", p=P)
    wptr = ap["t_Wpt"].rearrange("(dc p) o -> p dc o", p=P)
    for b in range(B):
        sxtr = ap["s_xt"][b].rearrange("c p t -> p c t")
        sxntr = ap["s_xnt"][b].rearrange("c p t -> p c t")
        for j in range(NPB):
            tj = b * NPB + j
            ln_tile(cx, bf, bf.xtile(tj), bf.BB[:, j * D:(j + 1) * D], "X", "BB")
            for dc in range(DCH):
                tr128(cx, bf, bf.BB[:, j * D + dc * P: j * D + (dc + 1) * P],
                      bf.LNT[:, dc * P:(dc + 1) * P], "BB", "LNT")
            cx.op("sync", lambda s, j=j: s.dma_start(
                out=sxtr[:, :, j * P:(j + 1) * P],
                in_=bf.LNT.rearrange("p (c q) -> p c q", q=P)),
                dma=True, reads=["LNT"], writes=["s_xt"])
            ln_tile(cx, bf, bf.BB[:, j * D:(j + 1) * D], bf.LSCR[:, :], "BB",
                    "LSCR", gamma=bf.CB[:, 0:D], beta=bf.CB[:, D:2 * D], gkey="CB")
            for dc in range(DCH):
                tr128(cx, bf, bf.LSCR[:, dc * P:(dc + 1) * P],
                      bf.LNT[:, dc * P:(dc + 1) * P], "LSCR", "LNT")
            cx.op("sync", lambda s, j=j: s.dma_start(
                out=sxntr[:, :, j * P:(j + 1) * P],
                in_=bf.LNT.rearrange("p (c q) -> p c q", q=P)),
                dma=True, reads=["LNT"], writes=["s_xnt"])
        brow = bf.CL[0:1, 2048 + b * D:2048 + (b + 1) * D]
        ln_tile(cx, bf, brow, bf.SCR3[0:1, 0:D], "CL", "SCR3", nrow=1)
        ln_tile(cx, bf, bf.SCR3[0:1, 0:D], bf.SCR2[0:1, 0:D], "SCR3", "SCR2",
                gamma=bf.CB[0:1, 0:D], beta=bf.CB[0:1, D:2 * D], gkey="CB", nrow=1)
        for dc in range(DCH):
            tr128(cx, bf, bf.SCR2[0:1, dc * P:(dc + 1) * P],
                  bf.LNT[:, dc:dc + 1], "SCR2", "LNT", nout=P)
        cx.op("sync", lambda s: s.dma_start(
            out=sxntr[:, :, SEQ_PER:SEQ_PER + 1], in_=bf.LNT[:, 0:DCH],
            allow_slow_non_contiguous=True),
            dma=True, reads=["LNT"], writes=["s_xnt"])
        cx.barrier()
        for j in range(NPB):
            tj = b * NPB + j
            cx.op("sync", lambda s, j=j: s.dma_start(
                out=bf.LNT.rearrange("p (c q) -> p c q", q=P),
                in_=sxtr[:, :, j * P:(j + 1) * P]),
                dma=True, reads=["s_xt"], writes=["LNT"])
            for dh in range(2):
                for dc in range(DCH):
                    cx.op("sync", lambda s, dc=dc, dh=dh: s.dma_start(
                        out=bf.WTA[:, 0:512],
                        in_=ap["t_Wbt"][dc * P:(dc + 1) * P,
                                        dh * 512:(dh + 1) * 512]),
                        dma=True, writes=["WTA"])
                    cx.op("tensor", lambda t, dc=dc: t.matmul(
                        bf.PSW[:, 0:512], bf.LNT[:, dc * P:(dc + 1) * P],
                        bf.WTA[:, 0:512],
                        start=(dc == 0), stop=(dc == DCH - 1)),
                        reads=["LNT", "WTA"], writes=["psw"])
                cx.op("vector", lambda v, tj=tj, dh=dh: v.tensor_add(
                    out=bf.xtile(tj)[:, dh * 512:(dh + 1) * 512],
                    in0=bf.xtile(tj)[:, dh * 512:(dh + 1) * 512],
                    in1=bf.PSW[:, 0:512]),
                    reads=["psw", "X"], writes=["X"])
            builds = [(lambda t, dc=dc: t.matmul(
                bf.PS0[:, 0:LORA_R], bf.LNT[:, dc * P:(dc + 1) * P],
                bf.IBT[:, dc * LORA_R:(dc + 1) * LORA_R],
                start=(dc == 0), stop=(dc == DCH - 1))) for dc in range(DCH)]
            cx.group("tensor", builds, reads=["LNT", "SMALL"], writes=["ps0"])
            cx.op("vector", lambda v, j=j: v.tensor_copy(
                out=bf.PIN[:, j * LORA_R:(j + 1) * LORA_R],
                in_=bf.PS0[:, 0:LORA_R]),
                reads=["ps0"], writes=["SMALL"])
            cx.op("sync", lambda s, j=j: s.dma_start(
                out=bf.LSCR.rearrange("p (c q) -> p c q", q=P),
                in_=sxntr[:, :, j * P + 1:(j + 1) * P + 1]),
                dma=True, reads=["s_xnt"], writes=["LSCR"])
            builds = [(lambda t, dc=dc: t.matmul(
                bf.PS1[:, 0:LORA_R], bf.LSCR[:, dc * P:(dc + 1) * P],
                bf.G[:, dc * LORA_R:(dc + 1) * LORA_R],
                start=(dc == 0), stop=(dc == DCH - 1))) for dc in range(DCH)]
            cx.group("tensor", builds, reads=["LSCR", "SMALL"], writes=["ps1"])
            cx.op("vector", lambda v, j=j: v.tensor_copy(
                out=bf.PERR[:, j * LORA_R:(j + 1) * LORA_R],
                in_=bf.PS1[:, 0:LORA_R]),
                reads=["ps1"], writes=["SMALL"])
            for dh in range(2):
                for dc in range(DCH):
                    cx.op("sync", lambda s, dc=dc, dh=dh: s.dma_start(
                        out=bf.WTB[:, 0:512],
                        in_=ap["t_Wpt"][dc * P:(dc + 1) * P,
                                        dh * 512:(dh + 1) * 512]),
                        dma=True, writes=["WTB"])
                    cx.op("tensor", lambda t, dc=dc: t.matmul(
                        bf.PSW[:, 512:1024], bf.LSCR[:, dc * P:(dc + 1) * P],
                        bf.WTB[:, 0:512],
                        start=(dc == 0), stop=(dc == DCH - 1)),
                        reads=["LSCR", "WTB"], writes=["psw"])
                cx.op("vector", lambda v, j=j, dh=dh: v.tensor_copy(
                    out=bf.BA[:, j * D + dh * 512: j * D + dh * 512 + 512],
                    in_=bf.PSW[:, 512:1024]),
                    reads=["psw"], writes=["BA"])
        for c in range(NCHB):
            jj = (2 * c, 2 * c + 1)
            for hc in range(DCH):
                builds = []
                for j in jj:
                    builds.append(lambda t, j=j, hc=hc: t.matmul(
                        bf.PS2[:, 0:LORA_R],
                        bf.BA[:, j * D + hc * P: j * D + (hc + 1) * P],
                        bf.PIN[:, j * LORA_R:(j + 1) * LORA_R],
                        start=(j == jj[0]), stop=(j == jj[1])))
                cx.group("tensor", builds, reads=["BA", "SMALL"], writes=["ps2"])
                cx.op("vector", lambda v, c=c, hc=hc: v.tensor_copy(
                    out=bf.SCR[:, c * P + hc * LORA_R: c * P + (hc + 1) * LORA_R],
                    in_=bf.PS2[:, 0:LORA_R]), reads=["ps2"], writes=["SCR"])
                builds = []
                for j in jj:
                    builds.append(lambda t, j=j, hc=hc: t.matmul(
                        bf.PS3[:, 0:LORA_R],
                        bf.BB[:, j * D + hc * P: j * D + (hc + 1) * P],
                        bf.PERR[:, j * LORA_R:(j + 1) * LORA_R],
                        start=(j == jj[0]), stop=(j == jj[1])))
                cx.group("tensor", builds, reads=["BB", "SMALL"], writes=["ps3"])
                cx.op("vector", lambda v, c=c, hc=hc: v.tensor_copy(
                    out=bf.SCR2[:, c * P + hc * LORA_R: c * P + (hc + 1) * LORA_R],
                    in_=bf.PS3[:, 0:LORA_R]), reads=["ps3"], writes=["SCR2"])
            cx.op("sync", lambda s, b=b, c=c: s.dma_start(
                out=ap["s_da"][b, c, :, :], in_=bf.SCR[:, c * P:(c + 1) * P]),
                dma=True, reads=["SCR"], writes=["s_da"])
            cx.op("sync", lambda s, b=b, c=c: s.dma_start(
                out=ap["s_db"][b, c, :, :], in_=bf.SCR2[:, c * P:(c + 1) * P]),
                dma=True, reads=["SCR2"], writes=["s_db"])
        for (buf, bk, off) in ((bf.SCR, "SCR", 0), (bf.SCR2, "SCR2", P)):
            cx.op("vector", lambda v, buf=buf: v.tensor_copy(
                out=bf.LSCR[:, 0:P], in_=buf[:, 0:P]),
                reads=[bk], writes=["LSCR"])
            for c in range(1, NCHB):
                cx.op("vector", lambda v, buf=buf, c=c: v.tensor_add(
                    out=bf.LSCR[:, 0:P], in0=bf.LSCR[:, 0:P],
                    in1=buf[:, c * P:(c + 1) * P]),
                    reads=[bk, "LSCR"], writes=["LSCR"])
            cx.op("sync", lambda s, b=b, off=off: s.dma_start(
                out=ap["g_sums_in"][:, b * 2 * P + off: b * 2 * P + off + P],
                in_=bf.LSCR[:, 0:P]),
                dma=True, reads=["LSCR"], writes=["g_sums_in"])
        cx.barrier()
    cx.op("gpsimd", lambda g: g.collective_compute(
        "AllGather", ALU.bypass, replica_groups=[list(range(NC))],
        ins=[ap["g_sums_in"].opt()], outs=[ap["g_sums_out"].opt()]),
        coll=True, reads=["g_sums_in"], writes=["g_sums_out"])
    for b in range(B):
        sxtr = ap["s_xt"][b].rearrange("c p t -> p c t")
        accA = bf.CB[:, 2 * D + 2 * P:2 * D + 3 * P]
        accB = bf.CB[:, 2 * D + 3 * P:2 * D + 4 * P]
        for (dst, off) in ((accA, 0), (accB, P)):
            for r in range(NC):
                cx.op("sync", lambda s, r=r, b=b, off=off: s.dma_start(
                    out=bf.CL[:, r * P:(r + 1) * P],
                    in_=ap["g_sums_out"][r, :, b * 2 * P + off: b * 2 * P + off + P]),
                    dma=True, reads=["g_sums_out"], writes=["CL"])
            cx.op("vector", lambda v: v.tensor_scalar(
                out=bf.LSCR[:, 0:P], in0=bf.CL[:, 0:P], scalar1=bf.PREFW[:, 0:1],
                scalar2=None, op0=ALU.mult),
                reads=["CL", "SMALL"], writes=["LSCR"])
            for r in range(1, NC):
                cx.op("vector", lambda v, r=r: v.scalar_tensor_tensor(
                    out=bf.LSCR[:, 0:P], in0=bf.CL[:, r * P:(r + 1) * P],
                    scalar=bf.PREFW[:, r:r + 1], op0=ALU.mult, op1=ALU.add,
                    in1=bf.LSCR[:, 0:P]),
                    reads=["CL", "SMALL", "LSCR"], writes=["LSCR"])
            cx.op("vector", lambda v, dst=dst: v.tensor_copy(out=dst,
                                                             in_=bf.LSCR[:, 0:P]),
                  reads=["LSCR"], writes=["CB"])
        for c in range(NCHB):
            cx.op("vector", lambda v: v.scalar_tensor_tensor(
                out=bf.LSCR[:, 0:P], in0=accA, scalar=0.0, op0=ALU.add,
                op1=ALU.mult, in1=accA, accum_out=bf.STAT[:, 8:9]),
                reads=["CB"], writes=["STAT", "LSCR"])
            cx.op("vector", lambda v: v.scalar_tensor_tensor(
                out=bf.LSCR[:, 0:P], in0=accB, scalar=0.0, op0=ALU.add,
                op1=ALU.mult, in1=accB, accum_out=bf.STAT[:, 9:10]),
                reads=["CB"], writes=["STAT", "LSCR"])
            cx.op("vector", lambda v: v.tensor_add(out=bf.STAT[:, 8:9],
                                                   in0=bf.STAT[:, 8:9],
                                                   in1=bf.STAT[:, 9:10]),
                  reads=["STAT"], writes=["STAT"])
            cx.op("tensor", lambda t: t.matmul(bf.PT[0:1, 0:1], bf.STAT[:, 8:9],
                                               bf.ONESC, start=True, stop=True),
                  reads=["STAT", "SMALL"], writes=["pt"])
            cx.op("vector", lambda v: v.tensor_copy(out=bf.STAT[0:1, 10:11],
                                                    in_=bf.PT[0:1, 0:1]),
                  reads=["pt"], writes=["STAT"])
            cx.op("scalar", lambda s: s.activation(
                out=bf.STAT[0:1, 10:11], in_=bf.STAT[0:1, 10:11], func=AF.Sqrt,
                scale=TTT_LR * TTT_LR), reads=["STAT"], writes=["STAT"])
            cx.op("vector", lambda v: v.tensor_scalar(
                out=bf.STAT[0:1, 10:11], in0=bf.STAT[0:1, 10:11], scalar1=1e-6,
                scalar2=None, op0=ALU.add), reads=["STAT"], writes=["STAT"])
            cx.op("vector", lambda v: v.reciprocal(out=bf.STAT[0:1, 10:11],
                                                   in_=bf.STAT[0:1, 10:11]),
                  reads=["STAT"], writes=["STAT"])
            cx.op("vector", lambda v: v.tensor_scalar(
                out=bf.STAT[0:1, 10:11], in0=bf.STAT[0:1, 10:11], scalar1=1.0,
                scalar2=-TTT_LR, op0=ALU.min, op1=ALU.mult),
                reads=["STAT"], writes=["STAT"])
            bcast_scalar(cx, bf, bf.STAT[0:1, 10:11], bf.STAT[:, 11:12],
                         "STAT", "STAT")
            cx.op("vector", lambda v: v.scalar_tensor_tensor(
                out=bf.SCR[:, 0:P], in0=accA, scalar=bf.STAT[:, 11:12],
                op0=ALU.mult, op1=ALU.add, in1=bf.IA),
                reads=["CB", "STAT", "SMALL"], writes=["SCR"])
            cx.op("vector", lambda v: v.scalar_tensor_tensor(
                out=bf.SCR[:, P:2 * P], in0=accB, scalar=bf.STAT[:, 11:12],
                op0=ALU.mult, op1=ALU.add, in1=bf.IBT),
                reads=["CB", "STAT", "SMALL"], writes=["SCR"])
            for hc in range(DCH):
                tr128(cx, bf, bf.SCR[:, hc * LORA_R:(hc + 1) * LORA_R],
                      bf.SCR2[0:LORA_R, hc * P:(hc + 1) * P], "SCR", "SCR2",
                      nout=LORA_R)
            for j in (2 * c, 2 * c + 1):
                tj = b * NPB + j
                cx.op("sync", lambda s, j=j: s.dma_start(
                    out=bf.LNT.rearrange("p (c q) -> p c q", q=P),
                    in_=sxtr[:, :, j * P:(j + 1) * P]),
                    dma=True, reads=["s_xt"], writes=["LNT"])
                builds = [(lambda t, dc=dc: t.matmul(
                    bf.PS0[:, 0:LORA_R], bf.LNT[:, dc * P:(dc + 1) * P],
                    bf.SCR[:, P + dc * LORA_R:P + (dc + 1) * LORA_R],
                    start=(dc == 0), stop=(dc == DCH - 1))) for dc in range(DCH)]
                cx.group("tensor", builds, reads=["LNT", "SCR"], writes=["ps0"])
                cx.op("vector", lambda v: v.tensor_copy(out=bf.LSCR[:, 0:LORA_R],
                                                        in_=bf.PS0[:, 0:LORA_R]),
                      reads=["ps0"], writes=["LSCR"])
                tr128(cx, bf, bf.LSCR[:, 0:LORA_R], bf.LSCR[0:LORA_R, 256:256 + P],
                      "LSCR", "LSCR", nout=LORA_R)
                for dh in range(2):
                    cx.op("tensor", lambda t, dh=dh: t.matmul(
                        bf.PSW[:, dh * 512:(dh + 1) * 512],
                        bf.LSCR[0:LORA_R, 256:256 + P],
                        bf.SCR2[0:LORA_R, dh * 512:(dh + 1) * 512],
                        start=True, stop=True),
                        reads=["LSCR", "SCR2"], writes=["psw"])
                    cx.op("vector", lambda v, tj=tj, dh=dh: v.scalar_tensor_tensor(
                        out=bf.xtile(tj)[:, dh * 512:(dh + 1) * 512],
                        in0=bf.PSW[:, dh * 512:(dh + 1) * 512], scalar=SCALING,
                        op0=ALU.mult, op1=ALU.add,
                        in1=bf.xtile(tj)[:, dh * 512:(dh + 1) * 512]),
                        reads=["psw", "X"], writes=["X"])
            if c < NCHB - 1:
                for (dst, nm) in ((accA, "s_da"), (accB, "s_db")):
                    cx.op("sync", lambda s, b=b, c=c, nm=nm: s.dma_start(
                        out=bf.LSCR[:, 512:512 + P], in_=ap[nm][b, c, :, :]),
                        dma=True, reads=[nm], writes=["LSCR"])
                    cx.op("vector", lambda v, dst=dst: v.tensor_add(
                        out=dst, in0=dst, in1=bf.LSCR[:, 512:512 + P]),
                        reads=["LSCR", "CB"], writes=["CB"])
        cx.barrier()


def emit_qffn(nc, cx, ap, bf):
    XC16 = bf.BA[:, :].bitcast(BF16)
    W16 = bf.BB[:, :].bitcast(BF16)
    CT2 = bf.CB[:, :].bitcast(BF16)
    cx.op("sync", lambda s: s.dma_start(out=bf.SCR3[:, 0:HCH], in_=ap["q_b1"]),
          dma=True, writes=["SCR3"])
    cx.op("sync", lambda s: s.dma_start(out=bf.SCR3[:, 32:32 + DCH],
                                        in_=ap["q_b2"]),
          dma=True, writes=["SCR3"])
    for t in range(NPT):
        ln_tile(cx, bf, bf.xtile(t), bf.LSCR[:, :], "X", "LSCR")
        cx.op("vector", lambda v, t=t: v.reduce_max(
            bf.SCR[:, t:t + 1], bf.LSCR[:, :], axis=AXL.X),
            reads=["LSCR"], writes=["SCR"])
        cx.op("sync", lambda s, t=t: s.dma_start(
            out=ap["s_ln"][t, :, :], in_=bf.LSCR[:, :]),
            dma=True, reads=["LSCR"], writes=["s_ln"])
    cx.op("vector", lambda v: v.reduce_max(bf.STAT[:, 8:9], bf.SCR[:, 0:NPT],
                                           axis=AXL.X),
          reads=["SCR"], writes=["STAT"])
    glob_max_allgather(nc, cx, ap, bf, slot=0)
    cx.op("vector", lambda v: v.tensor_scalar(
        out=bf.SA1, in0=bf.SW1, scalar1=bf.STAT[:, 14:15], scalar2=None,
        op0=ALU.mult), reads=["SMALL", "STAT"], writes=["SMALL"])
    cx.op("vector", lambda v: v.tensor_copy(out=bf.SA1R, in_=bf.SA1),
          reads=["SMALL"], writes=["SMALL"])
    cx.op("vector", lambda v: v.reciprocal(out=bf.SA1R, in_=bf.SA1R),
          reads=["SMALL"], writes=["SMALL"])
    cx.op("vector", lambda v: v.tensor_mul(out=bf.BQ1, in0=bf.SCR3[:, 0:HCH],
                                           in1=bf.SA1R),
          reads=["SCR3", "SMALL"], writes=["SMALL"])
    cx.op("vector", lambda v: v.tensor_scalar(out=bf.BQ1, in0=bf.BQ1,
                                              scalar1=C23, scalar2=-C23,
                                              op0=ALU.add, op1=ALU.add),
          reads=["SMALL"], writes=["SMALL"])
    cx.op("vector", lambda v: v.tensor_mul(out=bf.BQ1, in0=bf.BQ1, in1=bf.SA1),
          reads=["SMALL"], writes=["SMALL"])
    cx.op("vector", lambda v: v.memset(bf.STAT[:, 0:1], 0.0), writes=["STAT"])
    for b in range(B):
        for j in range(NPB):
            t = b * NPB + j
            cx.op("sync", lambda s, t=t: s.dma_start(
                out=bf.LSCR[:, :], in_=ap["s_ln"][t, :, :]),
                dma=True, reads=["s_ln"], writes=["LSCR"])
            cx.op("vector", lambda v: v.tensor_scalar(
                out=bf.SCR[:, 0:D], in0=bf.LSCR[:, :], scalar1=bf.STAT[:, 11:12],
                scalar2=-1.0, op0=ALU.mult, op1=ALU.max),
                reads=["LSCR", "STAT"], writes=["SCR"])
            cx.op("vector", lambda v: v.tensor_scalar(
                out=bf.F16T[:, 0:D], in0=bf.SCR[:, 0:D], scalar1=1536.0,
                scalar2=None, op0=ALU.add),
                reads=["SCR"], writes=["SCR2"])
            cx.op("vector", lambda v: v.tensor_scalar(
                out=bf.SCR[:, 0:512].bitcast(BF16)[:, 0:D], in0=bf.F16T[:, 0:D],
                scalar1=-1536.0, scalar2=0.0, op0=ALU.add, op1=ALU.max),
                reads=["SCR2"], writes=["SCR"])
            cx.op("sync", lambda s, j=j: s.dma_start(
                out=ap["s_xc"][j * P:(j + 1) * P, :],
                in_=bf.SCR[:, 0:512].bitcast(BF16)[:, 0:D]),
                dma=True, reads=["SCR"], writes=["s_xc"])
        cx.barrier()
        for dc in range(DCH):
            cx.op("sync", lambda s, dc=dc: s.dma_start(
                out=XC16[:, dc * SEQ_PER:(dc + 1) * SEQ_PER],
                in_=ap["s_xc"][:, dc * P:(dc + 1) * P], transpose=True),
                dma=True, reads=["s_xc"], writes=["BA"])
        def load_w1q(hc):
            buf = hc % 2
            cx.op("sync", lambda s, hc=hc, buf=buf: s.dma_start(
                out=W16[:, buf * 1024:(buf + 1) * 1024],
                in_=ap["s_q1ctl"][hc]),
                dma=True, reads=["s_q1ctl"], writes=[f"BBq{buf}"])
        load_w1q(0)
        load_w1q(1)
        for hc in range(HCH):
            buf = hc % 2
            for j in range(NPB):
                builds = [(lambda t, dc=dc, j=j, buf=buf: t.matmul(
                    bf.PSW[:, j * P:(j + 1) * P],
                    W16[:, buf * 1024 + dc * P:buf * 1024 + (dc + 1) * P],
                    XC16[:, dc * SEQ_PER + j * P: dc * SEQ_PER + (j + 1) * P],
                    start=(dc == 0), stop=(dc == DCH - 1))) for dc in range(DCH)]
                cx.group("tensor", builds, reads=[f"BBq{buf}", "BA"],
                         writes=["psw" if j < 4 else "pswb"])
            for j in range(NPB):
                cx.op("scalar", lambda s, hc=hc, j=j: s.activation(
                    out=bf.LSCR[:, j * P:(j + 1) * P],
                    in_=bf.PSW[:, j * P:(j + 1) * P], func=AF.Relu,
                    scale=bf.SA1[:, hc:hc + 1], bias=bf.BQ1[:, hc:hc + 1]),
                    reads=["psw" if j < 4 else "pswb", "SMALL"], writes=["LSCR"])
            if hc + 2 < HCH:
                load_w1q(hc + 2)
            cx.op("vector", lambda v: v.reduce_max(
                bf.STAT[:, 8:9], bf.LSCR[:, 0:SEQ_PER], axis=AXL.X),
                reads=["LSCR"], writes=["STAT"])
            cx.op("vector", lambda v: v.tensor_max(
                out=bf.STAT[:, 0:1], in0=bf.STAT[:, 0:1], in1=bf.STAT[:, 8:9]),
                reads=["STAT"], writes=["STAT"])
            cx.op("sync", lambda s, b=b, hc=hc: s.dma_start(
                out=ap["s_hl"][hc, :, b * SEQ_PER:(b + 1) * SEQ_PER],
                in_=bf.LSCR[:, 0:SEQ_PER]),
                dma=True, reads=["LSCR"], writes=["s_hl"])
        cx.barrier()
    cx.op("vector", lambda v: v.tensor_copy(out=bf.STAT[:, 8:9],
                                            in_=bf.STAT[:, 0:1]),
          reads=["STAT"], writes=["STAT"])
    glob_max_allgather(nc, cx, ap, bf, slot=1)
    cx.op("vector", lambda v: v.tensor_scalar(
        out=bf.SCR[:, 0:DCH], in0=bf.SW2, scalar1=bf.STAT[:, 14:15],
        scalar2=None, op0=ALU.mult), reads=["SMALL", "STAT"], writes=["SCR"])
    cx.op("vector", lambda v: v.tensor_copy(out=bf.SCR[:, 8:8 + DCH],
                                            in_=bf.SCR[:, 0:DCH]),
          reads=["SCR"], writes=["SCR"])
    cx.op("vector", lambda v: v.reciprocal(out=bf.SCR[:, 8:8 + DCH],
                                           in_=bf.SCR[:, 8:8 + DCH]),
          reads=["SCR"], writes=["SCR"])
    cx.op("vector", lambda v: v.tensor_mul(out=bf.SCR[:, 16:16 + DCH],
                                           in0=bf.SCR3[:, 32:32 + DCH],
                                           in1=bf.SCR[:, 8:8 + DCH]),
          reads=["SCR3", "SCR"], writes=["SCR"])
    cx.op("vector", lambda v: v.tensor_scalar(out=bf.SCR[:, 16:16 + DCH],
                                              in0=bf.SCR[:, 16:16 + DCH],
                                              scalar1=C23, scalar2=-C23,
                                              op0=ALU.add, op1=ALU.add),
          reads=["SCR"], writes=["SCR"])
    cx.op("vector", lambda v: v.tensor_mul(out=bf.SCR[:, 16:16 + DCH],
                                           in0=bf.SCR[:, 16:16 + DCH],
                                           in1=bf.SCR[:, 0:DCH]),
          reads=["SCR"], writes=["SCR"])
    for dc in range(DCH):
        rep_row_tile(cx, bf, bf.SCR[:, dc:dc + 1], bf.CL[:, dc * P:(dc + 1) * P],
                     "SCR", "CL")
        rep_row_tile(cx, bf, bf.SCR[:, 16 + dc:17 + dc],
                     bf.CL[:, D + dc * P:D + (dc + 1) * P], "SCR", "CL")
    for dh in range(2):
        for hc in range(HCH):
            cx.op("sync", lambda s, dh=dh, hc=hc: s.dma_start(
                out=W16[:, hc * 512:(hc + 1) * 512],
                in_=ap["s_q2ct"][hc, :, dh * 512:(dh + 1) * 512]),
                dma=True, reads=["s_q2ct"], writes=[f"BBq{hc % 8}"])
        for t in range(NPT):
            for hc in range(HCH):
                cx.op("sync", lambda s, t=t, hc=hc: s.dma_start(
                    out=bf.LSCR[:, 0:P],
                    in_=ap["s_hl"][hc, :, t * P:(t + 1) * P]),
                    dma=True, reads=["s_hl"], writes=["LSCR"])
                cx.op("vector", lambda v, hc=hc: v.tensor_scalar(
                    out=bf.F16T[:, 0:P], in0=bf.LSCR[:, 0:P],
                    scalar1=bf.STAT[:, 11:12], scalar2=1536.0,
                    op0=ALU.mult, op1=ALU.add),
                    reads=["LSCR", "STAT"], writes=["SCR2"])
                cx.op("vector", lambda v, hc=hc: v.tensor_scalar(
                    out=CT2[:, hc * P:(hc + 1) * P], in0=bf.F16T[:, 0:P],
                    scalar1=-1536.0, scalar2=None, op0=ALU.add),
                    reads=["SCR2"], writes=["CB"])
            builds = [(lambda t2, hc=hc: t2.matmul(
                bf.PSW[:, 0:512], CT2[:, hc * P:(hc + 1) * P],
                W16[:, hc * 512:(hc + 1) * 512],
                start=(hc == 0), stop=(hc == HCH - 1))) for hc in range(HCH)]
            cx.group("tensor", builds,
                     reads=["CB"] + [f"BBq{k}" for k in range(8)],
                     writes=["psw"])
            sl = slice(dh * 512, (dh + 1) * 512)
            cx.op("vector", lambda v, sl=sl: v.tensor_mul(
                out=bf.SCR2[:, 0:512], in0=bf.PSW[:, 0:512],
                in1=bf.CL[:, sl]),
                reads=["psw", "CL"], writes=["SCR2"])
            cx.op("vector", lambda v, sl=sl: v.tensor_add(
                out=bf.SCR2[:, 0:512], in0=bf.SCR2[:, 0:512],
                in1=bf.CL[:, D + sl.start:D + sl.stop]),
                reads=["SCR2", "CL"], writes=["SCR2"])
            cx.op("vector", lambda v, t=t, sl=sl: v.tensor_add(
                out=bf.xtile(t)[:, sl], in0=bf.xtile(t)[:, sl],
                in1=bf.SCR2[:, 0:512]),
                reads=["SCR2", "X"], writes=["X"])
        cx.barrier()


def glob_max_allgather(nc, cx, ap, bf, slot):
    cx.op("tensor", lambda t: t.transpose(bf.PT[0:1, 0:P], bf.STAT[:, 8:9],
                                          bf.IDENT[:, :]),
          reads=["STAT", "IDENT"], writes=["pt"])
    cx.op("vector", lambda v: v.tensor_copy(out=bf.LSCR[0:1, 0:P],
                                            in_=bf.PT[0:1, 0:P]),
          reads=["pt"], writes=["LSCR"])
    cx.op("vector", lambda v: v.reduce_max(bf.STAT[0:1, 9:10],
                                           bf.LSCR[0:1, 0:P], axis=AXL.X),
          reads=["LSCR"], writes=["STAT"])
    gi, go = f"g_max_in{slot}", f"g_max_out{slot}"
    cx.op("sync", lambda s: s.dma_start(
        out=ap[gi][0:1, 0:1], in_=bf.STAT[0:1, 9:10]),
        dma=True, reads=["STAT"], writes=[gi])
    cx.op("gpsimd", lambda g: g.collective_compute(
        "AllGather", ALU.bypass, replica_groups=[list(range(NC))],
        ins=[ap[gi].opt()], outs=[ap[go].opt()]),
        coll=True, reads=[gi], writes=[go])
    cx.op("sync", lambda s: s.dma_start(
        out=bf.LSCR[0:1, 0:NC], in_=ap[go].rearrange("n o -> (n o)")),
        dma=True, reads=[go], writes=["LSCR"])
    cx.op("vector", lambda v: v.reduce_max(bf.STAT[0:1, 9:10],
                                           bf.LSCR[0:1, 0:NC], axis=AXL.X),
          reads=["LSCR"], writes=["STAT"])
    cx.op("vector", lambda v: v.tensor_scalar(
        out=bf.STAT[0:1, 9:10], in0=bf.STAT[0:1, 9:10], scalar1=1e-8,
        scalar2=1.0 / 255.0, op0=ALU.max, op1=ALU.mult),
        reads=["STAT"], writes=["STAT"])
    cx.op("vector", lambda v: v.reciprocal(out=bf.STAT[0:1, 10:11],
                                           in_=bf.STAT[0:1, 9:10]),
          reads=["STAT"], writes=["STAT"])
    bcast_scalar(cx, bf, bf.STAT[0:1, 10:11], bf.STAT[:, 11:12], "STAT", "STAT")
    bcast_scalar(cx, bf, bf.STAT[0:1, 9:10], bf.STAT[:, 14:15], "STAT", "STAT")


def emit_attn_b1(nc, cx, ap, bf):
    for b in range(B):
        for j in range(NPB):
            tj = b * NPB + j
            ln_tile(cx, bf, bf.xtile(tj), bf.LNT[:, :], "X", "LNT")
            for dc in range(DCH):
                tr128(cx, bf, bf.LNT[:, dc * P:(dc + 1) * P],
                      bf.BA[:, dc * SEQ_PER + j * P: dc * SEQ_PER + (j + 1) * P],
                      "LNT", "BA")
        proj_fm_norm(nc, cx, ap, bf, b, "c_Wkt", "g_kt")
        for j in range(NPB):
            for dh in range(2):
                for dc in range(DCH):
                    cx.op("sync", lambda s, dc=dc, dh=dh: s.dma_start(
                        out=bf.WTA[:, 0:512],
                        in_=ap["c_Wvt"][dc * P:(dc + 1) * P,
                                        dh * 512:(dh + 1) * 512]),
                        dma=True, writes=["WTA"])
                    cx.op("tensor", lambda t, dc=dc, j=j: t.matmul(
                        bf.PSW[:, 0:512],
                        bf.BA[:, dc * SEQ_PER + j * P: dc * SEQ_PER + (j + 1) * P],
                        bf.WTA[:, 0:512],
                        start=(dc == 0), stop=(dc == DCH - 1)),
                        reads=["BA", "WTA"], writes=["psw"])
                cx.op("vector", lambda v, dh=dh: v.tensor_copy(
                    out=bf.LSCR[:, dh * 512:(dh + 1) * 512], in_=bf.PSW[:, 0:512]),
                    reads=["psw"], writes=["LSCR"])
            cx.op("sync", lambda s, b=b, j=j: s.dma_start(
                out=ap["g_v"][b * SEQ_PER + j * P: b * SEQ_PER + (j + 1) * P, :],
                in_=bf.LSCR[:, :]),
                dma=True, reads=["LSCR"], writes=["g_v"])
        proj_fm_norm(nc, cx, ap, bf, b, "c_Wqt", None)
        for dc in range(DCH):
            cx.op("sync", lambda s, dc=dc, b=b: s.dma_start(
                out=ap["s_qt"][dc, :, :] if b == 1 else
                ap["s_xnt"][0, dc, :, 0:SEQ_PER],
                in_=bf.BB[:, dc * SEQ_PER:(dc + 1) * SEQ_PER]),
                dma=True, reads=["BB"],
                writes=["s_qt" if b == 1 else "s_xnt"])
        cx.barrier()
    cx.op("gpsimd", lambda g: g.collective_compute(
        "AllGather", ALU.bypass, replica_groups=[list(range(NC))],
        ins=[ap["g_kt"].opt()], outs=[ap["g_kt_all"].opt()]),
        coll=True, reads=["g_kt"], writes=["g_kt_all"])
    cx.op("gpsimd", lambda g: g.collective_compute(
        "AllGather", ALU.bypass, replica_groups=[list(range(NC))],
        ins=[ap["g_v"].opt()], outs=[ap["g_v_all"].opt()]),
        coll=True, reads=["g_v"], writes=["g_v_all"])
    NKC = SEQ // P
    LCH = SEQ_PER // P
    for b in range(B):
        src = "s_xnt" if b == 0 else "s_qt"
        for dc in range(DCH):
            cx.op("sync", lambda s, dc=dc, src=src: s.dma_start(
                out=bf.BA[:, dc * SEQ_PER:(dc + 1) * SEQ_PER],
                in_=(ap["s_xnt"][0, dc, :, 0:SEQ_PER] if src == "s_xnt" else
                     ap["s_qt"][dc, :, :])),
                dma=True, reads=[src], writes=["BA"])
        for buf in range(2):
            vk_ = "VRKa" if buf == 0 else "VRKb"
            for lc in range(LCH):
                cx.op("vector", lambda v, buf=buf, lc=lc: v.tensor_copy(
                    out=bf.VRK[:, buf * 520 + lc * 65 + DHEAD:
                               buf * 520 + lc * 65 + DHEAD + 1],
                    in_=bf.ONESC),
                    reads=["SMALL"], writes=[vk_])
        for h in range(NHEAD):
            oc, pb = h // 2, (h % 2) * DHEAD

            def load_rank(r, b=b, h=h, oc=oc, pb=pb):
                buf = r % 2
                kk_ = "KRKa" if buf == 0 else "KRKb"
                vk_ = "VRKa" if buf == 0 else "VRKb"
                cx.op("sync", lambda s, r=r, buf=buf: s.dma_start(
                    out=bf.KRK[0:DHEAD, buf * SEQ_PER:(buf + 1) * SEQ_PER],
                    in_=ap["g_kt_all"][r, oc * P + pb: oc * P + pb + DHEAD,
                                       b * SEQ_PER:(b + 1) * SEQ_PER]),
                    dma=True, reads=["g_kt_all"], writes=[kk_])
                vdst = bf.VRK[:, buf * 520: buf * 520 + LCH * 65].rearrange(
                    "p (c d) -> p c d", d=65)[:, :, 0:DHEAD]
                cx.op("sync", lambda s, r=r, vdst=vdst: s.dma_start(
                    out=vdst,
                    in_=ap["g_v_all"][r, b * SEQ_PER:(b + 1) * SEQ_PER,
                                      h * DHEAD:(h + 1) * DHEAD].rearrange(
                        "(c p) d -> p c d", p=P)),
                    dma=True, reads=["g_v_all"], writes=[vk_])
            if pb:
                cx.op("sync", lambda s, oc=oc: s.dma_start(
                    out=bf.SCR3[0:DHEAD, 0:SEQ_PER],
                    in_=bf.BA[DHEAD:P, oc * SEQ_PER:(oc + 1) * SEQ_PER]),
                    dma=True, reads=["BA"], writes=["SCR3"])
            load_rank(0)
            load_rank(1)
            next_load = 2
            for kc in range(NKC):
                r, lc = kc // LCH, kc % LCH
                buf = r % 2
                g = kc % CG
                kk_ = "KRKa" if buf == 0 else "KRKb"
                QW = min(512, SEQ_PER)
                for qh in range(SEQ_PER // QW):
                    if pb:
                        rhs = bf.SCR3[0:DHEAD, qh * QW:(qh + 1) * QW]
                        rk = "SCR3"
                    else:
                        rhs = bf.BA[0:DHEAD, oc * SEQ_PER + qh * QW:
                                    oc * SEQ_PER + (qh + 1) * QW]
                        rk = "BA"
                    cx.op("tensor", lambda t, qh=qh, QW=QW, buf=buf, lc=lc,
                          rhs=rhs: t.matmul(
                        bf.PSW[:, qh * QW:(qh + 1) * QW],
                        bf.KRK[0:DHEAD, buf * SEQ_PER + lc * P:
                               buf * SEQ_PER + (lc + 1) * P],
                        rhs, start=True, stop=True),
                        reads=[kk_, rk], writes=["psw"])
                cx.op("vector", lambda v, g=g: v.tensor_scalar(
                    out=bf.CB[:, g * 1024:g * 1024 + SEQ_PER],
                    in0=bf.PSW[:, 0:SEQ_PER], scalar1=0.999, scalar2=-0.999,
                    op0=ALU.min, op1=ALU.max),
                    reads=["psw"], writes=["CB"])
                cx.op("vector", lambda v, g=g: v.tensor_mul(
                    out=bf.CL[:, g * 1024:g * 1024 + SEQ_PER],
                    in0=bf.CB[:, g * 1024:g * 1024 + SEQ_PER],
                    in1=bf.CB[:, g * 1024:g * 1024 + SEQ_PER]),
                    reads=["CB"], writes=["CL"])
                if g == CG - 1 or kc == NKC - 1:
                    gs = range(g + 1)
                    base = kc - g
                    for gg in gs:
                        cx.op("scalar", lambda s, gg=gg: s.activation(
                            out=bf.CL[:, gg * 1024:gg * 1024 + SEQ_PER],
                            in_=bf.CL[:, gg * 1024:gg * 1024 + SEQ_PER],
                            func=AF.Ln, scale=-1.0, bias=1.0),
                            reads=["CL"], writes=["CL"])
                    for gg in gs:
                        cx.op("scalar", lambda s, gg=gg: s.activation(
                            out=bf.CL[:, gg * 1024:gg * 1024 + SEQ_PER],
                            in_=bf.CL[:, gg * 1024:gg * 1024 + SEQ_PER],
                            func=AF.Sigmoid, scale=-0.5),
                            reads=["CL"], writes=["CL"])
                        cx.op("vector", lambda v, gg=gg: v.tensor_mul(
                            out=bf.CB[:, gg * 1024:gg * 1024 + SEQ_PER],
                            in0=bf.CB[:, gg * 1024:gg * 1024 + SEQ_PER],
                            in1=bf.CL[:, gg * 1024:gg * 1024 + SEQ_PER]),
                            reads=["CB", "CL"], writes=["CB"])
                        cx.op("scalar", lambda s, gg=gg: s.activation(
                            out=bf.CB[:, gg * 1024:gg * 1024 + SEQ_PER],
                            in_=bf.CB[:, gg * 1024:gg * 1024 + SEQ_PER],
                            func=AF.Arctan),
                            reads=["CB"], writes=["CB"])
                    for gg in gs:
                        cx.op("scalar", lambda s, gg=gg: s.activation(
                            out=bf.CB[:, gg * 1024:gg * 1024 + SEQ_PER],
                            in_=bf.CB[:, gg * 1024:gg * 1024 + SEQ_PER],
                            func=AF.Ln, scale=TWOPI, bias=bf.HALFC),
                            reads=["CB", "SMALL"], writes=["CB"])
                        cx.op("scalar", lambda s, gg=gg: s.activation(
                            out=bf.CB[:, gg * 1024:gg * 1024 + SEQ_PER],
                            in_=bf.CB[:, gg * 1024:gg * 1024 + SEQ_PER],
                            func=AF.Exp, scale=16.0),
                            reads=["CB"], writes=["CB"])
                    for gg in gs:
                        kk = base + gg
                        rr, ll = kk // LCH, kk % LCH
                        vbuf = rr % 2
                        vk_ = "VRKa" if vbuf == 0 else "VRKb"
                        v0 = vbuf * 520 + ll * 65
                        QW = min(512, SEQ_PER)
                        for qh in range(SEQ_PER // QW):
                            ps = bf.PS2 if qh == 0 else bf.PS3
                            pk = "ps2" if qh == 0 else "ps3"
                            cx.op("tensor", lambda t, gg=gg, kk=kk, qh=qh, ps=ps,
                                  v0=v0, QW=QW: t.matmul(
                                      ps[0:DHEAD + 1, 0:QW],
                                      bf.VRK[:, v0:v0 + DHEAD + 1],
                                      bf.CB[:, gg * 1024 + qh * QW:
                                            gg * 1024 + (qh + 1) * QW],
                                      start=(kk == 0), stop=(kk == NKC - 1)),
                                  reads=["CB", vk_], writes=[pk])
                    while (next_load < NC and
                           (next_load - 2) * LCH + LCH - 1 <= kc):
                        load_rank(next_load)
                        next_load += 1
            finish_head(nc, cx, ap, bf, b, h)
        wo_residual(nc, cx, ap, bf, b)
        cx.barrier()


def proj_fm_norm(nc, cx, ap, bf, b, wname, store):
    QW = min(512, SEQ_PER)
    NQH = SEQ_PER // QW
    for oc in range(DCH):
        cx.op("sync", lambda s, oc=oc: s.dma_start(
            out=bf.WTA[:, :], in_=ap[wname][oc]),
            dma=True, writes=["WTA"])
        cx.op("sync", lambda s, oc=oc: s.dma_start(
            out=bf.WTB[:, 0:NHEAD], in_=ap["ehead"][oc, :, :]),
            dma=True, writes=["WTB"])
        for qh in range(NQH):
            builds = [(lambda t, dc=dc, oc=oc, qh=qh: t.matmul(
                bf.PSW[:, 0:QW], bf.WTA[:, dc * P:(dc + 1) * P],
                bf.BA[:, dc * SEQ_PER + qh * QW: dc * SEQ_PER + qh * QW + QW],
                start=(dc == 0), stop=(dc == DCH - 1))) for dc in range(DCH)]
            cx.group("tensor", builds, reads=["WTA", "BA"], writes=["psw"])
            cx.op("vector", lambda v, oc=oc, qh=qh: v.tensor_copy(
                out=bf.BB[:, oc * SEQ_PER + qh * QW: oc * SEQ_PER + qh * QW + QW],
                in_=bf.PSW[:, 0:QW]),
                reads=["psw"], writes=["BB"])
            cx.op("vector", lambda v, oc=oc, qh=qh: v.tensor_mul(
                out=bf.LSCR[:, 0:QW],
                in0=bf.BB[:, oc * SEQ_PER + qh * QW: oc * SEQ_PER + qh * QW + QW],
                in1=bf.BB[:, oc * SEQ_PER + qh * QW: oc * SEQ_PER + qh * QW + QW]),
                reads=["BB"], writes=["LSCR"])
            ps = bf.PS0 if qh == 0 else bf.PS1
            pk = "ps0" if qh == 0 else "ps1"
            cx.op("tensor", lambda t, oc=oc, ps=ps: t.matmul(
                ps[0:NHEAD, 0:QW], bf.WTB[:, 0:NHEAD], bf.LSCR[:, 0:QW],
                start=(oc == 0), stop=(oc == DCH - 1)),
                reads=["WTB", "LSCR"], writes=[pk])
    for qh in range(NQH):
        ps = bf.PS0 if qh == 0 else bf.PS1
        pk = "ps0" if qh == 0 else "ps1"
        cx.op("scalar", lambda s, ps=ps, qh=qh: s.activation(
            out=bf.SCR2[0:NHEAD, qh * QW:(qh + 1) * QW], in_=ps[0:NHEAD, 0:QW],
            func=AF.Sqrt), reads=[pk], writes=["SCR2"])
    cx.op("vector", lambda v: v.tensor_scalar(
        out=bf.SCR2[0:NHEAD, 0:SEQ_PER], in0=bf.SCR2[0:NHEAD, 0:SEQ_PER],
        scalar1=1e-6, scalar2=None, op0=ALU.max),
        reads=["SCR2"], writes=["SCR2"])
    cx.op("vector", lambda v: v.reciprocal(out=bf.SCR2[0:NHEAD, 0:SEQ_PER],
                                           in_=bf.SCR2[0:NHEAD, 0:SEQ_PER]),
          reads=["SCR2"], writes=["SCR2"])
    for oc in range(DCH):
        cx.op("sync", lambda s, oc=oc: s.dma_start(
            out=bf.WTB[0:NHEAD, 0:P], in_=ap["eheadt"][oc, :, :]),
            dma=True, writes=["WTB"])
        for qh in range(NQH):
            cx.op("tensor", lambda t, oc=oc, qh=qh: t.matmul(
                bf.PSW[:, 0:QW], bf.WTB[0:NHEAD, 0:P],
                bf.SCR2[0:NHEAD, qh * QW:(qh + 1) * QW], start=True, stop=True),
                reads=["WTB", "SCR2"], writes=["psw"])
            cx.op("vector", lambda v, oc=oc, qh=qh: v.tensor_mul(
                out=bf.BB[:, oc * SEQ_PER + qh * QW: oc * SEQ_PER + qh * QW + QW],
                in0=bf.BB[:, oc * SEQ_PER + qh * QW: oc * SEQ_PER + qh * QW + QW],
                in1=bf.PSW[:, 0:QW]),
                reads=["BB", "psw"], writes=["BB"])
    if store is not None:
        for dc in range(DCH):
            cx.op("sync", lambda s, dc=dc, b=b: s.dma_start(
                out=ap[store][dc * P:(dc + 1) * P, b * SEQ_PER:(b + 1) * SEQ_PER],
                in_=bf.BB[:, dc * SEQ_PER:(dc + 1) * SEQ_PER]),
                dma=True, reads=["BB"], writes=[store])


def finish_head(nc, cx, ap, bf, b, h):
    oc, pb = h // 2, (h % 2) * DHEAD
    QW = min(512, SEQ_PER)
    NQH = SEQ_PER // QW
    for qh in range(NQH):
        ps = bf.PS2 if qh == 0 else bf.PS3
        pk = "ps2" if qh == 0 else "ps3"
        cx.op("vector", lambda v, ps=ps, qh=qh, pk=pk: v.tensor_scalar(
            out=bf.LSCR[0:1, qh * QW:(qh + 1) * QW],
            in0=ps[DHEAD:DHEAD + 1, 0:QW], scalar1=1e-6, scalar2=None,
            op0=ALU.add), reads=[pk], writes=["LSCR"])
    cx.op("vector", lambda v: v.reciprocal(out=bf.LSCR[0:1, 0:SEQ_PER],
                                           in_=bf.LSCR[0:1, 0:SEQ_PER]),
          reads=["LSCR"], writes=["LSCR"])
    for qh in range(NQH):
        ps = bf.PS2 if qh == 0 else bf.PS3
        pk = "ps2" if qh == 0 else "ps3"
        cx.op("tensor", lambda t, qh=qh: t.matmul(
            bf.PT[0:DHEAD, 0:QW], bf.ONESR[0:1, 0:DHEAD],
            bf.LSCR[0:1, qh * QW:(qh + 1) * QW], start=True, stop=True),
            reads=["LSCR", "ONESR"], writes=["pt"])
        cx.op("vector", lambda v, qh=qh: v.tensor_copy(
            out=bf.SCR2[0:DHEAD, 0:QW], in_=bf.PT[0:DHEAD, 0:QW]),
            reads=["pt"], writes=["SCR2"])
        cx.op("vector", lambda v, ps=ps, qh=qh, oc=oc, pb=pb, pk=pk: v.tensor_mul(
            out=bf.BB[pb:pb + DHEAD,
                      oc * SEQ_PER + qh * QW: oc * SEQ_PER + qh * QW + QW],
            in0=ps[0:DHEAD, 0:QW], in1=bf.SCR2[0:DHEAD, 0:QW]),
            reads=[pk, "SCR2"], writes=["BB"])


def wo_residual(nc, cx, ap, bf, b):
    cx.op("sync", lambda s: s.dma_start(out=bf.CL[:, 0:D], in_=ap["c_bo"]),
          dma=True, writes=["CL"])
    for oc in range(DCH):
        cx.op("sync", lambda s, oc=oc: s.dma_start(
            out=bf.BA[:, oc * D:(oc + 1) * D], in_=ap["c_Wot"][oc]),
            dma=True, writes=["BA"])
    for j in range(NPB):
        tj = b * NPB + j
        for dh in range(2):
            for oc in range(DCH):
                cx.op("tensor", lambda t, oc=oc, j=j, dh=dh: t.matmul(
                    bf.PSW[:, 0:512],
                    bf.BB[:, oc * SEQ_PER + j * P: oc * SEQ_PER + (j + 1) * P],
                    bf.BA[:, oc * D + dh * 512:oc * D + dh * 512 + 512],
                    start=(oc == 0), stop=(oc == DCH - 1)),
                    reads=["BB", "BA"], writes=["psw"])
            cx.op("vector", lambda v, dh=dh: v.tensor_add(
                out=bf.SCR2[:, 0:512], in0=bf.PSW[:, 0:512],
                in1=bf.CL[:, dh * 512:(dh + 1) * 512]),
                reads=["psw", "CL"], writes=["SCR2"])
            cx.op("vector", lambda v, tj=tj, dh=dh: v.tensor_add(
                out=bf.xtile(tj)[:, dh * 512:(dh + 1) * 512],
                in0=bf.xtile(tj)[:, dh * 512:(dh + 1) * 512],
                in1=bf.SCR2[:, 0:512]),
                reads=["SCR2", "X"], writes=["X"])


def _slab(wt):
    DCH_ = wt.shape[0] // 128
    w4 = wt.reshape(DCH_, 128, DCH_, 128)
    return np.ascontiguousarray(w4.transpose(2, 1, 0, 3).reshape(
        DCH_, 128, wt.shape[1]))


def host_inputs(full, seq=None):
    if seq is None:
        seq = full["L0_attn_a_x"].shape[1]
    set_dims(seq)
    x = np.ascontiguousarray(full["L0_attn_a_x"], dtype=np.float32)
    qstart = seq - QBLK
    rows = np.arange(QBLK) + qstart
    thr = rows - (seq - CTX)
    masks = np.where(np.arange(N_SINKS)[None, :] <= thr[:, None], 0.0, NEG)
    maskw = np.where((np.arange(WINDOW)[None, :] + N_SINKS) <= thr[:, None],
                     0.0, NEG)
    addmask = (thr >= 0).astype(np.float32)[:, None]
    eh = np.zeros((DCH, P, NHEAD), np.float32)
    eht = np.zeros((DCH, NHEAD, P), np.float32)
    for oc in range(DCH):
        for pp in range(P):
            hh = (oc * P + pp) // DHEAD
            eh[oc, pp, hh] = 1.0
            eht[oc, hh, pp] = 1.0
    base = {
        "aa_blk": x[:, qstart:, :],
        "aa_sink": x[:, 0:N_SINKS, :],
        "aa_Wq": full["L0_attn_a_Wq"], "aa_Wk": full["L0_attn_a_Wk"],
        "aa_Wv": full["L0_attn_a_Wv"], "aa_Wo": full["L0_attn_a_Wo"],
        "aa_maskw": maskw, "aa_masks": masks,
        "f0_w1t": full["L0_ffn_w1"].T, "f0_w1": full["L0_ffn_w1"],
        "f0_b1": full["L0_ffn_b1"].reshape(HCH, P).T,
        "f0_w2t": full["L0_ffn_w2"].T, "f0_w2": full["L0_ffn_w2"],
        "f0_b2": np.tile(full["L0_ffn_b2"][None, :], (P, 1)),
        "t_Wbt": full["L1_attn_a_Wb"].T,
        "t_Wpt": full["L1_attn_a_Wp"].T, "t_Wp": full["L1_attn_a_Wp"],
        "t_iA": full["L1_attn_a_init_A"],
        "t_iB_t": full["L1_attn_a_init_B"].T,
        "t_gamma": np.tile(full["L1_attn_a_ln_gamma"][None, :], (P, 1)),
        "t_beta": np.tile(full["L1_attn_a_ln_beta"][None, :], (P, 1)),
        "q_w1": full["L1_ffn_w1"],
        "q_b1": full["L1_ffn_b1"].reshape(HCH, P).T,
        "q_w2": full["L1_ffn_w2"],
        "q_b2": full["L1_ffn_b2"].reshape(DCH, P).T,
        "c_Wqt": _slab(full["L1_attn_b_Wq"].T),
        "c_Wkt": _slab(full["L1_attn_b_Wk"].T),
        "c_Wvt": full["L1_attn_b_Wv"].T,
        "c_Wot": np.ascontiguousarray(
            full["L1_attn_b_Wo"].T).reshape(DCH, P, D),
        "c_bo": np.tile(full["L1_attn_b_bo"][None, :], (P, 1)),
        "ehead": eh, "eheadt": eht,
        "ident": np.eye(P, dtype=np.float32),
        "ones_col": np.ones((P, 1), np.float32),
        "ones_row": np.ones((1, P), np.float32),
        "consts": np.tile(np.array([[LN_EPS, 0.5]], np.float32), (P, 1)),
    }
    base = {k: np.ascontiguousarray(v, dtype=np.float32) for k, v in base.items()}
    maps = []
    for i in range(NC):
        m = dict(base)
        m["x0"] = np.ascontiguousarray(x[:, i * SEQ_PER:(i + 1) * SEQ_PER, :])
        own = ((rows >= i * SEQ_PER) & (rows < (i + 1) * SEQ_PER))
        m["aa_addmask"] = np.ascontiguousarray(
            addmask * own[:, None].astype(np.float32), dtype=np.float32)
        oh = np.zeros((NC, 1), np.float32)
        if i + 1 < NC:
            oh[i + 1, 0] = 1.0
        m["onehot_next"] = oh
        pw = np.zeros((P, NC), np.float32)
        pw[:, :i] = 1.0
        m["prefw_rep"] = pw
        maps.append(m)
    return maps


def kernel(**inputs):
    maps = host_inputs(inputs, seq=inputs["L0_attn_a_x"].shape[1])
    nc = build_program()
    res = run_bass_kernel_spmd(nc, maps, core_ids=list(range(NC)))
    y = np.concatenate([r["y"] for r in res.results], axis=1)
    return np.ascontiguousarray(y, dtype=np.float32)


# version 05
# speedup vs baseline: 1.1678x; 1.0105x over previous
import numpy as np
import concourse.bass as bass
import concourse.mybir as mybir
from concourse.bass_utils import run_bass_kernel_spmd

F32 = mybir.dt.float32
F16 = mybir.dt.float16
BF16 = mybir.dt.bfloat16
AF = mybir.ActivationFunctionType
ALU = mybir.AluOpType
AXL = mybir.AxisListType

NC = 8
B = 2
D = 1024
H_FFN = 4096
NHEAD = 16
DHEAD = 64
CHUNK = 256
LORA_R = 16
N_SINKS = 4
WINDOW = 512
CTX = N_SINKS + WINDOW
QBLK = 640
NQT = QBLK // 128
LN_EPS = 1e-5
TTT_LR = 0.02
SCALING = 2.0
NEG = -1e30
TWOPI = float(2.0 / np.pi)
C23 = 8388608.0

P = 128
DCH = D // P
HCH = H_FFN // P
CG = 4

SEQ = SEQ_PER = TOK = NPT = NPB = NCHB = None


def set_dims(seq):
    global SEQ, SEQ_PER, TOK, NPT, NPB, NCHB
    SEQ = seq
    SEQ_PER = seq // NC
    TOK = 2 * SEQ_PER
    NPT = TOK // P
    NPB = NPT // 2
    NCHB = SEQ_PER // CHUNK


set_dims(8192)


class Ctx:
    ENGS = ("vector", "scalar", "tensor", "gpsimd", "sync")

    def __init__(self, nc, block):
        self.nc = nc
        self.block = block
        self.sems = {}
        self.cnt = {}
        self.waited = {e: {} for e in self.ENGS}
        self.lastw = {}
        self.readers = {}
        for e in self.ENGS:
            self._mk("S_" + e)
        self.NDL = 8
        for i in range(self.NDL):
            self._mk(f"S_dma{i}")
        self._mk("S_dmag")
        self._mk("S_coll")
        self._lane = 0

    def _mk(self, name):
        self.sems[name] = self.nc.alloc_semaphore(name)
        self.cnt[name] = 0

    def _need(self, eng, reads, writes):
        need = {}

        def add(s, c):
            if need.get(s, 0) < c:
                need[s] = c
        for r in reads:
            lw = self.lastw.get(r)
            if lw:
                add(*lw)
        for r in writes:
            lw = self.lastw.get(r)
            if lw:
                add(*lw)
            for s, c in self.readers.get(r, {}).items():
                add(s, c)
        return [(s, c) for s, c in need.items() if self.waited[eng].get(s, 0) < c]

    def _emit(self, eng, builds, reads, writes, sem, inc):
        waits = self._need(eng, reads, writes)
        self.cnt[sem] += inc
        tgt = self.cnt[sem]
        semh, sems = self.sems[sem], self.sems

        def body(e):
            for s, c in waits:
                e.wait_ge(sems[s], c)
            last = None
            for b in builds:
                last = b(e)
            last.then_inc(semh, inc)
        getattr(self.block, eng)(body)
        w = self.waited[eng]
        for s, c in waits:
            if w.get(s, 0) < c:
                w[s] = c

        for r in writes:
            self.lastw[r] = (sem, tgt)
            self.readers.pop(r, None)
        for r in reads:
            d = self.readers.setdefault(r, {})
            if d.get(sem, 0) < tgt:
                d[sem] = tgt

    def op(self, eng, build, reads=(), writes=(), dma=False, coll=False):
        if coll:
            sem, inc = "S_coll", 1
        elif dma:
            if eng == "gpsimd":
                sem, inc = "S_dmag", 16
            else:
                sem = f"S_dma{self._lane}"
                self._lane = (self._lane + 1) % self.NDL
                inc = 16
        else:
            sem, inc = "S_" + eng, 1
        if inc == 16 or coll:
            prev = self.cnt[sem]
            if prev > 0 and self.waited[eng].get(sem, 0) < prev:
                semh = self.sems[sem]

                def b2(e, _b=build, _s=semh, _p=prev):
                    e.wait_ge(_s, _p)
                    return _b(e)
                build = b2
                self.waited[eng][sem] = prev
        self._emit(eng, [build], reads, writes, sem, inc)

    def group(self, eng, builds, reads=(), writes=()):
        self._emit(eng, builds, reads, writes, "S_" + eng, 1)

    def barrier(self):
        snap = [(s, c) for s, c in self.cnt.items() if c > 0]
        sems = self.sems
        for eng in self.ENGS:
            todo = [(s, c) for s, c in snap if self.waited[eng].get(s, 0) < c]
            if not todo:
                continue

            def body(e, todo=tuple(todo)):
                for s, c in todo:
                    e.wait_ge(sems[s], c)
            getattr(self.block, eng)(body)
            for s, c in todo:
                self.waited[eng][s] = c


class NS:
    pass


def ln_tile(cx, bf, x_ap, out_ap, xkey, okey, gamma=None, beta=None, gkey=None,
            nrow=P, F=D):
    inv_n = 1.0 / F
    st = bf.STAT
    cx.op("vector", lambda v: v.reduce_sum(st[:nrow, 0:1], x_ap, axis=AXL.X),
          reads=[xkey], writes=["STAT"])
    cx.op("vector", lambda v: v.tensor_scalar(out=st[:nrow, 2:3], in0=st[:nrow, 0:1],
                                              scalar1=inv_n, scalar2=None, op0=ALU.mult),
          reads=["STAT"], writes=["STAT"])
    cx.op("vector", lambda v: v.scalar_tensor_tensor(
        out=bf.LSCR[:nrow, :F], in0=x_ap, scalar=st[:nrow, 2:3], in1=x_ap,
        op0=ALU.subtract, op1=ALU.mult, accum_out=st[:nrow, 1:2]),
        reads=[xkey, "STAT"], writes=["STAT", "LSCR"])
    cx.op("scalar", lambda s: s.activation(out=st[:nrow, 3:4], in_=st[:nrow, 1:2],
                                           func=AF.Sqrt, scale=inv_n,
                                           bias=bf.EPSC[:nrow, :]),
          reads=["STAT", "SMALL"], writes=["STAT"])
    cx.op("vector", lambda v: v.reciprocal(out=st[:nrow, 3:4], in_=st[:nrow, 3:4]),
          reads=["STAT"], writes=["STAT"])
    cx.op("vector", lambda v: v.tensor_scalar(out=out_ap, in0=x_ap,
                                              scalar1=st[:nrow, 2:3],
                                              scalar2=st[:nrow, 3:4],
                                              op0=ALU.subtract, op1=ALU.mult),
          reads=[xkey, "STAT"], writes=[okey])
    if gamma is not None:
        cx.op("vector", lambda v: v.tensor_mul(out=out_ap, in0=out_ap, in1=gamma),
              reads=[okey, gkey], writes=[okey])
        cx.op("vector", lambda v: v.tensor_add(out=out_ap, in0=out_ap, in1=beta),
              reads=[okey, gkey], writes=[okey])


def tr128(cx, bf, src_ap, dst_ap, skey, dkey, nout=None):
    npart = src_ap.shape[0]
    ncols = src_ap.shape[-1]
    cx.op("tensor", lambda t: t.transpose(bf.PT[:ncols, :npart], src_ap,
                                          bf.IDENT[:npart, :npart]),
          reads=[skey, "IDENT"], writes=["pt"])
    cx.op("vector", lambda v: v.tensor_copy(out=dst_ap, in_=bf.PT[:ncols, :npart]),
          reads=["pt"], writes=[dkey])


def bcast_scalar(cx, bf, src11, dst_col, skey, dkey):
    cx.op("tensor", lambda t: t.matmul(bf.PT[:, 0:1], bf.ONESR[0:1, :], src11,
                                       start=True, stop=True),
          reads=[skey, "ONESR"], writes=["pt"])
    cx.op("vector", lambda v: v.tensor_copy(out=dst_col, in_=bf.PT[:, 0:1]),
          reads=["pt"], writes=[dkey])


def rep_row_tile(cx, bf, src_col, dst_tile, skey, dkey):
    cx.op("tensor", lambda t: t.transpose(bf.PT[0:1, 0:P], src_col,
                                          bf.IDENT[:, :]),
          reads=[skey, "IDENT"], writes=["pt"])
    cx.op("vector", lambda v: v.tensor_copy(out=bf.LSCR[0:1, 0:P], in_=bf.PT[0:1, :P]),
          reads=["pt"], writes=["LSCR"])
    cx.op("tensor", lambda t: t.matmul(bf.PT[:, 0:P], bf.ONESR[0:1, :],
                                       bf.LSCR[0:1, 0:P], start=True, stop=True),
          reads=["LSCR", "ONESR"], writes=["pt"])
    cx.op("vector", lambda v: v.tensor_copy(out=dst_tile, in_=bf.PT[:, 0:P]),
          reads=["pt"], writes=[dkey])


def build_program():
    nc = bass.Bass("TRN2", target_bir_lowering=False, debug=False)
    ap = {}

    def din(name, shape, dt=F32):
        ap[name] = nc.dram_tensor(name, shape, dt, kind="ExternalInput").ap()

    def dout(name, shape, dt=F32):
        ap[name] = nc.dram_tensor(name, shape, dt, kind="ExternalOutput").ap()

    def dscr(name, shape, dt=F32):
        ap[name] = nc.dram_tensor(name, shape, dt).ap()

    din("x0", [B, SEQ_PER, D])
    dout("y", [B, SEQ_PER, D])
    din("aa_blk", [B, QBLK, D])
    din("aa_sink", [B, N_SINKS, D])
    din("aa_Wq", [D, D])
    din("aa_Wk", [D, D])
    din("aa_Wv", [D, D])
    din("aa_Wo", [D, D])
    din("aa_maskw", [QBLK, WINDOW])
    din("aa_masks", [QBLK, N_SINKS])
    din("aa_addmask", [QBLK, 1])
    din("f0_w1t", [D, H_FFN])
    din("f0_w1", [H_FFN, D])
    din("f0_b1", [P, HCH])
    din("f0_w2t", [H_FFN, D])
    din("f0_w2", [D, H_FFN])
    din("f0_b2", [P, D])
    din("t_Wbt", [D, D])
    din("t_Wpt", [D, D])
    din("t_Wp", [D, D])
    din("t_iA", [D, LORA_R])
    din("t_iB_t", [D, LORA_R])
    din("t_gamma", [P, D])
    din("t_beta", [P, D])
    din("onehot_next", [NC, 1])
    din("prefw_rep", [P, NC])
    din("q_w1", [H_FFN, D])
    din("q_b1", [P, HCH])
    din("q_w2", [D, H_FFN])
    din("q_b2", [P, DCH])
    din("c_Wqt", [DCH, P, D])
    din("c_Wkt", [DCH, P, D])
    din("c_Wvt", [D, D])
    din("c_Wot", [DCH, P, D])
    din("c_bo", [P, D])
    din("ehead", [DCH, P, NHEAD])
    din("eheadt", [DCH, NHEAD, P])
    din("ident", [P, P])
    din("ones_col", [P, 1])
    din("ones_row", [1, P])
    din("consts", [P, 2])

    dscr("s_w1sl", [HCH, P, D])
    dscr("s_w2s", [H_FFN, D])
    dscr("s_q1ctl", [HCH, P, D], BF16)
    dscr("s_q2ct", [HCH, P, D], BF16)
    dscr("s_xt", [B, DCH, P, SEQ_PER])
    dscr("s_xnt", [B, DCH, P, SEQ_PER + 1])
    dscr("s_qt", [DCH, P, SEQ_PER])
    dscr("s_vtok", [NQT, P, D])
    dscr("s_xc", [SEQ_PER, D], BF16)
    dscr("s_hl", [HCH, P, TOK])
    dscr("s_ln", [B * NPB, P, D])
    dscr("s_da", [B, NCHB, P, P])
    dscr("s_db", [B, NCHB, P, P])
    dscr("g_first_in", [B, D])
    dscr("g_first_out", [NC, B, D])
    dscr("g_sums_in", [P, 2 * 2 * P])
    dscr("g_sums_out", [NC, P, 2 * 2 * P])
    dscr("g_max_in0", [1, 1])
    dscr("g_max_out0", [NC, 1])
    dscr("g_max_in1", [1, 1])
    dscr("g_max_out1", [NC, 1])
    dscr("g_kt", [D, TOK])
    dscr("g_kt_all", [NC, D, TOK])
    dscr("g_v", [TOK, D])
    dscr("g_v_all", [NC, TOK, D])

    bf = NS()
    import contextlib
    with contextlib.ExitStack() as _stack:
        def _sb(name, shape, dt=F32):
            return _stack.enter_context(nc.sbuf_tensor(name, shape, dt))

        def _psum(name, shape):
            return _stack.enter_context(nc.psum_tensor(name, shape, F32))
        _X = _sb("X", [P, NPT * D])
        _BA = _sb("BA", [P, 8192])
        _BB = _sb("BB", [P, 8192])
        _CB = _sb("CB", [P, 4224])
        _CL = _sb("CL", [P, 4224])
        _LNT = _sb("LNT", [P, D])
        _LSCR = _sb("LSCR", [P, D])
        _SCR = _sb("SCR", [P, 1056])
        _SCR2 = _sb("SCR2", [P, 1056])
        _SCR3 = _sb("SCR3", [P, 1056])
        _WTA = _sb("WTA", [P, 1024])
        _WTB = _sb("WTB", [P, 1024])
        _SMALL = _sb("SMALL", [P, 1152])
        _STAT = _sb("STAT", [P, 16])
        _IDENT = _sb("IDENT", [P, P])
        _ONESR = _sb("ONESR", [1, P])
        _KRK = _sb("KRK", [64, 2048])
        _VRK = _sb("VRK", [P, 1040])
        _PS0 = _psum("PS0", [P, 512])
        _PS1 = _psum("PS1", [P, 512])
        _PS2 = _psum("PS2", [P, 512])
        _PS3 = _psum("PS3", [P, 512])
        _PSW = _psum("PSW", [P, 1024])
        _PT = _psum("PT", [P, 512])
        block = _stack.enter_context(nc.Block())
        bf.X, bf.BA, bf.BB, bf.CB, bf.CL = _X, _BA, _BB, _CB, _CL
        bf.LNT, bf.LSCR, bf.SCR, bf.SCR2, bf.SCR3 = _LNT, _LSCR, _SCR, _SCR2, _SCR3
        bf.WTA, bf.WTB, bf.SMALL, bf.STAT = _WTA, _WTB, _SMALL, _STAT
        bf.IDENT, bf.ONESR = _IDENT, _ONESR
        bf.KRK, bf.VRK = _KRK, _VRK
        bf.F16T = _SCR2[:, 0:512].bitcast(F16)
        bf.PS0, bf.PS1, bf.PS2, bf.PS3, bf.PSW, bf.PT = _PS0, _PS1, _PS2, _PS3, _PSW, _PT
        bf.A1 = _SMALL[:, 0:32]
        bf.A2 = _SMALL[:, 32:40]
        bf.SW1 = _SMALL[:, 40:72]
        bf.SW1R = _SMALL[:, 72:104]
        bf.SW2 = _SMALL[:, 104:112]
        bf.SW2R = _SMALL[:, 112:120]
        bf.IBT = _SMALL[:, 128:256]
        bf.IA = _SMALL[:, 256:384]
        bf.G = _SMALL[:, 384:512]
        bf.PIN = _SMALL[:, 512:640]
        bf.PERR = _SMALL[:, 640:768]
        bf.BQ1 = _SMALL[:, 768:800]
        bf.SA1 = _SMALL[:, 800:832]
        bf.SA1R = _SMALL[:, 832:864]
        bf.PREFW = _SMALL[:, 1024:1024 + NC]
        bf.ONESC = _SMALL[:, 1040:1041]
        bf.EPSC = _SMALL[:, 1044:1045]
        bf.HALFC = _SMALL[:, 1045:1046]
        cx = Ctx(nc, block)
        bf.xtile = lambda t: bf.X[:, t * D:(t + 1) * D]

        cx.op("sync", lambda s: s.dma_start(
            out=bf.X[:, :], in_=ap["x0"].rearrange("b (t p) d -> p b t d", p=P)),
            dma=True, writes=["X"])
        cx.op("sync", lambda s: s.dma_start(out=bf.IDENT[:, :], in_=ap["ident"]),
              dma=True, writes=["IDENT"])
        cx.op("sync", lambda s: s.dma_start(out=bf.ONESR[:, :], in_=ap["ones_row"]),
              dma=True, writes=["ONESR"])
        cx.op("sync", lambda s: s.dma_start(out=bf.PREFW, in_=ap["prefw_rep"]),
              dma=True, writes=["SMALL"])
        cx.op("sync", lambda s: s.dma_start(out=bf.ONESC, in_=ap["ones_col"]),
              dma=True, writes=["SMALL"])
        cx.op("sync", lambda s: s.dma_start(out=_SMALL[:, 1044:1046],
                                            in_=ap["consts"]),
              dma=True, writes=["SMALL"])
        cx.op("sync", lambda s: s.dma_start(
            out=bf.IBT, in_=ap["t_iB_t"].rearrange("(c p) r -> p c r", p=P)),
            dma=True, writes=["SMALL"])
        cx.op("sync", lambda s: s.dma_start(
            out=bf.IA, in_=ap["t_iA"].rearrange("(c p) r -> p c r", p=P)),
            dma=True, writes=["SMALL"])

        import os as _os
        phases = _os.environ.get("KMOD_PHASES", "all")
        def on(name):
            return phases == "all" or name in phases.split(",")
        emit_setup(nc, cx, ap, bf)
        cx.barrier()
        if on("attn_a"):
            emit_attn_a(nc, cx, ap, bf)
            cx.barrier()
        if on("ffn0a"):
            emit_ffn0(nc, cx, ap, bf)
            cx.barrier()
        if on("attn_b0"):
            emit_attn_b0(nc, cx, ap, bf)
            cx.barrier()
        if on("ffn0b"):
            emit_ffn0(nc, cx, ap, bf)
            cx.barrier()
        if on("ttt"):
            emit_ttt(nc, cx, ap, bf)
            cx.barrier()
        if on("qffn1"):
            emit_qffn(nc, cx, ap, bf)
            cx.barrier()
        if on("attn_b1"):
            emit_attn_b1(nc, cx, ap, bf)
            cx.barrier()
        if on("qffn2"):
            emit_qffn(nc, cx, ap, bf)
            cx.barrier()

        cx.op("sync", lambda s: s.dma_start(
            out=ap["y"].rearrange("b (t p) d -> p b t d", p=P), in_=bf.X[:, :]),
            dma=True, reads=["X"])
        sems = cx.sems
        finals = [(f"S_dma{i}", cx.cnt[f"S_dma{i}"]) for i in range(cx.NDL)]

        def _fin(e):
            for s, c in finals:
                if c > 0:
                    e.wait_ge(sems[s], c)
        block.sync(_fin)
    return nc


def emit_setup(nc, cx, ap, bf):
    for dc in range(DCH):
        for hc in range(DCH):
            cx.op("sync", lambda s, hc=hc, dc=dc: s.dma_start(
                out=bf.WTA[:, 0:P],
                in_=ap["t_Wp"][hc * P:(hc + 1) * P, dc * P:(dc + 1) * P]),
                dma=True, writes=["WTA"])
            cx.op("tensor", lambda t, hc=hc: t.matmul(
                bf.PS0[:, 0:LORA_R], bf.WTA[:, 0:P],
                bf.IA[:, hc * LORA_R:(hc + 1) * LORA_R],
                start=(hc == 0), stop=(hc == DCH - 1)),
                reads=["WTA", "SMALL"], writes=["ps0"])
        cx.op("vector", lambda v, dc=dc: v.tensor_copy(
            out=bf.G[:, dc * LORA_R:(dc + 1) * LORA_R], in_=bf.PS0[:, 0:LORA_R]),
            reads=["ps0"], writes=["SMALL"])
    for i in range(DCH):
        cx.op("sync", lambda s, i=i: s.dma_start(
            out=bf.BA[:, 0:H_FFN // 2],
            in_=ap["f0_w1t"][i * P:(i + 1) * P, 0:H_FFN // 2]),
            dma=True, writes=["BA"])
        cx.op("scalar", lambda s: s.activation(out=bf.BA[:, 0:H_FFN // 2],
                                               in_=bf.BA[:, 0:H_FFN // 2],
                                               func=AF.Sign),
              reads=["BA"], writes=["BA"])
        for hc in range(HCH // 2):
            cx.op("sync", lambda s, i=i, hc=hc: s.dma_start(
                out=ap["s_w1sl"][hc, :, i * P:(i + 1) * P],
                in_=bf.BA[:, hc * P:(hc + 1) * P]),
                dma=True, reads=["BA"], writes=["s_w1sl"])
        cx.op("sync", lambda s, i=i: s.dma_start(
            out=bf.BA[:, 0:H_FFN // 2],
            in_=ap["f0_w1t"][i * P:(i + 1) * P, H_FFN // 2:]),
            dma=True, writes=["BA"])
        cx.op("scalar", lambda s: s.activation(out=bf.BA[:, 0:H_FFN // 2],
                                               in_=bf.BA[:, 0:H_FFN // 2],
                                               func=AF.Sign),
              reads=["BA"], writes=["BA"])
        for hc in range(HCH // 2):
            cx.op("sync", lambda s, i=i, hc=hc: s.dma_start(
                out=ap["s_w1sl"][HCH // 2 + hc, :, i * P:(i + 1) * P],
                in_=bf.BA[:, hc * P:(hc + 1) * P]),
                dma=True, reads=["BA"], writes=["s_w1sl"])
    for i in range(HCH):
        cx.op("sync", lambda s, i=i: s.dma_start(
            out=bf.BA[:, 0:D], in_=ap["f0_w2t"][i * P:(i + 1) * P, :]),
            dma=True, writes=["BA"])
        cx.op("scalar", lambda s: s.activation(out=bf.BA[:, 0:D], in_=bf.BA[:, 0:D],
                                               func=AF.Sign),
              reads=["BA"], writes=["BA"])
        cx.op("sync", lambda s, i=i: s.dma_start(
            out=ap["s_w2s"][i * P:(i + 1) * P, :], in_=bf.BA[:, 0:D]),
            dma=True, reads=["BA"], writes=["s_w2s"])
    for i in range(HCH):
        cx.op("sync", lambda s, i=i: s.dma_start(
            out=bf.BA[:, 0:D], in_=ap["f0_w1"][i * P:(i + 1) * P, :]),
            dma=True, writes=["BA"])
        cx.op("vector", lambda v, i=i: v.reduce_sum(
            bf.A1[:, i:i + 1], bf.BA[:, 0:D], axis=AXL.X,
            apply_absolute_value=True),
            reads=["BA"], writes=["SMALL"])
    cx.op("vector", lambda v: v.tensor_scalar(out=bf.A1, in0=bf.A1,
                                              scalar1=1.0 / D, scalar2=None,
                                              op0=ALU.mult),
          reads=["SMALL"], writes=["SMALL"])
    for i in range(DCH):
        cx.op("sync", lambda s, i=i: s.dma_start(
            out=bf.BA[:, 0:H_FFN], in_=ap["f0_w2"][i * P:(i + 1) * P, :]),
            dma=True, writes=["BA"])
        cx.op("vector", lambda v, i=i: v.reduce_sum(
            bf.A2[:, i:i + 1], bf.BA[:, 0:H_FFN], axis=AXL.X,
            apply_absolute_value=True),
            reads=["BA"], writes=["SMALL"])
    cx.op("vector", lambda v: v.tensor_scalar(out=bf.A2, in0=bf.A2,
                                              scalar1=1.0 / H_FFN, scalar2=None,
                                              op0=ALU.mult),
          reads=["SMALL"], writes=["SMALL"])
    for i in range(HCH):
        cx.op("sync", lambda s, i=i: s.dma_start(
            out=bf.BA[:, 0:D], in_=ap["q_w1"][i * P:(i + 1) * P, :]),
            dma=True, writes=["BA"])
        cx.op("vector", lambda v: v.reduce_max(bf.STAT[:, 8:9], bf.BA[:, 0:D],
                                               axis=AXL.X, apply_absolute_value=True),
              reads=["BA"], writes=["STAT"])
        cx.op("vector", lambda v, i=i: v.tensor_scalar(
            out=bf.SW1[:, i:i + 1], in0=bf.STAT[:, 8:9], scalar1=1e-8,
            scalar2=1.0 / 127.0, op0=ALU.max, op1=ALU.mult),
            reads=["STAT"], writes=["SMALL"])
        cx.op("vector", lambda v, i=i: v.reciprocal(out=bf.SW1R[:, i:i + 1],
                                                    in_=bf.SW1[:, i:i + 1]),
              reads=["SMALL"], writes=["SMALL"])
        cx.op("vector", lambda v, i=i: v.tensor_scalar(
            out=bf.F16T[:, 0:D], in0=bf.BA[:, 0:D], scalar1=bf.SW1R[:, i:i + 1],
            scalar2=1536.0, op0=ALU.mult, op1=ALU.add),
            reads=["BA", "SMALL"], writes=["SCR2"])
        cx.op("vector", lambda v: v.tensor_scalar(
            out=bf.BB[:, 0:D], in0=bf.F16T[:, 0:D], scalar1=-1536.0,
            scalar2=-128.0, op0=ALU.add, op1=ALU.max),
            reads=["SCR2"], writes=["BB"])
        cx.op("vector", lambda v: v.tensor_scalar(
            out=bf.BB[:, 0:D], in0=bf.BB[:, 0:D], scalar1=127.0, scalar2=None,
            op0=ALU.min), reads=["BB"], writes=["BB"])
        for dc in range(DCH):
            tr128(cx, bf, bf.BB[:, dc * P:(dc + 1) * P], bf.LNT[:, 0:P], "BB", "LNT")
            cx.op("vector", lambda v: v.tensor_copy(
                out=bf.SCR2[:, 512:576].bitcast(BF16),
                in_=bf.LNT[:, 0:P]),
                reads=["LNT"], writes=["SCR2"])
            cx.op("sync", lambda s, i=i, dc=dc: s.dma_start(
                out=ap["s_q1ctl"][i, :, dc * P:(dc + 1) * P],
                in_=bf.SCR2[:, 512:576].bitcast(BF16)),
                dma=True, reads=["SCR2"], writes=["s_q1ctl"])
    for i in range(DCH):
        cx.op("sync", lambda s, i=i: s.dma_start(
            out=bf.BA[:, 0:H_FFN], in_=ap["q_w2"][i * P:(i + 1) * P, :]),
            dma=True, writes=["BA"])
        cx.op("vector", lambda v: v.reduce_max(bf.STAT[:, 8:9], bf.BA[:, 0:H_FFN],
                                               axis=AXL.X, apply_absolute_value=True),
              reads=["BA"], writes=["STAT"])
        cx.op("vector", lambda v, i=i: v.tensor_scalar(
            out=bf.SW2[:, i:i + 1], in0=bf.STAT[:, 8:9], scalar1=1e-8,
            scalar2=1.0 / 127.0, op0=ALU.max, op1=ALU.mult),
            reads=["STAT"], writes=["SMALL"])
        cx.op("vector", lambda v, i=i: v.reciprocal(out=bf.SW2R[:, i:i + 1],
                                                    in_=bf.SW2[:, i:i + 1]),
              reads=["SMALL"], writes=["SMALL"])
        for q in range(4):
            sl = slice(q * D, (q + 1) * D)
            cx.op("vector", lambda v, i=i, sl=sl: v.tensor_scalar(
                out=bf.F16T[:, 0:D], in0=bf.BA[:, sl], scalar1=bf.SW2R[:, i:i + 1],
                scalar2=1536.0, op0=ALU.mult, op1=ALU.add),
                reads=["BA", "SMALL"], writes=["SCR2"])
            cx.op("vector", lambda v, sl=sl: v.tensor_scalar(
                out=bf.BB[:, sl], in0=bf.F16T[:, 0:D], scalar1=-1536.0,
                scalar2=-128.0, op0=ALU.add, op1=ALU.max),
                reads=["SCR2"], writes=["BB"])
        cx.op("vector", lambda v: v.tensor_scalar(
            out=bf.BB[:, 0:H_FFN], in0=bf.BB[:, 0:H_FFN], scalar1=127.0,
            scalar2=None, op0=ALU.min), reads=["BB"], writes=["BB"])
        for hc in range(HCH):
            tr128(cx, bf, bf.BB[:, hc * P:(hc + 1) * P], bf.LNT[:, 0:P], "BB", "LNT")
            cx.op("vector", lambda v: v.tensor_copy(
                out=bf.SCR2[:, 512:576].bitcast(BF16),
                in_=bf.LNT[:, 0:P]),
                reads=["LNT"], writes=["SCR2"])
            cx.op("sync", lambda s, i=i, hc=hc: s.dma_start(
                out=ap["s_q2ct"][hc, :, i * P:(i + 1) * P],
                in_=bf.SCR2[:, 512:576].bitcast(BF16)),
                dma=True, reads=["SCR2"], writes=["s_q2ct"])


def emit_attn_a(nc, cx, ap, bf):
    wqr = ap["aa_Wq"].rearrange("(dc p) o -> p dc o", p=P)
    wkr = ap["aa_Wk"].rearrange("(dc p) o -> p dc o", p=P)
    wvr = ap["aa_Wv"].rearrange("(dc p) o -> p dc o", p=P)
    for b in range(B):
        for j in range(NQT):
            cx.op("sync", lambda s, b=b, j=j: s.dma_start(
                out=bf.LNT[:, :], in_=ap["aa_blk"][b, j * P:(j + 1) * P, :]),
                dma=True, writes=["LNT"])
            ln_tile(cx, bf, bf.LNT[:, :], bf.SCR[:, 0:D], "LNT", "SCR")
            for dc in range(DCH):
                tr128(cx, bf, bf.SCR[:, dc * P:(dc + 1) * P],
                      bf.BA[:, dc * QBLK + j * P: dc * QBLK + (j + 1) * P],
                      "SCR", "BA")
        cx.op("sync", lambda s, b=b: s.dma_start(
            out=bf.LNT[0:N_SINKS, :], in_=ap["aa_sink"][b, :, :]),
            dma=True, writes=["LNT"])
        ln_tile(cx, bf, bf.LNT[0:N_SINKS, :], bf.SCR[0:N_SINKS, 0:D], "LNT", "SCR",
                nrow=N_SINKS)
        for dc in range(DCH):
            tr128(cx, bf, bf.SCR[0:N_SINKS, dc * P:(dc + 1) * P],
                  bf.SCR3[:, dc * N_SINKS:(dc + 1) * N_SINKS], "SCR", "SCR3",
                  nout=N_SINKS)

        def blkT(dc, c0, c1):
            return bf.BA[:, dc * QBLK + c0: dc * QBLK + c1]

        for oc in range(DCH):
            cx.op("sync", lambda s, oc=oc: s.dma_start(
                out=bf.WTA[:, :], in_=wqr[:, :, oc * P:(oc + 1) * P]),
                dma=True, writes=["WTA"])
            for c0, c1 in ((0, 512), (512, QBLK)):
                n = c1 - c0
                builds = [(lambda t, dc=dc, c0=c0, c1=c1, n=n: t.matmul(
                    bf.PS0[:, :n], bf.WTA[:, dc * P:(dc + 1) * P],
                    blkT(dc, c0, c1), start=(dc == 0), stop=(dc == DCH - 1)))
                    for dc in range(DCH)]
                cx.group("tensor", builds, reads=["WTA", "BA"], writes=["ps0"])
                cx.op("vector", lambda v, oc=oc, c0=c0, c1=c1, n=n: v.tensor_copy(
                    out=bf.BB[:, oc * QBLK + c0: oc * QBLK + c1], in_=bf.PS0[:, :n]),
                    reads=["ps0"], writes=["BB"])
        for oc in range(DCH):
            cx.op("sync", lambda s, oc=oc: s.dma_start(
                out=bf.WTA[:, :], in_=wkr[:, :, oc * P:(oc + 1) * P]),
                dma=True, writes=["WTA"])
            builds = [(lambda t, dc=dc, oc=oc: t.matmul(
                bf.PS0[:, 0:N_SINKS], bf.WTA[:, dc * P:(dc + 1) * P],
                bf.SCR3[:, dc * N_SINKS:(dc + 1) * N_SINKS],
                start=(dc == 0), stop=(dc == DCH - 1))) for dc in range(DCH)]
            cx.group("tensor", builds, reads=["WTA", "SCR3"], writes=["ps0"])
            cx.op("vector", lambda v, oc=oc: v.tensor_copy(
                out=bf.CB[:, oc * CTX:oc * CTX + N_SINKS], in_=bf.PS0[:, 0:N_SINKS]),
                reads=["ps0"], writes=["CB"])
            for half in range(2):
                c0 = P + half * 256
                builds = [(lambda t, dc=dc, c0=c0: t.matmul(
                    bf.PS0[:, 0:256], bf.WTA[:, dc * P:(dc + 1) * P],
                    blkT(dc, c0, c0 + 256), start=(dc == 0), stop=(dc == DCH - 1)))
                    for dc in range(DCH)]
                cx.group("tensor", builds, reads=["WTA", "BA"], writes=["ps0"])
                cx.op("vector", lambda v, oc=oc, half=half: v.tensor_copy(
                    out=bf.CB[:, oc * CTX + N_SINKS + half * 256:
                              oc * CTX + N_SINKS + (half + 1) * 256],
                    in_=bf.PS0[:, 0:256]),
                    reads=["ps0"], writes=["CB"])
        for ci in range(NQT):
            nrow = N_SINKS if ci == 0 else P
            for dh in range(2):
                for dc in range(DCH):
                    if ci == 0:
                        lh = bf.SCR3[:, dc * N_SINKS:(dc + 1) * N_SINKS]
                        lk = "SCR3"
                    else:
                        lh = blkT(dc, ci * P, (ci + 1) * P)
                        lk = "BA"
                    cx.op("sync", lambda s, dc=dc, dh=dh: s.dma_start(
                        out=bf.WTB[:, 0:512],
                        in_=ap["aa_Wv"][dc * P:(dc + 1) * P,
                                        dh * 512:(dh + 1) * 512]),
                        dma=True, writes=["WTB"])
                    cx.op("tensor", lambda t, dc=dc, lh=lh, nrow=nrow: t.matmul(
                        bf.PS2[:nrow, :], lh, bf.WTB[:, 0:512],
                        start=(dc == 0), stop=(dc == DCH - 1)),
                        reads=["WTB", lk], writes=["ps2"])
                cx.op("vector", lambda v, nrow=nrow, dh=dh: v.tensor_copy(
                    out=bf.LSCR[:nrow, dh * 512:(dh + 1) * 512],
                    in_=bf.PS2[:nrow, :]),
                    reads=["ps2"], writes=["LSCR"])
            cx.op("sync", lambda s, ci=ci, nrow=nrow: s.dma_start(
                out=ap["s_vtok"][ci, 0:nrow, :], in_=bf.LSCR[0:nrow, :]),
                dma=True, reads=["LSCR"], writes=["s_vtok"])
        cx.barrier()
        for j in range(NQT):
            cx.op("sync", lambda s, j=j: s.dma_start(
                out=bf.CL[:, 0:WINDOW], in_=ap["aa_maskw"][j * P:(j + 1) * P, :]),
                dma=True, writes=["CL"])
            cx.op("sync", lambda s, j=j: s.dma_start(
                out=bf.CL[:, WINDOW:CTX], in_=ap["aa_masks"][j * P:(j + 1) * P, :]),
                dma=True, writes=["CL"])
            cx.op("sync", lambda s, j=j: s.dma_start(
                out=bf.STAT[:, 12:13], in_=ap["aa_addmask"][j * P:(j + 1) * P, :]),
                dma=True, writes=["STAT"])
            for h in range(NHEAD):
                oc, pb = h // 2, (h % 2) * DHEAD
                qT = bf.BB[pb:pb + DHEAD, oc * QBLK + j * P: oc * QBLK + (j + 1) * P]
                cx.group("tensor", [
                    lambda t: t.matmul(bf.PSW[:, 512:512 + N_SINKS], qT,
                                       bf.CB[pb:pb + DHEAD,
                                             oc * CTX:oc * CTX + N_SINKS],
                                       start=True, stop=True),
                    lambda t: t.matmul(bf.PSW[:, 0:256], qT,
                                       bf.CB[pb:pb + DHEAD,
                                             oc * CTX + N_SINKS:
                                             oc * CTX + N_SINKS + 256],
                                       start=True, stop=True),
                    lambda t: t.matmul(bf.PSW[:, 256:512], qT,
                                       bf.CB[pb:pb + DHEAD,
                                             oc * CTX + N_SINKS + 256:oc * CTX + CTX],
                                       start=True, stop=True),
                ], reads=["BB", "CB"], writes=["psw"])
                cx.op("vector", lambda v: v.scalar_tensor_tensor(
                    out=bf.SCR[:, 0:N_SINKS], in0=bf.PSW[:, 512:512 + N_SINKS],
                    scalar=0.125, op0=ALU.mult, op1=ALU.add,
                    in1=bf.CL[:, WINDOW:CTX]),
                    reads=["psw", "CL"], writes=["SCR"])
                cx.op("vector", lambda v: v.scalar_tensor_tensor(
                    out=bf.SCR[:, N_SINKS:CTX], in0=bf.PSW[:, 0:WINDOW],
                    scalar=0.125, op0=ALU.mult, op1=ALU.add, in1=bf.CL[:, 0:WINDOW]),
                    reads=["psw", "CL"], writes=["SCR"])
                cx.op("vector", lambda v: v.reduce_max(
                    bf.STAT[:, 8:9], bf.SCR[:, 0:CTX], axis=AXL.X, negate=True),
                    reads=["SCR"], writes=["STAT"])
                cx.op("scalar", lambda s: s.activation(
                    out=bf.SCR[:, 0:CTX], in_=bf.SCR[:, 0:CTX], func=AF.Exp,
                    bias=bf.STAT[:, 8:9], accum_out=bf.STAT[:, 9:10]),
                    reads=["SCR", "STAT"], writes=["SCR", "STAT"])
                cx.op("vector", lambda v: v.reciprocal(out=bf.STAT[:, 9:10],
                                                       in_=bf.STAT[:, 9:10]),
                      reads=["STAT"], writes=["STAT"])
                cx.op("vector", lambda v: v.tensor_scalar(
                    out=bf.SCR[:, 0:CTX], in0=bf.SCR[:, 0:CTX],
                    scalar1=bf.STAT[:, 9:10], scalar2=None, op0=ALU.mult),
                    reads=["SCR", "STAT"], writes=["SCR"])
                for ci in range(NQT):
                    if ci == 0:
                        pc, nrow = slice(0, N_SINKS), N_SINKS
                    else:
                        pc, nrow = slice(N_SINKS + (ci - 1) * P, N_SINKS + ci * P), P
                    cx.op("tensor", lambda t, pc=pc, nrow=nrow: t.transpose(
                        bf.PT[:nrow, 0:P], bf.SCR[:, pc], bf.IDENT[:, :]),
                        reads=["SCR", "IDENT"], writes=["pt"])
                    cx.op("vector", lambda v, nrow=nrow: v.tensor_copy(
                        out=bf.SCR2[0:nrow, 0:P], in_=bf.PT[0:nrow, 0:P]),
                        reads=["pt"], writes=["SCR2"])
                    cx.op("sync", lambda s, ci=ci, h=h, nrow=nrow: s.dma_start(
                        out=bf.SCR2[0:nrow, 128:128 + DHEAD],
                        in_=ap["s_vtok"][ci, 0:nrow, h * DHEAD:(h + 1) * DHEAD]),
                        dma=True, reads=["s_vtok"], writes=["SCR2"])
                    cx.op("tensor", lambda t, nrow=nrow, ci=ci: t.matmul(
                        bf.PS3[0:DHEAD, 0:P], bf.SCR2[0:nrow, 128:128 + DHEAD],
                        bf.SCR2[0:nrow, 0:P], start=(ci == 0), stop=(ci == NQT - 1)),
                        reads=["SCR2"], writes=["ps3"])
                cx.op("vector", lambda v: v.tensor_copy(
                    out=bf.SCR2[0:DHEAD, 256:256 + P], in_=bf.PS3[0:DHEAD, 0:P]),
                    reads=["ps3"], writes=["SCR2"])
                for dh in range(2):
                    cx.op("sync", lambda s, h=h, dh=dh: s.dma_start(
                        out=bf.WTA[0:DHEAD, 0:512],
                        in_=ap["aa_Wo"][h * DHEAD:(h + 1) * DHEAD,
                                        dh * 512:(dh + 1) * 512]),
                        dma=True, writes=["WTA"])
                    ps = bf.PS0 if dh == 0 else bf.PS1
                    cx.op("tensor", lambda t, h=h, ps=ps: t.matmul(
                        ps[:, :], bf.SCR2[0:DHEAD, 256:256 + P],
                        bf.WTA[0:DHEAD, 0:512],
                        start=(h == 0), stop=(h == NHEAD - 1)),
                        reads=["SCR2", "WTA"], writes=["ps0" if dh == 0 else "ps1"])
            qstart_t = (SEQ - QBLK) // P
            tj = b * NPB + ((qstart_t + j) % NPB)
            for dh in range(2):
                ps = bf.PS0 if dh == 0 else bf.PS1
                cx.op("vector", lambda v, tj=tj, dh=dh, ps=ps: v.scalar_tensor_tensor(
                    out=bf.xtile(tj)[:, dh * 512:(dh + 1) * 512], in0=ps[:, :],
                    scalar=bf.STAT[:, 12:13], op0=ALU.mult, op1=ALU.add,
                    in1=bf.xtile(tj)[:, dh * 512:(dh + 1) * 512]),
                    reads=["ps0" if dh == 0 else "ps1", "STAT", "X"], writes=["X"])
        cx.barrier()


def emit_ffn0(nc, cx, ap, bf):
    cx.op("sync", lambda s: s.dma_start(out=bf.SCR3[:, 0:HCH], in_=ap["f0_b1"]),
          dma=True, writes=["SCR3"])
    cx.op("sync", lambda s: s.dma_start(out=bf.SCR3[:, 32:32 + D], in_=ap["f0_b2"]),
          dma=True, writes=["SCR3"])
    for dc in range(DCH):
        rep_row_tile(cx, bf, bf.A2[:, dc:dc + 1],
                     bf.CL[:, dc * P:(dc + 1) * P], "SMALL", "CL")
    def load_wpair(idx):
        hc, buf = idx % HCH, idx % 2
        cx.op("sync", lambda s, hc=hc, buf=buf: s.dma_start(
            out=bf.BB[:, buf * 1024:(buf + 1) * 1024], in_=ap["s_w1sl"][hc]),
            dma=True, reads=["s_w1sl"], writes=[f"BBw{buf}"])
        cx.op("sync", lambda s, hc=hc, buf=buf: s.dma_start(
            out=bf.BB[:, 2048 + buf * 1024:2048 + (buf + 1) * 1024],
            in_=ap["s_w2s"][hc * P:(hc + 1) * P, :]),
            dma=True, reads=["s_w2s"], writes=[f"BBx{buf}"])
    for b in range(B):
        for j in range(NPB):
            tj = b * NPB + j
            ln_tile(cx, bf, bf.xtile(tj), bf.LNT[:, :], "X", "LNT")
            for dc in range(DCH):
                tr128(cx, bf, bf.LNT[:, dc * P:(dc + 1) * P],
                      bf.BA[:, dc * SEQ_PER + j * P: dc * SEQ_PER + (j + 1) * P],
                      "LNT", "BA")
        TT = 256
        load_wpair(0)
        load_wpair(1)
        NIT = (SEQ_PER // TT) * HCH
        for tt in range(SEQ_PER // TT):
            c0 = tt * TT
            for hc in range(HCH):
                idx = tt * HCH + hc
                buf = idx % 2
                builds = [(lambda t, dc=dc, c0=c0, buf=buf: t.matmul(
                    bf.PSW[:, buf * 512:buf * 512 + TT],
                    bf.BB[:, buf * 1024 + dc * P:buf * 1024 + (dc + 1) * P],
                    bf.BA[:, dc * SEQ_PER + c0: dc * SEQ_PER + c0 + TT],
                    start=(dc == 0), stop=(dc == DCH - 1))) for dc in range(DCH)]
                cx.group("tensor", builds, reads=[f"BBw{buf}", "BA"],
                         writes=[f"psw{buf}"])
                cx.op("scalar", lambda s, hc=hc, buf=buf: s.activation(
                    out=bf.SCR[:, buf * 256:buf * 256 + TT],
                    in_=bf.PSW[:, buf * 512:buf * 512 + TT], func=AF.Gelu,
                    scale=bf.A1[:, hc:hc + 1], bias=bf.SCR3[:, hc:hc + 1]),
                    reads=[f"psw{buf}", "SMALL", "SCR3"], writes=[f"SCRg{buf}"])
                for half in range(2):
                    for dh in range(2):
                        pk = f"ps{half * 2 + dh}"
                        ps = (bf.PS0, bf.PS1, bf.PS2, bf.PS3)[half * 2 + dh]
                        cx.op("tensor", lambda t, half=half, dh=dh, hc=hc, ps=ps,
                              buf=buf: t.matmul(
                                  ps[:, :],
                                  bf.SCR[:, buf * 256 + half * P:
                                         buf * 256 + (half + 1) * P],
                                  bf.BB[:, 2048 + buf * 1024 + dh * 512:
                                        2048 + buf * 1024 + (dh + 1) * 512],
                                  start=(hc == 0), stop=(hc == HCH - 1)),
                              reads=[f"SCRg{buf}", f"BBx{buf}"], writes=[pk])
                if idx + 2 < NIT:
                    load_wpair(idx + 2)
            for half in range(2):
                tj = b * NPB + (c0 + half * P) // P
                for dh in range(2):
                    pk = f"ps{half * 2 + dh}"
                    ps = (bf.PS0, bf.PS1, bf.PS2, bf.PS3)[half * 2 + dh]
                    sl = slice(dh * 512, (dh + 1) * 512)
                    cx.op("vector", lambda v, ps=ps, sl=sl, pk=pk: v.tensor_mul(
                        out=bf.SCR2[:, 0:512], in0=ps[:, :], in1=bf.CL[:, sl]),
                        reads=[pk, "CL"], writes=["SCR2"])
                    cx.op("vector", lambda v, sl=sl: v.tensor_add(
                        out=bf.SCR2[:, 0:512], in0=bf.SCR2[:, 0:512],
                        in1=bf.SCR3[:, 32 + sl.start:32 + sl.stop]),
                        reads=["SCR2", "SCR3"], writes=["SCR2"])
                    cx.op("vector", lambda v, tj=tj, sl=sl: v.tensor_add(
                        out=bf.xtile(tj)[:, sl], in0=bf.xtile(tj)[:, sl],
                        in1=bf.SCR2[:, 0:512]),
                        reads=["SCR2", "X"], writes=["X"])
        cx.barrier()


def emit_attn_b0(nc, cx, ap, bf):
    inv_sq = float(1.0 / np.sqrt(D))
    st = bf.STAT
    for j in range(NPB):
        t0, t1 = j, NPB + j
        ln_tile(cx, bf, bf.xtile(t0), bf.SCR[:, 0:D], "X", "SCR")
        ln_tile(cx, bf, bf.xtile(t1), bf.SCR2[:, 0:D], "X", "SCR2")
        for (c, i0, i1, k0, k1) in ((8, bf.SCR, bf.SCR, "SCR", "SCR"),
                                    (9, bf.SCR, bf.SCR2, "SCR", "SCR2"),
                                    (10, bf.SCR2, bf.SCR2, "SCR2", "SCR2")):
            cx.op("vector", lambda v, c=c, i0=i0, i1=i1: v.scalar_tensor_tensor(
                out=bf.LSCR[:, :], in0=i0[:, 0:D], scalar=0.0, op0=ALU.add,
                op1=ALU.mult, in1=i1[:, 0:D], accum_out=st[:, c:c + 1]),
                reads=[k0, k1], writes=["STAT", "LSCR"])
        cx.op("vector", lambda v: v.tensor_scalar(
            out=st[:, 8:11], in0=st[:, 8:11], scalar1=inv_sq, scalar2=None,
            op0=ALU.mult), reads=["STAT"], writes=["STAT"])
        cx.op("vector", lambda v: v.tensor_max(out=st[:, 11:12], in0=st[:, 8:9],
                                               in1=st[:, 9:10]),
              reads=["STAT"], writes=["STAT"])
        cx.op("vector", lambda v: v.tensor_scalar(out=st[:, 11:12],
                                                  in0=st[:, 11:12], scalar1=-1.0,
                                                  scalar2=None, op0=ALU.mult),
              reads=["STAT"], writes=["STAT"])
        cx.op("vector", lambda v: v.tensor_max(out=st[:, 12:13], in0=st[:, 9:10],
                                               in1=st[:, 10:11]),
              reads=["STAT"], writes=["STAT"])
        cx.op("vector", lambda v: v.tensor_scalar(out=st[:, 12:13],
                                                  in0=st[:, 12:13], scalar1=-1.0,
                                                  scalar2=None, op0=ALU.mult),
              reads=["STAT"], writes=["STAT"])
        for (c, sc, mc) in ((0, 8, 11), (1, 9, 11), (2, 9, 12), (3, 10, 12)):
            cx.op("scalar", lambda s, c=c, sc=sc, mc=mc: s.activation(
                out=st[:, c:c + 1], in_=st[:, sc:sc + 1], func=AF.Exp,
                bias=st[:, mc:mc + 1]),
                reads=["STAT"], writes=["STAT"])
        cx.op("vector", lambda v: v.tensor_add(out=st[:, 4:5], in0=st[:, 0:1],
                                               in1=st[:, 1:2]),
              reads=["STAT"], writes=["STAT"])
        cx.op("vector", lambda v: v.reciprocal(out=st[:, 4:5], in_=st[:, 4:5]),
              reads=["STAT"], writes=["STAT"])
        cx.op("vector", lambda v: v.tensor_add(out=st[:, 5:6], in0=st[:, 2:3],
                                               in1=st[:, 3:4]),
              reads=["STAT"], writes=["STAT"])
        cx.op("vector", lambda v: v.reciprocal(out=st[:, 5:6], in_=st[:, 5:6]),
              reads=["STAT"], writes=["STAT"])
        for (c, r) in ((0, 4), (1, 4), (2, 5), (3, 5)):
            cx.op("vector", lambda v, c=c, r=r: v.tensor_scalar(
                out=st[:, c:c + 1], in0=st[:, c:c + 1], scalar1=st[:, r:r + 1],
                scalar2=None, op0=ALU.mult),
                reads=["STAT"], writes=["STAT"])
        cx.op("vector", lambda v, t0=t0: v.scalar_tensor_tensor(
            out=bf.LSCR[:, :], in0=bf.SCR2[:, 0:D], scalar=st[:, 1:2],
            op0=ALU.mult, op1=ALU.add, in1=bf.xtile(t0)),
            reads=["SCR2", "STAT", "X"], writes=["LSCR"])
        cx.op("vector", lambda v, t0=t0: v.scalar_tensor_tensor(
            out=bf.xtile(t0), in0=bf.SCR[:, 0:D], scalar=st[:, 0:1],
            op0=ALU.mult, op1=ALU.add, in1=bf.LSCR[:, :]),
            reads=["SCR", "STAT", "LSCR"], writes=["X"])
        cx.op("vector", lambda v, t1=t1: v.scalar_tensor_tensor(
            out=bf.LSCR[:, :], in0=bf.SCR[:, 0:D], scalar=st[:, 2:3],
            op0=ALU.mult, op1=ALU.add, in1=bf.xtile(t1)),
            reads=["SCR", "STAT", "X"], writes=["LSCR"])
        cx.op("vector", lambda v, t1=t1: v.scalar_tensor_tensor(
            out=bf.xtile(t1), in0=bf.SCR2[:, 0:D], scalar=st[:, 3:4],
            op0=ALU.mult, op1=ALU.add, in1=bf.LSCR[:, :]),
            reads=["SCR2", "STAT", "LSCR"], writes=["X"])


def emit_ttt(nc, cx, ap, bf):
    SEG = SEQ_PER + 1
    for b in range(B):
        cx.op("sync", lambda s, b=b: s.dma_start(
            out=ap["g_first_in"][b:b + 1, :],
            in_=bf.X[0:1, b * NPB * D: b * NPB * D + D]),
            dma=True, reads=["X"], writes=["g_first_in"])
    cx.op("gpsimd", lambda g: g.collective_compute(
        "AllGather", ALU.bypass, replica_groups=[list(range(NC))],
        ins=[ap["g_first_in"].opt()], outs=[ap["g_first_out"].opt()]),
        coll=True, reads=["g_first_in"], writes=["g_first_out"])
    cx.op("sync", lambda s: s.dma_start(
        out=bf.CL[0:NC, 0:B * D],
        in_=ap["g_first_out"].rearrange("n b d -> n (b d)")),
        dma=True, reads=["g_first_out"], writes=["CL"])
    cx.op("sync", lambda s: s.dma_start(out=bf.SCR2[0:NC, 0:1],
                                        in_=ap["onehot_next"]),
          dma=True, writes=["SCR2"])
    for q in range(B * D // 512):
        cx.op("tensor", lambda t, q=q: t.matmul(
            bf.PSW[0:1, 0:512], bf.SCR2[0:NC, 0:1],
            bf.CL[0:NC, q * 512:(q + 1) * 512], start=True, stop=True),
            reads=["CL", "SCR2"], writes=["psw"])
        cx.op("vector", lambda v, q=q: v.tensor_copy(
            out=bf.CL[0:1, 2048 + q * 512:2048 + (q + 1) * 512],
            in_=bf.PSW[0:1, 0:512]),
            reads=["psw"], writes=["CL"])
    cx.op("sync", lambda s: s.dma_start(out=bf.CB[:, 0:D], in_=ap["t_gamma"]),
          dma=True, writes=["CB"])
    cx.op("sync", lambda s: s.dma_start(out=bf.CB[:, D:2 * D], in_=ap["t_beta"]),
          dma=True, writes=["CB"])
    wbtr = ap["t_Wbt"].rearrange("(dc p) o -> p dc o", p=P)
    wptr = ap["t_Wpt"].rearrange("(dc p) o -> p dc o", p=P)
    for b in range(B):
        sxtr = ap["s_xt"][b].rearrange("c p t -> p c t")
        sxntr = ap["s_xnt"][b].rearrange("c p t -> p c t")
        for j in range(NPB):
            tj = b * NPB + j
            ln_tile(cx, bf, bf.xtile(tj), bf.BB[:, j * D:(j + 1) * D], "X", "BB")
            for dc in range(DCH):
                tr128(cx, bf, bf.BB[:, j * D + dc * P: j * D + (dc + 1) * P],
                      bf.LNT[:, dc * P:(dc + 1) * P], "BB", "LNT")
            cx.op("sync", lambda s, j=j: s.dma_start(
                out=sxtr[:, :, j * P:(j + 1) * P],
                in_=bf.LNT.rearrange("p (c q) -> p c q", q=P)),
                dma=True, reads=["LNT"], writes=["s_xt"])
            ln_tile(cx, bf, bf.BB[:, j * D:(j + 1) * D], bf.LSCR[:, :], "BB",
                    "LSCR", gamma=bf.CB[:, 0:D], beta=bf.CB[:, D:2 * D], gkey="CB")
            for dc in range(DCH):
                tr128(cx, bf, bf.LSCR[:, dc * P:(dc + 1) * P],
                      bf.LNT[:, dc * P:(dc + 1) * P], "LSCR", "LNT")
            cx.op("sync", lambda s, j=j: s.dma_start(
                out=sxntr[:, :, j * P:(j + 1) * P],
                in_=bf.LNT.rearrange("p (c q) -> p c q", q=P)),
                dma=True, reads=["LNT"], writes=["s_xnt"])
        brow = bf.CL[0:1, 2048 + b * D:2048 + (b + 1) * D]
        ln_tile(cx, bf, brow, bf.SCR3[0:1, 0:D], "CL", "SCR3", nrow=1)
        ln_tile(cx, bf, bf.SCR3[0:1, 0:D], bf.SCR2[0:1, 0:D], "SCR3", "SCR2",
                gamma=bf.CB[0:1, 0:D], beta=bf.CB[0:1, D:2 * D], gkey="CB", nrow=1)
        for dc in range(DCH):
            tr128(cx, bf, bf.SCR2[0:1, dc * P:(dc + 1) * P],
                  bf.LNT[:, dc:dc + 1], "SCR2", "LNT", nout=P)
        cx.op("sync", lambda s: s.dma_start(
            out=sxntr[:, :, SEQ_PER:SEQ_PER + 1], in_=bf.LNT[:, 0:DCH],
            allow_slow_non_contiguous=True),
            dma=True, reads=["LNT"], writes=["s_xnt"])
        cx.barrier()
        for j in range(NPB):
            tj = b * NPB + j
            cx.op("sync", lambda s, j=j: s.dma_start(
                out=bf.LNT.rearrange("p (c q) -> p c q", q=P),
                in_=sxtr[:, :, j * P:(j + 1) * P]),
                dma=True, reads=["s_xt"], writes=["LNT"])
            for dh in range(2):
                for dc in range(DCH):
                    cx.op("sync", lambda s, dc=dc, dh=dh: s.dma_start(
                        out=bf.WTA[:, 0:512],
                        in_=ap["t_Wbt"][dc * P:(dc + 1) * P,
                                        dh * 512:(dh + 1) * 512]),
                        dma=True, writes=["WTA"])
                    cx.op("tensor", lambda t, dc=dc: t.matmul(
                        bf.PSW[:, 0:512], bf.LNT[:, dc * P:(dc + 1) * P],
                        bf.WTA[:, 0:512],
                        start=(dc == 0), stop=(dc == DCH - 1)),
                        reads=["LNT", "WTA"], writes=["psw"])
                cx.op("vector", lambda v, tj=tj, dh=dh: v.tensor_add(
                    out=bf.xtile(tj)[:, dh * 512:(dh + 1) * 512],
                    in0=bf.xtile(tj)[:, dh * 512:(dh + 1) * 512],
                    in1=bf.PSW[:, 0:512]),
                    reads=["psw", "X"], writes=["X"])
            builds = [(lambda t, dc=dc: t.matmul(
                bf.PS0[:, 0:LORA_R], bf.LNT[:, dc * P:(dc + 1) * P],
                bf.IBT[:, dc * LORA_R:(dc + 1) * LORA_R],
                start=(dc == 0), stop=(dc == DCH - 1))) for dc in range(DCH)]
            cx.group("tensor", builds, reads=["LNT", "SMALL"], writes=["ps0"])
            cx.op("vector", lambda v, j=j: v.tensor_copy(
                out=bf.PIN[:, j * LORA_R:(j + 1) * LORA_R],
                in_=bf.PS0[:, 0:LORA_R]),
                reads=["ps0"], writes=["SMALL"])
            cx.op("sync", lambda s, j=j: s.dma_start(
                out=bf.LSCR.rearrange("p (c q) -> p c q", q=P),
                in_=sxntr[:, :, j * P + 1:(j + 1) * P + 1]),
                dma=True, reads=["s_xnt"], writes=["LSCR"])
            builds = [(lambda t, dc=dc: t.matmul(
                bf.PS1[:, 0:LORA_R], bf.LSCR[:, dc * P:(dc + 1) * P],
                bf.G[:, dc * LORA_R:(dc + 1) * LORA_R],
                start=(dc == 0), stop=(dc == DCH - 1))) for dc in range(DCH)]
            cx.group("tensor", builds, reads=["LSCR", "SMALL"], writes=["ps1"])
            cx.op("vector", lambda v, j=j: v.tensor_copy(
                out=bf.PERR[:, j * LORA_R:(j + 1) * LORA_R],
                in_=bf.PS1[:, 0:LORA_R]),
                reads=["ps1"], writes=["SMALL"])
            for dh in range(2):
                for dc in range(DCH):
                    cx.op("sync", lambda s, dc=dc, dh=dh: s.dma_start(
                        out=bf.WTB[:, 0:512],
                        in_=ap["t_Wpt"][dc * P:(dc + 1) * P,
                                        dh * 512:(dh + 1) * 512]),
                        dma=True, writes=["WTB"])
                    cx.op("tensor", lambda t, dc=dc: t.matmul(
                        bf.PSW[:, 512:1024], bf.LSCR[:, dc * P:(dc + 1) * P],
                        bf.WTB[:, 0:512],
                        start=(dc == 0), stop=(dc == DCH - 1)),
                        reads=["LSCR", "WTB"], writes=["psw"])
                cx.op("vector", lambda v, j=j, dh=dh: v.tensor_copy(
                    out=bf.BA[:, j * D + dh * 512: j * D + dh * 512 + 512],
                    in_=bf.PSW[:, 512:1024]),
                    reads=["psw"], writes=["BA"])
        for c in range(NCHB):
            jj = (2 * c, 2 * c + 1)
            for hc in range(DCH):
                builds = []
                for j in jj:
                    builds.append(lambda t, j=j, hc=hc: t.matmul(
                        bf.PS2[:, 0:LORA_R],
                        bf.BA[:, j * D + hc * P: j * D + (hc + 1) * P],
                        bf.PIN[:, j * LORA_R:(j + 1) * LORA_R],
                        start=(j == jj[0]), stop=(j == jj[1])))
                cx.group("tensor", builds, reads=["BA", "SMALL"], writes=["ps2"])
                cx.op("vector", lambda v, c=c, hc=hc: v.tensor_copy(
                    out=bf.SCR[:, c * P + hc * LORA_R: c * P + (hc + 1) * LORA_R],
                    in_=bf.PS2[:, 0:LORA_R]), reads=["ps2"], writes=["SCR"])
                builds = []
                for j in jj:
                    builds.append(lambda t, j=j, hc=hc: t.matmul(
                        bf.PS3[:, 0:LORA_R],
                        bf.BB[:, j * D + hc * P: j * D + (hc + 1) * P],
                        bf.PERR[:, j * LORA_R:(j + 1) * LORA_R],
                        start=(j == jj[0]), stop=(j == jj[1])))
                cx.group("tensor", builds, reads=["BB", "SMALL"], writes=["ps3"])
                cx.op("vector", lambda v, c=c, hc=hc: v.tensor_copy(
                    out=bf.SCR2[:, c * P + hc * LORA_R: c * P + (hc + 1) * LORA_R],
                    in_=bf.PS3[:, 0:LORA_R]), reads=["ps3"], writes=["SCR2"])
            cx.op("sync", lambda s, b=b, c=c: s.dma_start(
                out=ap["s_da"][b, c, :, :], in_=bf.SCR[:, c * P:(c + 1) * P]),
                dma=True, reads=["SCR"], writes=["s_da"])
            cx.op("sync", lambda s, b=b, c=c: s.dma_start(
                out=ap["s_db"][b, c, :, :], in_=bf.SCR2[:, c * P:(c + 1) * P]),
                dma=True, reads=["SCR2"], writes=["s_db"])
        for (buf, bk, off) in ((bf.SCR, "SCR", 0), (bf.SCR2, "SCR2", P)):
            cx.op("vector", lambda v, buf=buf: v.tensor_copy(
                out=bf.LSCR[:, 0:P], in_=buf[:, 0:P]),
                reads=[bk], writes=["LSCR"])
            for c in range(1, NCHB):
                cx.op("vector", lambda v, buf=buf, c=c: v.tensor_add(
                    out=bf.LSCR[:, 0:P], in0=bf.LSCR[:, 0:P],
                    in1=buf[:, c * P:(c + 1) * P]),
                    reads=[bk, "LSCR"], writes=["LSCR"])
            cx.op("sync", lambda s, b=b, off=off: s.dma_start(
                out=ap["g_sums_in"][:, b * 2 * P + off: b * 2 * P + off + P],
                in_=bf.LSCR[:, 0:P]),
                dma=True, reads=["LSCR"], writes=["g_sums_in"])
        cx.barrier()
    cx.op("gpsimd", lambda g: g.collective_compute(
        "AllGather", ALU.bypass, replica_groups=[list(range(NC))],
        ins=[ap["g_sums_in"].opt()], outs=[ap["g_sums_out"].opt()]),
        coll=True, reads=["g_sums_in"], writes=["g_sums_out"])
    for b in range(B):
        sxtr = ap["s_xt"][b].rearrange("c p t -> p c t")
        accA = bf.CB[:, 2 * D + 2 * P:2 * D + 3 * P]
        accB = bf.CB[:, 2 * D + 3 * P:2 * D + 4 * P]
        for (dst, off) in ((accA, 0), (accB, P)):
            for r in range(NC):
                cx.op("sync", lambda s, r=r, b=b, off=off: s.dma_start(
                    out=bf.CL[:, r * P:(r + 1) * P],
                    in_=ap["g_sums_out"][r, :, b * 2 * P + off: b * 2 * P + off + P]),
                    dma=True, reads=["g_sums_out"], writes=["CL"])
            cx.op("vector", lambda v: v.tensor_scalar(
                out=bf.LSCR[:, 0:P], in0=bf.CL[:, 0:P], scalar1=bf.PREFW[:, 0:1],
                scalar2=None, op0=ALU.mult),
                reads=["CL", "SMALL"], writes=["LSCR"])
            for r in range(1, NC):
                cx.op("vector", lambda v, r=r: v.scalar_tensor_tensor(
                    out=bf.LSCR[:, 0:P], in0=bf.CL[:, r * P:(r + 1) * P],
                    scalar=bf.PREFW[:, r:r + 1], op0=ALU.mult, op1=ALU.add,
                    in1=bf.LSCR[:, 0:P]),
                    reads=["CL", "SMALL", "LSCR"], writes=["LSCR"])
            cx.op("vector", lambda v, dst=dst: v.tensor_copy(out=dst,
                                                             in_=bf.LSCR[:, 0:P]),
                  reads=["LSCR"], writes=["CB"])
        for c in range(NCHB):
            cx.op("vector", lambda v: v.scalar_tensor_tensor(
                out=bf.LSCR[:, 0:P], in0=accA, scalar=0.0, op0=ALU.add,
                op1=ALU.mult, in1=accA, accum_out=bf.STAT[:, 8:9]),
                reads=["CB"], writes=["STAT", "LSCR"])
            cx.op("vector", lambda v: v.scalar_tensor_tensor(
                out=bf.LSCR[:, 0:P], in0=accB, scalar=0.0, op0=ALU.add,
                op1=ALU.mult, in1=accB, accum_out=bf.STAT[:, 9:10]),
                reads=["CB"], writes=["STAT", "LSCR"])
            cx.op("vector", lambda v: v.tensor_add(out=bf.STAT[:, 8:9],
                                                   in0=bf.STAT[:, 8:9],
                                                   in1=bf.STAT[:, 9:10]),
                  reads=["STAT"], writes=["STAT"])
            cx.op("tensor", lambda t: t.matmul(bf.PT[0:1, 0:1], bf.STAT[:, 8:9],
                                               bf.ONESC, start=True, stop=True),
                  reads=["STAT", "SMALL"], writes=["pt"])
            cx.op("vector", lambda v: v.tensor_copy(out=bf.STAT[0:1, 10:11],
                                                    in_=bf.PT[0:1, 0:1]),
                  reads=["pt"], writes=["STAT"])
            cx.op("scalar", lambda s: s.activation(
                out=bf.STAT[0:1, 10:11], in_=bf.STAT[0:1, 10:11], func=AF.Sqrt,
                scale=TTT_LR * TTT_LR), reads=["STAT"], writes=["STAT"])
            cx.op("vector", lambda v: v.tensor_scalar(
                out=bf.STAT[0:1, 10:11], in0=bf.STAT[0:1, 10:11], scalar1=1e-6,
                scalar2=None, op0=ALU.add), reads=["STAT"], writes=["STAT"])
            cx.op("vector", lambda v: v.reciprocal(out=bf.STAT[0:1, 10:11],
                                                   in_=bf.STAT[0:1, 10:11]),
                  reads=["STAT"], writes=["STAT"])
            cx.op("vector", lambda v: v.tensor_scalar(
                out=bf.STAT[0:1, 10:11], in0=bf.STAT[0:1, 10:11], scalar1=1.0,
                scalar2=-TTT_LR, op0=ALU.min, op1=ALU.mult),
                reads=["STAT"], writes=["STAT"])
            bcast_scalar(cx, bf, bf.STAT[0:1, 10:11], bf.STAT[:, 11:12],
                         "STAT", "STAT")
            cx.op("vector", lambda v: v.scalar_tensor_tensor(
                out=bf.SCR[:, 0:P], in0=accA, scalar=bf.STAT[:, 11:12],
                op0=ALU.mult, op1=ALU.add, in1=bf.IA),
                reads=["CB", "STAT", "SMALL"], writes=["SCR"])
            cx.op("vector", lambda v: v.scalar_tensor_tensor(
                out=bf.SCR[:, P:2 * P], in0=accB, scalar=bf.STAT[:, 11:12],
                op0=ALU.mult, op1=ALU.add, in1=bf.IBT),
                reads=["CB", "STAT", "SMALL"], writes=["SCR"])
            for hc in range(DCH):
                tr128(cx, bf, bf.SCR[:, hc * LORA_R:(hc + 1) * LORA_R],
                      bf.SCR2[0:LORA_R, hc * P:(hc + 1) * P], "SCR", "SCR2",
                      nout=LORA_R)
            for j in (2 * c, 2 * c + 1):
                tj = b * NPB + j
                cx.op("sync", lambda s, j=j: s.dma_start(
                    out=bf.LNT.rearrange("p (c q) -> p c q", q=P),
                    in_=sxtr[:, :, j * P:(j + 1) * P]),
                    dma=True, reads=["s_xt"], writes=["LNT"])
                builds = [(lambda t, dc=dc: t.matmul(
                    bf.PS0[:, 0:LORA_R], bf.LNT[:, dc * P:(dc + 1) * P],
                    bf.SCR[:, P + dc * LORA_R:P + (dc + 1) * LORA_R],
                    start=(dc == 0), stop=(dc == DCH - 1))) for dc in range(DCH)]
                cx.group("tensor", builds, reads=["LNT", "SCR"], writes=["ps0"])
                cx.op("vector", lambda v: v.tensor_copy(out=bf.LSCR[:, 0:LORA_R],
                                                        in_=bf.PS0[:, 0:LORA_R]),
                      reads=["ps0"], writes=["LSCR"])
                tr128(cx, bf, bf.LSCR[:, 0:LORA_R], bf.LSCR[0:LORA_R, 256:256 + P],
                      "LSCR", "LSCR", nout=LORA_R)
                for dh in range(2):
                    cx.op("tensor", lambda t, dh=dh: t.matmul(
                        bf.PSW[:, dh * 512:(dh + 1) * 512],
                        bf.LSCR[0:LORA_R, 256:256 + P],
                        bf.SCR2[0:LORA_R, dh * 512:(dh + 1) * 512],
                        start=True, stop=True),
                        reads=["LSCR", "SCR2"], writes=["psw"])
                    cx.op("vector", lambda v, tj=tj, dh=dh: v.scalar_tensor_tensor(
                        out=bf.xtile(tj)[:, dh * 512:(dh + 1) * 512],
                        in0=bf.PSW[:, dh * 512:(dh + 1) * 512], scalar=SCALING,
                        op0=ALU.mult, op1=ALU.add,
                        in1=bf.xtile(tj)[:, dh * 512:(dh + 1) * 512]),
                        reads=["psw", "X"], writes=["X"])
            if c < NCHB - 1:
                for (dst, nm) in ((accA, "s_da"), (accB, "s_db")):
                    cx.op("sync", lambda s, b=b, c=c, nm=nm: s.dma_start(
                        out=bf.LSCR[:, 512:512 + P], in_=ap[nm][b, c, :, :]),
                        dma=True, reads=[nm], writes=["LSCR"])
                    cx.op("vector", lambda v, dst=dst: v.tensor_add(
                        out=dst, in0=dst, in1=bf.LSCR[:, 512:512 + P]),
                        reads=["LSCR", "CB"], writes=["CB"])
        cx.barrier()


def emit_qffn(nc, cx, ap, bf):
    XC16 = bf.BA[:, :].bitcast(BF16)
    W16 = bf.BB[:, :].bitcast(BF16)
    CT2 = bf.CB[:, :].bitcast(BF16)
    cx.op("sync", lambda s: s.dma_start(out=bf.SCR3[:, 0:HCH], in_=ap["q_b1"]),
          dma=True, writes=["SCR3"])
    cx.op("sync", lambda s: s.dma_start(out=bf.SCR3[:, 32:32 + DCH],
                                        in_=ap["q_b2"]),
          dma=True, writes=["SCR3"])
    for t in range(NPT):
        ln_tile(cx, bf, bf.xtile(t), bf.LSCR[:, :], "X", "LSCR")
        cx.op("vector", lambda v, t=t: v.reduce_max(
            bf.SCR[:, t:t + 1], bf.LSCR[:, :], axis=AXL.X),
            reads=["LSCR"], writes=["SCR"])
        cx.op("sync", lambda s, t=t: s.dma_start(
            out=ap["s_ln"][t, :, :], in_=bf.LSCR[:, :]),
            dma=True, reads=["LSCR"], writes=["s_ln"])
    cx.op("vector", lambda v: v.reduce_max(bf.STAT[:, 8:9], bf.SCR[:, 0:NPT],
                                           axis=AXL.X),
          reads=["SCR"], writes=["STAT"])
    glob_max_allgather(nc, cx, ap, bf, slot=0)
    cx.op("vector", lambda v: v.tensor_scalar(
        out=bf.SA1, in0=bf.SW1, scalar1=bf.STAT[:, 14:15], scalar2=None,
        op0=ALU.mult), reads=["SMALL", "STAT"], writes=["SMALL"])
    cx.op("vector", lambda v: v.tensor_copy(out=bf.SA1R, in_=bf.SA1),
          reads=["SMALL"], writes=["SMALL"])
    cx.op("vector", lambda v: v.reciprocal(out=bf.SA1R, in_=bf.SA1R),
          reads=["SMALL"], writes=["SMALL"])
    cx.op("vector", lambda v: v.tensor_mul(out=bf.BQ1, in0=bf.SCR3[:, 0:HCH],
                                           in1=bf.SA1R),
          reads=["SCR3", "SMALL"], writes=["SMALL"])
    cx.op("vector", lambda v: v.tensor_scalar(out=bf.BQ1, in0=bf.BQ1,
                                              scalar1=C23, scalar2=-C23,
                                              op0=ALU.add, op1=ALU.add),
          reads=["SMALL"], writes=["SMALL"])
    cx.op("vector", lambda v: v.tensor_mul(out=bf.BQ1, in0=bf.BQ1, in1=bf.SA1),
          reads=["SMALL"], writes=["SMALL"])
    cx.op("vector", lambda v: v.memset(bf.STAT[:, 0:1], 0.0), writes=["STAT"])
    for b in range(B):
        for j in range(NPB):
            t = b * NPB + j
            cx.op("sync", lambda s, t=t: s.dma_start(
                out=bf.LSCR[:, :], in_=ap["s_ln"][t, :, :]),
                dma=True, reads=["s_ln"], writes=["LSCR"])
            cx.op("vector", lambda v: v.tensor_scalar(
                out=bf.SCR[:, 0:D], in0=bf.LSCR[:, :], scalar1=bf.STAT[:, 11:12],
                scalar2=-1.0, op0=ALU.mult, op1=ALU.max),
                reads=["LSCR", "STAT"], writes=["SCR"])
            cx.op("vector", lambda v: v.tensor_scalar(
                out=bf.F16T[:, 0:D], in0=bf.SCR[:, 0:D], scalar1=1536.0,
                scalar2=None, op0=ALU.add),
                reads=["SCR"], writes=["SCR2"])
            cx.op("vector", lambda v: v.tensor_scalar(
                out=bf.SCR[:, 0:512].bitcast(BF16)[:, 0:D], in0=bf.F16T[:, 0:D],
                scalar1=-1536.0, scalar2=0.0, op0=ALU.add, op1=ALU.max),
                reads=["SCR2"], writes=["SCR"])
            cx.op("sync", lambda s, j=j: s.dma_start(
                out=ap["s_xc"][j * P:(j + 1) * P, :],
                in_=bf.SCR[:, 0:512].bitcast(BF16)[:, 0:D]),
                dma=True, reads=["SCR"], writes=["s_xc"])
        cx.barrier()
        for dc in range(DCH):
            cx.op("sync", lambda s, dc=dc: s.dma_start(
                out=XC16[:, dc * SEQ_PER:(dc + 1) * SEQ_PER],
                in_=ap["s_xc"][:, dc * P:(dc + 1) * P], transpose=True),
                dma=True, reads=["s_xc"], writes=["BA"])
        def load_w1q(hc):
            buf = hc % 2
            cx.op("sync", lambda s, hc=hc, buf=buf: s.dma_start(
                out=W16[:, buf * 1024:(buf + 1) * 1024],
                in_=ap["s_q1ctl"][hc]),
                dma=True, reads=["s_q1ctl"], writes=[f"BBq{buf}"])
        load_w1q(0)
        load_w1q(1)
        for hc in range(HCH):
            buf = hc % 2
            for j in range(NPB):
                builds = [(lambda t, dc=dc, j=j, buf=buf: t.matmul(
                    bf.PSW[:, j * P:(j + 1) * P],
                    W16[:, buf * 1024 + dc * P:buf * 1024 + (dc + 1) * P],
                    XC16[:, dc * SEQ_PER + j * P: dc * SEQ_PER + (j + 1) * P],
                    start=(dc == 0), stop=(dc == DCH - 1))) for dc in range(DCH)]
                cx.group("tensor", builds, reads=[f"BBq{buf}", "BA"],
                         writes=["psw" if j < 4 else "pswb"])
            for j in range(NPB):
                cx.op("scalar", lambda s, hc=hc, j=j: s.activation(
                    out=bf.LSCR[:, j * P:(j + 1) * P],
                    in_=bf.PSW[:, j * P:(j + 1) * P], func=AF.Relu,
                    scale=bf.SA1[:, hc:hc + 1], bias=bf.BQ1[:, hc:hc + 1]),
                    reads=["psw" if j < 4 else "pswb", "SMALL"], writes=["LSCR"])
            if hc + 2 < HCH:
                load_w1q(hc + 2)
            cx.op("vector", lambda v: v.reduce_max(
                bf.STAT[:, 8:9], bf.LSCR[:, 0:SEQ_PER], axis=AXL.X),
                reads=["LSCR"], writes=["STAT"])
            cx.op("vector", lambda v: v.tensor_max(
                out=bf.STAT[:, 0:1], in0=bf.STAT[:, 0:1], in1=bf.STAT[:, 8:9]),
                reads=["STAT"], writes=["STAT"])
            cx.op("sync", lambda s, b=b, hc=hc: s.dma_start(
                out=ap["s_hl"][hc, :, b * SEQ_PER:(b + 1) * SEQ_PER],
                in_=bf.LSCR[:, 0:SEQ_PER]),
                dma=True, reads=["LSCR"], writes=["s_hl"])
        cx.barrier()
    cx.op("vector", lambda v: v.tensor_copy(out=bf.STAT[:, 8:9],
                                            in_=bf.STAT[:, 0:1]),
          reads=["STAT"], writes=["STAT"])
    glob_max_allgather(nc, cx, ap, bf, slot=1)
    cx.op("vector", lambda v: v.tensor_scalar(
        out=bf.SCR[:, 0:DCH], in0=bf.SW2, scalar1=bf.STAT[:, 14:15],
        scalar2=None, op0=ALU.mult), reads=["SMALL", "STAT"], writes=["SCR"])
    cx.op("vector", lambda v: v.tensor_copy(out=bf.SCR[:, 8:8 + DCH],
                                            in_=bf.SCR[:, 0:DCH]),
          reads=["SCR"], writes=["SCR"])
    cx.op("vector", lambda v: v.reciprocal(out=bf.SCR[:, 8:8 + DCH],
                                           in_=bf.SCR[:, 8:8 + DCH]),
          reads=["SCR"], writes=["SCR"])
    cx.op("vector", lambda v: v.tensor_mul(out=bf.SCR[:, 16:16 + DCH],
                                           in0=bf.SCR3[:, 32:32 + DCH],
                                           in1=bf.SCR[:, 8:8 + DCH]),
          reads=["SCR3", "SCR"], writes=["SCR"])
    cx.op("vector", lambda v: v.tensor_scalar(out=bf.SCR[:, 16:16 + DCH],
                                              in0=bf.SCR[:, 16:16 + DCH],
                                              scalar1=C23, scalar2=-C23,
                                              op0=ALU.add, op1=ALU.add),
          reads=["SCR"], writes=["SCR"])
    cx.op("vector", lambda v: v.tensor_mul(out=bf.SCR[:, 16:16 + DCH],
                                           in0=bf.SCR[:, 16:16 + DCH],
                                           in1=bf.SCR[:, 0:DCH]),
          reads=["SCR"], writes=["SCR"])
    for dc in range(DCH):
        rep_row_tile(cx, bf, bf.SCR[:, dc:dc + 1], bf.CL[:, dc * P:(dc + 1) * P],
                     "SCR", "CL")
        rep_row_tile(cx, bf, bf.SCR[:, 16 + dc:17 + dc],
                     bf.CL[:, D + dc * P:D + (dc + 1) * P], "SCR", "CL")
    for dh in range(2):
        for hc in range(HCH):
            cx.op("sync", lambda s, dh=dh, hc=hc: s.dma_start(
                out=W16[:, hc * 512:(hc + 1) * 512],
                in_=ap["s_q2ct"][hc, :, dh * 512:(dh + 1) * 512]),
                dma=True, reads=["s_q2ct"], writes=[f"BBq{hc % 8}"])
        for t in range(NPT):
            for hc in range(HCH):
                cx.op("sync", lambda s, t=t, hc=hc: s.dma_start(
                    out=bf.LSCR[:, 0:P],
                    in_=ap["s_hl"][hc, :, t * P:(t + 1) * P]),
                    dma=True, reads=["s_hl"], writes=["LSCR"])
                cx.op("vector", lambda v, hc=hc: v.tensor_scalar(
                    out=bf.F16T[:, 0:P], in0=bf.LSCR[:, 0:P],
                    scalar1=bf.STAT[:, 11:12], scalar2=1536.0,
                    op0=ALU.mult, op1=ALU.add),
                    reads=["LSCR", "STAT"], writes=["SCR2"])
                cx.op("vector", lambda v, hc=hc: v.tensor_scalar(
                    out=CT2[:, hc * P:(hc + 1) * P], in0=bf.F16T[:, 0:P],
                    scalar1=-1536.0, scalar2=None, op0=ALU.add),
                    reads=["SCR2"], writes=["CB"])
            builds = [(lambda t2, hc=hc: t2.matmul(
                bf.PSW[:, 0:512], CT2[:, hc * P:(hc + 1) * P],
                W16[:, hc * 512:(hc + 1) * 512],
                start=(hc == 0), stop=(hc == HCH - 1))) for hc in range(HCH)]
            cx.group("tensor", builds,
                     reads=["CB"] + [f"BBq{k}" for k in range(8)],
                     writes=["psw"])
            sl = slice(dh * 512, (dh + 1) * 512)
            cx.op("vector", lambda v, sl=sl: v.tensor_mul(
                out=bf.SCR2[:, 0:512], in0=bf.PSW[:, 0:512],
                in1=bf.CL[:, sl]),
                reads=["psw", "CL"], writes=["SCR2"])
            cx.op("vector", lambda v, sl=sl: v.tensor_add(
                out=bf.SCR2[:, 0:512], in0=bf.SCR2[:, 0:512],
                in1=bf.CL[:, D + sl.start:D + sl.stop]),
                reads=["SCR2", "CL"], writes=["SCR2"])
            cx.op("vector", lambda v, t=t, sl=sl: v.tensor_add(
                out=bf.xtile(t)[:, sl], in0=bf.xtile(t)[:, sl],
                in1=bf.SCR2[:, 0:512]),
                reads=["SCR2", "X"], writes=["X"])
        cx.barrier()


def glob_max_allgather(nc, cx, ap, bf, slot):
    cx.op("tensor", lambda t: t.transpose(bf.PT[0:1, 0:P], bf.STAT[:, 8:9],
                                          bf.IDENT[:, :]),
          reads=["STAT", "IDENT"], writes=["pt"])
    cx.op("vector", lambda v: v.tensor_copy(out=bf.LSCR[0:1, 0:P],
                                            in_=bf.PT[0:1, 0:P]),
          reads=["pt"], writes=["LSCR"])
    cx.op("vector", lambda v: v.reduce_max(bf.STAT[0:1, 9:10],
                                           bf.LSCR[0:1, 0:P], axis=AXL.X),
          reads=["LSCR"], writes=["STAT"])
    gi, go = f"g_max_in{slot}", f"g_max_out{slot}"
    cx.op("sync", lambda s: s.dma_start(
        out=ap[gi][0:1, 0:1], in_=bf.STAT[0:1, 9:10]),
        dma=True, reads=["STAT"], writes=[gi])
    cx.op("gpsimd", lambda g: g.collective_compute(
        "AllGather", ALU.bypass, replica_groups=[list(range(NC))],
        ins=[ap[gi].opt()], outs=[ap[go].opt()]),
        coll=True, reads=[gi], writes=[go])
    cx.op("sync", lambda s: s.dma_start(
        out=bf.LSCR[0:1, 0:NC], in_=ap[go].rearrange("n o -> (n o)")),
        dma=True, reads=[go], writes=["LSCR"])
    cx.op("vector", lambda v: v.reduce_max(bf.STAT[0:1, 9:10],
                                           bf.LSCR[0:1, 0:NC], axis=AXL.X),
          reads=["LSCR"], writes=["STAT"])
    cx.op("vector", lambda v: v.tensor_scalar(
        out=bf.STAT[0:1, 9:10], in0=bf.STAT[0:1, 9:10], scalar1=1e-8,
        scalar2=1.0 / 255.0, op0=ALU.max, op1=ALU.mult),
        reads=["STAT"], writes=["STAT"])
    cx.op("vector", lambda v: v.reciprocal(out=bf.STAT[0:1, 10:11],
                                           in_=bf.STAT[0:1, 9:10]),
          reads=["STAT"], writes=["STAT"])
    bcast_scalar(cx, bf, bf.STAT[0:1, 10:11], bf.STAT[:, 11:12], "STAT", "STAT")
    bcast_scalar(cx, bf, bf.STAT[0:1, 9:10], bf.STAT[:, 14:15], "STAT", "STAT")


def emit_attn_b1(nc, cx, ap, bf):
    for b in range(B):
        for j in range(NPB):
            tj = b * NPB + j
            ln_tile(cx, bf, bf.xtile(tj), bf.LNT[:, :], "X", "LNT")
            for dc in range(DCH):
                tr128(cx, bf, bf.LNT[:, dc * P:(dc + 1) * P],
                      bf.BA[:, dc * SEQ_PER + j * P: dc * SEQ_PER + (j + 1) * P],
                      "LNT", "BA")
        proj_fm_norm(nc, cx, ap, bf, b, "c_Wkt", "g_kt")
        for j in range(NPB):
            for dh in range(2):
                for dc in range(DCH):
                    cx.op("sync", lambda s, dc=dc, dh=dh: s.dma_start(
                        out=bf.WTA[:, 0:512],
                        in_=ap["c_Wvt"][dc * P:(dc + 1) * P,
                                        dh * 512:(dh + 1) * 512]),
                        dma=True, writes=["WTA"])
                    cx.op("tensor", lambda t, dc=dc, j=j: t.matmul(
                        bf.PSW[:, 0:512],
                        bf.BA[:, dc * SEQ_PER + j * P: dc * SEQ_PER + (j + 1) * P],
                        bf.WTA[:, 0:512],
                        start=(dc == 0), stop=(dc == DCH - 1)),
                        reads=["BA", "WTA"], writes=["psw"])
                cx.op("vector", lambda v, dh=dh: v.tensor_copy(
                    out=bf.LSCR[:, dh * 512:(dh + 1) * 512], in_=bf.PSW[:, 0:512]),
                    reads=["psw"], writes=["LSCR"])
            cx.op("sync", lambda s, b=b, j=j: s.dma_start(
                out=ap["g_v"][b * SEQ_PER + j * P: b * SEQ_PER + (j + 1) * P, :],
                in_=bf.LSCR[:, :]),
                dma=True, reads=["LSCR"], writes=["g_v"])
        proj_fm_norm(nc, cx, ap, bf, b, "c_Wqt", None)
        for dc in range(DCH):
            cx.op("sync", lambda s, dc=dc, b=b: s.dma_start(
                out=ap["s_qt"][dc, :, :] if b == 1 else
                ap["s_xnt"][0, dc, :, 0:SEQ_PER],
                in_=bf.BB[:, dc * SEQ_PER:(dc + 1) * SEQ_PER]),
                dma=True, reads=["BB"],
                writes=["s_qt" if b == 1 else "s_xnt"])
        cx.barrier()
    cx.op("gpsimd", lambda g: g.collective_compute(
        "AllGather", ALU.bypass, replica_groups=[list(range(NC))],
        ins=[ap["g_kt"].opt()], outs=[ap["g_kt_all"].opt()]),
        coll=True, reads=["g_kt"], writes=["g_kt_all"])
    cx.op("gpsimd", lambda g: g.collective_compute(
        "AllGather", ALU.bypass, replica_groups=[list(range(NC))],
        ins=[ap["g_v"].opt()], outs=[ap["g_v_all"].opt()]),
        coll=True, reads=["g_v"], writes=["g_v_all"])
    NKC = SEQ // P
    LCH = SEQ_PER // P
    for b in range(B):
        src = "s_xnt" if b == 0 else "s_qt"
        for dc in range(DCH):
            cx.op("sync", lambda s, dc=dc, src=src: s.dma_start(
                out=bf.BA[:, dc * SEQ_PER:(dc + 1) * SEQ_PER],
                in_=(ap["s_xnt"][0, dc, :, 0:SEQ_PER] if src == "s_xnt" else
                     ap["s_qt"][dc, :, :])),
                dma=True, reads=[src], writes=["BA"])
        for buf in range(2):
            vk_ = "VRKa" if buf == 0 else "VRKb"
            for lc in range(LCH):
                cx.op("vector", lambda v, buf=buf, lc=lc: v.tensor_copy(
                    out=bf.VRK[:, buf * 520 + lc * 65 + DHEAD:
                               buf * 520 + lc * 65 + DHEAD + 1],
                    in_=bf.ONESC),
                    reads=["SMALL"], writes=[vk_])
        for h in range(NHEAD):
            oc, pb = h // 2, (h % 2) * DHEAD

            def load_rank(r, b=b, h=h, oc=oc, pb=pb):
                buf = r % 2
                kk_ = "KRKa" if buf == 0 else "KRKb"
                vk_ = "VRKa" if buf == 0 else "VRKb"
                cx.op("sync", lambda s, r=r, buf=buf: s.dma_start(
                    out=bf.KRK[0:DHEAD, buf * SEQ_PER:(buf + 1) * SEQ_PER],
                    in_=ap["g_kt_all"][r, oc * P + pb: oc * P + pb + DHEAD,
                                       b * SEQ_PER:(b + 1) * SEQ_PER]),
                    dma=True, reads=["g_kt_all"], writes=[kk_])
                vdst = bf.VRK[:, buf * 520: buf * 520 + LCH * 65].rearrange(
                    "p (c d) -> p c d", d=65)[:, :, 0:DHEAD]
                cx.op("sync", lambda s, r=r, vdst=vdst: s.dma_start(
                    out=vdst,
                    in_=ap["g_v_all"][r, b * SEQ_PER:(b + 1) * SEQ_PER,
                                      h * DHEAD:(h + 1) * DHEAD].rearrange(
                        "(c p) d -> p c d", p=P)),
                    dma=True, reads=["g_v_all"], writes=[vk_])
            if pb:
                cx.op("sync", lambda s, oc=oc: s.dma_start(
                    out=bf.SCR3[0:DHEAD, 0:SEQ_PER],
                    in_=bf.BA[DHEAD:P, oc * SEQ_PER:(oc + 1) * SEQ_PER]),
                    dma=True, reads=["BA"], writes=["SCR3"])
            load_rank(0)
            load_rank(1)
            nl = [2]
            QW = min(512, SEQ_PER)
            NQH = SEQ_PER // QW

            def flush_set(m, b=b, h=h):
                tk = [k for k in (m * 2, m * 2 + 1) if k < NKC]
                for kk in tk:
                    g = kk % 4
                    cx.op("scalar", lambda s, g=g: s.activation(
                        out=bf.CL[:, g * 1024:g * 1024 + SEQ_PER],
                        in_=bf.CL[:, g * 1024:g * 1024 + SEQ_PER],
                        func=AF.Ln, scale=-1.0, bias=1.0),
                        reads=["CL"], writes=["CL"])
                for kk in tk:
                    g = kk % 4
                    cx.op("scalar", lambda s, g=g: s.activation(
                        out=bf.CL[:, g * 1024:g * 1024 + SEQ_PER],
                        in_=bf.CL[:, g * 1024:g * 1024 + SEQ_PER],
                        func=AF.Sigmoid, scale=-0.5),
                        reads=["CL"], writes=["CL"])
                for kk in tk:
                    g = kk % 4
                    cx.op("vector", lambda v, g=g: v.tensor_mul(
                        out=bf.CB[:, g * 1024:g * 1024 + SEQ_PER],
                        in0=bf.CB[:, g * 1024:g * 1024 + SEQ_PER],
                        in1=bf.CL[:, g * 1024:g * 1024 + SEQ_PER]),
                        reads=["CB", "CL"], writes=["CB"])
                for kk in tk:
                    g = kk % 4
                    cx.op("scalar", lambda s, g=g: s.activation(
                        out=bf.CB[:, g * 1024:g * 1024 + SEQ_PER],
                        in_=bf.CB[:, g * 1024:g * 1024 + SEQ_PER],
                        func=AF.Arctan),
                        reads=["CB"], writes=["CB"])
                for kk in tk:
                    g = kk % 4
                    if kk % 2 == 1:
                        cx.op("vector", lambda v, g=g: v.tensor_scalar(
                            out=bf.CB[:, g * 1024:g * 1024 + SEQ_PER],
                            in0=bf.CB[:, g * 1024:g * 1024 + SEQ_PER],
                            scalar1=TWOPI, scalar2=0.5,
                            op0=ALU.mult, op1=ALU.add),
                            reads=["CB"], writes=["CB"])
                        for _ in range(4):
                            cx.op("vector", lambda v, g=g: v.tensor_mul(
                                out=bf.CB[:, g * 1024:g * 1024 + SEQ_PER],
                                in0=bf.CB[:, g * 1024:g * 1024 + SEQ_PER],
                                in1=bf.CB[:, g * 1024:g * 1024 + SEQ_PER]),
                                reads=["CB"], writes=["CB"])
                    else:
                        cx.op("scalar", lambda s, g=g: s.activation(
                            out=bf.CB[:, g * 1024:g * 1024 + SEQ_PER],
                            in_=bf.CB[:, g * 1024:g * 1024 + SEQ_PER],
                            func=AF.Ln, scale=TWOPI, bias=bf.HALFC),
                            reads=["CB", "SMALL"], writes=["CB"])
                        cx.op("scalar", lambda s, g=g: s.activation(
                            out=bf.CB[:, g * 1024:g * 1024 + SEQ_PER],
                            in_=bf.CB[:, g * 1024:g * 1024 + SEQ_PER],
                            func=AF.Exp, scale=16.0),
                            reads=["CB"], writes=["CB"])
                for kk in tk:
                    g = kk % 4
                    rr, ll = kk // LCH, kk % LCH
                    vbuf = rr % 2
                    vk_ = "VRKa" if vbuf == 0 else "VRKb"
                    v0 = vbuf * 520 + ll * 65
                    for qh in range(NQH):
                        ps = bf.PS2 if qh == 0 else bf.PS3
                        pk = "ps2" if qh == 0 else "ps3"
                        cx.op("tensor", lambda t, g=g, kk=kk, qh=qh, ps=ps,
                              v0=v0: t.matmul(
                                  ps[0:DHEAD + 1, 0:QW],
                                  bf.VRK[:, v0:v0 + DHEAD + 1],
                                  bf.CB[:, g * 1024 + qh * QW:
                                        g * 1024 + (qh + 1) * QW],
                                  start=(kk == 0), stop=(kk == NKC - 1)),
                              reads=["CB", vk_], writes=[pk])
                kcf = tk[-1]
                while nl[0] < NC and (nl[0] - 2) * LCH + LCH - 1 <= kcf:
                    load_rank(nl[0])
                    nl[0] += 1

            for kc in range(NKC):
                r, lc = kc // LCH, kc % LCH
                buf = r % 2
                g = kc % 4
                kk_ = "KRKa" if buf == 0 else "KRKb"
                odd = kc % 2
                for qh in range(NQH):
                    if pb:
                        rhs = bf.SCR3[0:DHEAD, qh * QW:(qh + 1) * QW]
                        rk = "SCR3"
                    else:
                        rhs = bf.BA[0:DHEAD, oc * SEQ_PER + qh * QW:
                                    oc * SEQ_PER + (qh + 1) * QW]
                        rk = "BA"
                    if odd:
                        dst = (bf.PS0 if qh == 0 else bf.PS1)[:, 0:QW]
                        pk = "ps0" if qh == 0 else "ps1"
                    else:
                        dst = bf.PSW[:, qh * QW:(qh + 1) * QW]
                        pk = "psw"
                    cx.op("tensor", lambda t, dst=dst, buf=buf, lc=lc,
                          rhs=rhs: t.matmul(
                        dst,
                        bf.KRK[0:DHEAD, buf * SEQ_PER + lc * P:
                               buf * SEQ_PER + (lc + 1) * P],
                        rhs, start=True, stop=True),
                        reads=[kk_, rk], writes=[pk])
                if odd:
                    for qh in range(NQH):
                        src = (bf.PS0 if qh == 0 else bf.PS1)[:, 0:QW]
                        pk = "ps0" if qh == 0 else "ps1"
                        cx.op("vector", lambda v, g=g, qh=qh, src=src:
                              v.tensor_scalar(
                            out=bf.CB[:, g * 1024 + qh * QW:
                                      g * 1024 + (qh + 1) * QW],
                            in0=src, scalar1=0.999, scalar2=-0.999,
                            op0=ALU.min, op1=ALU.max),
                            reads=[pk], writes=["CB"])
                else:
                    cx.op("vector", lambda v, g=g: v.tensor_scalar(
                        out=bf.CB[:, g * 1024:g * 1024 + SEQ_PER],
                        in0=bf.PSW[:, 0:SEQ_PER], scalar1=0.999, scalar2=-0.999,
                        op0=ALU.min, op1=ALU.max),
                        reads=["psw"], writes=["CB"])
                cx.op("vector", lambda v, g=g: v.tensor_mul(
                    out=bf.CL[:, g * 1024:g * 1024 + SEQ_PER],
                    in0=bf.CB[:, g * 1024:g * 1024 + SEQ_PER],
                    in1=bf.CB[:, g * 1024:g * 1024 + SEQ_PER]),
                    reads=["CB"], writes=["CL"])
                if kc % 2 == 1 and kc // 2 >= 1:
                    flush_set(kc // 2 - 1)
            flush_set(NKC // 2 - 1)
            finish_head(nc, cx, ap, bf, b, h)
        wo_residual(nc, cx, ap, bf, b)
        cx.barrier()


def proj_fm_norm(nc, cx, ap, bf, b, wname, store):
    QW = min(512, SEQ_PER)
    NQH = SEQ_PER // QW
    for oc in range(DCH):
        cx.op("sync", lambda s, oc=oc: s.dma_start(
            out=bf.WTA[:, :], in_=ap[wname][oc]),
            dma=True, writes=["WTA"])
        cx.op("sync", lambda s, oc=oc: s.dma_start(
            out=bf.WTB[:, 0:NHEAD], in_=ap["ehead"][oc, :, :]),
            dma=True, writes=["WTB"])
        for qh in range(NQH):
            builds = [(lambda t, dc=dc, oc=oc, qh=qh: t.matmul(
                bf.PSW[:, 0:QW], bf.WTA[:, dc * P:(dc + 1) * P],
                bf.BA[:, dc * SEQ_PER + qh * QW: dc * SEQ_PER + qh * QW + QW],
                start=(dc == 0), stop=(dc == DCH - 1))) for dc in range(DCH)]
            cx.group("tensor", builds, reads=["WTA", "BA"], writes=["psw"])
            cx.op("vector", lambda v, oc=oc, qh=qh: v.tensor_copy(
                out=bf.BB[:, oc * SEQ_PER + qh * QW: oc * SEQ_PER + qh * QW + QW],
                in_=bf.PSW[:, 0:QW]),
                reads=["psw"], writes=["BB"])
            cx.op("vector", lambda v, oc=oc, qh=qh: v.tensor_mul(
                out=bf.LSCR[:, 0:QW],
                in0=bf.BB[:, oc * SEQ_PER + qh * QW: oc * SEQ_PER + qh * QW + QW],
                in1=bf.BB[:, oc * SEQ_PER + qh * QW: oc * SEQ_PER + qh * QW + QW]),
                reads=["BB"], writes=["LSCR"])
            ps = bf.PS0 if qh == 0 else bf.PS1
            pk = "ps0" if qh == 0 else "ps1"
            cx.op("tensor", lambda t, oc=oc, ps=ps: t.matmul(
                ps[0:NHEAD, 0:QW], bf.WTB[:, 0:NHEAD], bf.LSCR[:, 0:QW],
                start=(oc == 0), stop=(oc == DCH - 1)),
                reads=["WTB", "LSCR"], writes=[pk])
    for qh in range(NQH):
        ps = bf.PS0 if qh == 0 else bf.PS1
        pk = "ps0" if qh == 0 else "ps1"
        cx.op("scalar", lambda s, ps=ps, qh=qh: s.activation(
            out=bf.SCR2[0:NHEAD, qh * QW:(qh + 1) * QW], in_=ps[0:NHEAD, 0:QW],
            func=AF.Sqrt), reads=[pk], writes=["SCR2"])
    cx.op("vector", lambda v: v.tensor_scalar(
        out=bf.SCR2[0:NHEAD, 0:SEQ_PER], in0=bf.SCR2[0:NHEAD, 0:SEQ_PER],
        scalar1=1e-6, scalar2=None, op0=ALU.max),
        reads=["SCR2"], writes=["SCR2"])
    cx.op("vector", lambda v: v.reciprocal(out=bf.SCR2[0:NHEAD, 0:SEQ_PER],
                                           in_=bf.SCR2[0:NHEAD, 0:SEQ_PER]),
          reads=["SCR2"], writes=["SCR2"])
    for oc in range(DCH):
        cx.op("sync", lambda s, oc=oc: s.dma_start(
            out=bf.WTB[0:NHEAD, 0:P], in_=ap["eheadt"][oc, :, :]),
            dma=True, writes=["WTB"])
        for qh in range(NQH):
            cx.op("tensor", lambda t, oc=oc, qh=qh: t.matmul(
                bf.PSW[:, 0:QW], bf.WTB[0:NHEAD, 0:P],
                bf.SCR2[0:NHEAD, qh * QW:(qh + 1) * QW], start=True, stop=True),
                reads=["WTB", "SCR2"], writes=["psw"])
            cx.op("vector", lambda v, oc=oc, qh=qh: v.tensor_mul(
                out=bf.BB[:, oc * SEQ_PER + qh * QW: oc * SEQ_PER + qh * QW + QW],
                in0=bf.BB[:, oc * SEQ_PER + qh * QW: oc * SEQ_PER + qh * QW + QW],
                in1=bf.PSW[:, 0:QW]),
                reads=["BB", "psw"], writes=["BB"])
    if store is not None:
        for dc in range(DCH):
            cx.op("sync", lambda s, dc=dc, b=b: s.dma_start(
                out=ap[store][dc * P:(dc + 1) * P, b * SEQ_PER:(b + 1) * SEQ_PER],
                in_=bf.BB[:, dc * SEQ_PER:(dc + 1) * SEQ_PER]),
                dma=True, reads=["BB"], writes=[store])


def finish_head(nc, cx, ap, bf, b, h):
    oc, pb = h // 2, (h % 2) * DHEAD
    QW = min(512, SEQ_PER)
    NQH = SEQ_PER // QW
    for qh in range(NQH):
        ps = bf.PS2 if qh == 0 else bf.PS3
        pk = "ps2" if qh == 0 else "ps3"
        cx.op("vector", lambda v, ps=ps, qh=qh, pk=pk: v.tensor_scalar(
            out=bf.LSCR[0:1, qh * QW:(qh + 1) * QW],
            in0=ps[DHEAD:DHEAD + 1, 0:QW], scalar1=1e-6, scalar2=None,
            op0=ALU.add), reads=[pk], writes=["LSCR"])
    cx.op("vector", lambda v: v.reciprocal(out=bf.LSCR[0:1, 0:SEQ_PER],
                                           in_=bf.LSCR[0:1, 0:SEQ_PER]),
          reads=["LSCR"], writes=["LSCR"])
    for qh in range(NQH):
        ps = bf.PS2 if qh == 0 else bf.PS3
        pk = "ps2" if qh == 0 else "ps3"
        cx.op("tensor", lambda t, qh=qh: t.matmul(
            bf.PT[0:DHEAD, 0:QW], bf.ONESR[0:1, 0:DHEAD],
            bf.LSCR[0:1, qh * QW:(qh + 1) * QW], start=True, stop=True),
            reads=["LSCR", "ONESR"], writes=["pt"])
        cx.op("vector", lambda v, qh=qh: v.tensor_copy(
            out=bf.SCR2[0:DHEAD, 0:QW], in_=bf.PT[0:DHEAD, 0:QW]),
            reads=["pt"], writes=["SCR2"])
        cx.op("vector", lambda v, ps=ps, qh=qh, oc=oc, pb=pb, pk=pk: v.tensor_mul(
            out=bf.BB[pb:pb + DHEAD,
                      oc * SEQ_PER + qh * QW: oc * SEQ_PER + qh * QW + QW],
            in0=ps[0:DHEAD, 0:QW], in1=bf.SCR2[0:DHEAD, 0:QW]),
            reads=[pk, "SCR2"], writes=["BB"])


def wo_residual(nc, cx, ap, bf, b):
    cx.op("sync", lambda s: s.dma_start(out=bf.CL[:, 0:D], in_=ap["c_bo"]),
          dma=True, writes=["CL"])
    for oc in range(DCH):
        cx.op("sync", lambda s, oc=oc: s.dma_start(
            out=bf.BA[:, oc * D:(oc + 1) * D], in_=ap["c_Wot"][oc]),
            dma=True, writes=["BA"])
    for j in range(NPB):
        tj = b * NPB + j
        for dh in range(2):
            for oc in range(DCH):
                cx.op("tensor", lambda t, oc=oc, j=j, dh=dh: t.matmul(
                    bf.PSW[:, 0:512],
                    bf.BB[:, oc * SEQ_PER + j * P: oc * SEQ_PER + (j + 1) * P],
                    bf.BA[:, oc * D + dh * 512:oc * D + dh * 512 + 512],
                    start=(oc == 0), stop=(oc == DCH - 1)),
                    reads=["BB", "BA"], writes=["psw"])
            cx.op("vector", lambda v, dh=dh: v.tensor_add(
                out=bf.SCR2[:, 0:512], in0=bf.PSW[:, 0:512],
                in1=bf.CL[:, dh * 512:(dh + 1) * 512]),
                reads=["psw", "CL"], writes=["SCR2"])
            cx.op("vector", lambda v, tj=tj, dh=dh: v.tensor_add(
                out=bf.xtile(tj)[:, dh * 512:(dh + 1) * 512],
                in0=bf.xtile(tj)[:, dh * 512:(dh + 1) * 512],
                in1=bf.SCR2[:, 0:512]),
                reads=["SCR2", "X"], writes=["X"])


def _slab(wt):
    DCH_ = wt.shape[0] // 128
    w4 = wt.reshape(DCH_, 128, DCH_, 128)
    return np.ascontiguousarray(w4.transpose(2, 1, 0, 3).reshape(
        DCH_, 128, wt.shape[1]))


def host_inputs(full, seq=None):
    if seq is None:
        seq = full["L0_attn_a_x"].shape[1]
    set_dims(seq)
    x = np.ascontiguousarray(full["L0_attn_a_x"], dtype=np.float32)
    qstart = seq - QBLK
    rows = np.arange(QBLK) + qstart
    thr = rows - (seq - CTX)
    masks = np.where(np.arange(N_SINKS)[None, :] <= thr[:, None], 0.0, NEG)
    maskw = np.where((np.arange(WINDOW)[None, :] + N_SINKS) <= thr[:, None],
                     0.0, NEG)
    addmask = (thr >= 0).astype(np.float32)[:, None]
    eh = np.zeros((DCH, P, NHEAD), np.float32)
    eht = np.zeros((DCH, NHEAD, P), np.float32)
    for oc in range(DCH):
        for pp in range(P):
            hh = (oc * P + pp) // DHEAD
            eh[oc, pp, hh] = 1.0
            eht[oc, hh, pp] = 1.0
    base = {
        "aa_blk": x[:, qstart:, :],
        "aa_sink": x[:, 0:N_SINKS, :],
        "aa_Wq": full["L0_attn_a_Wq"], "aa_Wk": full["L0_attn_a_Wk"],
        "aa_Wv": full["L0_attn_a_Wv"], "aa_Wo": full["L0_attn_a_Wo"],
        "aa_maskw": maskw, "aa_masks": masks,
        "f0_w1t": full["L0_ffn_w1"].T, "f0_w1": full["L0_ffn_w1"],
        "f0_b1": full["L0_ffn_b1"].reshape(HCH, P).T,
        "f0_w2t": full["L0_ffn_w2"].T, "f0_w2": full["L0_ffn_w2"],
        "f0_b2": np.tile(full["L0_ffn_b2"][None, :], (P, 1)),
        "t_Wbt": full["L1_attn_a_Wb"].T,
        "t_Wpt": full["L1_attn_a_Wp"].T, "t_Wp": full["L1_attn_a_Wp"],
        "t_iA": full["L1_attn_a_init_A"],
        "t_iB_t": full["L1_attn_a_init_B"].T,
        "t_gamma": np.tile(full["L1_attn_a_ln_gamma"][None, :], (P, 1)),
        "t_beta": np.tile(full["L1_attn_a_ln_beta"][None, :], (P, 1)),
        "q_w1": full["L1_ffn_w1"],
        "q_b1": full["L1_ffn_b1"].reshape(HCH, P).T,
        "q_w2": full["L1_ffn_w2"],
        "q_b2": full["L1_ffn_b2"].reshape(DCH, P).T,
        "c_Wqt": _slab(full["L1_attn_b_Wq"].T),
        "c_Wkt": _slab(full["L1_attn_b_Wk"].T),
        "c_Wvt": full["L1_attn_b_Wv"].T,
        "c_Wot": np.ascontiguousarray(
            full["L1_attn_b_Wo"].T).reshape(DCH, P, D),
        "c_bo": np.tile(full["L1_attn_b_bo"][None, :], (P, 1)),
        "ehead": eh, "eheadt": eht,
        "ident": np.eye(P, dtype=np.float32),
        "ones_col": np.ones((P, 1), np.float32),
        "ones_row": np.ones((1, P), np.float32),
        "consts": np.tile(np.array([[LN_EPS, 0.5]], np.float32), (P, 1)),
    }
    base = {k: np.ascontiguousarray(v, dtype=np.float32) for k, v in base.items()}
    maps = []
    for i in range(NC):
        m = dict(base)
        m["x0"] = np.ascontiguousarray(x[:, i * SEQ_PER:(i + 1) * SEQ_PER, :])
        own = ((rows >= i * SEQ_PER) & (rows < (i + 1) * SEQ_PER))
        m["aa_addmask"] = np.ascontiguousarray(
            addmask * own[:, None].astype(np.float32), dtype=np.float32)
        oh = np.zeros((NC, 1), np.float32)
        if i + 1 < NC:
            oh[i + 1, 0] = 1.0
        m["onehot_next"] = oh
        pw = np.zeros((P, NC), np.float32)
        pw[:, :i] = 1.0
        m["prefw_rep"] = pw
        maps.append(m)
    return maps


def kernel(**inputs):
    maps = host_inputs(inputs, seq=inputs["L0_attn_a_x"].shape[1])
    nc = build_program()
    res = run_bass_kernel_spmd(nc, maps, core_ids=list(range(NC)))
    y = np.concatenate([r["y"] for r in res.results], axis=1)
    return np.ascontiguousarray(y, dtype=np.float32)
